# version 03
# baseline (speedup 1.0000x reference)
import numpy as np
import contextlib
import math
import sys
sys.path.insert(0, '/opt/trn_rl_repo')
import concourse.bass as bass
import concourse.mybir as mybir
import bass_rust
from concourse.bass_utils import run_bass_kernel_spmd


F32 = mybir.dt.float32; BF16 = mybir.dt.bfloat16
I32 = mybir.dt.int32; I16 = mybir.dt.int16; U16 = mybir.dt.uint16
AX = mybir.AxisListType; ALU = mybir.AluOpType; AF = mybir.ActivationFunctionType
ENGINES = ("sync", "scalar", "vector", "gpsimd", "tensor")
NT = 16
TOK = NT * 128
DM = 4096
NI = DM // 128
RS32 = 1.0 / math.sqrt(32.0)
NPIECE = 16

"""framework.

Ops are declared in global program order with explicit dependency edges
(producer op objects). Each op runs on one engine; semaphore wait targets are
computed automatically. Same-engine deps rely on program order (no wait
emitted) except when force=True.
"""

import sys






class Op:
    __slots__ = ("engine", "fn", "deps", "incs", "post", "idx", "tag", "self_inc")

    def __init__(self, engine, fn, deps, incs, idx, tag, self_inc=False):
        self.engine = engine
        self.fn = fn
        self.deps = deps
        self.incs = incs
        self.post = {}
        self.idx = idx
        self.tag = tag
        self.self_inc = self_inc


class Prog:
    def __init__(self, nc):
        self.nc = nc
        self.ops = []
        self.sem_counts = {}
        self.sem_handles = {}
        self.chain_counts = {e: 0 for e in ENGINES}
        self._last = {}
        self.cur_engine = None
        self.cur_eng_obj = None
        self.barrier = None

    def set_barrier(self, deps, tag):
        bar = self.add("sync", lambda s: s.nop(), deps=deps,
                       inc=("s_bar", 1), tag=tag)
        self.barrier = bar
        return bar

    def mid(self, instr):
        e = self.cur_engine
        self.chain_counts[e] += 1
        h = self.sem_handles["ch_" + e]
        instr.then_inc(h, 1)
        self.cur_eng_obj.wait_ge(h, self.chain_counts[e])

    def add(self, engine, fn, deps=(), inc=None, tag="", self_inc=False):
        if inc is None:
            incs = []
        elif isinstance(inc, tuple):
            incs = [inc]
        else:
            incs = list(inc)
        if isinstance(deps, Op):
            deps = [deps]
        deps = [d for d in deps if d is not None]
        if not deps and self.barrier is not None:
            deps = [self.barrier]
        if engine in ("vector", "scalar", "gpsimd"):
            prev = self._last.get(engine)
            if prev is not None and prev.incs:
                deps = list(deps) + [prev]
        op = Op(engine, fn, deps, incs, len(self.ops), tag, self_inc)
        self.ops.append(op)
        self._last[engine] = op
        return op

    def _resolve(self):
        for op in self.ops:
            for (s, k) in op.incs:
                self.sem_counts[s] = self.sem_counts.get(s, 0) + k
                op.post[s] = self.sem_counts[s]

    def emit(self, es):
        self._resolve()
        for s in self.sem_counts:
            self.sem_handles[s] = es.enter_context(self.nc.semaphore(s))
        for e in ENGINES:
            self.sem_handles["ch_" + e] = es.enter_context(self.nc.semaphore("ch_" + e))
        nc = self.nc
        with nc.Block() as block:
            for engname in ENGINES:
                eng_ops = [o for o in self.ops if o.engine == engname]
                if not eng_ops:
                    continue

                def make(eng_ops=eng_ops, engname=engname):
                    def body(eng):
                        self.cur_engine = engname
                        self.cur_eng_obj = eng
                        for op in eng_ops:
                            waits = {}
                            for d in op.deps:
                                if not d.incs:
                                    raise RuntimeError(
                                        f"dep {d.tag} of {op.tag} has no inc")
                                s = d.incs[0][0]
                                v = d.post[s]
                                if waits.get(s, -1) < v:
                                    waits[s] = v
                            for s, v in waits.items():
                                eng.wait_ge(self.sem_handles[s], v)
                            ret = op.fn(eng)
                            if op.incs and not op.self_inc:
                                if ret is None:
                                    raise RuntimeError(
                                        f"op {op.tag} has incs but returned no instr")
                                last = ret[-1] if isinstance(ret, (list, tuple)) else ret
                                for (s, k) in op.incs:
                                    last.then_inc(self.sem_handles[s], k)
                        return None
                    return body

                getattr(block, engname)(make())
        return block



import sys







def alloc_ln(T, tag, slots):
    ln = {}
    for fld in ("sum", "ssq", "mneg", "varep", "r0", "rstd", "t1", "nt", "madj"):
        ln[fld] = T(f"ln{tag}_{fld}", [128, slots], F32)
    return ln


def emit_ln_smalls(P, A, ln, t, dep_stats, tag):
    def smalls_fn(g):
        P.mid(g.tensor_scalar(out=ln["mneg"][:, t:t+1], in0=ln["sum"][:, t:t+1],
                              scalar1=-1.0/DM, scalar2=None, op0=ALU.mult))
        P.mid(g.tensor_tensor(out=ln["t1"][:, t:t+1], in0=ln["mneg"][:, t:t+1],
                              in1=ln["mneg"][:, t:t+1], op=ALU.mult))
        P.mid(g.tensor_scalar(out=ln["varep"][:, t:t+1], in0=ln["ssq"][:, t:t+1],
                              scalar1=1.0/DM, scalar2=1e-5, op0=ALU.mult, op1=ALU.add))
        return g.tensor_tensor(out=ln["varep"][:, t:t+1], in0=ln["varep"][:, t:t+1],
                               in1=ln["t1"][:, t:t+1], op=ALU.subtract)
    smalls = A("gpsimd", smalls_fn, deps=dep_stats, inc=("s_g", 1), tag=f"sm{tag}{t}")

    def rstd0_fn(sc):
        P.mid(sc.activation(out=ln["t1"][:, t:t+1], in_=ln["varep"][:, t:t+1],
                            func=AF.Ln, bias=0.0, scale=1.0))
        return sc.activation(out=ln["r0"][:, t:t+1], in_=ln["t1"][:, t:t+1],
                             func=AF.Exp, bias=0.0, scale=-0.5)
    rstd0 = A("scalar", rstd0_fn, deps=[smalls], inc=("s_sc", 1), tag=f"r0{tag}{t}")

    def newton_fn(g):
        P.mid(g.tensor_tensor(out=ln["nt"][:, t:t+1], in0=ln["r0"][:, t:t+1],
                              in1=ln["r0"][:, t:t+1], op=ALU.mult))
        P.mid(g.tensor_tensor(out=ln["nt"][:, t:t+1], in0=ln["nt"][:, t:t+1],
                              in1=ln["varep"][:, t:t+1], op=ALU.mult))
        P.mid(g.tensor_scalar(out=ln["nt"][:, t:t+1], in0=ln["nt"][:, t:t+1],
                              scalar1=-0.5, scalar2=1.5, op0=ALU.mult, op1=ALU.add))
        P.mid(g.tensor_tensor(out=ln["rstd"][:, t:t+1], in0=ln["r0"][:, t:t+1],
                              in1=ln["nt"][:, t:t+1], op=ALU.mult))
        return g.tensor_tensor(out=ln["madj"][:, t:t+1], in0=ln["mneg"][:, t:t+1],
                               in1=ln["rstd"][:, t:t+1], op=ALU.mult)
    newton = A("gpsimd", newton_fn, deps=[rstd0], inc=("s_g", 1), tag=f"nw{tag}{t}")
    return newton


def emit_ln_input(P, A, ctx, t, xap, ln, deps_x, dep_u_free, tag, three_stage=False):
    u_bf = ctx["u_bf"]; uT = ctx["uT"]; red32 = ctx["red32"]
    xsq_out = ctx.get("sq_scr")
    xsq_out = u_bf[:] if xsq_out is None else xsq_out[:]
    xsq = A("scalar", lambda sc: sc.activation(
        out=xsq_out, in_=xap, func=AF.Square,
        accum_out=(None if three_stage else ln["ssq"][:, t:t+1])),
        deps=deps_x + [dep_u_free], inc=("s_sc", 1), tag=f"xsq{tag}{t}")

    if three_stage:
        def ssq_fn(v):
            P.mid(v.tensor_reduce(out=ctx["red256"][:],
                                  in_=xsq_out.rearrange("p (a b) -> p a b", a=256),
                                  axis=AX.X, op=ALU.add))
            P.mid(v.tensor_reduce(out=red32[:, 0:16],
                                  in_=ctx["red256"][:].rearrange("p (a b) -> p a b", a=16),
                                  axis=AX.X, op=ALU.add))
            return v.tensor_reduce(out=ln["ssq"][:, t:t+1], in_=red32[:, 0:16],
                                   axis=AX.X, op=ALU.add)
        xssq = A("vector", ssq_fn, deps=[xsq], inc=("s_v", 1), tag=f"xssq{tag}{t}")
        stats_dep = [xssq]
    else:
        stats_dep = [xsq]

    def xsum_fn(v):
        P.mid(v.tensor_reduce(out=red32[:], in_=xap.rearrange("p (a b) -> p a b", a=32),
                              axis=AX.X, op=ALU.add))
        return v.tensor_reduce(out=ln["sum"][:, t:t+1], in_=red32[:],
                               axis=AX.X, op=ALU.add)
    xsum = A("vector", xsum_fn, deps=deps_x, inc=("s_v", 1), tag=f"xsum{tag}{t}")

    newton = emit_ln_smalls(P, A, ln, t, stats_dep + [xsum], tag)

    uop = A("vector", lambda v: v.tensor_scalar(
        out=u_bf[:], in0=xap, scalar1=ln["mneg"][:, t:t+1],
        scalar2=ln["rstd"][:, t:t+1], op0=ALU.add, op1=ALU.mult),
        deps=[newton, xsq], inc=("s_v", 1), tag=f"u{tag}{t}")
    return uop, xsq, newton


def emit_utr(P, A, ctx, dep_u, dep_free, tag):
    u_bf = ctx["u_bf"]; uT = ctx["uT"]
    def utr_fn(s):
        for i in range(NI):
            s.dma_start(out=uT[:, i, :], in_=u_bf[:, i*128:(i+1)*128],
                        transpose=True).then_inc(P.sem_handles["s_t"], 16)
        return "self"
    return A("sync", utr_fn, deps=[dep_u, dep_free], inc=("s_t", 16*NI),
             tag=f"utr{tag}", self_inc=True)


def emit_quarter_stats_epilogue(P, A, ctx, t, xap, ln_y, make_quarter, prev_readers,
                                x_store, x_out_ap, tag, ysrc_sbuf=None):
    red32 = ctx["red32"]; tmin = ctx["tmin"]
    ysum4 = ctx["ysum4"]; yssq4 = ctx["yssq4"]
    ps_y = ctx.get("ps_y"); ysb = ctx.get("ysb")
    ystats = []; ysqs = []; evacs = [None, None]
    quarter_ops = []
    q_readers = {}
    prev_ysqr = None
    prev_act_ps = None
    for q in range(4):
        if ysrc_sbuf is not None:
            src = ysrc_sbuf[:, q*1024:(q+1)*1024]
        else:
            src = ps_y[:, q % 2, :]
        extra = []
        if q >= 2:
            extra += q_readers.get(q - 2, [])
        elif prev_readers is not None:
            extra += prev_readers.get(q + 2, [])
        at = make_quarter(q, extra)
        quarter_ops.append(at)
        q_readers[q] = [at]

        def yst_fn(v, src=src, q=q):
            P.mid(v.tensor_reduce(out=red32[:, 0:8],
                                  in_=src.rearrange("p (a b) -> p a b", a=8),
                                  axis=AX.X, op=ALU.add))
            return v.tensor_reduce(out=ysum4[:, q:q+1], in_=red32[:, 0:8],
                                   axis=AX.X, op=ALU.add)
        yst = A("vector", yst_fn, deps=[at, prev_act_ps], inc=("s_v", 1), tag=f"yst{tag}{q}_{t}")
        ystats.append(yst)

        def ysq0_fn(sc, src=src):
            r = None
            for c2 in range(2):
                r = sc.activation(out=tmin[:, c2*512:(c2+1)*512],
                                  in_=src[:, c2*512:(c2+1)*512], func=AF.Square)
            return r
        ysq0 = A("scalar", ysq0_fn, deps=[at, yst, prev_ysqr], inc=("s_sc", 1), tag=f"ysq0{tag}{q}_{t}")

        def ysqr_fn(v, q=q):
            P.mid(v.tensor_reduce(out=red32[:, 8:16],
                                  in_=tmin[:].rearrange("p (a b) -> p a b", a=8),
                                  axis=AX.X, op=ALU.add))
            return v.tensor_reduce(out=yssq4[:, q:q+1], in_=red32[:, 8:16],
                                   axis=AX.X, op=ALU.add)
        ysq = A("vector", ysqr_fn, deps=[ysq0], inc=("s_v", 1), tag=f"ysq{tag}{q}_{t}")
        ysqs.append(ysq); prev_ysqr = ysq
        if ysrc_sbuf is None:
            prev_act_ps = ysq0
        if ysrc_sbuf is None:
            q_readers[q] += [yst, ysq0]

        if ysrc_sbuf is None and q < 2:
            def yev_fn(sc, q=q, src=src):
                r = None
                for c2 in range(2):
                    r = sc.copy(out=ysb[:, q*1024 + c2*512:q*1024 + (c2+1)*512],
                                in_=src[:, c2*512:(c2+1)*512])
                return r
            ev = A("scalar", yev_fn, deps=[at, yst, ysq0], inc=("s_sc", 1),
                   tag=f"yev{tag}{q}_{t}")
            evacs[q] = ev
            prev_act_ps = ev
            q_readers[q].append(ev)

    def ysum_fn(g):
        P.mid(g.tensor_tensor(out=ln_y["sum"][:, t:t+1], in0=ysum4[:, 0:1], in1=ysum4[:, 1:2], op=ALU.add))
        P.mid(g.tensor_tensor(out=ln_y["t1"][:, t:t+1], in0=ysum4[:, 2:3], in1=ysum4[:, 3:4], op=ALU.add))
        P.mid(g.tensor_tensor(out=ln_y["sum"][:, t:t+1], in0=ln_y["sum"][:, t:t+1], in1=ln_y["t1"][:, t:t+1], op=ALU.add))
        P.mid(g.tensor_tensor(out=ln_y["ssq"][:, t:t+1], in0=yssq4[:, 0:1], in1=yssq4[:, 1:2], op=ALU.add))
        P.mid(g.tensor_tensor(out=ln_y["t1"][:, t:t+1], in0=yssq4[:, 2:3], in1=yssq4[:, 3:4], op=ALU.add))
        return g.tensor_tensor(out=ln_y["ssq"][:, t:t+1], in0=ln_y["ssq"][:, t:t+1], in1=ln_y["t1"][:, t:t+1], op=ALU.add)
    ysum = A("gpsimd", ysum_fn, deps=ystats + ysqs, inc=("s_g", 1), tag=f"ysum{tag}{t}")

    newton = emit_ln_smalls(P, A, ln_y, t, [ysum], "y" + tag)

    yaps = []
    for q in range(4):
        if ysrc_sbuf is not None:
            src = ysrc_sbuf[:, q*1024:(q+1)*1024]
            dps = [newton, quarter_ops[q], ysqs[q]]
        elif q < 2:
            src = ysb[:, q*1024:(q+1)*1024]
            dps = [newton, evacs[q]]
        else:
            src = ps_y[:, q % 2, :]
            dps = [newton, ystats[q], ysqs[q], quarter_ops[q]]
        yap = A("vector", lambda v, q=q, src=src: v.scalar_tensor_tensor(
            out=xap[:, q*1024:(q+1)*1024], in0=src,
            scalar=ln_y["rstd"][:, t:t+1], in1=xap[:, q*1024:(q+1)*1024],
            op0=ALU.mult, op1=ALU.add),
            deps=dps, inc=("s_v", 1), tag=f"yap{tag}{q}_{t}")
        yaps.append(yap)
        q_readers.setdefault(q, []).append(yap)

    xmadj = A("vector", lambda v: v.tensor_scalar(
        out=xap, in0=xap, scalar1=ln_y["madj"][:, t:t+1], scalar2=None, op0=ALU.add),
        deps=yaps + [newton], inc=("s_v", 1), tag=f"xmadj{tag}{t}")

    if x_store:
        sto = A("sync", lambda s: s.dma_start(
            out=x_out_ap, in_=xap), deps=[xmadj], inc=(f"s_st{t % 2}", 16), tag=f"sto{tag}{t}")
    else:
        sto = xmadj
    return sto, q_readers, ysqs[3]












TOK = NT * 128; DM = 4096; NI = DM // 128


def host_prep_l0a(inp):
    H, HKV, D = 32, 8, 128
    Wq = np.asarray(inp['L0_attn_Wq']); bq = np.asarray(inp['L0_attn_bq'])
    Wq_sel = np.ascontiguousarray(Wq.reshape(H, D, DM)[::4].reshape(HKV * D, DM).T)
    bq_sel = np.ascontiguousarray(bq.reshape(H, D)[::4].reshape(1, HKV * D))
    WoT = np.ascontiguousarray(np.asarray(inp['L0_attn_Wo']).T)
    Mz = np.ascontiguousarray(np.concatenate(
        [np.asarray(inp['L0_attn_M']), np.asarray(inp['L0_attn_z'])[:, :, None]], axis=2))
    return {"wqselT": Wq_sel, "bqsel": bq_sel, "woT": WoT, "mz": Mz}


def build_l0a(nc, x_in, x_out, es, P, uniq=""):
    nc_ = nc
    wqselT = nc.dram_tensor(uniq + "wqselT", [DM, 1024], F32, kind="ExternalInput").ap()
    bqsel = nc.dram_tensor(uniq + "bqsel", [1, 1024], F32, kind="ExternalInput").ap()
    woT = nc.dram_tensor(uniq + "woT", [DM, DM], F32, kind="ExternalInput").ap()
    mz = nc.dram_tensor(uniq + "mz", [8, 128, 129], F32, kind="ExternalInput").ap()

    T = lambda n, s, d: es.enter_context(nc.sbuf_tensor(uniq + n, s, d))
    wq_bf = T("wq_bf", [128, NI, 1024], BF16)
    wo_bf = T("wo_bf", [128, 8, DM], BF16)
    mz_bf = T("mz_bf", [128, 8, 129], BF16)
    bq_bf = T("bq_bf", [1, 1024], BF16)
    ones1 = T("ones1", [1, 128], BF16)
    xbuf = T("xbuf", [128, 2, DM], F32)
    u_bf = T("u_bf", [128, DM], BF16)
    uT = T("uT", [128, NI, 128], BF16)
    tmin = T("tmin", [128, 1024], F32)
    sig_bf = T("sig_bf", [128, 1024], BF16)
    sigT = T("sigT", [128, 8, 128], BF16)
    retr_bf = T("retr_bf", [128, 1024], BF16)
    retrT = T("retrT", [128, 8, 128], BF16)
    den8 = T("den8", [128, 8], F32)
    rden8 = T("rden8", [128, 8], F32)
    red32 = T("red32", [128, 32], F32)
    ysb = T("ysb", [128, 2048], F32)
    ysum4 = T("ysum4", [128, 4], F32)
    yssq4 = T("yssq4", [128, 4], F32)
    ln = {}
    for pre in ("x", "y"):
        for fld in ("sum", "ssq", "mneg", "varep", "r0", "rstd", "t1", "nt", "madj"):
            ln[pre + fld] = T(f"ln{pre}_{fld}", [128, NT], F32)

    ps_att = es.enter_context(nc.psum_tensor(uniq + "ps_att", [128, 2, 1024], F32))
    ps_q = es.enter_context(nc.psum_tensor(uniq + "ps_q", [128, 1024], F32))
    ps_num = es.enter_context(nc.psum_tensor(uniq + "ps_num", [128, 8, 128], F32))

    A = P.add
    wload = []
    for j in range(4):
        src = wqselT.rearrange("(n p) c -> p n c", p=128)[:, j * 8:(j + 1) * 8, :]
        wload.append(A("gpsimd", lambda g, j=j, src=src: g.dma_start(
            out=wq_bf[:, j * 8:(j + 1) * 8, :], in_=src), inc=("s_w", 16), tag=f"wq{j}"))
    wload.append(A("gpsimd", lambda g: g.dma_start(
        out=mz_bf[:], in_=mz.rearrange("h p c -> p h c")), inc=("s_w", 16), tag="mz"))
    wload.append(A("gpsimd", lambda g: g.dma_start(
        out=bq_bf[:], in_=bqsel[:]), inc=("s_w", 16), tag="bq"))
    pones = A("vector", lambda v: v.memset(ones1[:], 1.0), inc=("s_v", 1), tag="ones")

    fold_done = []
    prev_add = None
    for h in range(8):
        l0 = A("sync", lambda s, h=h: s.dma_start(
            out=xbuf[:, 0, :], in_=woT[(4 * h) * 128:(4 * h + 1) * 128, :]),
            deps=[prev_add], inc=("s_fold", 16), tag=f"fold_l0_{h}")
        prev = l0
        for g in range(1, 4):
            lg = A("sync", lambda s, h=h, g=g: s.dma_start(
                out=xbuf[:, 1, :], in_=woT[(4 * h + g) * 128:(4 * h + g + 1) * 128, :]),
                deps=[prev], inc=("s_fold", 16), tag=f"fold_l{g}_{h}")
            ad = A("vector", lambda v: v.tensor_tensor(
                out=xbuf[:, 0, :], in0=xbuf[:, 0, :], in1=xbuf[:, 1, :], op=ALU.add),
                deps=[lg], inc=("s_v", 1), tag=f"fold_a{g}_{h}")
            prev = ad
        cst = A("vector", lambda v, h=h: v.tensor_copy(out=wo_bf[:, h, :], in_=xbuf[:, 0, :]),
                deps=[prev], inc=("s_v", 1), tag=f"fold_c{h}")
        prev_add = cst
        fold_done.append(cst)

    prev_tile = {}
    for t in range(NT):
        sl = t % 2
        xap = xbuf[:, sl, :]
        deps_ldx = [fold_done[-1]] if t < 2 else [prev_tile[(t - 2, 'sto')]]
        if t >= 1:
            deps_ldx.append(prev_tile[(t - 1, 'xsum')])
        ldx = A("sync", lambda s, t=t, xap=xap: s.dma_start(
            out=xap, in_=x_in[t * 128:(t + 1) * 128, :]), deps=deps_ldx,
            inc=(f"s_x{t % 2}", 16), tag=f"ldx{t}")

        dep_u_free = prev_tile.get((t - 1, 'ysq3'))
        xsq = A("scalar", lambda sc, t=t, xap=xap: sc.activation(
            out=u_bf[:], in_=xap, func=AF.Square, accum_out=ln["xssq"][:, t:t + 1]),
            deps=[ldx, dep_u_free], inc=("s_sc", 1), tag=f"xsq{t}")

        def xsum_fn(v, t=t, xap=xap):
            P.mid(v.tensor_reduce(out=red32[:], in_=xap.rearrange("p (a b) -> p a b", a=32),
                            axis=AX.X, op=ALU.add))
            return v.tensor_reduce(out=ln["xsum"][:, t:t + 1], in_=red32[:],
                                   axis=AX.X, op=ALU.add)
        xsum = A("vector", xsum_fn, deps=[ldx], inc=("s_v", 1), tag=f"xsum{t}")

        def smalls_fn(g, t=t, pre="x"):
            P.mid(g.tensor_scalar(out=ln[pre + "mneg"][:, t:t + 1], in0=ln[pre + "sum"][:, t:t + 1],
                            scalar1=-1.0 / DM, scalar2=None, op0=ALU.mult))
            P.mid(g.tensor_tensor(out=ln[pre + "t1"][:, t:t + 1], in0=ln[pre + "mneg"][:, t:t + 1],
                            in1=ln[pre + "mneg"][:, t:t + 1], op=ALU.mult))
            P.mid(g.tensor_scalar(out=ln[pre + "varep"][:, t:t + 1], in0=ln[pre + "ssq"][:, t:t + 1],
                            scalar1=1.0 / DM, scalar2=1e-5, op0=ALU.mult, op1=ALU.add))
            return g.tensor_tensor(out=ln[pre + "varep"][:, t:t + 1],
                                   in0=ln[pre + "varep"][:, t:t + 1],
                                   in1=ln[pre + "t1"][:, t:t + 1], op=ALU.subtract)
        smalls = A("gpsimd", smalls_fn, deps=[xsq, xsum], inc=("s_g", 1), tag=f"smx{t}")

        def rstd0_fn(sc, t=t, pre="x"):
            P.mid(sc.activation(out=ln[pre + "t1"][:, t:t + 1], in_=ln[pre + "varep"][:, t:t + 1],
                          func=AF.Ln, bias=0.0, scale=1.0))
            return sc.activation(out=ln[pre + "r0"][:, t:t + 1], in_=ln[pre + "t1"][:, t:t + 1],
                                 func=AF.Exp, bias=0.0, scale=-0.5)
        rstd0 = A("scalar", rstd0_fn, deps=[smalls], inc=("s_sc", 1), tag=f"r0x{t}")

        def newton_fn(g, t=t, pre="x"):
            P.mid(g.tensor_tensor(out=ln[pre + "nt"][:, t:t + 1], in0=ln[pre + "r0"][:, t:t + 1],
                            in1=ln[pre + "r0"][:, t:t + 1], op=ALU.mult))
            P.mid(g.tensor_tensor(out=ln[pre + "nt"][:, t:t + 1], in0=ln[pre + "nt"][:, t:t + 1],
                            in1=ln[pre + "varep"][:, t:t + 1], op=ALU.mult))
            P.mid(g.tensor_scalar(out=ln[pre + "nt"][:, t:t + 1], in0=ln[pre + "nt"][:, t:t + 1],
                            scalar1=-0.5, scalar2=1.5, op0=ALU.mult, op1=ALU.add))
            return g.tensor_tensor(out=ln[pre + "rstd"][:, t:t + 1], in0=ln[pre + "r0"][:, t:t + 1],
                                   in1=ln[pre + "nt"][:, t:t + 1], op=ALU.mult)
        newton = A("gpsimd", newton_fn, deps=[rstd0], inc=("s_g", 1), tag=f"nwx{t}")

        uop = A("vector", lambda v, t=t, xap=xap: v.tensor_scalar(
            out=u_bf[:], in0=xap, scalar1=ln["xmneg"][:, t:t + 1],
            scalar2=ln["xrstd"][:, t:t + 1], op0=ALU.add, op1=ALU.mult),
            deps=[newton, xsq], inc=("s_v", 1), tag=f"u{t}")

        def utr_fn(s, t=t):
            for i in range(NI):
                s.dma_start(out=uT[:, i, :], in_=u_bf[:, i * 128:(i + 1) * 128],
                            transpose=True).then_inc(P.sem_handles["s_t"], 16)
            return "self"
        utr = A("sync", utr_fn, deps=[uop, prev_tile.get((t - 1, 'qselmm'))],
                inc=("s_t", 16 * NI), tag=f"utr{t}", self_inc=True)

        def qsel_fn(te, t=t):
            r = None
            for c in range(2):
                te.matmul(ps_q[:, c * 512:(c + 1) * 512], ones1[:],
                          bq_bf[:, c * 512:(c + 1) * 512], start=True, stop=False)
                for i in range(NI):
                    r = te.matmul(ps_q[:, c * 512:(c + 1) * 512], uT[:, i, :],
                                  wq_bf[:, i, c * 512:(c + 1) * 512],
                                  start=False, stop=(i == NI - 1))
            return r
        qselmm = A("tensor", qsel_fn,
                   deps=[utr, pones] + wload + ([prev_tile[(t - 1, 'sigstt')], prev_tile[(t - 1, 'dencp')]] if t else []),
                   inc=("s_mm", 1), tag=f"qselmm{t}")

        sigmin = A("vector", lambda v: v.tensor_scalar(
            out=tmin[:], in0=ps_q[:], scalar1=0.0, scalar2=None, op0=ALU.min),
            deps=[qselmm, prev_tile.get((t - 1, 'ysq3s'))], inc=("s_v", 1), tag=f"sigmin{t}")
        sigexp = A("scalar", lambda sc: sc.activation(
            out=tmin[:], in_=tmin[:], func=AF.Exp, bias=0.0, scale=1.0),
            deps=[sigmin], inc=("s_sc", 1), tag=f"sigexp{t}")
        sigstt = A("vector", lambda v: v.scalar_tensor_tensor(
            out=sig_bf[:], in0=ps_q[:], scalar=0.0, in1=tmin[:], op0=ALU.max, op1=ALU.add),
            deps=[sigexp], inc=("s_v", 1), tag=f"sigstt{t}")

        def sigtr_fn(s):
            for h in range(8):
                s.dma_start(out=sigT[:, h, :], in_=sig_bf[:, h * 128:(h + 1) * 128],
                            transpose=True).then_inc(P.sem_handles["s_t"], 16)
            return "self"
        sigtr = A("sync", sigtr_fn, deps=[sigstt, prev_tile.get((t - 1, 'num7'))],
                  inc=("s_t", 16 * 8), tag=f"sigtr{t}", self_inc=True)

        nums = []; retrs = []
        num_war = [prev_tile[(t - 1, 'retr7')]] if t >= 1 else []
        den_war = [sigstt]
        for h in range(8):
            nm = A("tensor", lambda te, h=h: te.matmul(
                ps_num[:, h, :], sigT[:, h, :], mz_bf[:, h, 0:128], start=True, stop=True),
                deps=[sigtr] + num_war, inc=("s_mm", 1), tag=f"num{h}_{t}")
            nums.append(nm)
            A("tensor", lambda te, h=h: te.matmul(
                ps_q[:, h:h + 1], sigT[:, h, :], mz_bf[:, h, 128:129], start=True, stop=True),
                deps=[sigtr] + den_war, inc=("s_mm", 1), tag=f"den{h}_{t}")
        dencp = A("scalar", lambda sc: sc.copy(out=den8[:], in_=ps_q[:, 0:8]),
                  deps=[P.ops[-1]], inc=("s_sc", 1), tag=f"dencp{t}")
        deneps = A("gpsimd", lambda g: g.tensor_scalar(
            out=den8[:], in0=den8[:], scalar1=1e-6, scalar2=None, op0=ALU.add),
            deps=[dencp], inc=("s_g", 1), tag=f"deneps{t}")

        def dln_fn(sc):
            P.mid(sc.activation(out=rden8[:], in_=den8[:], func=AF.Ln, bias=0.0, scale=1.0))
            return sc.activation(out=rden8[:], in_=rden8[:], func=AF.Exp,
                                 bias=0.0, scale=-1.0)
        dexp = A("scalar", dln_fn, deps=[deneps], inc=("s_sc", 1), tag=f"dexp{t}")
        for h in range(8):
            rt = A("vector", lambda v, h=h: v.tensor_scalar(
                out=retr_bf[:, h * 128:(h + 1) * 128], in0=ps_num[:, h, :],
                scalar1=rden8[:, h:h + 1], scalar2=None, op0=ALU.mult),
                deps=[dexp, nums[h]], inc=("s_v", 1), tag=f"retr{h}_{t}")
            retrs.append(rt)
        prev_tile[(t, 'retr7')] = retrs[7]

        def retrtr_fn(s):
            for h in range(8):
                s.dma_start(out=retrT[:, h, :], in_=retr_bf[:, h * 128:(h + 1) * 128],
                            transpose=True).then_inc(P.sem_handles["s_t"], 16)
            return "self"
        retrtr = A("sync", retrtr_fn,
                   deps=retrs + [prev_tile.get((t - 1, 'att3'))],
                   inc=("s_t", 16 * 8), tag=f"retrtr{t}", self_inc=True)

        atts = [None] * 4; ystats = []; ysqs = []; evacs = [None, None]
        yaps = [None] * 4
        for q in range(4):
            dps = [retrtr] + ([fold_done[-1]] if t == 0 else [])
            if q >= 2:
                dps += [evacs[q - 2], ystats[q - 2], ysqs[q - 2]]
            elif t >= 1:
                dps += [prev_tile[(t - 1, f'yap{q+2}')]]

            def att_fn(te, q=q):
                r = None
                for c2 in range(2):
                    for h in range(8):
                        r = te.matmul(ps_att[:, q % 2, c2 * 512:(c2 + 1) * 512],
                                      retrT[:, h, :],
                                      wo_bf[:, h, q * 1024 + c2 * 512: q * 1024 + (c2 + 1) * 512],
                                      start=(h == 0), stop=(h == 7))
                return r
            at = A("tensor", att_fn, deps=dps, inc=("s_mm", 1), tag=f"att{q}_{t}")
            atts[q] = at
            prev_tile[(t, f'att{q}')] = at

            def yst_fn(v, q=q, t=t):
                P.mid(v.tensor_reduce(out=red32[:, 0:8],
                                in_=ps_att[:, q % 2, :].rearrange("p (a b) -> p a b", a=8),
                                axis=AX.X, op=ALU.add))
                return v.tensor_reduce(out=ysum4[:, q:q + 1], in_=red32[:, 0:8],
                                       axis=AX.X, op=ALU.add)
            yst = A("vector", yst_fn, deps=[at], inc=("s_v", 1), tag=f"yst{q}_{t}")
            ystats.append(yst)
            def ysq0_fn(sc, q=q):
                r = None
                for c2 in range(2):
                    r = sc.activation(out=tmin[:, c2 * 512:(c2 + 1) * 512],
                                      in_=ps_att[:, q % 2, c2 * 512:(c2 + 1) * 512],
                                      func=AF.Square)
                return r
            ysq0 = A("scalar", ysq0_fn, deps=[at, yst], inc=("s_sc", 1), tag=f"ysq0_{q}_{t}")

            def ysqr_fn(v, q=q):
                P.mid(v.tensor_reduce(out=red32[:, 8:16],
                                      in_=tmin[:].rearrange("p (a b) -> p a b", a=8),
                                      axis=AX.X, op=ALU.add))
                return v.tensor_reduce(out=yssq4[:, q:q + 1], in_=red32[:, 8:16],
                                       axis=AX.X, op=ALU.add)
            ysq = A("vector", ysqr_fn, deps=[ysq0], inc=("s_v", 1), tag=f"ysq{q}_{t}")
            ysqs.append(ysq)
            if q == 3:
                prev_tile[(t, 'ysq3')] = ysq
                prev_tile[(t, 'ysq3s')] = ysq0
            if q < 2:
                evdeps = [at, yst, ysq]
                if t >= 1:
                    evdeps.append(prev_tile[(t - 1, f'yap{q}')])
                def yev_fn(sc, q=q):
                    r = None
                    for c2 in range(2):
                        r = sc.copy(out=ysb[:, q * 1024 + c2 * 512:q * 1024 + (c2 + 1) * 512],
                                    in_=ps_att[:, q % 2, c2 * 512:(c2 + 1) * 512])
                    return r
                ev = A("scalar", yev_fn, deps=evdeps, inc=("s_sc", 1), tag=f"yev{q}_{t}")
                evacs[q] = ev

        def ysum_fn(g, t=t):
            P.mid(g.tensor_tensor(out=ln["ysum"][:, t:t + 1], in0=ysum4[:, 0:1], in1=ysum4[:, 1:2], op=ALU.add))
            P.mid(g.tensor_tensor(out=ln["yt1"][:, t:t + 1], in0=ysum4[:, 2:3], in1=ysum4[:, 3:4], op=ALU.add))
            P.mid(g.tensor_tensor(out=ln["ysum"][:, t:t + 1], in0=ln["ysum"][:, t:t + 1], in1=ln["yt1"][:, t:t + 1], op=ALU.add))
            P.mid(g.tensor_tensor(out=ln["yssq"][:, t:t + 1], in0=yssq4[:, 0:1], in1=yssq4[:, 1:2], op=ALU.add))
            P.mid(g.tensor_tensor(out=ln["yt1"][:, t:t + 1], in0=yssq4[:, 2:3], in1=yssq4[:, 3:4], op=ALU.add))
            return g.tensor_tensor(out=ln["yssq"][:, t:t + 1], in0=ln["yssq"][:, t:t + 1], in1=ln["yt1"][:, t:t + 1], op=ALU.add)
        ysum = A("gpsimd", ysum_fn, deps=ystats + ysqs, inc=("s_g", 1), tag=f"ysum{t}")

        ysmalls = A("gpsimd", lambda g, t=t: smalls_fn(g, t=t, pre="y"), deps=[ysum],
                    inc=("s_g", 1), tag=f"smy{t}")
        yrstd0 = A("scalar", lambda sc, t=t: rstd0_fn(sc, t=t, pre="y"), deps=[ysmalls],
                   inc=("s_sc", 1), tag=f"r0y{t}")

        def ynewton_fn(g, t=t):
            P.mid(newton_fn(g, t=t, pre="y"))
            return g.tensor_tensor(out=ln["ymadj"][:, t:t + 1], in0=ln["ymneg"][:, t:t + 1],
                                   in1=ln["yrstd"][:, t:t + 1], op=ALU.mult)
        ynewton = A("gpsimd", ynewton_fn, deps=[yrstd0], inc=("s_g", 1), tag=f"nwy{t}")

        for q in range(4):
            if q < 2:
                src_ap = ysb[:, q * 1024:(q + 1) * 1024]
                dps = [ynewton, evacs[q]]
            else:
                src_ap = ps_att[:, q % 2, :]
                dps = [ynewton, ystats[q], ysqs[q], atts[q]]
            yap = A("vector", lambda v, q=q, t=t, xap=xap, src_ap=src_ap: v.scalar_tensor_tensor(
                out=xap[:, q * 1024:(q + 1) * 1024], in0=src_ap,
                scalar=ln["yrstd"][:, t:t + 1], in1=xap[:, q * 1024:(q + 1) * 1024],
                op0=ALU.mult, op1=ALU.add),
                deps=dps, inc=("s_v", 1), tag=f"yap{q}_{t}")
            yaps[q] = yap; prev_tile[(t, f'yap{q}')] = yap

        xmadj = A("vector", lambda v, t=t, xap=xap: v.tensor_scalar(
            out=xap, in0=xap, scalar1=ln["ymadj"][:, t:t + 1], scalar2=None, op0=ALU.add),
            deps=yaps + [ynewton], inc=("s_v", 1), tag=f"xmadj{t}")

        sto = A("sync", lambda s, t=t, xap=xap: s.dma_start(
            out=x_out[t * 128:(t + 1) * 128, :], in_=xap), deps=[xmadj],
            inc=(f"s_st{t % 2}", 16), tag=f"sto{t}")
        prev_tile[(t, 'sto')] = sto
        prev_tile[(t, 'xsum')] = xsum
        prev_tile[(t, 'qselmm')] = qselmm
        prev_tile[(t, 'sigstt')] = sigstt
        prev_tile[(t, 'num7')] = nums[7]
        prev_tile[(t, 'dencp')] = dencp
        prev_tile[(t, 'att3')] = atts[3]

    return [prev_tile[(t, 'sto')] for t in range(NT)]



def host_prep_l0f(inp):
    WT = np.ascontiguousarray(np.asarray(inp['L0_ffn_W']).T)
    WqT = np.ascontiguousarray(np.asarray(inp['L0_ffn_Wq']).T)
    WkT = np.ascontiguousarray(np.asarray(inp['L0_ffn_Wk']).T)
    AT = np.ascontiguousarray(np.asarray(inp['L0_ffn_A']).reshape(128, DM).T)
    Bm = np.asarray(inp['L0_ffn_Bm'])
    BmT = np.ascontiguousarray(np.transpose(Bm, (0, 2, 1)).reshape(128, DM))
    b = np.ascontiguousarray(np.asarray(inp['L0_ffn_b']).reshape(1, DM))
    return {"f_wT": WT, "f_wqT": WqT, "f_wkT": WkT, "f_aT": AT, "f_bmT": BmT, "f_b": b}


def build_l0f(nc, x_in, x_out, es, P, uniq=""):
    wT = nc.dram_tensor(uniq + "f_wT", [DM, DM], F32, kind="ExternalInput").ap()
    wqT = nc.dram_tensor(uniq + "f_wqT", [DM, 256], F32, kind="ExternalInput").ap()
    wkT = nc.dram_tensor(uniq + "f_wkT", [DM, 256], F32, kind="ExternalInput").ap()
    aT = nc.dram_tensor(uniq + "f_aT", [DM, 128], F32, kind="ExternalInput").ap()
    bmT = nc.dram_tensor(uniq + "f_bmT", [128, DM], F32, kind="ExternalInput").ap()
    bia = nc.dram_tensor(uniq + "f_b", [1, DM], F32, kind="ExternalInput").ap()
    wmaster = nc.dram_tensor(uniq + "f_wm", [DM, DM], BF16, kind="Internal").ap()

    T = lambda n, s, d: es.enter_context(nc.sbuf_tensor(uniq + "f_" + n, s, d))
    wq_bf = T("wq_bf", [128, NI, 256], BF16)
    wk_bf = T("wk_bf", [128, NI, 256], BF16)
    a_bf = T("a_bf", [128, NI, 128], BF16)
    bm_bf = T("bm_bf", [128, DM], BF16)
    b_bf = T("b_bf", [1, DM], BF16)
    ones1 = T("ones1", [1, 128], BF16)
    wstage = T("wstage", [128, 2, DM], BF16)
    wstr = T("wstr", [128, 2, 4, 1024], BF16)
    xbuf = T("xbuf", [128, 2, DM], F32)
    u_bf = T("u_bf", [128, DM], BF16)
    uT = T("uT", [128, NI, 128], BF16)
    tmin = T("tmin", [128, 1024], F32)
    ysb = T("ysb", [128, 2048], F32)
    qk_sb = T("qk_sb", [128, 256], F32)
    prod = T("prod", [128, 256], F32)
    sc8 = T("sc8", [128, 8], F32)
    m8 = T("m8", [128, 1], F32)
    mneg8 = T("mneg8", [128, 1], F32)
    e8 = T("e8", [128, 8], F32)
    s8 = T("s8", [128, 1], F32)
    rs8 = T("rs8", [128, 1], F32)
    w8 = T("w8", [128, 8], F32)
    wexp = T("wexp", [128, 128], F32)
    wh_bf = T("wh_bf", [128, 128], BF16)
    whT = T("whT", [128, 128], BF16)
    red32 = T("red32", [128, 32], F32)
    ysum4 = T("ysum4", [128, 4], F32)
    yssq4 = T("yssq4", [128, 4], F32)
    ln_x = alloc_ln(T, "x" + uniq, NT)
    ln_y = alloc_ln(T, "y" + uniq, NT)

    ps_y = es.enter_context(nc.psum_tensor(uniq + "f_ps_y", [128, 2, 1024], F32))
    ps_qk = es.enter_context(nc.psum_tensor(uniq + "f_ps_qk", [128, 512], F32))
    ps_h = es.enter_context(nc.psum_tensor(uniq + "f_ps_h", [128, 128], F32))

    ctx = {"u_bf": u_bf, "uT": uT, "red32": red32, "tmin": tmin,
           "ysum4": ysum4, "yssq4": yssq4, "ps_y": ps_y, "ysb": ysb}

    A = P.add
    wload = []
    wload.append(A("gpsimd", lambda g: g.dma_start(
        out=wq_bf[:], in_=wqT.rearrange("(n p) c -> p n c", p=128)), inc=("s_w", 16), tag="wq"))
    wload.append(A("gpsimd", lambda g: g.dma_start(
        out=wk_bf[:], in_=wkT.rearrange("(n p) c -> p n c", p=128)), inc=("s_w", 16), tag="wk"))
    wload.append(A("gpsimd", lambda g: g.dma_start(
        out=a_bf[:], in_=aT.rearrange("(n p) c -> p n c", p=128)), inc=("s_w", 16), tag="at"))
    wload.append(A("gpsimd", lambda g: g.dma_start(
        out=bm_bf[:], in_=bmT[:]), inc=("s_w", 16), tag="bm"))
    wload.append(A("gpsimd", lambda g: g.dma_start(
        out=b_bf[:], in_=bia[:]), inc=("s_w", 16), tag="b"))
    pones = A("vector", lambda v: v.memset(ones1[:], 1.0), inc=("s_v", 1), tag="ones")

    mst_done = []
    prev_st = [None, None]
    for i in range(NI):
        ld = A("gpsimd", lambda g, i=i: g.dma_start(
            out=wstage[:, i % 2, :], in_=wT[i*128:(i+1)*128, :]),
            deps=[prev_st[i % 2]], inc=(f"s_ml{i % 2}", 16), tag=f"mstl{i}")
        st = A("sync", lambda s, i=i: s.dma_start(
            out=wmaster[i*128:(i+1)*128, :], in_=wstage[:, i % 2, :]),
            deps=[ld], inc=(f"s_ms{i % 2}", 16), tag=f"msts{i}")
        prev_st[i % 2] = st
        mst_done.append(st)

    prev = {}
    for t in range(NT):
        sl = t % 2
        xap = xbuf[:, sl, :]
        deps_ldx = [mst_done[-1], mst_done[-2]] if t < 2 else [prev[(t - 2, 'sto')]]
        ldx = A("sync", lambda s, t=t, xap=xap: s.dma_start(
            out=xap, in_=x_in[t*128:(t+1)*128, :]), deps=deps_ldx,
            inc=(f"s_x{t % 2}", 16), tag=f"ldx{t}")

        uop, xsq, _ = emit_ln_input(P, A, ctx, t, xap, ln_x, [ldx],
                                    prev.get((t - 1, 'epi')), "f")
        utr = emit_utr(P, A, ctx, uop, prev.get((t - 1, 'hmm')), f"f{t}")

        def qk_fn(te, t=t):
            r = None
            for i in range(NI):
                te.matmul(ps_qk[:, 0:256], uT[:, i, :], wq_bf[:, i, :],
                          start=(i == 0), stop=(i == NI - 1))
            for i in range(NI):
                r = te.matmul(ps_qk[:, 256:512], uT[:, i, :], wk_bf[:, i, :],
                              start=(i == 0), stop=(i == NI - 1))
            return r
        qkmm = A("tensor", qk_fn, deps=[utr] + wload + [prev.get((t - 1, 'prod'))],
                 inc=("s_mm", 1), tag=f"qkmm{t}")

        def h_fn(te, t=t):
            r = None
            for i in range(NI):
                r = te.matmul(ps_h[:], uT[:, i, :], a_bf[:, i, :],
                              start=(i == 0), stop=(i == NI - 1))
            return r
        hmm_ = A("tensor", h_fn, deps=[utr] + wload + [prev.get((t - 1, 'whmul'))],
                 inc=("s_mm", 1), tag=f"hmm{t}")
        prev[(t, 'hmm')] = hmm_

        qev = A("scalar", lambda sc: sc.copy(out=qk_sb[:], in_=ps_qk[:, 0:256]),
                deps=[qkmm], inc=("s_sc", 1), tag=f"qev{t}")
        prodo = A("vector", lambda v: v.tensor_tensor(
            out=prod[:], in0=ps_qk[:, 256:512], in1=qk_sb[:], op=ALU.mult),
            deps=[qev], inc=("s_v", 1), tag=f"prod{t}")
        prev[(t, 'prod')] = prodo

        def sc8_fn(v):
            P.mid(v.tensor_reduce(out=sc8[:], in_=prod[:].rearrange("p (a b) -> p a b", a=8),
                                  axis=AX.X, op=ALU.add))
            return v.tensor_reduce(out=m8[:], in_=sc8[:], axis=AX.X, op=ALU.max)
        sc8o = A("vector", sc8_fn, deps=[prodo], inc=("s_v", 1), tag=f"sc8{t}")

        mn8 = A("gpsimd", lambda g: g.tensor_scalar(
            out=mneg8[:], in0=m8[:], scalar1=-RS32, scalar2=None, op0=ALU.mult),
            deps=[sc8o], inc=("s_g", 1), tag=f"mn8{t}")
        e8o = A("scalar", lambda sc: sc.activation(
            out=e8[:], in_=sc8[:], func=AF.Exp, bias=mneg8[:], scale=RS32),
            deps=[mn8], inc=("s_sc", 1), tag=f"e8{t}")
        s8o = A("vector", lambda v: v.tensor_reduce(
            out=s8[:], in_=e8[:], axis=AX.X, op=ALU.add),
            deps=[e8o], inc=("s_v", 1), tag=f"s8{t}")

        def rs_fn(sc):
            P.mid(sc.activation(out=rs8[:], in_=s8[:], func=AF.Ln, bias=0.0, scale=1.0))
            return sc.activation(out=rs8[:], in_=rs8[:], func=AF.Exp, bias=0.0, scale=-1.0)
        rso = A("scalar", rs_fn, deps=[s8o], inc=("s_sc", 1), tag=f"rs{t}")

        w8o = A("vector", lambda v: v.tensor_scalar(
            out=w8[:], in0=e8[:], scalar1=rs8[:], scalar2=None, op0=ALU.mult),
            deps=[rso], inc=("s_v", 1), tag=f"w8{t}")
        wexpo = A("vector", lambda v: v.tensor_copy(
            out=wexp[:].rearrange("p (a b) -> p a b", a=8),
            in_=w8[:].rearrange("p (a b) -> p a b", b=1).broadcast_to((128, 8, 16))),
            deps=[w8o], inc=("s_v", 1), tag=f"wexp{t}")
        whmul = A("vector", lambda v: v.tensor_tensor(
            out=wh_bf[:], in0=ps_h[:], in1=wexp[:], op=ALU.mult),
            deps=[wexpo, hmm_], inc=("s_v", 1), tag=f"whmul{t}")
        prev[(t, 'whmul')] = whmul
        whtr = A("sync", lambda s: s.dma_start(out=whT[:], in_=wh_bf[:], transpose=True),
                 deps=[whmul, prev.get((t - 1, 'q3mm'))], inc=("s_t", 16), tag=f"whtr{t}")

        def make_quarter(q, extra_deps):
            bdeps = [pones] + wload + list(extra_deps)

            def bias_fn(te, q=q):
                r = None
                for c in range(2):
                    r = te.matmul(ps_y[:, q % 2, c*512:(c+1)*512], ones1[:],
                                  b_bf[:, q*1024 + c*512:q*1024 + (c+1)*512],
                                  start=True, stop=False)
                return r
            bmmop = A("tensor", bias_fn, deps=bdeps, inc=("s_mm", 1), tag=f"bias{q}_{t}")

            last_g = bmmop
            for j in range(8):
                gi = q * 8 + j
                wdeps = [prev.get(('g', t * 32 + gi - 2))]
                if t == 0:
                    wdeps += [mst_done[-1], mst_done[-2]]
                wld = A("sync", lambda s, q=q, j=j, sl2=(q*8+j) % 2: s.dma_start(
                    out=wstr[:, sl2, :, :],
                    in_=wmaster.rearrange("(n p) c -> p n c", p=128)[:, 4*j:4*j+4, q*1024:(q+1)*1024]),
                    deps=wdeps, inc=(f"s_wl{(q*8+j) % 2}", 16), tag=f"wld{q}_{j}_{t}")

                def g_fn(te, q=q, j=j, sl2=(q*8+j) % 2):
                    r = None
                    for ii in range(4):
                        for c in range(2):
                            r = te.matmul(ps_y[:, q % 2, c*512:(c+1)*512],
                                          uT[:, 4*j + ii, :],
                                          wstr[:, sl2, ii, c*512:(c+1)*512],
                                          start=False, stop=False)
                    return r
                gop = A("tensor", g_fn, deps=[wld, last_g, utr],
                        inc=("s_mm", 1), tag=f"qmmg{q}_{j}_{t}")
                prev[('g', t * 32 + gi)] = gop
                last_g = gop

            def comb_fn(te, q=q):
                r = None
                for c in range(2):
                    r = te.matmul(ps_y[:, q % 2, c*512:(c+1)*512], whT[:],
                                  bm_bf[:, q*1024 + c*512:q*1024 + (c+1)*512],
                                  start=False, stop=True)
                return r
            qmm = A("tensor", comb_fn, deps=[last_g, whtr], inc=("s_mm", 1), tag=f"comb{q}_{t}")
            if q == 3:
                prev[(t, 'q3mm')] = qmm
            return qmm

        sto, q_readers, _ = emit_quarter_stats_epilogue(
            P, A, ctx, t, xap, ln_y, make_quarter, prev.get((t - 1, 'qr')), True,
            x_out[t*128:(t+1)*128, :], f"f{t}_")
        prev[(t, 'qr')] = q_readers
        prev[(t, 'sto')] = sto
        prev[(t, 'epi')] = sto

    return [prev[(t, 'sto')] for t in range(NT)]



def host_prep_l1a(inp):
    L = 4096
    k = np.arange(L)
    Om = np.exp(-2j*np.pi*k/L)
    T0 = np.float32(np.imag(2*(1-Om)/(1+Om))).reshape(128, 32)
    cc = 2.0/(1+Om)
    cr = np.float32(cc.real).reshape(128, 32)
    ci = np.float32(cc.imag).reshape(128, 32)
    n0 = np.arange(64)
    F64r = np.float32(np.cos(2*np.pi*np.outer(n0, n0)/64))
    F64i = np.float32(np.sin(2*np.pi*np.outer(n0, n0)/64))
    Twr = np.float32(np.cos(2*np.pi*np.outer(n0, n0)/L))
    Twi = np.float32(np.sin(2*np.pi*np.outer(n0, n0)/L))
    I64 = np.eye(64, dtype=np.float32)
    I128 = np.eye(128, dtype=np.float32)
    return {
        "a_T0": np.ascontiguousarray(T0), "a_cr": np.ascontiguousarray(cr),
        "a_ci": np.ascontiguousarray(ci),
        "a_f64r": F64r, "a_f64in": np.ascontiguousarray(-F64i), "a_f64i": F64i,
        "a_twr": Twr, "a_twi": Twi, "a_i64": I64, "a_i128": I128,
        "a_lam": np.ascontiguousarray(np.asarray(inp['L1_attn_Lambda_ri'])),
        "a_p": np.ascontiguousarray(np.asarray(inp['L1_attn_p_ri'])),
        "a_q": np.ascontiguousarray(np.asarray(inp['L1_attn_q_ri'])),
        "a_vc": np.ascontiguousarray(np.asarray(inp['L1_attn_Vc_ri']).reshape(64, 128)),
        "a_ct": np.ascontiguousarray(np.asarray(inp['L1_attn_Ct_ri']).reshape(64, 2)),
        "a_bp": np.ascontiguousarray(np.asarray(inp['L1_attn_Bp'])),
        "a_d": np.ascontiguousarray(np.asarray(inp['L1_attn_D']).reshape(1, 1)),
        "a_ls": np.ascontiguousarray(np.asarray(inp['L1_attn_log_step']).reshape(1, 1)),
    }


def build_l1a(nc, x_in, x_out, es, P, uniq=""):
    D = lambda n, s: nc.dram_tensor(uniq + n, s, F32, kind="ExternalInput").ap()
    t0_d = D("a_T0", [128, 32]); cr_d = D("a_cr", [128, 32]); ci_d = D("a_ci", [128, 32])
    f64r_d = D("a_f64r", [64, 64]); f64in_d = D("a_f64in", [64, 64]); f64i_d = D("a_f64i", [64, 64])
    twr_d = D("a_twr", [64, 64]); twi_d = D("a_twi", [64, 64])
    i64_d = D("a_i64", [64, 64]); i128_d = D("a_i128", [128, 128])
    lam_d = D("a_lam", [64, 2]); p_d = D("a_p", [64, 2]); q_d = D("a_q", [64, 2])
    vc_d = D("a_vc", [64, 128]); ct_d = D("a_ct", [64, 2]); bp_d = D("a_bp", [64, 1])
    dd_d = D("a_d", [1, 1]); ls_d = D("a_ls", [1, 1])
    scr_d = nc.dram_tensor(uniq + "a_scr", [16384], F32, kind="Internal").ap()

    T = lambda n, s, d: es.enter_context(nc.sbuf_tensor(uniq + "as_" + n, s, d))
    sm = T("sm", [64, 224], F32)
    vpk = T("vpk", [64, 16], F32)
    bcz = T("bcz", [128, 16, 64], F32)
    Gt = T("Gt", [128, 32], F32)
    crs = T("crs", [128, 32], F32); cis = T("cis", [128, 32], F32)
    di3 = T("di3", [128, 2048], F32)
    d23 = T("d23", [128, 2048], F32)
    recr = T("recr", [128, 2048], F32)
    reci = T("reci", [128, 2048], F32)
    kk = T("kk", [128, 8, 32], F32)
    cw = T("cw", [128, 8, 32], F32)
    f64r = T("f64r", [64, 64], F32); f64in = T("f64in", [64, 64], F32); f64i = T("f64i", [64, 64], F32)
    twr = T("twr", [64, 64], F32); twi = T("twi", [64, 64], F32)
    i64 = T("i64", [64, 64], F32); i128 = T("i128", [128, 128], F32)
    amr = T("amr", [64, 64], F32); ami = T("ami", [64, 64], F32)
    a2r = T("a2r", [64, 64], F32); a2i = T("a2i", [64, 64], F32)
    a2rt = T("a2rt", [64, 64], F32); a2it = T("a2it", [64, 64], F32)
    tw1 = T("tw1", [64, 64], F32); tw2 = T("tw2", [64, 64], F32)
    xsb = T("xsb", [64, 64], F32)
    ksb = T("ksb", [1, 4096], F32)
    kbf = T("kbf", [1, 4352], BF16)
    ball = T("ball", [128, 35 * 128], BF16)
    dscal = T("dscal", [128, 1], F32)
    dg = T("dg", [128, 128], F32)
    rstep = T("rstep", [1, 1], F32)
    rstep_b = T("rstep_b", [128, 1], F32)
    onesf = T("onesf", [1, 128], F32)
    xbuf = T("xbuf", [128, 2, DM], F32)
    u_bf = T("u_bf", [128, DM], BF16)
    uT = T("uT", [128, NI, 128], BF16)
    tmin = T("tmin", [128, 1024], F32)
    ysb = T("ysb", [128, 2048], F32)
    red32 = T("red32", [128, 32], F32)
    ysum4 = T("ysum4", [128, 4], F32)
    yssq4 = T("yssq4", [128, 4], F32)
    ln_x = alloc_ln(T, "x" + uniq, NT)
    ln_y = alloc_ln(T, "y" + uniq, NT)

    ps_y = es.enter_context(nc.psum_tensor(uniq + "a_ps_y", [128, 2, 1024], F32))
    ps_k = es.enter_context(nc.psum_tensor(uniq + "a_ps_k", [64, 256], F32))
    ps_b = es.enter_context(nc.psum_tensor(uniq + "a_ps_b", [128, 8], F32))

    ctx = {"u_bf": u_bf, "uT": uT, "red32": red32, "tmin": tmin,
           "ysum4": ysum4, "yssq4": yssq4, "ps_y": ps_y, "ysb": ysb}

    A = P.add
    lds = []
    for ap_d, ap_s, tg in [(t0_d, Gt, "t0"), (cr_d, crs, "cr"), (ci_d, cis, "ci"),
                           (f64r_d, f64r, "f64r"), (f64in_d, f64in, "f64in"), (f64i_d, f64i, "f64i"),
                           (twr_d, twr, "twr"), (twi_d, twi, "twi"), (i64_d, i64, "i64"),
                           (i128_d, i128, "i128"), (dd_d, rstep, "dd0")]:
        if tg == "dd0":
            continue
        lds.append(A("sync", lambda s, a=ap_d, b=ap_s: s.dma_start(out=b[:], in_=a[:]),
                     inc=("s_pre", 16), tag="ld" + tg))
    small_map = [(lam_d, 0, 2), (p_d, 2, 2), (q_d, 4, 2), (ct_d, 6, 2), (vc_d, 8, 128), (bp_d, 136, 1)]
    for ap_d, off, w in small_map:
        lds.append(A("sync", lambda s, a=ap_d, o=off, w=w: s.dma_start(
            out=sm[:, o:o + w], in_=a[:]), inc=("s_pre", 16), tag=f"lds{off}"))
    ldls = A("sync", lambda s: s.dma_start(out=rstep[:], in_=ls_d[:]), inc=("s_pre", 16), tag="ldls")
    lddd = A("sync", lambda s: s.dma_start(out=dscal[0:1, :], in_=dd_d[:]), inc=("s_pre", 16), tag="lddd")
    lds += [ldls, lddd]

    rse = A("scalar", lambda sc: sc.activation(out=rstep[:], in_=rstep[:], func=AF.Exp,
                                               bias=0.0, scale=-1.0),
            deps=lds, inc=("s_sc", 1), tag="rse")
    pones = A("vector", lambda v: v.memset(onesf[:], 1.0), inc=("s_v", 1), tag="onesf")

    def bc_fn(te):
        te.matmul(ps_b[:, 0:1], onesf[:], rstep[:], start=True, stop=True)
        return te.matmul(ps_b[:, 1:2], onesf[:], dscal[0:1, :], start=True, stop=True)
    bcmm = A("tensor", bc_fn, deps=[rse, pones] + lds, inc=("s_mm", 1), tag="bcmm")
    rsb = A("vector", lambda v: v.tensor_copy(out=rstep_b[:], in_=ps_b[:, 0:1]),
            deps=[bcmm], inc=("s_v", 1), tag="rsbc")
    dbc = A("vector", lambda v: v.tensor_copy(out=dscal[:], in_=ps_b[:, 1:2]),
            deps=[bcmm], inc=("s_v", 1), tag="dbcc")

    wbp = A("sync", lambda s: s.dma_start(out=scr_d[0:64], in_=sm[:, 136:137]),
            deps=lds, inc=("s_pre", 16), tag="wbp")
    rbp = A("sync", lambda s: s.dma_start(
        out=sm[:, 140:204].rearrange("p m -> p m") if False else sm[:, 140:204],
        in_=scr_d[0:64].rearrange("(o m) -> o m", o=1).broadcast_to((64, 64))),
        deps=[wbp], inc=("s_pre", 16), tag="rbp")

    def bv_fn(v):
        vcr = sm[:, 8:136].rearrange("p (m two) -> p m two", two=2)[:, :, 0:1].rearrange("p m one -> p (m one)")
        vci = sm[:, 8:136].rearrange("p (m two) -> p m two", two=2)[:, :, 1:2].rearrange("p m one -> p (m one)")
        bpb = sm[:, 140:204]
        P.mid(v.tensor_tensor(out=vpk[:, 8:9].broadcast_to((64, 1)) if False else cw[0:64, 0, 0:64] if False else sm[:, 76+128:140+64] if False else di3[0:64, 0:64], in0=vcr, in1=bpb, op=ALU.mult))
        P.mid(v.tensor_reduce(out=vpk[:, 8:9], in_=di3[0:64, 0:64], axis=AX.X, op=ALU.add))
        P.mid(v.tensor_tensor(out=di3[0:64, 0:64], in0=vci, in1=bpb, op=ALU.mult))
        return v.tensor_reduce(out=vpk[:, 9:10], in_=di3[0:64, 0:64], axis=AX.X, op=ALU.add)
    bv = A("vector", bv_fn, deps=lds + [rbp], inc=("s_v", 1), tag="bv")

    def vprod_fn(v):
        ctr = sm[:, 6:7]; cti = sm[:, 7:8]
        pr = sm[:, 2:3]; pi = sm[:, 3:4]
        qr = sm[:, 4:5]; qi = sm[:, 5:6]
        br = vpk[:, 8:9]; bi = vpk[:, 9:10]
        P.mid(v.tensor_tensor(out=vpk[:, 0:1], in0=ctr, in1=br, op=ALU.mult))
        P.mid(v.tensor_tensor(out=vpk[:, 10:11], in0=cti, in1=bi, op=ALU.mult))
        P.mid(v.tensor_tensor(out=vpk[:, 0:1], in0=vpk[:, 0:1], in1=vpk[:, 10:11], op=ALU.add))
        P.mid(v.tensor_tensor(out=vpk[:, 1:2], in0=ctr, in1=bi, op=ALU.mult))
        P.mid(v.tensor_tensor(out=vpk[:, 10:11], in0=cti, in1=br, op=ALU.mult))
        P.mid(v.tensor_tensor(out=vpk[:, 1:2], in0=vpk[:, 1:2], in1=vpk[:, 10:11], op=ALU.subtract))
        P.mid(v.tensor_tensor(out=vpk[:, 2:3], in0=ctr, in1=pr, op=ALU.mult))
        P.mid(v.tensor_tensor(out=vpk[:, 10:11], in0=cti, in1=pi, op=ALU.mult))
        P.mid(v.tensor_tensor(out=vpk[:, 2:3], in0=vpk[:, 2:3], in1=vpk[:, 10:11], op=ALU.add))
        P.mid(v.tensor_tensor(out=vpk[:, 3:4], in0=ctr, in1=pi, op=ALU.mult))
        P.mid(v.tensor_tensor(out=vpk[:, 10:11], in0=cti, in1=pr, op=ALU.mult))
        P.mid(v.tensor_tensor(out=vpk[:, 3:4], in0=vpk[:, 3:4], in1=vpk[:, 10:11], op=ALU.subtract))
        P.mid(v.tensor_tensor(out=vpk[:, 4:5], in0=qr, in1=br, op=ALU.mult))
        P.mid(v.tensor_tensor(out=vpk[:, 10:11], in0=qi, in1=bi, op=ALU.mult))
        P.mid(v.tensor_tensor(out=vpk[:, 4:5], in0=vpk[:, 4:5], in1=vpk[:, 10:11], op=ALU.add))
        P.mid(v.tensor_tensor(out=vpk[:, 5:6], in0=qr, in1=bi, op=ALU.mult))
        P.mid(v.tensor_tensor(out=vpk[:, 10:11], in0=qi, in1=br, op=ALU.mult))
        P.mid(v.tensor_tensor(out=vpk[:, 5:6], in0=vpk[:, 5:6], in1=vpk[:, 10:11], op=ALU.subtract))
        P.mid(v.tensor_tensor(out=vpk[:, 6:7], in0=qr, in1=pr, op=ALU.mult))
        P.mid(v.tensor_tensor(out=vpk[:, 10:11], in0=qi, in1=pi, op=ALU.mult))
        P.mid(v.tensor_tensor(out=vpk[:, 6:7], in0=vpk[:, 6:7], in1=vpk[:, 10:11], op=ALU.add))
        P.mid(v.tensor_tensor(out=vpk[:, 7:8], in0=qr, in1=pi, op=ALU.mult))
        P.mid(v.tensor_tensor(out=vpk[:, 10:11], in0=qi, in1=pr, op=ALU.mult))
        P.mid(v.tensor_tensor(out=vpk[:, 7:8], in0=vpk[:, 7:8], in1=vpk[:, 10:11], op=ALU.subtract))
        P.mid(v.tensor_copy(out=vpk[:, 8:9], in_=sm[:, 0:1]))
        P.mid(v.tensor_copy(out=vpk[:, 9:10], in_=sm[:, 1:2]))
        return v.tensor_tensor(out=vpk[:, 10:11], in0=sm[:, 0:1], in1=sm[:, 0:1], op=ALU.mult)
    vprod = A("vector", vprod_fn, deps=[bv], inc=("s_v", 1), tag="vprod")

    wpk = A("sync", lambda s: s.dma_start(
        out=scr_d[256:256 + 11 * 64].rearrange("(x n) -> x n", x=11).transpose([1, 0]),
        in_=vpk[:, 0:11], allow_slow_non_contiguous=True),
        deps=[vprod], inc=("s_pre", 16), tag="wpk")
    rpk = A("sync", lambda s: s.dma_start(
        out=bcz[:, 0:11, :],
        in_=scr_d[256:256 + 11 * 64].rearrange("(x n) -> x n", x=11).unsqueeze(0).broadcast_to((128, 11, 64))),
        deps=[wpk], inc=("s_pre", 16), tag="rpk")

    def gt_fn(v):
        return v.tensor_scalar(out=Gt[:], in0=Gt[:], scalar1=rstep_b[:], scalar2=None, op0=ALU.mult)
    gt = A("vector", gt_fn, deps=[rsb] + lds, inc=("s_v", 1), tag="gt")

    def cauchy_fn(v):
        lam_i = bcz[:, 9:10, :].broadcast_to((128, 32, 64))
        lam_r = bcz[:, 8:9, :].broadcast_to((128, 32, 64))
        lr2 = bcz[:, 10:11, :].broadcast_to((128, 32, 64))
        d3 = di3[:].rearrange("p (j n) -> p j n", j=32)
        d2 = d23[:].rearrange("p (j n) -> p j n", j=32)
        rr = recr[:].rearrange("p (j n) -> p j n", j=32)
        ri = reci[:].rearrange("p (j n) -> p j n", j=32)
        g3 = Gt[:].rearrange("p (j o) -> p j o", o=1).broadcast_to((128, 32, 64))
        P.mid(v.tensor_tensor(out=d3, in0=g3, in1=lam_i, op=ALU.subtract))
        P.mid(v.tensor_tensor(out=d2, in0=d3, in1=d3, op=ALU.mult))
        P.mid(v.tensor_tensor(out=d2, in0=d2, in1=lr2, op=ALU.add))
        P.mid(v.reciprocal(out=d23[:], in_=d23[:]))
        P.mid(v.tensor_tensor(out=rr, in0=lam_r, in1=d2, op=ALU.mult))
        P.mid(v.tensor_scalar(out=recr[:], in0=recr[:], scalar1=-1.0, scalar2=None, op0=ALU.mult))
        P.mid(v.tensor_tensor(out=ri, in0=d3, in1=d2, op=ALU.mult))
        return v.tensor_scalar(out=reci[:], in0=reci[:], scalar1=-1.0, scalar2=None, op0=ALU.mult)
    cau = A("vector", cauchy_fn, deps=[gt, rpk] + lds, inc=("s_v", 1), tag="cauchy")

    def ksums_fn(v):
        rr = recr[:].rearrange("p (j n) -> p j n", j=32)
        ri = reci[:].rearrange("p (j n) -> p j n", j=32)
        for vi_ in range(4):
            vr = bcz[:, 2*vi_:2*vi_+1, :].broadcast_to((128, 32, 64))
            vv = bcz[:, 2*vi_+1:2*vi_+2, :].broadcast_to((128, 32, 64))
            d3 = di3[:].rearrange("p (j n) -> p j n", j=32)
            P.mid(v.tensor_tensor(out=d3, in0=vr, in1=rr, op=ALU.mult))
            P.mid(v.tensor_reduce(out=kk[:, 2*vi_, :], in_=d3, axis=AX.X, op=ALU.add))
            P.mid(v.tensor_tensor(out=d3, in0=vv, in1=ri, op=ALU.mult))
            P.mid(v.tensor_reduce(out=cw[:, 0, :], in_=d3, axis=AX.X, op=ALU.add))
            P.mid(v.tensor_tensor(out=kk[:, 2*vi_, :], in0=kk[:, 2*vi_, :], in1=cw[:, 0, :], op=ALU.subtract))
            P.mid(v.tensor_tensor(out=d3, in0=vr, in1=ri, op=ALU.mult))
            P.mid(v.tensor_reduce(out=kk[:, 2*vi_+1, :], in_=d3, axis=AX.X, op=ALU.add))
            P.mid(v.tensor_tensor(out=d3, in0=vv, in1=rr, op=ALU.mult))
            P.mid(v.tensor_reduce(out=cw[:, 0, :], in_=d3, axis=AX.X, op=ALU.add))
            P.mid(v.tensor_tensor(out=kk[:, 2*vi_+1, :], in0=kk[:, 2*vi_+1, :], in1=cw[:, 0, :], op=ALU.add))
        return v.nop()
    ksums = A("vector", ksums_fn, deps=[cau], inc=("s_v", 1), tag="ksums")

    def aroots_fn(v):
        k00r, k00i = kk[:, 0, :], kk[:, 1, :]
        k01r, k01i = kk[:, 2, :], kk[:, 3, :]
        k10r, k10i = kk[:, 4, :], kk[:, 5, :]
        k11r, k11i = kk[:, 6, :], kk[:, 7, :]
        wr, wi = cw[:, 0, :], cw[:, 1, :]
        w2, iwr = cw[:, 2, :], cw[:, 3, :]
        iwi, ur = cw[:, 4, :], cw[:, 5, :]
        ui, t = cw[:, 6, :], cw[:, 7, :]
        P.mid(v.tensor_scalar(out=wr, in0=k11r, scalar1=1.0, scalar2=None, op0=ALU.add))
        P.mid(v.tensor_copy(out=wi, in_=k11i))
        P.mid(v.tensor_tensor(out=w2, in0=wr, in1=wr, op=ALU.mult))
        P.mid(v.tensor_tensor(out=t, in0=wi, in1=wi, op=ALU.mult))
        P.mid(v.tensor_tensor(out=w2, in0=w2, in1=t, op=ALU.add))
        P.mid(v.reciprocal(out=w2, in_=w2))
        P.mid(v.tensor_tensor(out=iwr, in0=wr, in1=w2, op=ALU.mult))
        P.mid(v.tensor_tensor(out=iwi, in0=wi, in1=w2, op=ALU.mult))
        P.mid(v.tensor_scalar(out=iwi, in0=iwi, scalar1=-1.0, scalar2=None, op0=ALU.mult))
        P.mid(v.tensor_tensor(out=ur, in0=k01r, in1=k10r, op=ALU.mult))
        P.mid(v.tensor_tensor(out=t, in0=k01i, in1=k10i, op=ALU.mult))
        P.mid(v.tensor_tensor(out=ur, in0=ur, in1=t, op=ALU.subtract))
        P.mid(v.tensor_tensor(out=ui, in0=k01r, in1=k10i, op=ALU.mult))
        P.mid(v.tensor_tensor(out=t, in0=k01i, in1=k10r, op=ALU.mult))
        P.mid(v.tensor_tensor(out=ui, in0=ui, in1=t, op=ALU.add))
        P.mid(v.tensor_tensor(out=t, in0=ur, in1=iwr, op=ALU.mult))
        P.mid(v.tensor_tensor(out=wr, in0=ui, in1=iwi, op=ALU.mult))
        P.mid(v.tensor_tensor(out=wr, in0=t, in1=wr, op=ALU.subtract))
        P.mid(v.tensor_tensor(out=t, in0=ur, in1=iwi, op=ALU.mult))
        P.mid(v.tensor_tensor(out=wi, in0=ui, in1=iwr, op=ALU.mult))
        P.mid(v.tensor_tensor(out=wi, in0=t, in1=wi, op=ALU.add))
        P.mid(v.tensor_tensor(out=ur, in0=k00r, in1=wr, op=ALU.subtract))
        P.mid(v.tensor_tensor(out=ui, in0=k00i, in1=wi, op=ALU.subtract))
        P.mid(v.tensor_tensor(out=kk[:, 0, :], in0=ur, in1=crs[:], op=ALU.mult))
        P.mid(v.tensor_tensor(out=t, in0=ui, in1=cis[:], op=ALU.mult))
        P.mid(v.tensor_tensor(out=kk[:, 0, :], in0=kk[:, 0, :], in1=t, op=ALU.subtract))
        P.mid(v.tensor_tensor(out=kk[:, 1, :], in0=ur, in1=cis[:], op=ALU.mult))
        P.mid(v.tensor_tensor(out=t, in0=ui, in1=crs[:], op=ALU.mult))
        return v.tensor_tensor(out=kk[:, 1, :], in0=kk[:, 1, :], in1=t, op=ALU.add)
    aro = A("vector", aroots_fn, deps=[ksums], inc=("s_v", 1), tag="aroots")

    war = A("sync", lambda s: s.dma_start(out=scr_d[1024:1024+4096].rearrange("(p j) -> p j", p=128),
                                          in_=kk[:, 0, :]), deps=[aro], inc=("s_pre", 16), tag="war")
    wai = A("sync", lambda s: s.dma_start(out=scr_d[5120:5120+4096].rearrange("(p j) -> p j", p=128),
                                          in_=kk[:, 1, :]), deps=[aro], inc=("s_pre", 16), tag="wai")
    rar = A("sync", lambda s: s.dma_start(out=amr[:], in_=scr_d[1024:1024+4096].rearrange("(a b) -> a b", a=64)),
            deps=[war, wai], inc=("s_pre", 16), tag="rar")
    rai = A("sync", lambda s: s.dma_start(out=ami[:], in_=scr_d[5120:5120+4096].rearrange("(a b) -> a b", a=64)),
            deps=[war, wai], inc=("s_pre", 16), tag="rai")

    def st1_fn(te):
        te.matmul(ps_k[:, 0:64], f64r[:], amr[:], start=True, stop=False)
        te.matmul(ps_k[:, 0:64], f64in[:], ami[:], start=False, stop=True)
        te.matmul(ps_k[:, 64:128], f64r[:], ami[:], start=True, stop=False)
        return te.matmul(ps_k[:, 64:128], f64i[:], amr[:], start=False, stop=True)
    st1 = A("tensor", st1_fn, deps=[rar, rai] + lds, inc=("s_mm", 1), tag="st1")

    def tw_fn(v):
        P.mid(v.tensor_tensor(out=tw1[:], in0=ps_k[:, 0:64], in1=twr[:], op=ALU.mult))
        P.mid(v.tensor_tensor(out=tw2[:], in0=ps_k[:, 64:128], in1=twi[:], op=ALU.mult))
        P.mid(v.tensor_tensor(out=a2r[:], in0=tw1[:], in1=tw2[:], op=ALU.subtract))
        P.mid(v.tensor_tensor(out=tw1[:], in0=ps_k[:, 0:64], in1=twi[:], op=ALU.mult))
        P.mid(v.tensor_tensor(out=tw2[:], in0=ps_k[:, 64:128], in1=twr[:], op=ALU.mult))
        return v.tensor_tensor(out=a2i[:], in0=tw1[:], in1=tw2[:], op=ALU.add)
    tw = A("vector", tw_fn, deps=[st1], inc=("s_v", 1), tag="tw")

    def tr_fn(te):
        te.transpose(ps_k[:, 128:192], a2r[:], i64[:])
        return te.transpose(ps_k[:, 192:256], a2i[:], i64[:])
    tr = A("tensor", tr_fn, deps=[tw], inc=("s_mm", 1), tag="tr")
    trev = A("vector", lambda v: [P.mid(v.tensor_copy(out=a2rt[:], in_=ps_k[:, 128:192])),
                                  v.tensor_copy(out=a2it[:], in_=ps_k[:, 192:256])][-1],
             deps=[tr], inc=("s_v", 1), tag="trev")

    def st3_fn(te):
        te.matmul(ps_k[:, 0:64], a2rt[:], f64r[:], start=True, stop=False)
        return te.matmul(ps_k[:, 0:64], a2it[:], f64in[:], start=False, stop=True)
    st3 = A("tensor", st3_fn, deps=[trev], inc=("s_mm", 1), tag="st3")
    xev = A("vector", lambda v: v.tensor_scalar(out=xsb[:], in0=ps_k[:, 0:64],
                                                scalar1=1.0/4096, scalar2=None, op0=ALU.mult),
            deps=[st3], inc=("s_v", 1), tag="xev")

    wk = A("sync", lambda s: s.dma_start(
        out=scr_d[1024:1024+4096].rearrange("(b a) -> a b", a=64), in_=xsb[:],
        allow_slow_non_contiguous=True), deps=[xev], inc=("s_pre", 16), tag="wk")
    rk = A("sync", lambda s: s.dma_start(out=ksb[:], in_=scr_d[1024:1024+4096].rearrange("(o n) -> o n", o=1)),
           deps=[wk], inc=("s_pre", 16), tag="rk")

    kz = A("vector", lambda v: v.memset(kbf[:], 0.0), inc=("s_v", 1), tag="kz")
    kcast = A("vector", lambda v: v.tensor_copy(out=kbf[0:1, 127:127+4096], in_=ksb[:]),
              deps=[rk, kz], inc=("s_v", 1), tag="kcast")

    bz = A("vector", lambda v: v.memset(ball[:, 0:384], 0.0), inc=("s_v", 1), tag="bz")
    def ballfn(s):
        for r in range(128):
            s.dma_start(out=ball[r:r+1, 384:4480],
                        in_=kbf[0:1, 127 - r: 4223 - r]).then_inc(P.sem_handles["s_pre"], 16)
        return "self"
    ballop = A("sync", ballfn, deps=[kcast], inc=("s_pre", 16 * 128), tag="ball", self_inc=True)

    dgo = A("vector", lambda v: v.tensor_scalar(out=dg[:], in0=i128[:], scalar1=dscal[:],
                                                scalar2=None, op0=ALU.mult),
            deps=[dbc] + lds, inc=("s_v", 1), tag="dg")
    dadd = A("vector", lambda v: v.tensor_tensor(out=ball[:, 384:512], in0=ball[:, 384:512],
                                                 in1=dg[:], op=ALU.add),
             deps=[dgo, ballop, bz], inc=("s_v", 1), tag="dadd")

    prevt = {}
    for t in range(NT):
        sl = t % 2
        xap = xbuf[:, sl, :]
        deps_ldx = [dadd] if t < 2 else [prevt[(t - 2, 'sto')]]
        ldx = A("sync", lambda s, t=t, xap=xap: s.dma_start(
            out=xap, in_=x_in[t*128:(t+1)*128, :]), deps=deps_ldx,
            inc=(f"s_x{t % 2}", 16), tag=f"ldx{t}")

        uop, xsq, _ = emit_ln_input(P, A, ctx, t, xap, ln_x, [ldx],
                                    prevt.get((t - 1, 'epi')), "a")
        utr = emit_utr(P, A, ctx, uop, prevt.get((t - 1, 'lastmm')), f"a{t}")

        def make_quarter(q, extra_deps, t=t, utr=utr):
            def q_fn(te, q=q):
                r = None
                for c2 in range(2):
                    Js = 8 * q + 4 * c2
                    imax = min(32, Js + 4)
                    for I in range(imax):
                        off = (Js - I + 3) * 128
                        r = te.matmul(ps_y[:, q % 2, c2*512:(c2+1)*512],
                                      uT[:, I, :], ball[:, off:off + 512],
                                      start=(I == 0), stop=(I == imax - 1))
                return r
            deps = [utr, dadd] + list(extra_deps)
            op = A("tensor", q_fn, deps=deps, inc=("s_mm", 1), tag=f"tq{q}_{t}")
            if q == 3:
                prevt[(t, 'lastmm')] = op
            return op

        sto, q_readers, _ = emit_quarter_stats_epilogue(
            P, A, ctx, t, xap, ln_y, make_quarter, prevt.get((t - 1, 'qr')), True,
            x_out[t*128:(t+1)*128, :], f"a{t}_")
        prevt[(t, 'qr')] = q_readers
        prevt[(t, 'sto')] = sto
        prevt[(t, 'epi')] = sto

    return [prevt[(t, 'sto')] for t in range(NT)]



def host_prep_dequant(inp, core):
    qw = np.asarray(inp['L1_ffn_qweight'])
    lut8 = np.asarray(inp['L1_ffn_lut8'])
    lut4 = np.asarray(inp['L1_ffn_lut4'])
    base = core * 512
    o_idx = np.zeros((128, 64), np.int64)
    for c in range(8):
        for r in range(16):
            o_idx[16 * c + r, :] = base + np.arange(64) * 8 + c
    qq = qw[:, o_idx, :]
    qq = np.transpose(qq, (2, 1, 0, 3))
    qq = qq.reshape(16, 4, 128, 8, 128)
    qq = np.ascontiguousarray(np.transpose(qq, (0, 2, 1, 3, 4)))
    return {
        "dq_qw": qq.reshape(16, 128, 4 * 8 * 128),
        "dq_l8": np.ascontiguousarray(lut8[base:base + 512]),
        "dq_l4": np.ascontiguousarray(lut4[base:base + 512]),
    }


def build_dequant(nc, es, P, uniq="", wl_all_t=None, dd_all_t=None):
    qw_d = nc.dram_tensor(uniq + "dq_qw", [16, 128, 4096], I32, kind="ExternalInput").ap()
    l8_d = nc.dram_tensor(uniq + "dq_l8", [512, 256], F32, kind="ExternalInput").ap()
    l4_d = nc.dram_tensor(uniq + "dq_l4", [512, 16], F32, kind="ExternalInput").ap()
    fused = wl_all_t is not None
    okind = "Internal" if fused else "ExternalOutput"
    wl_t = nc.dram_tensor(uniq + "dq_wl", [DM, 512], BF16, kind=okind)
    dd_t = nc.dram_tensor(uniq + "dq_d", [DM, 512], BF16, kind=okind)
    wl_o = wl_t.ap()
    dd_o = dd_t.ap()
    st_d = nc.dram_tensor(uniq + "dq_stage", [1024, 256], BF16, kind="Internal").ap()
    ex_d = nc.dram_tensor(uniq + "dq_exd", [2, 128, 1024], BF16, kind="Internal").ap()

    T = lambda n, s, d: es.enter_context(nc.sbuf_tensor(uniq + "dqs_" + n, s, d))
    l8b = T("l8b", [128, 4, 256], BF16)
    l4b = T("l4b", [128, 4, 16], BF16)
    l4t = T("l4t", [128, 4, 256], BF16)
    db = T("db", [128, 4, 256], BF16)
    wqw = T("wqw", [128, 2, 4096], I32)
    shp = T("shp", [128, 1024], I32)
    pvec = T("pvec", [128, 1], I32)
    bits = T("bits", [128, 1024], I32)
    idx32 = T("idx32", [128, 1024], I32)
    idx16 = T("idx16", [128, 2, 1024], U16)
    data = T("data", [128, 2, 1024], BF16)
    outg = T("outg", [128, 2, 1024], BF16)
    wl_sb = T("wl_sb", [128, 4, 4096], BF16)
    d_sb = T("d_sb", [128, 4, 4096], BF16)
    tT = T("tT", [128, 2, 128], BF16)

    A = P.add
    ld8 = A("gpsimd", lambda g: g.dma_start(
        out=l8b[:], in_=l8_d.rearrange("(n p) c -> p n c", p=128)), inc=("s_w", 16), tag="ld8")
    ld4 = A("gpsimd", lambda g: g.dma_start(
        out=l4b[:], in_=l4_d.rearrange("(n p) c -> p n c", p=128)), inc=("s_w", 16), tag="ld4")

    l4to = A("vector", lambda v: v.tensor_copy(
        out=l4t[:].rearrange("p n (hi lo) -> p n hi lo", hi=16),
        in_=l4b[:].rearrange("p n (o lo) -> p n o lo", o=1).broadcast_to((128, 4, 16, 16))),
        deps=[ld4, ld8], inc=("s_v", 1), tag="l4t")
    dbo = A("vector", lambda v: v.tensor_tensor(out=db[:], in0=l8b[:], in1=l4t[:], op=ALU.subtract),
            deps=[l4to], inc=("s_v", 1), tag="db")
    stg1 = A("sync", lambda s: s.dma_start(
        out=st_d[0:512, :].rearrange("(n p) c -> p n c", p=128), in_=l4t[:]),
        deps=[l4to, dbo], inc=("s_ms0", 16), tag="stg1")
    stg2 = A("sync", lambda s: s.dma_start(
        out=st_d[512:1024, :].rearrange("(n p) c -> p n c", p=128), in_=db[:]),
        deps=[dbo], inc=("s_ms0", 16), tag="stg2")

    pio = A("gpsimd", lambda g: g.iota(pvec[:], [[0, 1]], base=0, channel_multiplier=1),
            inc=("s_g", 1), tag="pio")
    sio = A("gpsimd", lambda g: g.iota(shp[:], [[0, 4], [0, 128], [16, 2]], base=0,
                                       channel_multiplier=0),
            deps=[pio], inc=("s_g", 1), tag="sio")
    def shp_fn(v):
        P.mid(v.tensor_scalar(out=pvec[:], in0=pvec[:], scalar1=15, scalar2=None, op0=ALU.bitwise_and))
        P.mid(v.tensor_copy(out=bits[:], in_=pvec[:].broadcast_to((128, 1024))))
        return v.tensor_tensor(out=shp[:], in0=shp[:], in1=bits[:], op=ALU.add)
    shpo = A("vector", shp_fn, deps=[sio], inc=("s_v", 1), tag="shp")

    pdz = A("vector", lambda v: v.memset(data[:], 0.0), inc=("s_v", 1), tag="pdz")
    poz = A("vector", lambda v: v.memset(outg[:], 0.0), deps=[pdz], inc=("s_v", 1), tag="poz")
    import bass_rust
    prev = {}
    tstores = []
    for pc in range(NPIECE):
        sl = pc % 2
        wdeps = [prev.get(('idx', pc - 2))]
        ldq = A("sync", lambda s, pc=pc, sl=sl: s.dma_start(
            out=wqw[:, sl, :], in_=qw_d[pc]), deps=wdeps,
            inc=(f"s_x{sl}", 16), tag=f"ldq{pc}")

        def unpack_fn(v, pc=pc, sl=sl):
            wv = wqw[:, sl, :].rearrange("p (g pl w) -> p g pl w", g=4, pl=8)
            for pl in range(8):
                wap = wv[:, :, pl, :].unsqueeze(3).broadcast_to((128, 4, 128, 2))
                P.mid(v.tensor_tensor(out=bits[:].rearrange("p (g q1 q0) -> p g q1 q0", g=4, q1=128),
                                      in0=wap,
                                      in1=shp[:].rearrange("p (g q1 q0) -> p g q1 q0", g=4, q1=128),
                                      op=ALU.logical_shift_right))
                if pl == 0:
                    P.mid(v.tensor_scalar(out=idx32[:], in0=bits[:], scalar1=1, scalar2=None,
                                          op0=ALU.bitwise_and))
                else:
                    P.mid(v.tensor_scalar(out=bits[:], in0=bits[:], scalar1=1, scalar2=pl,
                                          op0=ALU.bitwise_and, op1=ALU.logical_shift_left))
                    P.mid(v.tensor_tensor(out=idx32[:], in0=idx32[:], in1=bits[:], op=ALU.bitwise_or))
            return v.tensor_copy(out=idx16[:, sl, :], in_=idx32[:])
        idxo = A("vector", unpack_fn, deps=[ldq, shpo, prev.get(('gather_last', pc - 2))],
                 inc=("s_v", 1), tag=f"idx{pc}")
        prev[('idx', pc)] = idxo

        def dld_fn(s, pc=pc, sl=sl):
            for c in range(8):
                s.dma_start(out=data[16 * c:16 * c + 8, sl, :],
                            in_=bass_rust.AP(tensor=st_d.tensor,
                                             offset=(pc * 32 + c) * 256,
                                             ap=[[0, 8], [2048, 4], [1, 256]])
                            ).then_inc(P.sem_handles[f"s_wl{sl}"], 16)
                s.dma_start(out=data[16 * c + 8:16 * c + 16, sl, :],
                            in_=bass_rust.AP(tensor=st_d.tensor,
                                             offset=512 * 256 + (pc * 32 + c) * 256,
                                             ap=[[0, 8], [2048, 4], [1, 256]])
                            ).then_inc(P.sem_handles[f"s_wl{sl}"], 16)
            return "self"
        dld = A("sync", dld_fn,
            deps=[stg1, stg2, pdz, poz, prev.get(('gather_last', pc - 2))],
            inc=(f"s_wl{sl}", 16 * 16), tag=f"dld{pc}", self_inc=True)

        g_last = None
        for gl in range(4):
            for ch in range(4):
                gsl = (gl * 4 + ch) % 2
                gk = pc * 16 + gl * 4 + ch
                gdeps = [idxo, dld, prev.get(('exg', gk - 2))]
                gop = A("gpsimd", lambda g, pc=pc, sl=sl, gl=gl, ch=ch, gsl=gsl: g.indirect_copy(
                    out=outg[:, gsl, :], data=data[:, sl, gl * 256:(gl + 1) * 256],
                    idxs=idx16[:, sl, (gl * 256 + ch * 64):(gl * 256 + (ch + 1) * 64)],
                    i_know_ap_gather_is_preferred=True),
                    deps=gdeps, inc=("s_g", 1), tag=f"g{pc}_{gl}_{ch}")
                g_o = pc * 4 + gl
                dstp = (8 * g_o) % 128
                tile = g_o // 16
                ex = A("scalar", lambda s, gsl=gsl: s.dma_start(
                    out=ex_d[gsl], in_=outg[:, gsl, :]),
                    deps=[gop, poz], inc=(f"s_ex{gsl}", 16), tag=f"ex{pc}_{gl}_{ch}")

                def ex2_fn(s, gsl=gsl, dstp=dstp, tile=tile, ch=ch):
                    s.dma_start(out=wl_sb[dstp:dstp + 8, tile, ch * 1024:(ch + 1) * 1024],
                                in_=ex_d[gsl].rearrange("(a b) f -> a b f", b=16)[:, 0, :]
                                ).then_inc(P.sem_handles[f"s_ex{gsl}"], 16)
                    s.dma_start(out=d_sb[dstp:dstp + 8, tile, ch * 1024:(ch + 1) * 1024],
                                in_=ex_d[gsl].rearrange("(a b) f -> a b f", b=16)[:, 8, :]
                                ).then_inc(P.sem_handles[f"s_ex{gsl}"], 16)
                    return "self"
                ex2 = A("scalar", ex2_fn, deps=[ex], inc=(f"s_ex{gsl}", 32), tag=f"ex2{pc}_{gl}_{ch}",
                        self_inc=True)
                prev[('exg', gk)] = ex2
                g_last = ex2
        prev[('gather_last', pc)] = g_last

    all_ex = [prev[('exg', gk)] for gk in range(NPIECE * 16)]
    tprev = [None, None]
    for which, (sbuf, outd) in enumerate([(wl_sb, wl_o), (d_sb, dd_o)]):
        for ot in range(4):
            for ib in range(32):
                k = which * 128 + ot * 32 + ib
                sl = k % 2
                tr = A("sync", lambda s, sbuf=sbuf, ot=ot, ib=ib, sl=sl: s.dma_start(
                    out=tT[:, sl, :], in_=sbuf[:, ot, ib * 128:(ib + 1) * 128], transpose=True),
                    deps=all_ex if k == 0 else [all_ex[-1], tprev[sl]],
                    inc=(f"s_t{sl}", 16), tag=f"tr{k}")
                st = A("sync", lambda s, outd=outd, ot=ot, ib=ib, sl=sl: s.dma_start(
                    out=outd[ib * 128:(ib + 1) * 128, ot * 128:(ot + 1) * 128], in_=tT[:, sl, :]),
                    deps=[tr], inc=(f"s_st{sl}", 16), tag=f"st{k}")
                tprev[sl] = st
                tstores.append(st)

    if not fused:
        def fin_fn(s):
            s.wait_ge(P.sem_handles["s_st0"], 16 * 128)
            s.wait_ge(P.sem_handles["s_st1"], 16 * 128)
            return s.nop()
        A("sync", fin_fn, deps=[], tag="fin")
        return P

    ccw = A("gpsimd", lambda g: g.collective_compute(
        "AllGather", ALU.bypass, replica_groups=[list(range(8))],
        ins=[wl_t.ap().opt()], outs=[wl_all_t.ap().opt()]),
        deps=tstores, inc=("s_ccw", 1), tag="ccw")
    ccd = A("gpsimd", lambda g: g.collective_compute(
        "AllGather", ALU.bypass, replica_groups=[list(range(8))],
        ins=[dd_t.ap().opt()], outs=[dd_all_t.ap().opt()]),
        deps=tstores, inc=("s_ccd", 1), tag="ccd")
    return [ccw, ccd]



def host_prep_l1f(inp):
    return {"lf_b": np.ascontiguousarray(np.asarray(inp['L1_ffn_bias']).reshape(1, DM))}


def build_l1f(nc, x_in, x_out, es, P, uniq="", wl_all=None, dd_all=None):
    wl8 = wl_all.rearrange("(r n p) c -> p r n c", p=128, r=8)
    dd8 = dd_all.rearrange("(r n p) c -> p r n c", p=128, r=8)
    bia = nc.dram_tensor(uniq + "lf_b", [1, DM], F32, kind="ExternalInput").ap()

    T = lambda n, s, d: es.enter_context(nc.sbuf_tensor(uniq + "lfs_" + n, s, d))
    b_bf = T("b_bf", [1, DM], BF16)
    ones1 = T("ones1", [1, 128], BF16)
    wstr = T("wstr", [128, 2, 4, 1024], BF16)
    dstr = T("dstr", [128, 2, 4, 1024], BF16)
    xbuf = T("xbuf", [128, 2, DM], F32)
    u_bf = T("u_bf", [128, DM], BF16)
    u_f = T("u_f", [128, DM], F32)
    sq_scr = T("sq_scr", [128, DM], F32)
    red256 = T("red256", [128, 256], F32)
    uT = T("uT", [128, NI, 128], BF16)
    tmin = T("tmin", [128, 1024], F32)
    ycb = T("ycb", [128, DM], F32)
    e2 = T("e2", [128, 1], F32)
    mask = T("mask", [128, 1], F32)
    red32 = T("red32", [128, 32], F32)
    ysum4 = T("ysum4", [128, 4], F32)
    yssq4 = T("yssq4", [128, 4], F32)
    ln_x = alloc_ln(T, "x" + uniq + "f1", NT)
    ln_y = alloc_ln(T, "y" + uniq + "f1", NT)

    ps_y = es.enter_context(nc.psum_tensor(uniq + "lf_ps_y", [128, 2, 1024], F32))
    ps_d = es.enter_context(nc.psum_tensor(uniq + "lf_ps_d", [128, 2, 1024], F32))

    ctx = {"u_bf": u_f, "uT": uT, "red32": red32, "tmin": tmin, "sq_scr": sq_scr,
           "red256": red256, "ysum4": ysum4, "yssq4": yssq4}

    A = P.add
    wload = [A("gpsimd", lambda g: g.dma_start(out=b_bf[:], in_=bia[:]), inc=("s_w", 16), tag="lfb")]
    pones = A("vector", lambda v: v.memset(ones1[:], 1.0), inc=("s_v", 1), tag="lfones")

    prev = {}
    for t in range(NT):
        sl = t % 2
        xap = xbuf[:, sl, :]
        deps_ldx = [] if t < 2 else [prev[(t - 2, 'sto')]]
        ldx = A("sync", lambda s, t=t, xap=xap: s.dma_start(
            out=xap, in_=x_in[t*128:(t+1)*128, :]), deps=deps_ldx,
            inc=(f"s_x{sl}", 16), tag=f"lfldx{t}")

        uop, xsq, _ = emit_ln_input(P, A, ctx, t, xap, ln_x, [ldx],
                                    prev.get((t - 1, 'e2o')), "lf", three_stage=True)
        ucast = A("vector", lambda v: v.tensor_copy(out=u_bf[:], in_=u_f[:]),
                  deps=[uop, prev.get((t - 1, 'utr'))], inc=("s_v", 1), tag=f"lfuc{t}")
        utr = emit_utr(P, A, {"u_bf": u_bf, "uT": uT}, ucast, prev.get((t - 1, 'lastmm')), f"lf{t}")
        prev[(t, 'utr')] = utr

        esq = A("scalar", lambda sc: sc.activation(out=sq_scr[:], in_=u_f[:], func=AF.Square),
                deps=[uop], inc=("s_sc", 1), tag=f"lfesq{t}")
        prev[(t, 'esq')] = esq

        def e2_fn(v):
            P.mid(v.tensor_reduce(out=red256[:], in_=sq_scr[:].rearrange("p (a b) -> p a b", a=256),
                                  axis=AX.X, op=ALU.add))
            P.mid(v.tensor_reduce(out=red32[:, 0:16],
                                  in_=red256[:].rearrange("p (a b) -> p a b", a=16),
                                  axis=AX.X, op=ALU.add))
            return v.tensor_reduce(out=e2[:], in_=red32[:, 0:16], axis=AX.X, op=ALU.add)
        e2o = A("vector", e2_fn, deps=[esq], inc=("s_v", 1), tag=f"lfe2{t}")
        prev[(t, 'e2o')] = e2o
        msk = A("gpsimd", lambda g: g.tensor_scalar(
            out=mask[:], in0=e2[:], scalar1=4096.0, scalar2=None, op0=ALU.is_gt),
            deps=[e2o], inc=("s_g", 1), tag=f"lfmsk{t}")

        def make_quarter(q, extra_deps, t=t, utr=utr, msk=msk):
            def bias_fn(te, q=q):
                r = None
                for c in range(2):
                    r = te.matmul(ps_y[:, q % 2, c*512:(c+1)*512], ones1[:],
                                  b_bf[:, q*1024 + c*512:q*1024 + (c+1)*512],
                                  start=True, stop=False)
                return r
            bmm = A("tensor", bias_fn, deps=[pones] + wload + list(extra_deps),
                    inc=("s_mm", 1), tag=f"lfb{q}_{t}")
            last = bmm
            for j in range(8):
                gi = q * 8 + j
                wsl = gi % 2
                wdeps = [prev.get(('g', t * 32 + gi - 2))]
                def wld_fn(s, q=q, j=j, wsl=wsl):
                    s.dma_start(out=wstr[:, wsl, :, 0:512],
                                in_=wl8[:, 2*q, 4*j:4*j+4, :]
                                ).then_inc(P.sem_handles[f"s_wl{wsl}"], 16)
                    s.dma_start(out=wstr[:, wsl, :, 512:1024],
                                in_=wl8[:, 2*q+1, 4*j:4*j+4, :]
                                ).then_inc(P.sem_handles[f"s_wl{wsl}"], 16)
                    return "self"
                wld = A("sync", wld_fn, deps=wdeps, inc=(f"s_wl{wsl}", 32),
                        tag=f"lfwl{q}_{j}_{t}", self_inc=True)

                def dld_fn(s, q=q, j=j, wsl=wsl):
                    s.dma_start(out=dstr[:, wsl, :, 0:512],
                                in_=dd8[:, 2*q, 4*j:4*j+4, :]
                                ).then_inc(P.sem_handles[f"s_dl{wsl}"], 16)
                    s.dma_start(out=dstr[:, wsl, :, 512:1024],
                                in_=dd8[:, 2*q+1, 4*j:4*j+4, :]
                                ).then_inc(P.sem_handles[f"s_dl{wsl}"], 16)
                    return "self"
                dldw = A("sync", dld_fn, deps=wdeps, inc=(f"s_dl{wsl}", 32),
                        tag=f"lfdl{q}_{j}_{t}", self_inc=True)

                def g_fn(te, q=q, j=j, wsl=wsl):
                    r = None
                    for ii in range(4):
                        for c in range(2):
                            te.matmul(ps_y[:, q % 2, c*512:(c+1)*512],
                                      uT[:, 4*j + ii, :],
                                      wstr[:, wsl, ii, c*512:(c+1)*512],
                                      start=False, stop=(j == 7 and ii == 3))
                            r = te.matmul(ps_d[:, q % 2, c*512:(c+1)*512],
                                          uT[:, 4*j + ii, :],
                                          dstr[:, wsl, ii, c*512:(c+1)*512],
                                          start=(j == 0 and ii == 0),
                                          stop=(j == 7 and ii == 3))
                    return r
                gop = A("tensor", g_fn, deps=[wld, dldw, last, utr],
                        inc=("s_mm", 1), tag=f"lfg{q}_{j}_{t}")
                prev[('g', t * 32 + gi)] = gop
                last = gop
            if q == 3:
                prev[(t, 'lastmm')] = last
            def yev_fn(sc, q=q):
                r = None
                for c2 in range(2):
                    r = sc.copy(out=ycb[:, q*1024 + c2*512:q*1024 + (c2+1)*512],
                                in_=ps_y[:, q % 2, c2*512:(c2+1)*512])
                return r
            yev = A("scalar", yev_fn, deps=[last], inc=("s_sc", 1), tag=f"lfyev{q}_{t}")
            cmb = A("vector", lambda v, q=q: v.scalar_tensor_tensor(
                out=ycb[:, q*1024:(q+1)*1024], in0=ps_d[:, q % 2, :],
                scalar=mask[:], in1=ycb[:, q*1024:(q+1)*1024],
                op0=ALU.mult, op1=ALU.add),
                deps=[yev, msk], inc=("s_v", 1), tag=f"lfcmb{q}_{t}")
            return cmb

        sto, q_readers, _ = emit_quarter_stats_epilogue(
            P, A, ctx, t, xap, ln_y, make_quarter, prev.get((t - 1, 'qr')), True,
            x_out[t*128:(t+1)*128, :], f"lf{t}_", ysrc_sbuf=ycb)
        prev[(t, 'qr')] = q_readers
        prev[(t, 'sto')] = sto

    return [prev[(t, 'sto')] for t in range(NT)]


def kernel(**inputs):
    inp = inputs
    x = np.ascontiguousarray(np.asarray(inp['L0_attn_hidden_states'],
                                        dtype=np.float32).reshape(16384, DM))

    nc = bass.Bass("TRN2", target_bir_lowering=False)
    x0 = nc.dram_tensor("x_in", [TOK, DM], F32, kind="ExternalInput").ap()
    x1 = nc.dram_tensor("x1", [TOK, DM], F32, kind="Internal").ap()
    x2 = nc.dram_tensor("x2", [TOK, DM], F32, kind="Internal").ap()
    x3 = nc.dram_tensor("x3", [TOK, DM], F32, kind="Internal").ap()
    xo = nc.dram_tensor("x_out", [TOK, DM], F32, kind="ExternalOutput").ap()
    wl_all_t = nc.dram_tensor("dq_wl_all", [8 * DM, 512], BF16, kind="Internal")
    dd_all_t = nc.dram_tensor("dq_dd_all", [8 * DM, 512], BF16, kind="Internal")

    es = contextlib.ExitStack()
    with es:
        P = Prog(nc)
        with contextlib.ExitStack() as es_p:
            term = build_dequant(nc, es_p, P, wl_all_t=wl_all_t, dd_all_t=dd_all_t)
        P.set_barrier(term, "barD")
        with contextlib.ExitStack() as es_p:
            term = build_l0a(nc, x0, x1, es_p, P)
        P.set_barrier(term, "barA")
        with contextlib.ExitStack() as es_p:
            term = build_l0f(nc, x1, x2, es_p, P)
        P.set_barrier(term, "barF")
        with contextlib.ExitStack() as es_p:
            term = build_l1a(nc, x2, x3, es_p, P)
        P.set_barrier(term, "barB")
        with contextlib.ExitStack() as es_p:
            term = build_l1f(nc, x3, xo, es_p, P,
                             wl_all=wl_all_t.ap(), dd_all=dd_all_t.ap())
        P.add("sync", lambda s: s.nop(), deps=term, inc=("s_fin", 1), tag="fin")
        P.emit(es)

    w = {}
    w.update(host_prep_l0a(inp))
    w.update(host_prep_l0f(inp))
    w.update(host_prep_l1a(inp))
    w.update(host_prep_l1f(inp))
    in_maps = []
    for c in range(8):
        m = dict(w)
        m["x_in"] = np.ascontiguousarray(x[c*TOK:(c+1)*TOK])
        m.update(host_prep_dequant(inp, c))
        in_maps.append(m)
    res = run_bass_kernel_spmd(nc, in_maps, core_ids=list(range(8)))
    xcat = np.concatenate([np.asarray(r["x_out"]) for r in res.results], axis=0)
    return np.ascontiguousarray(xcat.reshape(4, 4096, DM)).astype(np.float32)


# version 04
# speedup vs baseline: 1.1315x; 1.1315x over previous
import numpy as np
import contextlib
import math
import sys
sys.path.insert(0, '/opt/trn_rl_repo')
import concourse.bass as bass
import concourse.mybir as mybir
import bass_rust
from concourse.bass_utils import run_bass_kernel_spmd


F32 = mybir.dt.float32; BF16 = mybir.dt.bfloat16
I32 = mybir.dt.int32; I16 = mybir.dt.int16; U16 = mybir.dt.uint16
AX = mybir.AxisListType; ALU = mybir.AluOpType; AF = mybir.ActivationFunctionType
ENGINES = ("sync", "scalar", "vector", "gpsimd", "tensor")
NT = 16
TOK = NT * 128
DM = 4096
NI = DM // 128
RS32 = 1.0 / math.sqrt(32.0)
NPIECE = 16

"""framework.

Ops are declared in global program order with explicit dependency edges
(producer op objects). Each op runs on one engine; semaphore wait targets are
computed automatically. Same-engine deps rely on program order (no wait
emitted) except when force=True.
"""

import sys






class Op:
    __slots__ = ("engine", "fn", "deps", "incs", "post", "idx", "tag", "self_inc")

    def __init__(self, engine, fn, deps, incs, idx, tag, self_inc=False):
        self.engine = engine
        self.fn = fn
        self.deps = deps
        self.incs = incs
        self.post = {}
        self.idx = idx
        self.tag = tag
        self.self_inc = self_inc


class Prog:
    def __init__(self, nc):
        self.nc = nc
        self.ops = []
        self.sem_counts = {}
        self.sem_handles = {}
        self.chain_counts = {e: 0 for e in ENGINES}
        self._last = {}
        self.cur_engine = None
        self.cur_eng_obj = None
        self.barrier = None

    def set_barrier(self, deps, tag):
        bar = self.add("sync", lambda s: s.nop(), deps=deps,
                       inc=("s_bar", 1), tag=tag)
        self.barrier = bar
        return bar

    def mid(self, instr):
        e = self.cur_engine
        self.chain_counts[e] += 1
        h = self.sem_handles["ch_" + e]
        instr.then_inc(h, 1)
        self.cur_eng_obj.wait_ge(h, self.chain_counts[e])

    def add(self, engine, fn, deps=(), inc=None, tag="", self_inc=False):
        if inc is None:
            incs = []
        elif isinstance(inc, tuple):
            incs = [inc]
        else:
            incs = list(inc)
        if isinstance(deps, Op):
            deps = [deps]
        deps = [d for d in deps if d is not None]
        if not deps and self.barrier is not None:
            deps = [self.barrier]
        if engine in ("vector", "scalar", "gpsimd"):
            prev = self._last.get(engine)
            if prev is not None and prev.incs:
                deps = list(deps) + [prev]
        op = Op(engine, fn, deps, incs, len(self.ops), tag, self_inc)
        self.ops.append(op)
        self._last[engine] = op
        return op

    def _resolve(self):
        for op in self.ops:
            for (s, k) in op.incs:
                self.sem_counts[s] = self.sem_counts.get(s, 0) + k
                op.post[s] = self.sem_counts[s]

    def emit(self, es):
        self._resolve()
        for s in self.sem_counts:
            self.sem_handles[s] = es.enter_context(self.nc.semaphore(s))
        for e in ENGINES:
            self.sem_handles["ch_" + e] = es.enter_context(self.nc.semaphore("ch_" + e))
        nc = self.nc
        with nc.Block() as block:
            for engname in ENGINES:
                eng_ops = [o for o in self.ops if o.engine == engname]
                if not eng_ops:
                    continue

                def make(eng_ops=eng_ops, engname=engname):
                    def body(eng):
                        self.cur_engine = engname
                        self.cur_eng_obj = eng
                        for op in eng_ops:
                            waits = {}
                            for d in op.deps:
                                if not d.incs:
                                    raise RuntimeError(
                                        f"dep {d.tag} of {op.tag} has no inc")
                                s = d.incs[0][0]
                                v = d.post[s]
                                if waits.get(s, -1) < v:
                                    waits[s] = v
                            for s, v in waits.items():
                                eng.wait_ge(self.sem_handles[s], v)
                            ret = op.fn(eng)
                            if op.incs and not op.self_inc:
                                if ret is None:
                                    raise RuntimeError(
                                        f"op {op.tag} has incs but returned no instr")
                                last = ret[-1] if isinstance(ret, (list, tuple)) else ret
                                for (s, k) in op.incs:
                                    last.then_inc(self.sem_handles[s], k)
                        return None
                    return body

                getattr(block, engname)(make())
        return block



import sys







def alloc_ln(T, tag, slots):
    ln = {}
    for fld in ("sum", "ssq", "mneg", "varep", "r0", "rstd", "t1", "nt", "madj"):
        ln[fld] = T(f"ln{tag}_{fld}", [128, slots], F32)
    return ln


def emit_ln_smalls(P, A, ln, t, dep_stats, tag):
    def smalls_fn(g):
        P.mid(g.tensor_scalar(out=ln["mneg"][:, t:t+1], in0=ln["sum"][:, t:t+1],
                              scalar1=-1.0/DM, scalar2=None, op0=ALU.mult))
        P.mid(g.tensor_tensor(out=ln["t1"][:, t:t+1], in0=ln["mneg"][:, t:t+1],
                              in1=ln["mneg"][:, t:t+1], op=ALU.mult))
        P.mid(g.tensor_scalar(out=ln["varep"][:, t:t+1], in0=ln["ssq"][:, t:t+1],
                              scalar1=1.0/DM, scalar2=1e-5, op0=ALU.mult, op1=ALU.add))
        return g.tensor_tensor(out=ln["varep"][:, t:t+1], in0=ln["varep"][:, t:t+1],
                               in1=ln["t1"][:, t:t+1], op=ALU.subtract)
    smalls = A("gpsimd", smalls_fn, deps=dep_stats, inc=("s_g", 1), tag=f"sm{tag}{t}")

    def rstd0_fn(sc):
        P.mid(sc.activation(out=ln["t1"][:, t:t+1], in_=ln["varep"][:, t:t+1],
                            func=AF.Ln, bias=0.0, scale=1.0))
        return sc.activation(out=ln["r0"][:, t:t+1], in_=ln["t1"][:, t:t+1],
                             func=AF.Exp, bias=0.0, scale=-0.5)
    rstd0 = A("scalar", rstd0_fn, deps=[smalls], inc=("s_sc", 1), tag=f"r0{tag}{t}")

    def newton_fn(g):
        P.mid(g.tensor_tensor(out=ln["nt"][:, t:t+1], in0=ln["r0"][:, t:t+1],
                              in1=ln["r0"][:, t:t+1], op=ALU.mult))
        P.mid(g.tensor_tensor(out=ln["nt"][:, t:t+1], in0=ln["nt"][:, t:t+1],
                              in1=ln["varep"][:, t:t+1], op=ALU.mult))
        P.mid(g.tensor_scalar(out=ln["nt"][:, t:t+1], in0=ln["nt"][:, t:t+1],
                              scalar1=-0.5, scalar2=1.5, op0=ALU.mult, op1=ALU.add))
        P.mid(g.tensor_tensor(out=ln["rstd"][:, t:t+1], in0=ln["r0"][:, t:t+1],
                              in1=ln["nt"][:, t:t+1], op=ALU.mult))
        return g.tensor_tensor(out=ln["madj"][:, t:t+1], in0=ln["mneg"][:, t:t+1],
                               in1=ln["rstd"][:, t:t+1], op=ALU.mult)
    newton = A("gpsimd", newton_fn, deps=[rstd0], inc=("s_g", 1), tag=f"nw{tag}{t}")
    return newton


def emit_ln_input(P, A, ctx, t, xap, ln, deps_x, dep_u_free, tag, three_stage=False):
    u_bf = ctx["u_bf"]; uT = ctx["uT"]; red32 = ctx["red32"]
    xsq_out = ctx.get("sq_scr")
    xsq_out = u_bf[:] if xsq_out is None else xsq_out[:]
    xsq = A("scalar", lambda sc: sc.activation(
        out=xsq_out, in_=xap, func=AF.Square,
        accum_out=(None if three_stage else ln["ssq"][:, t:t+1])),
        deps=deps_x + [dep_u_free], inc=("s_sc", 1), tag=f"xsq{tag}{t}")

    if three_stage:
        def ssq_fn(v):
            P.mid(v.tensor_reduce(out=ctx["red256"][:],
                                  in_=xsq_out.rearrange("p (a b) -> p a b", a=256),
                                  axis=AX.X, op=ALU.add))
            P.mid(v.tensor_reduce(out=red32[:, 0:16],
                                  in_=ctx["red256"][:].rearrange("p (a b) -> p a b", a=16),
                                  axis=AX.X, op=ALU.add))
            return v.tensor_reduce(out=ln["ssq"][:, t:t+1], in_=red32[:, 0:16],
                                   axis=AX.X, op=ALU.add)
        xssq = A("vector", ssq_fn, deps=[xsq], inc=("s_v", 1), tag=f"xssq{tag}{t}")
        stats_dep = [xssq]
    else:
        stats_dep = [xsq]

    def xsum_fn(v):
        P.mid(v.tensor_reduce(out=red32[:], in_=xap.rearrange("p (a b) -> p a b", a=32),
                              axis=AX.X, op=ALU.add))
        return v.tensor_reduce(out=ln["sum"][:, t:t+1], in_=red32[:],
                               axis=AX.X, op=ALU.add)
    xsum = A("vector", xsum_fn, deps=deps_x, inc=("s_v", 1), tag=f"xsum{tag}{t}")

    newton = emit_ln_smalls(P, A, ln, t, stats_dep + [xsum], tag)

    uop = A("vector", lambda v: v.tensor_scalar(
        out=u_bf[:], in0=xap, scalar1=ln["mneg"][:, t:t+1],
        scalar2=ln["rstd"][:, t:t+1], op0=ALU.add, op1=ALU.mult),
        deps=[newton, xsq], inc=("s_v", 1), tag=f"u{tag}{t}")
    return uop, xsq, newton


def emit_utr(P, A, ctx, dep_u, dep_free, tag):
    u_bf = ctx["u_bf"]; uT = ctx["uT"]
    def utr_fn(s):
        for i in range(NI):
            s.dma_start(out=uT[:, i, :], in_=u_bf[:, i*128:(i+1)*128],
                        transpose=True).then_inc(P.sem_handles["s_t"], 16)
        return "self"
    return A("sync", utr_fn, deps=[dep_u, dep_free], inc=("s_t", 16*NI),
             tag=f"utr{tag}", self_inc=True)


def emit_quarter_stats_epilogue(P, A, ctx, t, xap, ln_y, make_quarter, prev_readers,
                                x_store, x_out_ap, tag, ysrc_sbuf=None):
    red32 = ctx["red32"]; tmin = ctx["tmin"]
    ysum4 = ctx["ysum4"]; yssq4 = ctx["yssq4"]
    ps_y = ctx.get("ps_y"); ysb = ctx.get("ysb")
    ystats = []; ysqs = []; evacs = [None, None]
    quarter_ops = []
    q_readers = {}
    prev_ysqr = None
    prev_act_ps = None
    for q in range(4):
        if ysrc_sbuf is not None:
            src = ysrc_sbuf[:, q*1024:(q+1)*1024]
        else:
            src = ps_y[:, q % 2, :]
        extra = []
        if q >= 2:
            extra += q_readers.get(q - 2, [])
        elif prev_readers is not None:
            extra += prev_readers.get(q + 2, [])
        at = make_quarter(q, extra)
        quarter_ops.append(at)
        q_readers[q] = [at]

        def yst_fn(v, src=src, q=q):
            P.mid(v.tensor_reduce(out=red32[:, 0:8],
                                  in_=src.rearrange("p (a b) -> p a b", a=8),
                                  axis=AX.X, op=ALU.add))
            return v.tensor_reduce(out=ysum4[:, q:q+1], in_=red32[:, 0:8],
                                   axis=AX.X, op=ALU.add)
        yst = A("vector", yst_fn, deps=[at, prev_act_ps], inc=("s_v", 1), tag=f"yst{tag}{q}_{t}")
        ystats.append(yst)

        def ysq0_fn(sc, src=src):
            r = None
            for c2 in range(2):
                r = sc.activation(out=tmin[:, c2*512:(c2+1)*512],
                                  in_=src[:, c2*512:(c2+1)*512], func=AF.Square)
            return r
        ysq0 = A("scalar", ysq0_fn, deps=[at, yst, prev_ysqr], inc=("s_sc", 1), tag=f"ysq0{tag}{q}_{t}")

        def ysqr_fn(v, q=q):
            P.mid(v.tensor_reduce(out=red32[:, 8:16],
                                  in_=tmin[:].rearrange("p (a b) -> p a b", a=8),
                                  axis=AX.X, op=ALU.add))
            return v.tensor_reduce(out=yssq4[:, q:q+1], in_=red32[:, 8:16],
                                   axis=AX.X, op=ALU.add)
        ysq = A("vector", ysqr_fn, deps=[ysq0], inc=("s_v", 1), tag=f"ysq{tag}{q}_{t}")
        ysqs.append(ysq); prev_ysqr = ysq
        if ysrc_sbuf is None:
            prev_act_ps = ysq0
        if ysrc_sbuf is None:
            q_readers[q] += [yst, ysq0]

        if ysrc_sbuf is None and q < 2:
            def yev_fn(sc, q=q, src=src):
                r = None
                for c2 in range(2):
                    r = sc.copy(out=ysb[:, q*1024 + c2*512:q*1024 + (c2+1)*512],
                                in_=src[:, c2*512:(c2+1)*512])
                return r
            ev = A("scalar", yev_fn, deps=[at, yst, ysq0], inc=("s_sc", 1),
                   tag=f"yev{tag}{q}_{t}")
            evacs[q] = ev
            prev_act_ps = ev
            q_readers[q].append(ev)

    def ysum_fn(g):
        P.mid(g.tensor_tensor(out=ln_y["sum"][:, t:t+1], in0=ysum4[:, 0:1], in1=ysum4[:, 1:2], op=ALU.add))
        P.mid(g.tensor_tensor(out=ln_y["t1"][:, t:t+1], in0=ysum4[:, 2:3], in1=ysum4[:, 3:4], op=ALU.add))
        P.mid(g.tensor_tensor(out=ln_y["sum"][:, t:t+1], in0=ln_y["sum"][:, t:t+1], in1=ln_y["t1"][:, t:t+1], op=ALU.add))
        P.mid(g.tensor_tensor(out=ln_y["ssq"][:, t:t+1], in0=yssq4[:, 0:1], in1=yssq4[:, 1:2], op=ALU.add))
        P.mid(g.tensor_tensor(out=ln_y["t1"][:, t:t+1], in0=yssq4[:, 2:3], in1=yssq4[:, 3:4], op=ALU.add))
        return g.tensor_tensor(out=ln_y["ssq"][:, t:t+1], in0=ln_y["ssq"][:, t:t+1], in1=ln_y["t1"][:, t:t+1], op=ALU.add)
    ysum = A("gpsimd", ysum_fn, deps=ystats + ysqs, inc=("s_g", 1), tag=f"ysum{tag}{t}")

    newton = emit_ln_smalls(P, A, ln_y, t, [ysum], "y" + tag)

    yaps = []
    for q in range(4):
        if ysrc_sbuf is not None:
            src = ysrc_sbuf[:, q*1024:(q+1)*1024]
            dps = [newton, quarter_ops[q], ysqs[q]]
        elif q < 2:
            src = ysb[:, q*1024:(q+1)*1024]
            dps = [newton, evacs[q]]
        else:
            src = ps_y[:, q % 2, :]
            dps = [newton, ystats[q], ysqs[q], quarter_ops[q]]
        yap = A("vector", lambda v, q=q, src=src: v.scalar_tensor_tensor(
            out=xap[:, q*1024:(q+1)*1024], in0=src,
            scalar=ln_y["rstd"][:, t:t+1], in1=xap[:, q*1024:(q+1)*1024],
            op0=ALU.mult, op1=ALU.add),
            deps=dps, inc=("s_v", 1), tag=f"yap{tag}{q}_{t}")
        yaps.append(yap)
        q_readers.setdefault(q, []).append(yap)

    xmadj = A("vector", lambda v: v.tensor_scalar(
        out=xap, in0=xap, scalar1=ln_y["madj"][:, t:t+1], scalar2=None, op0=ALU.add),
        deps=yaps + [newton], inc=("s_v", 1), tag=f"xmadj{tag}{t}")

    if x_store:
        sto = A("sync", lambda s: s.dma_start(
            out=x_out_ap, in_=xap), deps=[xmadj], inc=(f"s_st{t % 2}", 16), tag=f"sto{tag}{t}")
    else:
        sto = xmadj
    return sto, q_readers, ysqs[3]












TOK = NT * 128; DM = 4096; NI = DM // 128


def host_prep_l0a(inp):
    H, HKV, D = 32, 8, 128
    Wq = np.asarray(inp['L0_attn_Wq']); bq = np.asarray(inp['L0_attn_bq'])
    Wq_sel = np.ascontiguousarray(Wq.reshape(H, D, DM)[::4].reshape(HKV * D, DM).T)
    bq_sel = np.ascontiguousarray(bq.reshape(H, D)[::4].reshape(1, HKV * D))
    WoT = np.ascontiguousarray(np.asarray(inp['L0_attn_Wo']).T)
    Mz = np.ascontiguousarray(np.concatenate(
        [np.asarray(inp['L0_attn_M']), np.asarray(inp['L0_attn_z'])[:, :, None]], axis=2))
    return {"wqselT": Wq_sel, "bqsel": bq_sel, "woT": WoT, "mz": Mz}


def build_l0a(nc, x_in, x_out, es, P, uniq=""):
    nc_ = nc
    wqselT = nc.dram_tensor(uniq + "wqselT", [DM, 1024], F32, kind="ExternalInput").ap()
    bqsel = nc.dram_tensor(uniq + "bqsel", [1, 1024], F32, kind="ExternalInput").ap()
    woT = nc.dram_tensor(uniq + "woT", [DM, DM], F32, kind="ExternalInput").ap()
    mz = nc.dram_tensor(uniq + "mz", [8, 128, 129], F32, kind="ExternalInput").ap()

    T = lambda n, s, d: es.enter_context(nc.sbuf_tensor(uniq + n, s, d))
    wq_bf = T("wq_bf", [128, NI, 1024], BF16)
    wo_bf = T("wo_bf", [128, 8, DM], BF16)
    mz_bf = T("mz_bf", [128, 8, 129], BF16)
    bq_bf = T("bq_bf", [1, 1024], BF16)
    ones1 = T("ones1", [1, 128], BF16)
    xbuf = T("xbuf", [128, 2, DM], F32)
    u_bf = T("u_bf", [128, DM], BF16)
    uT = T("uT", [128, NI, 128], BF16)
    tmin = T("tmin", [128, 1024], F32)
    sig_bf = T("sig_bf", [128, 1024], BF16)
    sigT = T("sigT", [128, 8, 128], BF16)
    retr_bf = T("retr_bf", [128, 1024], BF16)
    retrT = T("retrT", [128, 8, 128], BF16)
    den8 = T("den8", [128, 8], F32)
    rden8 = T("rden8", [128, 8], F32)
    red32 = T("red32", [128, 32], F32)
    ysb = T("ysb", [128, 2048], F32)
    ysum4 = T("ysum4", [128, 4], F32)
    yssq4 = T("yssq4", [128, 4], F32)
    ln = {}
    for pre in ("x", "y"):
        for fld in ("sum", "ssq", "mneg", "varep", "r0", "rstd", "t1", "nt", "madj"):
            ln[pre + fld] = T(f"ln{pre}_{fld}", [128, NT], F32)

    ps_att = es.enter_context(nc.psum_tensor(uniq + "ps_att", [128, 2, 1024], F32))
    ps_q = es.enter_context(nc.psum_tensor(uniq + "ps_q", [128, 1024], F32))
    ps_num = es.enter_context(nc.psum_tensor(uniq + "ps_num", [128, 8, 128], F32))

    A = P.add
    wload = []
    for j in range(4):
        src = wqselT.rearrange("(n p) c -> p n c", p=128)[:, j * 8:(j + 1) * 8, :]
        wload.append(A("gpsimd", lambda g, j=j, src=src: g.dma_start(
            out=wq_bf[:, j * 8:(j + 1) * 8, :], in_=src), inc=("s_w", 16), tag=f"wq{j}"))
    wload.append(A("gpsimd", lambda g: g.dma_start(
        out=mz_bf[:], in_=mz.rearrange("h p c -> p h c")), inc=("s_w", 16), tag="mz"))
    wload.append(A("gpsimd", lambda g: g.dma_start(
        out=bq_bf[:], in_=bqsel[:]), inc=("s_w", 16), tag="bq"))
    pones = A("vector", lambda v: v.memset(ones1[:], 1.0), inc=("s_v", 1), tag="ones")

    fold_done = []
    prev_add = None
    for h in range(8):
        l0 = A("sync", lambda s, h=h: s.dma_start(
            out=xbuf[:, 0, :], in_=woT[(4 * h) * 128:(4 * h + 1) * 128, :]),
            deps=[prev_add], inc=("s_fold", 16), tag=f"fold_l0_{h}")
        prev = l0
        for g in range(1, 4):
            lg = A("sync", lambda s, h=h, g=g: s.dma_start(
                out=xbuf[:, 1, :], in_=woT[(4 * h + g) * 128:(4 * h + g + 1) * 128, :]),
                deps=[prev], inc=("s_fold", 16), tag=f"fold_l{g}_{h}")
            ad = A("vector", lambda v: v.tensor_tensor(
                out=xbuf[:, 0, :], in0=xbuf[:, 0, :], in1=xbuf[:, 1, :], op=ALU.add),
                deps=[lg], inc=("s_v", 1), tag=f"fold_a{g}_{h}")
            prev = ad
        cst = A("vector", lambda v, h=h: v.tensor_copy(out=wo_bf[:, h, :], in_=xbuf[:, 0, :]),
                deps=[prev], inc=("s_v", 1), tag=f"fold_c{h}")
        prev_add = cst
        fold_done.append(cst)

    prev_tile = {}
    for t in range(NT):
        sl = t % 2
        xap = xbuf[:, sl, :]
        deps_ldx = [fold_done[-1]] if t < 2 else [prev_tile[(t - 2, 'sto')]]
        if t >= 1:
            deps_ldx.append(prev_tile[(t - 1, 'xsum')])
        ldx = A("sync", lambda s, t=t, xap=xap: s.dma_start(
            out=xap, in_=x_in[t * 128:(t + 1) * 128, :]), deps=deps_ldx,
            inc=(f"s_x{t % 2}", 16), tag=f"ldx{t}")

        dep_u_free = prev_tile.get((t - 1, 'ysq3'))
        xsq = A("scalar", lambda sc, t=t, xap=xap: sc.activation(
            out=u_bf[:], in_=xap, func=AF.Square, accum_out=ln["xssq"][:, t:t + 1]),
            deps=[ldx, dep_u_free], inc=("s_sc", 1), tag=f"xsq{t}")

        def xsum_fn(v, t=t, xap=xap):
            P.mid(v.tensor_reduce(out=red32[:], in_=xap.rearrange("p (a b) -> p a b", a=32),
                            axis=AX.X, op=ALU.add))
            return v.tensor_reduce(out=ln["xsum"][:, t:t + 1], in_=red32[:],
                                   axis=AX.X, op=ALU.add)
        xsum = A("vector", xsum_fn, deps=[ldx], inc=("s_v", 1), tag=f"xsum{t}")

        def smalls_fn(g, t=t, pre="x"):
            P.mid(g.tensor_scalar(out=ln[pre + "mneg"][:, t:t + 1], in0=ln[pre + "sum"][:, t:t + 1],
                            scalar1=-1.0 / DM, scalar2=None, op0=ALU.mult))
            P.mid(g.tensor_tensor(out=ln[pre + "t1"][:, t:t + 1], in0=ln[pre + "mneg"][:, t:t + 1],
                            in1=ln[pre + "mneg"][:, t:t + 1], op=ALU.mult))
            P.mid(g.tensor_scalar(out=ln[pre + "varep"][:, t:t + 1], in0=ln[pre + "ssq"][:, t:t + 1],
                            scalar1=1.0 / DM, scalar2=1e-5, op0=ALU.mult, op1=ALU.add))
            return g.tensor_tensor(out=ln[pre + "varep"][:, t:t + 1],
                                   in0=ln[pre + "varep"][:, t:t + 1],
                                   in1=ln[pre + "t1"][:, t:t + 1], op=ALU.subtract)
        smalls = A("gpsimd", smalls_fn, deps=[xsq, xsum], inc=("s_g", 1), tag=f"smx{t}")

        def rstd0_fn(sc, t=t, pre="x"):
            P.mid(sc.activation(out=ln[pre + "t1"][:, t:t + 1], in_=ln[pre + "varep"][:, t:t + 1],
                          func=AF.Ln, bias=0.0, scale=1.0))
            return sc.activation(out=ln[pre + "r0"][:, t:t + 1], in_=ln[pre + "t1"][:, t:t + 1],
                                 func=AF.Exp, bias=0.0, scale=-0.5)
        rstd0 = A("scalar", rstd0_fn, deps=[smalls], inc=("s_sc", 1), tag=f"r0x{t}")

        def newton_fn(g, t=t, pre="x"):
            P.mid(g.tensor_tensor(out=ln[pre + "nt"][:, t:t + 1], in0=ln[pre + "r0"][:, t:t + 1],
                            in1=ln[pre + "r0"][:, t:t + 1], op=ALU.mult))
            P.mid(g.tensor_tensor(out=ln[pre + "nt"][:, t:t + 1], in0=ln[pre + "nt"][:, t:t + 1],
                            in1=ln[pre + "varep"][:, t:t + 1], op=ALU.mult))
            P.mid(g.tensor_scalar(out=ln[pre + "nt"][:, t:t + 1], in0=ln[pre + "nt"][:, t:t + 1],
                            scalar1=-0.5, scalar2=1.5, op0=ALU.mult, op1=ALU.add))
            return g.tensor_tensor(out=ln[pre + "rstd"][:, t:t + 1], in0=ln[pre + "r0"][:, t:t + 1],
                                   in1=ln[pre + "nt"][:, t:t + 1], op=ALU.mult)
        newton = A("gpsimd", newton_fn, deps=[rstd0], inc=("s_g", 1), tag=f"nwx{t}")

        uop = A("vector", lambda v, t=t, xap=xap: v.tensor_scalar(
            out=u_bf[:], in0=xap, scalar1=ln["xmneg"][:, t:t + 1],
            scalar2=ln["xrstd"][:, t:t + 1], op0=ALU.add, op1=ALU.mult),
            deps=[newton, xsq], inc=("s_v", 1), tag=f"u{t}")

        def utr_fn(s, t=t):
            for i in range(NI):
                s.dma_start(out=uT[:, i, :], in_=u_bf[:, i * 128:(i + 1) * 128],
                            transpose=True).then_inc(P.sem_handles["s_t"], 16)
            return "self"
        utr = A("sync", utr_fn, deps=[uop, prev_tile.get((t - 1, 'qselmm'))],
                inc=("s_t", 16 * NI), tag=f"utr{t}", self_inc=True)

        def qsel_fn(te, t=t):
            r = None
            for c in range(2):
                te.matmul(ps_q[:, c * 512:(c + 1) * 512], ones1[:],
                          bq_bf[:, c * 512:(c + 1) * 512], start=True, stop=False)
                for i in range(NI):
                    r = te.matmul(ps_q[:, c * 512:(c + 1) * 512], uT[:, i, :],
                                  wq_bf[:, i, c * 512:(c + 1) * 512],
                                  start=False, stop=(i == NI - 1))
            return r
        qselmm = A("tensor", qsel_fn,
                   deps=[utr, pones] + wload + ([prev_tile[(t - 1, 'sigstt')], prev_tile[(t - 1, 'dencp')]] if t else []),
                   inc=("s_mm", 1), tag=f"qselmm{t}")

        sigmin = A("vector", lambda v: v.tensor_scalar(
            out=tmin[:], in0=ps_q[:], scalar1=0.0, scalar2=None, op0=ALU.min),
            deps=[qselmm, prev_tile.get((t - 1, 'ysq3s'))], inc=("s_v", 1), tag=f"sigmin{t}")
        sigexp = A("scalar", lambda sc: sc.activation(
            out=tmin[:], in_=tmin[:], func=AF.Exp, bias=0.0, scale=1.0),
            deps=[sigmin], inc=("s_sc", 1), tag=f"sigexp{t}")
        sigstt = A("vector", lambda v: v.scalar_tensor_tensor(
            out=sig_bf[:], in0=ps_q[:], scalar=0.0, in1=tmin[:], op0=ALU.max, op1=ALU.add),
            deps=[sigexp], inc=("s_v", 1), tag=f"sigstt{t}")

        def sigtr_fn(s):
            for h in range(8):
                s.dma_start(out=sigT[:, h, :], in_=sig_bf[:, h * 128:(h + 1) * 128],
                            transpose=True).then_inc(P.sem_handles["s_t"], 16)
            return "self"
        sigtr = A("sync", sigtr_fn, deps=[sigstt, prev_tile.get((t - 1, 'num7'))],
                  inc=("s_t", 16 * 8), tag=f"sigtr{t}", self_inc=True)

        nums = []; retrs = []
        num_war = [prev_tile[(t - 1, 'retr7')]] if t >= 1 else []
        den_war = [sigstt]
        for h in range(8):
            nm = A("tensor", lambda te, h=h: te.matmul(
                ps_num[:, h, :], sigT[:, h, :], mz_bf[:, h, 0:128], start=True, stop=True),
                deps=[sigtr] + num_war, inc=("s_mm", 1), tag=f"num{h}_{t}")
            nums.append(nm)
            A("tensor", lambda te, h=h: te.matmul(
                ps_q[:, h:h + 1], sigT[:, h, :], mz_bf[:, h, 128:129], start=True, stop=True),
                deps=[sigtr] + den_war, inc=("s_mm", 1), tag=f"den{h}_{t}")
        dencp = A("scalar", lambda sc: sc.copy(out=den8[:], in_=ps_q[:, 0:8]),
                  deps=[P.ops[-1]], inc=("s_sc", 1), tag=f"dencp{t}")
        deneps = A("gpsimd", lambda g: g.tensor_scalar(
            out=den8[:], in0=den8[:], scalar1=1e-6, scalar2=None, op0=ALU.add),
            deps=[dencp], inc=("s_g", 1), tag=f"deneps{t}")

        def dln_fn(sc):
            P.mid(sc.activation(out=rden8[:], in_=den8[:], func=AF.Ln, bias=0.0, scale=1.0))
            return sc.activation(out=rden8[:], in_=rden8[:], func=AF.Exp,
                                 bias=0.0, scale=-1.0)
        dexp = A("scalar", dln_fn, deps=[deneps], inc=("s_sc", 1), tag=f"dexp{t}")
        for h in range(8):
            rt = A("vector", lambda v, h=h: v.tensor_scalar(
                out=retr_bf[:, h * 128:(h + 1) * 128], in0=ps_num[:, h, :],
                scalar1=rden8[:, h:h + 1], scalar2=None, op0=ALU.mult),
                deps=[dexp, nums[h]], inc=("s_v", 1), tag=f"retr{h}_{t}")
            retrs.append(rt)
        prev_tile[(t, 'retr7')] = retrs[7]

        def retrtr_fn(s):
            for h in range(8):
                s.dma_start(out=retrT[:, h, :], in_=retr_bf[:, h * 128:(h + 1) * 128],
                            transpose=True).then_inc(P.sem_handles["s_t"], 16)
            return "self"
        retrtr = A("sync", retrtr_fn,
                   deps=retrs + [prev_tile.get((t - 1, 'att3'))],
                   inc=("s_t", 16 * 8), tag=f"retrtr{t}", self_inc=True)

        atts = [None] * 4; ystats = []; ysqs = []; evacs = [None, None]
        yaps = [None] * 4
        for q in range(4):
            dps = [retrtr] + ([fold_done[-1]] if t == 0 else [])
            if q >= 2:
                dps += [evacs[q - 2], ystats[q - 2], ysqs[q - 2]]
            elif t >= 1:
                dps += [prev_tile[(t - 1, f'yap{q+2}')]]

            def att_fn(te, q=q):
                r = None
                for c2 in range(2):
                    for h in range(8):
                        r = te.matmul(ps_att[:, q % 2, c2 * 512:(c2 + 1) * 512],
                                      retrT[:, h, :],
                                      wo_bf[:, h, q * 1024 + c2 * 512: q * 1024 + (c2 + 1) * 512],
                                      start=(h == 0), stop=(h == 7))
                return r
            at = A("tensor", att_fn, deps=dps, inc=("s_mm", 1), tag=f"att{q}_{t}")
            atts[q] = at
            prev_tile[(t, f'att{q}')] = at

            def yst_fn(v, q=q, t=t):
                P.mid(v.tensor_reduce(out=red32[:, 0:8],
                                in_=ps_att[:, q % 2, :].rearrange("p (a b) -> p a b", a=8),
                                axis=AX.X, op=ALU.add))
                return v.tensor_reduce(out=ysum4[:, q:q + 1], in_=red32[:, 0:8],
                                       axis=AX.X, op=ALU.add)
            yst = A("vector", yst_fn, deps=[at], inc=("s_v", 1), tag=f"yst{q}_{t}")
            ystats.append(yst)
            def ysq0_fn(sc, q=q):
                r = None
                for c2 in range(2):
                    r = sc.activation(out=tmin[:, c2 * 512:(c2 + 1) * 512],
                                      in_=ps_att[:, q % 2, c2 * 512:(c2 + 1) * 512],
                                      func=AF.Square)
                return r
            ysq0 = A("scalar", ysq0_fn, deps=[at, yst], inc=("s_sc", 1), tag=f"ysq0_{q}_{t}")

            def ysqr_fn(v, q=q):
                P.mid(v.tensor_reduce(out=red32[:, 8:16],
                                      in_=tmin[:].rearrange("p (a b) -> p a b", a=8),
                                      axis=AX.X, op=ALU.add))
                return v.tensor_reduce(out=yssq4[:, q:q + 1], in_=red32[:, 8:16],
                                       axis=AX.X, op=ALU.add)
            ysq = A("vector", ysqr_fn, deps=[ysq0], inc=("s_v", 1), tag=f"ysq{q}_{t}")
            ysqs.append(ysq)
            if q == 3:
                prev_tile[(t, 'ysq3')] = ysq
                prev_tile[(t, 'ysq3s')] = ysq0
            if q < 2:
                evdeps = [at, yst, ysq]
                if t >= 1:
                    evdeps.append(prev_tile[(t - 1, f'yap{q}')])
                def yev_fn(sc, q=q):
                    r = None
                    for c2 in range(2):
                        r = sc.copy(out=ysb[:, q * 1024 + c2 * 512:q * 1024 + (c2 + 1) * 512],
                                    in_=ps_att[:, q % 2, c2 * 512:(c2 + 1) * 512])
                    return r
                ev = A("scalar", yev_fn, deps=evdeps, inc=("s_sc", 1), tag=f"yev{q}_{t}")
                evacs[q] = ev

        def ysum_fn(g, t=t):
            P.mid(g.tensor_tensor(out=ln["ysum"][:, t:t + 1], in0=ysum4[:, 0:1], in1=ysum4[:, 1:2], op=ALU.add))
            P.mid(g.tensor_tensor(out=ln["yt1"][:, t:t + 1], in0=ysum4[:, 2:3], in1=ysum4[:, 3:4], op=ALU.add))
            P.mid(g.tensor_tensor(out=ln["ysum"][:, t:t + 1], in0=ln["ysum"][:, t:t + 1], in1=ln["yt1"][:, t:t + 1], op=ALU.add))
            P.mid(g.tensor_tensor(out=ln["yssq"][:, t:t + 1], in0=yssq4[:, 0:1], in1=yssq4[:, 1:2], op=ALU.add))
            P.mid(g.tensor_tensor(out=ln["yt1"][:, t:t + 1], in0=yssq4[:, 2:3], in1=yssq4[:, 3:4], op=ALU.add))
            return g.tensor_tensor(out=ln["yssq"][:, t:t + 1], in0=ln["yssq"][:, t:t + 1], in1=ln["yt1"][:, t:t + 1], op=ALU.add)
        ysum = A("gpsimd", ysum_fn, deps=ystats + ysqs, inc=("s_g", 1), tag=f"ysum{t}")

        ysmalls = A("gpsimd", lambda g, t=t: smalls_fn(g, t=t, pre="y"), deps=[ysum],
                    inc=("s_g", 1), tag=f"smy{t}")
        yrstd0 = A("scalar", lambda sc, t=t: rstd0_fn(sc, t=t, pre="y"), deps=[ysmalls],
                   inc=("s_sc", 1), tag=f"r0y{t}")

        def ynewton_fn(g, t=t):
            P.mid(newton_fn(g, t=t, pre="y"))
            return g.tensor_tensor(out=ln["ymadj"][:, t:t + 1], in0=ln["ymneg"][:, t:t + 1],
                                   in1=ln["yrstd"][:, t:t + 1], op=ALU.mult)
        ynewton = A("gpsimd", ynewton_fn, deps=[yrstd0], inc=("s_g", 1), tag=f"nwy{t}")

        for q in range(4):
            if q < 2:
                src_ap = ysb[:, q * 1024:(q + 1) * 1024]
                dps = [ynewton, evacs[q]]
            else:
                src_ap = ps_att[:, q % 2, :]
                dps = [ynewton, ystats[q], ysqs[q], atts[q]]
            yap = A("vector", lambda v, q=q, t=t, xap=xap, src_ap=src_ap: v.scalar_tensor_tensor(
                out=xap[:, q * 1024:(q + 1) * 1024], in0=src_ap,
                scalar=ln["yrstd"][:, t:t + 1], in1=xap[:, q * 1024:(q + 1) * 1024],
                op0=ALU.mult, op1=ALU.add),
                deps=dps, inc=("s_v", 1), tag=f"yap{q}_{t}")
            yaps[q] = yap; prev_tile[(t, f'yap{q}')] = yap

        xmadj = A("vector", lambda v, t=t, xap=xap: v.tensor_scalar(
            out=xap, in0=xap, scalar1=ln["ymadj"][:, t:t + 1], scalar2=None, op0=ALU.add),
            deps=yaps + [ynewton], inc=("s_v", 1), tag=f"xmadj{t}")

        sto = A("sync", lambda s, t=t, xap=xap: s.dma_start(
            out=x_out[t * 128:(t + 1) * 128, :], in_=xap), deps=[xmadj],
            inc=(f"s_st{t % 2}", 16), tag=f"sto{t}")
        prev_tile[(t, 'sto')] = sto
        prev_tile[(t, 'xsum')] = xsum
        prev_tile[(t, 'qselmm')] = qselmm
        prev_tile[(t, 'sigstt')] = sigstt
        prev_tile[(t, 'num7')] = nums[7]
        prev_tile[(t, 'dencp')] = dencp
        prev_tile[(t, 'att3')] = atts[3]

    return [prev_tile[(t, 'sto')] for t in range(NT)]



def host_prep_l0f(inp):
    WT = np.ascontiguousarray(np.asarray(inp['L0_ffn_W']).T)
    WqT = np.ascontiguousarray(np.asarray(inp['L0_ffn_Wq']).T)
    WkT = np.ascontiguousarray(np.asarray(inp['L0_ffn_Wk']).T)
    AT = np.ascontiguousarray(np.asarray(inp['L0_ffn_A']).reshape(128, DM).T)
    Bm = np.asarray(inp['L0_ffn_Bm'])
    BmT = np.ascontiguousarray(np.transpose(Bm, (0, 2, 1)).reshape(128, DM))
    b = np.ascontiguousarray(np.asarray(inp['L0_ffn_b']).reshape(1, DM))
    return {"f_wT": WT, "f_wqT": WqT, "f_wkT": WkT, "f_aT": AT, "f_bmT": BmT, "f_b": b}


def build_l0f(nc, x_in, x_out, es, P, uniq=""):
    wT = nc.dram_tensor(uniq + "f_wT", [DM, DM], F32, kind="ExternalInput").ap()
    wqT = nc.dram_tensor(uniq + "f_wqT", [DM, 256], F32, kind="ExternalInput").ap()
    wkT = nc.dram_tensor(uniq + "f_wkT", [DM, 256], F32, kind="ExternalInput").ap()
    aT = nc.dram_tensor(uniq + "f_aT", [DM, 128], F32, kind="ExternalInput").ap()
    bmT = nc.dram_tensor(uniq + "f_bmT", [128, DM], F32, kind="ExternalInput").ap()
    bia = nc.dram_tensor(uniq + "f_b", [1, DM], F32, kind="ExternalInput").ap()
    wmaster = nc.dram_tensor(uniq + "f_wm", [DM, DM], BF16, kind="Internal").ap()

    T = lambda n, s, d: es.enter_context(nc.sbuf_tensor(uniq + "f_" + n, s, d))
    wq_bf = T("wq_bf", [128, NI, 256], BF16)
    wk_bf = T("wk_bf", [128, NI, 256], BF16)
    a_bf = T("a_bf", [128, NI, 128], BF16)
    bm_bf = T("bm_bf", [128, DM], BF16)
    b_bf = T("b_bf", [1, DM], BF16)
    ones1 = T("ones1", [1, 128], BF16)
    wstage = T("wstage", [128, 2, DM], BF16)
    wstr = T("wstr", [128, 2, 4, 1024], BF16)
    xbuf = T("xbuf", [128, 2, DM], F32)
    u_bf = T("u_bf", [128, DM], BF16)
    uT = T("uT", [128, NI, 128], BF16)
    tmin = T("tmin", [128, 1024], F32)
    ysb = T("ysb", [128, 2048], F32)
    qk_sb = T("qk_sb", [128, 256], F32)
    prod = T("prod", [128, 256], F32)
    sc8 = T("sc8", [128, 8], F32)
    m8 = T("m8", [128, 1], F32)
    mneg8 = T("mneg8", [128, 1], F32)
    e8 = T("e8", [128, 8], F32)
    s8 = T("s8", [128, 1], F32)
    rs8 = T("rs8", [128, 1], F32)
    w8 = T("w8", [128, 8], F32)
    wexp = T("wexp", [128, 128], F32)
    wh_bf = T("wh_bf", [128, 128], BF16)
    whT = T("whT", [128, 128], BF16)
    red32 = T("red32", [128, 32], F32)
    ysum4 = T("ysum4", [128, 4], F32)
    yssq4 = T("yssq4", [128, 4], F32)
    ln_x = alloc_ln(T, "x" + uniq, NT)
    ln_y = alloc_ln(T, "y" + uniq, NT)

    ps_y = es.enter_context(nc.psum_tensor(uniq + "f_ps_y", [128, 2, 1024], F32))
    ps_qk = es.enter_context(nc.psum_tensor(uniq + "f_ps_qk", [128, 512], F32))
    ps_h = es.enter_context(nc.psum_tensor(uniq + "f_ps_h", [128, 128], F32))

    ctx = {"u_bf": u_bf, "uT": uT, "red32": red32, "tmin": tmin,
           "ysum4": ysum4, "yssq4": yssq4, "ps_y": ps_y, "ysb": ysb}

    A = P.add
    wload = []
    wload.append(A("gpsimd", lambda g: g.dma_start(
        out=wq_bf[:], in_=wqT.rearrange("(n p) c -> p n c", p=128)), inc=("s_w", 16), tag="wq"))
    wload.append(A("gpsimd", lambda g: g.dma_start(
        out=wk_bf[:], in_=wkT.rearrange("(n p) c -> p n c", p=128)), inc=("s_w", 16), tag="wk"))
    wload.append(A("gpsimd", lambda g: g.dma_start(
        out=a_bf[:], in_=aT.rearrange("(n p) c -> p n c", p=128)), inc=("s_w", 16), tag="at"))
    wload.append(A("gpsimd", lambda g: g.dma_start(
        out=bm_bf[:], in_=bmT[:]), inc=("s_w", 16), tag="bm"))
    wload.append(A("gpsimd", lambda g: g.dma_start(
        out=b_bf[:], in_=bia[:]), inc=("s_w", 16), tag="b"))
    pones = A("vector", lambda v: v.memset(ones1[:], 1.0), inc=("s_v", 1), tag="ones")

    mst_done = []
    prev_st = [None, None]
    for i in range(NI):
        ld = A("gpsimd", lambda g, i=i: g.dma_start(
            out=wstage[:, i % 2, :], in_=wT[i*128:(i+1)*128, :]),
            deps=[prev_st[i % 2]], inc=(f"s_ml{i % 2}", 16), tag=f"mstl{i}")
        st = A("sync", lambda s, i=i: s.dma_start(
            out=wmaster[i*128:(i+1)*128, :], in_=wstage[:, i % 2, :]),
            deps=[ld], inc=(f"s_ms{i % 2}", 16), tag=f"msts{i}")
        prev_st[i % 2] = st
        mst_done.append(st)

    prev = {}
    for t in range(NT):
        sl = t % 2
        xap = xbuf[:, sl, :]
        deps_ldx = [mst_done[-1], mst_done[-2]] if t < 2 else [prev[(t - 2, 'sto')]]
        ldx = A("sync", lambda s, t=t, xap=xap: s.dma_start(
            out=xap, in_=x_in[t*128:(t+1)*128, :]), deps=deps_ldx,
            inc=(f"s_x{t % 2}", 16), tag=f"ldx{t}")

        uop, xsq, _ = emit_ln_input(P, A, ctx, t, xap, ln_x, [ldx],
                                    prev.get((t - 1, 'epi')), "f")
        utr = emit_utr(P, A, ctx, uop, prev.get((t - 1, 'hmm')), f"f{t}")

        def qk_fn(te, t=t):
            r = None
            for i in range(NI):
                te.matmul(ps_qk[:, 0:256], uT[:, i, :], wq_bf[:, i, :],
                          start=(i == 0), stop=(i == NI - 1))
            for i in range(NI):
                r = te.matmul(ps_qk[:, 256:512], uT[:, i, :], wk_bf[:, i, :],
                              start=(i == 0), stop=(i == NI - 1))
            return r
        qkmm = A("tensor", qk_fn, deps=[utr] + wload + [prev.get((t - 1, 'prod'))],
                 inc=("s_mm", 1), tag=f"qkmm{t}")

        def h_fn(te, t=t):
            r = None
            for i in range(NI):
                r = te.matmul(ps_h[:], uT[:, i, :], a_bf[:, i, :],
                              start=(i == 0), stop=(i == NI - 1))
            return r
        hmm_ = A("tensor", h_fn, deps=[utr] + wload + [prev.get((t - 1, 'whmul'))],
                 inc=("s_mm", 1), tag=f"hmm{t}")
        prev[(t, 'hmm')] = hmm_

        qev = A("scalar", lambda sc: sc.copy(out=qk_sb[:], in_=ps_qk[:, 0:256]),
                deps=[qkmm], inc=("s_sc", 1), tag=f"qev{t}")
        prodo = A("vector", lambda v: v.tensor_tensor(
            out=prod[:], in0=ps_qk[:, 256:512], in1=qk_sb[:], op=ALU.mult),
            deps=[qev], inc=("s_v", 1), tag=f"prod{t}")
        prev[(t, 'prod')] = prodo

        def sc8_fn(v):
            P.mid(v.tensor_reduce(out=sc8[:], in_=prod[:].rearrange("p (a b) -> p a b", a=8),
                                  axis=AX.X, op=ALU.add))
            return v.tensor_reduce(out=m8[:], in_=sc8[:], axis=AX.X, op=ALU.max)
        sc8o = A("vector", sc8_fn, deps=[prodo], inc=("s_v", 1), tag=f"sc8{t}")

        mn8 = A("gpsimd", lambda g: g.tensor_scalar(
            out=mneg8[:], in0=m8[:], scalar1=-RS32, scalar2=None, op0=ALU.mult),
            deps=[sc8o], inc=("s_g", 1), tag=f"mn8{t}")
        e8o = A("scalar", lambda sc: sc.activation(
            out=e8[:], in_=sc8[:], func=AF.Exp, bias=mneg8[:], scale=RS32),
            deps=[mn8], inc=("s_sc", 1), tag=f"e8{t}")
        s8o = A("vector", lambda v: v.tensor_reduce(
            out=s8[:], in_=e8[:], axis=AX.X, op=ALU.add),
            deps=[e8o], inc=("s_v", 1), tag=f"s8{t}")

        def rs_fn(sc):
            P.mid(sc.activation(out=rs8[:], in_=s8[:], func=AF.Ln, bias=0.0, scale=1.0))
            return sc.activation(out=rs8[:], in_=rs8[:], func=AF.Exp, bias=0.0, scale=-1.0)
        rso = A("scalar", rs_fn, deps=[s8o], inc=("s_sc", 1), tag=f"rs{t}")

        w8o = A("vector", lambda v: v.tensor_scalar(
            out=w8[:], in0=e8[:], scalar1=rs8[:], scalar2=None, op0=ALU.mult),
            deps=[rso], inc=("s_v", 1), tag=f"w8{t}")
        wexpo = A("vector", lambda v: v.tensor_copy(
            out=wexp[:].rearrange("p (a b) -> p a b", a=8),
            in_=w8[:].rearrange("p (a b) -> p a b", b=1).broadcast_to((128, 8, 16))),
            deps=[w8o], inc=("s_v", 1), tag=f"wexp{t}")
        whmul = A("vector", lambda v: v.tensor_tensor(
            out=wh_bf[:], in0=ps_h[:], in1=wexp[:], op=ALU.mult),
            deps=[wexpo, hmm_], inc=("s_v", 1), tag=f"whmul{t}")
        prev[(t, 'whmul')] = whmul
        whtr = A("sync", lambda s: s.dma_start(out=whT[:], in_=wh_bf[:], transpose=True),
                 deps=[whmul, prev.get((t - 1, 'q3mm'))], inc=("s_t", 16), tag=f"whtr{t}")

        def make_quarter(q, extra_deps):
            bdeps = [pones] + wload + list(extra_deps)

            def bias_fn(te, q=q):
                r = None
                for c in range(2):
                    r = te.matmul(ps_y[:, q % 2, c*512:(c+1)*512], ones1[:],
                                  b_bf[:, q*1024 + c*512:q*1024 + (c+1)*512],
                                  start=True, stop=False)
                return r
            bmmop = A("tensor", bias_fn, deps=bdeps, inc=("s_mm", 1), tag=f"bias{q}_{t}")

            last_g = bmmop
            for j in range(8):
                gi = q * 8 + j
                wdeps = [prev.get(('g', t * 32 + gi - 2))]
                if t == 0:
                    wdeps += [mst_done[-1], mst_done[-2]]
                wld = A("sync", lambda s, q=q, j=j, sl2=(q*8+j) % 2: s.dma_start(
                    out=wstr[:, sl2, :, :],
                    in_=wmaster.rearrange("(n p) c -> p n c", p=128)[:, 4*j:4*j+4, q*1024:(q+1)*1024]),
                    deps=wdeps, inc=(f"s_wl{(q*8+j) % 2}", 16), tag=f"wld{q}_{j}_{t}")

                def g_fn(te, q=q, j=j, sl2=(q*8+j) % 2):
                    r = None
                    for ii in range(4):
                        for c in range(2):
                            r = te.matmul(ps_y[:, q % 2, c*512:(c+1)*512],
                                          uT[:, 4*j + ii, :],
                                          wstr[:, sl2, ii, c*512:(c+1)*512],
                                          start=False, stop=False)
                    return r
                gop = A("tensor", g_fn, deps=[wld, last_g, utr],
                        inc=("s_mm", 1), tag=f"qmmg{q}_{j}_{t}")
                prev[('g', t * 32 + gi)] = gop
                last_g = gop

            def comb_fn(te, q=q):
                r = None
                for c in range(2):
                    r = te.matmul(ps_y[:, q % 2, c*512:(c+1)*512], whT[:],
                                  bm_bf[:, q*1024 + c*512:q*1024 + (c+1)*512],
                                  start=False, stop=True)
                return r
            qmm = A("tensor", comb_fn, deps=[last_g, whtr], inc=("s_mm", 1), tag=f"comb{q}_{t}")
            if q == 3:
                prev[(t, 'q3mm')] = qmm
            return qmm

        sto, q_readers, _ = emit_quarter_stats_epilogue(
            P, A, ctx, t, xap, ln_y, make_quarter, prev.get((t - 1, 'qr')), True,
            x_out[t*128:(t+1)*128, :], f"f{t}_")
        prev[(t, 'qr')] = q_readers
        prev[(t, 'sto')] = sto
        prev[(t, 'epi')] = sto

    return [prev[(t, 'sto')] for t in range(NT)]



def host_prep_l1a(inp):
    L = 4096
    k = np.arange(L)
    Om = np.exp(-2j*np.pi*k/L)
    T0 = np.float32(np.imag(2*(1-Om)/(1+Om))).reshape(128, 32)
    cc = 2.0/(1+Om)
    cr = np.float32(cc.real).reshape(128, 32)
    ci = np.float32(cc.imag).reshape(128, 32)
    n0 = np.arange(64)
    F64r = np.float32(np.cos(2*np.pi*np.outer(n0, n0)/64))
    F64i = np.float32(np.sin(2*np.pi*np.outer(n0, n0)/64))
    Twr = np.float32(np.cos(2*np.pi*np.outer(n0, n0)/L))
    Twi = np.float32(np.sin(2*np.pi*np.outer(n0, n0)/L))
    I64 = np.eye(64, dtype=np.float32)
    I128 = np.eye(128, dtype=np.float32)
    return {
        "a_T0": np.ascontiguousarray(T0), "a_cr": np.ascontiguousarray(cr),
        "a_ci": np.ascontiguousarray(ci),
        "a_f64r": F64r, "a_f64in": np.ascontiguousarray(-F64i), "a_f64i": F64i,
        "a_twr": Twr, "a_twi": Twi, "a_i64": I64, "a_i128": I128,
        "a_lam": np.ascontiguousarray(np.asarray(inp['L1_attn_Lambda_ri'])),
        "a_p": np.ascontiguousarray(np.asarray(inp['L1_attn_p_ri'])),
        "a_q": np.ascontiguousarray(np.asarray(inp['L1_attn_q_ri'])),
        "a_vc": np.ascontiguousarray(np.asarray(inp['L1_attn_Vc_ri']).reshape(64, 128)),
        "a_ct": np.ascontiguousarray(np.asarray(inp['L1_attn_Ct_ri']).reshape(64, 2)),
        "a_bp": np.ascontiguousarray(np.asarray(inp['L1_attn_Bp'])),
        "a_d": np.ascontiguousarray(np.asarray(inp['L1_attn_D']).reshape(1, 1)),
        "a_ls": np.ascontiguousarray(np.asarray(inp['L1_attn_log_step']).reshape(1, 1)),
    }


def build_l1a(nc, x_in, x_out, es, P, uniq=""):
    D = lambda n, s: nc.dram_tensor(uniq + n, s, F32, kind="ExternalInput").ap()
    t0_d = D("a_T0", [128, 32]); cr_d = D("a_cr", [128, 32]); ci_d = D("a_ci", [128, 32])
    f64r_d = D("a_f64r", [64, 64]); f64in_d = D("a_f64in", [64, 64]); f64i_d = D("a_f64i", [64, 64])
    twr_d = D("a_twr", [64, 64]); twi_d = D("a_twi", [64, 64])
    i64_d = D("a_i64", [64, 64]); i128_d = D("a_i128", [128, 128])
    lam_d = D("a_lam", [64, 2]); p_d = D("a_p", [64, 2]); q_d = D("a_q", [64, 2])
    vc_d = D("a_vc", [64, 128]); ct_d = D("a_ct", [64, 2]); bp_d = D("a_bp", [64, 1])
    dd_d = D("a_d", [1, 1]); ls_d = D("a_ls", [1, 1])
    scr_d = nc.dram_tensor(uniq + "a_scr", [16384], F32, kind="Internal").ap()

    T = lambda n, s, d: es.enter_context(nc.sbuf_tensor(uniq + "as_" + n, s, d))
    sm = T("sm", [64, 224], F32)
    vpk = T("vpk", [64, 16], F32)
    bcz = T("bcz", [128, 16, 64], F32)
    Gt = T("Gt", [128, 32], F32)
    crs = T("crs", [128, 32], F32); cis = T("cis", [128, 32], F32)
    di3 = T("di3", [128, 2048], F32)
    d23 = T("d23", [128, 2048], F32)
    recr = T("recr", [128, 2048], F32)
    reci = T("reci", [128, 2048], F32)
    kk = T("kk", [128, 8, 32], F32)
    cw = T("cw", [128, 8, 32], F32)
    f64r = T("f64r", [64, 64], F32); f64in = T("f64in", [64, 64], F32); f64i = T("f64i", [64, 64], F32)
    twr = T("twr", [64, 64], F32); twi = T("twi", [64, 64], F32)
    i64 = T("i64", [64, 64], F32); i128 = T("i128", [128, 128], F32)
    amr = T("amr", [64, 64], F32); ami = T("ami", [64, 64], F32)
    a2r = T("a2r", [64, 64], F32); a2i = T("a2i", [64, 64], F32)
    a2rt = T("a2rt", [64, 64], F32); a2it = T("a2it", [64, 64], F32)
    tw1 = T("tw1", [64, 64], F32); tw2 = T("tw2", [64, 64], F32)
    xsb = T("xsb", [64, 64], F32)
    ksb = T("ksb", [1, 4096], F32)
    kbf = T("kbf", [1, 4352], BF16)
    ball = T("ball", [128, 35 * 128], BF16)
    dscal = T("dscal", [128, 1], F32)
    dg = T("dg", [128, 128], F32)
    rstep = T("rstep", [1, 1], F32)
    rstep_b = T("rstep_b", [128, 1], F32)
    onesf = T("onesf", [1, 128], F32)
    xbuf = T("xbuf", [128, 2, DM], F32)
    u_bf = T("u_bf", [128, DM], BF16)
    uT = T("uT", [128, NI, 128], BF16)
    tmin = T("tmin", [128, 1024], F32)
    ysb = T("ysb", [128, 2048], F32)
    red32 = T("red32", [128, 32], F32)
    ysum4 = T("ysum4", [128, 4], F32)
    yssq4 = T("yssq4", [128, 4], F32)
    ln_x = alloc_ln(T, "x" + uniq, NT)
    ln_y = alloc_ln(T, "y" + uniq, NT)

    ps_y = es.enter_context(nc.psum_tensor(uniq + "a_ps_y", [128, 2, 1024], F32))
    ps_k = es.enter_context(nc.psum_tensor(uniq + "a_ps_k", [64, 256], F32))
    ps_b = es.enter_context(nc.psum_tensor(uniq + "a_ps_b", [128, 8], F32))

    ctx = {"u_bf": u_bf, "uT": uT, "red32": red32, "tmin": tmin,
           "ysum4": ysum4, "yssq4": yssq4, "ps_y": ps_y, "ysb": ysb}

    A = P.add
    lds = []
    for ap_d, ap_s, tg in [(t0_d, Gt, "t0"), (cr_d, crs, "cr"), (ci_d, cis, "ci"),
                           (f64r_d, f64r, "f64r"), (f64in_d, f64in, "f64in"), (f64i_d, f64i, "f64i"),
                           (twr_d, twr, "twr"), (twi_d, twi, "twi"), (i64_d, i64, "i64"),
                           (i128_d, i128, "i128"), (dd_d, rstep, "dd0")]:
        if tg == "dd0":
            continue
        lds.append(A("sync", lambda s, a=ap_d, b=ap_s: s.dma_start(out=b[:], in_=a[:]),
                     inc=("s_pre", 16), tag="ld" + tg))
    small_map = [(lam_d, 0, 2), (p_d, 2, 2), (q_d, 4, 2), (ct_d, 6, 2), (vc_d, 8, 128), (bp_d, 136, 1)]
    for ap_d, off, w in small_map:
        lds.append(A("sync", lambda s, a=ap_d, o=off, w=w: s.dma_start(
            out=sm[:, o:o + w], in_=a[:]), inc=("s_pre", 16), tag=f"lds{off}"))
    ldls = A("sync", lambda s: s.dma_start(out=rstep[:], in_=ls_d[:]), inc=("s_pre", 16), tag="ldls")
    lddd = A("sync", lambda s: s.dma_start(out=dscal[0:1, :], in_=dd_d[:]), inc=("s_pre", 16), tag="lddd")
    lds += [ldls, lddd]

    rse = A("scalar", lambda sc: sc.activation(out=rstep[:], in_=rstep[:], func=AF.Exp,
                                               bias=0.0, scale=-1.0),
            deps=lds, inc=("s_sc", 1), tag="rse")
    pones = A("vector", lambda v: v.memset(onesf[:], 1.0), inc=("s_v", 1), tag="onesf")

    def bc_fn(te):
        te.matmul(ps_b[:, 0:1], onesf[:], rstep[:], start=True, stop=True)
        return te.matmul(ps_b[:, 1:2], onesf[:], dscal[0:1, :], start=True, stop=True)
    bcmm = A("tensor", bc_fn, deps=[rse, pones] + lds, inc=("s_mm", 1), tag="bcmm")
    rsb = A("vector", lambda v: v.tensor_copy(out=rstep_b[:], in_=ps_b[:, 0:1]),
            deps=[bcmm], inc=("s_v", 1), tag="rsbc")
    dbc = A("vector", lambda v: v.tensor_copy(out=dscal[:], in_=ps_b[:, 1:2]),
            deps=[bcmm], inc=("s_v", 1), tag="dbcc")

    wbp = A("sync", lambda s: s.dma_start(out=scr_d[0:64], in_=sm[:, 136:137]),
            deps=lds, inc=("s_pre", 16), tag="wbp")
    rbp = A("sync", lambda s: s.dma_start(
        out=sm[:, 140:204].rearrange("p m -> p m") if False else sm[:, 140:204],
        in_=scr_d[0:64].rearrange("(o m) -> o m", o=1).broadcast_to((64, 64))),
        deps=[wbp], inc=("s_pre", 16), tag="rbp")

    def bv_fn(v):
        vcr = sm[:, 8:136].rearrange("p (m two) -> p m two", two=2)[:, :, 0:1].rearrange("p m one -> p (m one)")
        vci = sm[:, 8:136].rearrange("p (m two) -> p m two", two=2)[:, :, 1:2].rearrange("p m one -> p (m one)")
        bpb = sm[:, 140:204]
        P.mid(v.tensor_tensor(out=vpk[:, 8:9].broadcast_to((64, 1)) if False else cw[0:64, 0, 0:64] if False else sm[:, 76+128:140+64] if False else di3[0:64, 0:64], in0=vcr, in1=bpb, op=ALU.mult))
        P.mid(v.tensor_reduce(out=vpk[:, 8:9], in_=di3[0:64, 0:64], axis=AX.X, op=ALU.add))
        P.mid(v.tensor_tensor(out=di3[0:64, 0:64], in0=vci, in1=bpb, op=ALU.mult))
        return v.tensor_reduce(out=vpk[:, 9:10], in_=di3[0:64, 0:64], axis=AX.X, op=ALU.add)
    bv = A("vector", bv_fn, deps=lds + [rbp], inc=("s_v", 1), tag="bv")

    def vprod_fn(v):
        ctr = sm[:, 6:7]; cti = sm[:, 7:8]
        pr = sm[:, 2:3]; pi = sm[:, 3:4]
        qr = sm[:, 4:5]; qi = sm[:, 5:6]
        br = vpk[:, 8:9]; bi = vpk[:, 9:10]
        P.mid(v.tensor_tensor(out=vpk[:, 0:1], in0=ctr, in1=br, op=ALU.mult))
        P.mid(v.tensor_tensor(out=vpk[:, 10:11], in0=cti, in1=bi, op=ALU.mult))
        P.mid(v.tensor_tensor(out=vpk[:, 0:1], in0=vpk[:, 0:1], in1=vpk[:, 10:11], op=ALU.add))
        P.mid(v.tensor_tensor(out=vpk[:, 1:2], in0=ctr, in1=bi, op=ALU.mult))
        P.mid(v.tensor_tensor(out=vpk[:, 10:11], in0=cti, in1=br, op=ALU.mult))
        P.mid(v.tensor_tensor(out=vpk[:, 1:2], in0=vpk[:, 1:2], in1=vpk[:, 10:11], op=ALU.subtract))
        P.mid(v.tensor_tensor(out=vpk[:, 2:3], in0=ctr, in1=pr, op=ALU.mult))
        P.mid(v.tensor_tensor(out=vpk[:, 10:11], in0=cti, in1=pi, op=ALU.mult))
        P.mid(v.tensor_tensor(out=vpk[:, 2:3], in0=vpk[:, 2:3], in1=vpk[:, 10:11], op=ALU.add))
        P.mid(v.tensor_tensor(out=vpk[:, 3:4], in0=ctr, in1=pi, op=ALU.mult))
        P.mid(v.tensor_tensor(out=vpk[:, 10:11], in0=cti, in1=pr, op=ALU.mult))
        P.mid(v.tensor_tensor(out=vpk[:, 3:4], in0=vpk[:, 3:4], in1=vpk[:, 10:11], op=ALU.subtract))
        P.mid(v.tensor_tensor(out=vpk[:, 4:5], in0=qr, in1=br, op=ALU.mult))
        P.mid(v.tensor_tensor(out=vpk[:, 10:11], in0=qi, in1=bi, op=ALU.mult))
        P.mid(v.tensor_tensor(out=vpk[:, 4:5], in0=vpk[:, 4:5], in1=vpk[:, 10:11], op=ALU.add))
        P.mid(v.tensor_tensor(out=vpk[:, 5:6], in0=qr, in1=bi, op=ALU.mult))
        P.mid(v.tensor_tensor(out=vpk[:, 10:11], in0=qi, in1=br, op=ALU.mult))
        P.mid(v.tensor_tensor(out=vpk[:, 5:6], in0=vpk[:, 5:6], in1=vpk[:, 10:11], op=ALU.subtract))
        P.mid(v.tensor_tensor(out=vpk[:, 6:7], in0=qr, in1=pr, op=ALU.mult))
        P.mid(v.tensor_tensor(out=vpk[:, 10:11], in0=qi, in1=pi, op=ALU.mult))
        P.mid(v.tensor_tensor(out=vpk[:, 6:7], in0=vpk[:, 6:7], in1=vpk[:, 10:11], op=ALU.add))
        P.mid(v.tensor_tensor(out=vpk[:, 7:8], in0=qr, in1=pi, op=ALU.mult))
        P.mid(v.tensor_tensor(out=vpk[:, 10:11], in0=qi, in1=pr, op=ALU.mult))
        P.mid(v.tensor_tensor(out=vpk[:, 7:8], in0=vpk[:, 7:8], in1=vpk[:, 10:11], op=ALU.subtract))
        P.mid(v.tensor_copy(out=vpk[:, 8:9], in_=sm[:, 0:1]))
        P.mid(v.tensor_copy(out=vpk[:, 9:10], in_=sm[:, 1:2]))
        return v.tensor_tensor(out=vpk[:, 10:11], in0=sm[:, 0:1], in1=sm[:, 0:1], op=ALU.mult)
    vprod = A("vector", vprod_fn, deps=[bv], inc=("s_v", 1), tag="vprod")

    wpk = A("sync", lambda s: s.dma_start(
        out=scr_d[256:256 + 11 * 64].rearrange("(x n) -> x n", x=11).transpose([1, 0]),
        in_=vpk[:, 0:11], allow_slow_non_contiguous=True),
        deps=[vprod], inc=("s_pre", 16), tag="wpk")
    rpk = A("sync", lambda s: s.dma_start(
        out=bcz[:, 0:11, :],
        in_=scr_d[256:256 + 11 * 64].rearrange("(x n) -> x n", x=11).unsqueeze(0).broadcast_to((128, 11, 64))),
        deps=[wpk], inc=("s_pre", 16), tag="rpk")

    def gt_fn(v):
        return v.tensor_scalar(out=Gt[:], in0=Gt[:], scalar1=rstep_b[:], scalar2=None, op0=ALU.mult)
    gt = A("vector", gt_fn, deps=[rsb] + lds, inc=("s_v", 1), tag="gt")

    def cauchy_fn(v):
        lam_i = bcz[:, 9:10, :].broadcast_to((128, 32, 64))
        lam_r = bcz[:, 8:9, :].broadcast_to((128, 32, 64))
        lr2 = bcz[:, 10:11, :].broadcast_to((128, 32, 64))
        d3 = di3[:].rearrange("p (j n) -> p j n", j=32)
        d2 = d23[:].rearrange("p (j n) -> p j n", j=32)
        rr = recr[:].rearrange("p (j n) -> p j n", j=32)
        ri = reci[:].rearrange("p (j n) -> p j n", j=32)
        g3 = Gt[:].rearrange("p (j o) -> p j o", o=1).broadcast_to((128, 32, 64))
        P.mid(v.tensor_tensor(out=d3, in0=g3, in1=lam_i, op=ALU.subtract))
        P.mid(v.tensor_tensor(out=d2, in0=d3, in1=d3, op=ALU.mult))
        P.mid(v.tensor_tensor(out=d2, in0=d2, in1=lr2, op=ALU.add))
        P.mid(v.reciprocal(out=d23[:], in_=d23[:]))
        P.mid(v.tensor_tensor(out=rr, in0=lam_r, in1=d2, op=ALU.mult))
        P.mid(v.tensor_scalar(out=recr[:], in0=recr[:], scalar1=-1.0, scalar2=None, op0=ALU.mult))
        P.mid(v.tensor_tensor(out=ri, in0=d3, in1=d2, op=ALU.mult))
        return v.tensor_scalar(out=reci[:], in0=reci[:], scalar1=-1.0, scalar2=None, op0=ALU.mult)
    cau = A("vector", cauchy_fn, deps=[gt, rpk] + lds, inc=("s_v", 1), tag="cauchy")

    def ksums_fn(v):
        rr = recr[:].rearrange("p (j n) -> p j n", j=32)
        ri = reci[:].rearrange("p (j n) -> p j n", j=32)
        for vi_ in range(4):
            vr = bcz[:, 2*vi_:2*vi_+1, :].broadcast_to((128, 32, 64))
            vv = bcz[:, 2*vi_+1:2*vi_+2, :].broadcast_to((128, 32, 64))
            d3 = di3[:].rearrange("p (j n) -> p j n", j=32)
            P.mid(v.tensor_tensor(out=d3, in0=vr, in1=rr, op=ALU.mult))
            P.mid(v.tensor_reduce(out=kk[:, 2*vi_, :], in_=d3, axis=AX.X, op=ALU.add))
            P.mid(v.tensor_tensor(out=d3, in0=vv, in1=ri, op=ALU.mult))
            P.mid(v.tensor_reduce(out=cw[:, 0, :], in_=d3, axis=AX.X, op=ALU.add))
            P.mid(v.tensor_tensor(out=kk[:, 2*vi_, :], in0=kk[:, 2*vi_, :], in1=cw[:, 0, :], op=ALU.subtract))
            P.mid(v.tensor_tensor(out=d3, in0=vr, in1=ri, op=ALU.mult))
            P.mid(v.tensor_reduce(out=kk[:, 2*vi_+1, :], in_=d3, axis=AX.X, op=ALU.add))
            P.mid(v.tensor_tensor(out=d3, in0=vv, in1=rr, op=ALU.mult))
            P.mid(v.tensor_reduce(out=cw[:, 0, :], in_=d3, axis=AX.X, op=ALU.add))
            P.mid(v.tensor_tensor(out=kk[:, 2*vi_+1, :], in0=kk[:, 2*vi_+1, :], in1=cw[:, 0, :], op=ALU.add))
        return v.nop()
    ksums = A("vector", ksums_fn, deps=[cau], inc=("s_v", 1), tag="ksums")

    def aroots_fn(v):
        k00r, k00i = kk[:, 0, :], kk[:, 1, :]
        k01r, k01i = kk[:, 2, :], kk[:, 3, :]
        k10r, k10i = kk[:, 4, :], kk[:, 5, :]
        k11r, k11i = kk[:, 6, :], kk[:, 7, :]
        wr, wi = cw[:, 0, :], cw[:, 1, :]
        w2, iwr = cw[:, 2, :], cw[:, 3, :]
        iwi, ur = cw[:, 4, :], cw[:, 5, :]
        ui, t = cw[:, 6, :], cw[:, 7, :]
        P.mid(v.tensor_scalar(out=wr, in0=k11r, scalar1=1.0, scalar2=None, op0=ALU.add))
        P.mid(v.tensor_copy(out=wi, in_=k11i))
        P.mid(v.tensor_tensor(out=w2, in0=wr, in1=wr, op=ALU.mult))
        P.mid(v.tensor_tensor(out=t, in0=wi, in1=wi, op=ALU.mult))
        P.mid(v.tensor_tensor(out=w2, in0=w2, in1=t, op=ALU.add))
        P.mid(v.reciprocal(out=w2, in_=w2))
        P.mid(v.tensor_tensor(out=iwr, in0=wr, in1=w2, op=ALU.mult))
        P.mid(v.tensor_tensor(out=iwi, in0=wi, in1=w2, op=ALU.mult))
        P.mid(v.tensor_scalar(out=iwi, in0=iwi, scalar1=-1.0, scalar2=None, op0=ALU.mult))
        P.mid(v.tensor_tensor(out=ur, in0=k01r, in1=k10r, op=ALU.mult))
        P.mid(v.tensor_tensor(out=t, in0=k01i, in1=k10i, op=ALU.mult))
        P.mid(v.tensor_tensor(out=ur, in0=ur, in1=t, op=ALU.subtract))
        P.mid(v.tensor_tensor(out=ui, in0=k01r, in1=k10i, op=ALU.mult))
        P.mid(v.tensor_tensor(out=t, in0=k01i, in1=k10r, op=ALU.mult))
        P.mid(v.tensor_tensor(out=ui, in0=ui, in1=t, op=ALU.add))
        P.mid(v.tensor_tensor(out=t, in0=ur, in1=iwr, op=ALU.mult))
        P.mid(v.tensor_tensor(out=wr, in0=ui, in1=iwi, op=ALU.mult))
        P.mid(v.tensor_tensor(out=wr, in0=t, in1=wr, op=ALU.subtract))
        P.mid(v.tensor_tensor(out=t, in0=ur, in1=iwi, op=ALU.mult))
        P.mid(v.tensor_tensor(out=wi, in0=ui, in1=iwr, op=ALU.mult))
        P.mid(v.tensor_tensor(out=wi, in0=t, in1=wi, op=ALU.add))
        P.mid(v.tensor_tensor(out=ur, in0=k00r, in1=wr, op=ALU.subtract))
        P.mid(v.tensor_tensor(out=ui, in0=k00i, in1=wi, op=ALU.subtract))
        P.mid(v.tensor_tensor(out=kk[:, 0, :], in0=ur, in1=crs[:], op=ALU.mult))
        P.mid(v.tensor_tensor(out=t, in0=ui, in1=cis[:], op=ALU.mult))
        P.mid(v.tensor_tensor(out=kk[:, 0, :], in0=kk[:, 0, :], in1=t, op=ALU.subtract))
        P.mid(v.tensor_tensor(out=kk[:, 1, :], in0=ur, in1=cis[:], op=ALU.mult))
        P.mid(v.tensor_tensor(out=t, in0=ui, in1=crs[:], op=ALU.mult))
        return v.tensor_tensor(out=kk[:, 1, :], in0=kk[:, 1, :], in1=t, op=ALU.add)
    aro = A("vector", aroots_fn, deps=[ksums], inc=("s_v", 1), tag="aroots")

    war = A("sync", lambda s: s.dma_start(out=scr_d[1024:1024+4096].rearrange("(p j) -> p j", p=128),
                                          in_=kk[:, 0, :]), deps=[aro], inc=("s_pre", 16), tag="war")
    wai = A("sync", lambda s: s.dma_start(out=scr_d[5120:5120+4096].rearrange("(p j) -> p j", p=128),
                                          in_=kk[:, 1, :]), deps=[aro], inc=("s_pre", 16), tag="wai")
    rar = A("sync", lambda s: s.dma_start(out=amr[:], in_=scr_d[1024:1024+4096].rearrange("(a b) -> a b", a=64)),
            deps=[war, wai], inc=("s_pre", 16), tag="rar")
    rai = A("sync", lambda s: s.dma_start(out=ami[:], in_=scr_d[5120:5120+4096].rearrange("(a b) -> a b", a=64)),
            deps=[war, wai], inc=("s_pre", 16), tag="rai")

    def st1_fn(te):
        te.matmul(ps_k[:, 0:64], f64r[:], amr[:], start=True, stop=False)
        te.matmul(ps_k[:, 0:64], f64in[:], ami[:], start=False, stop=True)
        te.matmul(ps_k[:, 64:128], f64r[:], ami[:], start=True, stop=False)
        return te.matmul(ps_k[:, 64:128], f64i[:], amr[:], start=False, stop=True)
    st1 = A("tensor", st1_fn, deps=[rar, rai] + lds, inc=("s_mm", 1), tag="st1")

    def tw_fn(v):
        P.mid(v.tensor_tensor(out=tw1[:], in0=ps_k[:, 0:64], in1=twr[:], op=ALU.mult))
        P.mid(v.tensor_tensor(out=tw2[:], in0=ps_k[:, 64:128], in1=twi[:], op=ALU.mult))
        P.mid(v.tensor_tensor(out=a2r[:], in0=tw1[:], in1=tw2[:], op=ALU.subtract))
        P.mid(v.tensor_tensor(out=tw1[:], in0=ps_k[:, 0:64], in1=twi[:], op=ALU.mult))
        P.mid(v.tensor_tensor(out=tw2[:], in0=ps_k[:, 64:128], in1=twr[:], op=ALU.mult))
        return v.tensor_tensor(out=a2i[:], in0=tw1[:], in1=tw2[:], op=ALU.add)
    tw = A("vector", tw_fn, deps=[st1], inc=("s_v", 1), tag="tw")

    def tr_fn(te):
        te.transpose(ps_k[:, 128:192], a2r[:], i64[:])
        return te.transpose(ps_k[:, 192:256], a2i[:], i64[:])
    tr = A("tensor", tr_fn, deps=[tw], inc=("s_mm", 1), tag="tr")
    trev = A("vector", lambda v: [P.mid(v.tensor_copy(out=a2rt[:], in_=ps_k[:, 128:192])),
                                  v.tensor_copy(out=a2it[:], in_=ps_k[:, 192:256])][-1],
             deps=[tr], inc=("s_v", 1), tag="trev")

    def st3_fn(te):
        te.matmul(ps_k[:, 0:64], a2rt[:], f64r[:], start=True, stop=False)
        return te.matmul(ps_k[:, 0:64], a2it[:], f64in[:], start=False, stop=True)
    st3 = A("tensor", st3_fn, deps=[trev], inc=("s_mm", 1), tag="st3")
    xev = A("vector", lambda v: v.tensor_scalar(out=xsb[:], in0=ps_k[:, 0:64],
                                                scalar1=1.0/4096, scalar2=None, op0=ALU.mult),
            deps=[st3], inc=("s_v", 1), tag="xev")

    wk = A("sync", lambda s: s.dma_start(
        out=scr_d[1024:1024+4096].rearrange("(b a) -> a b", a=64), in_=xsb[:],
        allow_slow_non_contiguous=True), deps=[xev], inc=("s_pre", 16), tag="wk")
    rk = A("sync", lambda s: s.dma_start(out=ksb[:], in_=scr_d[1024:1024+4096].rearrange("(o n) -> o n", o=1)),
           deps=[wk], inc=("s_pre", 16), tag="rk")

    kz = A("vector", lambda v: v.memset(kbf[:], 0.0), inc=("s_v", 1), tag="kz")
    kcast = A("vector", lambda v: v.tensor_copy(out=kbf[0:1, 127:127+4096], in_=ksb[:]),
              deps=[rk, kz], inc=("s_v", 1), tag="kcast")

    bz = A("vector", lambda v: v.memset(ball[:, 0:384], 0.0), inc=("s_v", 1), tag="bz")
    def ballfn(s):
        for r in range(128):
            s.dma_start(out=ball[r:r+1, 384:4480],
                        in_=kbf[0:1, 127 - r: 4223 - r]).then_inc(P.sem_handles["s_pre"], 16)
        return "self"
    ballop = A("sync", ballfn, deps=[kcast], inc=("s_pre", 16 * 128), tag="ball", self_inc=True)

    dgo = A("vector", lambda v: v.tensor_scalar(out=dg[:], in0=i128[:], scalar1=dscal[:],
                                                scalar2=None, op0=ALU.mult),
            deps=[dbc] + lds, inc=("s_v", 1), tag="dg")
    dadd = A("vector", lambda v: v.tensor_tensor(out=ball[:, 384:512], in0=ball[:, 384:512],
                                                 in1=dg[:], op=ALU.add),
             deps=[dgo, ballop, bz], inc=("s_v", 1), tag="dadd")

    prevt = {}
    for t in range(NT):
        sl = t % 2
        xap = xbuf[:, sl, :]
        deps_ldx = [dadd] if t < 2 else [prevt[(t - 2, 'sto')]]
        ldx = A("sync", lambda s, t=t, xap=xap: s.dma_start(
            out=xap, in_=x_in[t*128:(t+1)*128, :]), deps=deps_ldx,
            inc=(f"s_x{t % 2}", 16), tag=f"ldx{t}")

        uop, xsq, _ = emit_ln_input(P, A, ctx, t, xap, ln_x, [ldx],
                                    prevt.get((t - 1, 'epi')), "a")
        utr = emit_utr(P, A, ctx, uop, prevt.get((t - 1, 'lastmm')), f"a{t}")

        def make_quarter(q, extra_deps, t=t, utr=utr):
            def q_fn(te, q=q):
                r = None
                for c2 in range(2):
                    Js = 8 * q + 4 * c2
                    imax = min(32, Js + 4)
                    for I in range(imax):
                        off = (Js - I + 3) * 128
                        r = te.matmul(ps_y[:, q % 2, c2*512:(c2+1)*512],
                                      uT[:, I, :], ball[:, off:off + 512],
                                      start=(I == 0), stop=(I == imax - 1))
                return r
            deps = [utr, dadd] + list(extra_deps)
            op = A("tensor", q_fn, deps=deps, inc=("s_mm", 1), tag=f"tq{q}_{t}")
            if q == 3:
                prevt[(t, 'lastmm')] = op
            return op

        sto, q_readers, _ = emit_quarter_stats_epilogue(
            P, A, ctx, t, xap, ln_y, make_quarter, prevt.get((t - 1, 'qr')), True,
            x_out[t*128:(t+1)*128, :], f"a{t}_")
        prevt[(t, 'qr')] = q_readers
        prevt[(t, 'sto')] = sto
        prevt[(t, 'epi')] = sto

    return [prevt[(t, 'sto')] for t in range(NT)]



def host_prep_dequant(inp, core):
    qw = np.asarray(inp['L1_ffn_qweight'])
    lut8 = np.asarray(inp['L1_ffn_lut8'])
    lut4 = np.asarray(inp['L1_ffn_lut4'])
    base = core * 512
    o_idx = np.zeros((128, 64), np.int64)
    for c in range(8):
        for r in range(16):
            o_idx[16 * c + r, :] = base + np.arange(64) * 8 + c
    qq = qw[:, o_idx, :]
    qq = np.transpose(qq, (2, 1, 0, 3))
    qq = qq.reshape(16, 4, 128, 8, 128)
    qq = np.ascontiguousarray(np.transpose(qq, (0, 2, 1, 3, 4)))
    return {
        "dq_qw": qq.reshape(16, 128, 4 * 8 * 128),
        "dq_l8": np.ascontiguousarray(lut8[base:base + 512]),
        "dq_l4": np.ascontiguousarray(lut4[base:base + 512]),
    }


def build_dequant(nc, es, P, uniq="", wl_all_t=None, dd_all_t=None):
    qw_d = nc.dram_tensor(uniq + "dq_qw", [16, 128, 4096], I32, kind="ExternalInput").ap()
    l8_d = nc.dram_tensor(uniq + "dq_l8", [512, 256], F32, kind="ExternalInput").ap()
    l4_d = nc.dram_tensor(uniq + "dq_l4", [512, 16], F32, kind="ExternalInput").ap()
    fused = wl_all_t is not None
    okind = "Internal" if fused else "ExternalOutput"
    wl_t = nc.dram_tensor(uniq + "dq_wl", [DM, 512], BF16, kind=okind)
    dd_t = nc.dram_tensor(uniq + "dq_d", [DM, 512], BF16, kind=okind)
    wl_o = wl_t.ap()
    dd_o = dd_t.ap()
    st_d = nc.dram_tensor(uniq + "dq_stage", [1024, 256], BF16, kind="Internal").ap()

    T = lambda n, s, d: es.enter_context(nc.sbuf_tensor(uniq + "dqs_" + n, s, d))
    l8b = T("l8b", [128, 4, 256], BF16)
    l4b = T("l4b", [128, 4, 16], BF16)
    l4t = T("l4t", [128, 4, 256], BF16)
    db = T("db", [128, 4, 256], BF16)
    wqw = T("wqw", [128, 2, 4096], I32)
    shp = T("shp", [128, 1024], I32)
    pvec = T("pvec", [128, 1], I32)
    bits = T("bits", [128, 1024], I32)
    idx32 = T("idx32", [128, 1024], I32)
    idx16 = T("idx16", [128, 2, 1024], U16)
    outg = T("outg", [128, 2, 1024], BF16)
    dall = T("dall", [128, 64, 256], BF16)
    wlT_sb = T("wlT_sb", [128, 32, 512], BF16)
    ddT_sb = T("ddT_sb", [128, 32, 512], BF16)
    rowv = T("rowv", [128, 1], I32)
    colv = T("colv", [128, 128], I32)
    rowf = T("rowf", [128, 1], F32)
    colf = T("colf", [128, 128], F32)
    idT = T("idT", [128, 128], BF16)
    ps_tr = es.enter_context(nc.psum_tensor(uniq + "dq_ps_tr", [128, 2, 8, 128], BF16))

    A = P.add
    ld8 = A("gpsimd", lambda g: g.dma_start(
        out=l8b[:], in_=l8_d.rearrange("(n p) c -> p n c", p=128)), inc=("s_w", 16), tag="ld8")
    ld4 = A("gpsimd", lambda g: g.dma_start(
        out=l4b[:], in_=l4_d.rearrange("(n p) c -> p n c", p=128)), inc=("s_w", 16), tag="ld4")

    l4to = A("vector", lambda v: v.tensor_copy(
        out=l4t[:].rearrange("p n (hi lo) -> p n hi lo", hi=16),
        in_=l4b[:].rearrange("p n (o lo) -> p n o lo", o=1).broadcast_to((128, 4, 16, 16))),
        deps=[ld4, ld8], inc=("s_v", 1), tag="l4t")
    dbo = A("vector", lambda v: v.tensor_tensor(out=db[:], in0=l8b[:], in1=l4t[:], op=ALU.subtract),
            deps=[l4to], inc=("s_v", 1), tag="db")
    stg1 = A("sync", lambda s: s.dma_start(
        out=st_d[0:512, :].rearrange("(n p) c -> p n c", p=128), in_=l4t[:]),
        deps=[l4to, dbo], inc=("s_ms0", 16), tag="stg1")
    stg2 = A("sync", lambda s: s.dma_start(
        out=st_d[512:1024, :].rearrange("(n p) c -> p n c", p=128), in_=db[:]),
        deps=[dbo], inc=("s_ms0", 16), tag="stg2")

    def dall_fn(g):
        for c in range(8):
            for m in range(2):
                g.dma_start(
                    out=dall[16 * c + 8 * m:16 * c + 8 * m + 8, :, :],
                    in_=bass_rust.AP(tensor=st_d.tensor, offset=(512 * m + c) * 256,
                                     ap=[[0, 8], [8 * 256, 64], [1, 256]])
                ).then_inc(P.sem_handles["s_w"], 16)
        return "self"
    dallop = A("gpsimd", dall_fn, deps=[stg1, stg2], inc=("s_w", 16 * 16),
               tag="dall", self_inc=True)

    rio = A("gpsimd", lambda g: g.iota(rowv[:], [[0, 1]], base=0, channel_multiplier=1),
            inc=("s_g", 1), tag="rio")
    cio = A("gpsimd", lambda g: g.iota(colv[:], [[1, 128]], base=0, channel_multiplier=0),
            deps=[rio], inc=("s_g", 1), tag="cio")
    def ido_fn(g):
        P.mid(g.tensor_copy(out=rowf[:], in_=rowv[:]))
        P.mid(g.tensor_copy(out=colf[:], in_=colv[:]))
        return g.tensor_scalar(
            out=idT[:], in0=colf[:], scalar1=rowf[:], scalar2=None, op0=ALU.is_equal)
    ido = A("gpsimd", ido_fn, deps=[cio], inc=("s_g", 1), tag="ido")

    pio = A("gpsimd", lambda g: g.iota(pvec[:], [[0, 1]], base=0, channel_multiplier=1),
            inc=("s_g", 1), tag="pio")
    sio = A("gpsimd", lambda g: g.iota(shp[:], [[0, 4], [0, 128], [16, 2]], base=0,
                                       channel_multiplier=0),
            deps=[pio], inc=("s_g", 1), tag="sio")
    def shp_fn(v):
        P.mid(v.tensor_scalar(out=pvec[:], in0=pvec[:], scalar1=15, scalar2=None, op0=ALU.bitwise_and))
        P.mid(v.tensor_copy(out=bits[:], in_=pvec[:].broadcast_to((128, 1024))))
        return v.tensor_tensor(out=shp[:], in0=shp[:], in1=bits[:], op=ALU.add)
    shpo = A("vector", shp_fn, deps=[sio], inc=("s_v", 1), tag="shp")

    prev = {}
    wl_copies = []
    dd_copies = []
    for pc in range(NPIECE):
        sl = pc % 2
        wdeps = [prev.get(('idx', pc - 2))]
        ldq = A("sync", lambda s, pc=pc, sl=sl: s.dma_start(
            out=wqw[:, sl, :], in_=qw_d[pc]), deps=wdeps,
            inc=(f"s_x{sl}", 16), tag=f"ldq{pc}")

        def unpack_fn(v, pc=pc, sl=sl):
            wv = wqw[:, sl, :].rearrange("p (g pl w) -> p g pl w", g=4, pl=8)
            for pl in range(8):
                wap = wv[:, :, pl, :].unsqueeze(3).broadcast_to((128, 4, 128, 2))
                P.mid(v.tensor_tensor(out=bits[:].rearrange("p (g q1 q0) -> p g q1 q0", g=4, q1=128),
                                      in0=wap,
                                      in1=shp[:].rearrange("p (g q1 q0) -> p g q1 q0", g=4, q1=128),
                                      op=ALU.logical_shift_right))
                if pl == 0:
                    P.mid(v.tensor_scalar(out=idx32[:], in0=bits[:], scalar1=1, scalar2=None,
                                          op0=ALU.bitwise_and))
                else:
                    P.mid(v.tensor_scalar(out=bits[:], in0=bits[:], scalar1=1, scalar2=pl,
                                          op0=ALU.bitwise_and, op1=ALU.logical_shift_left))
                    P.mid(v.tensor_tensor(out=idx32[:], in0=idx32[:], in1=bits[:], op=ALU.bitwise_or))
            return v.tensor_copy(out=idx16[:, sl, :], in_=idx32[:])
        idxo = A("vector", unpack_fn, deps=[ldq, shpo, prev.get(('gather_last', pc - 2))],
                 inc=("s_v", 1), tag=f"idx{pc}")
        prev[('idx', pc)] = idxo

        g_last = None
        for gl in range(4):
            for ch in range(4):
                gk = pc * 16 + gl * 4 + ch
                gsl = gk % 2
                gdeps = [idxo, dallop, prev.get(('tr', gk - 2))]
                gop = A("gpsimd", lambda g, pc=pc, sl=sl, gl=gl, ch=ch, gsl=gsl: g.indirect_copy(
                    out=outg[:, gsl, :], data=dall[:, 4 * pc + gl, :],
                    idxs=idx16[:, sl, (gl * 256 + ch * 64):(gl * 256 + (ch + 1) * 64)],
                    i_know_ap_gather_is_preferred=True),
                    deps=gdeps, inc=("s_g", 1), tag=f"g{pc}_{gl}_{ch}")
                g_last = gop

                def tr_fn(te, gsl=gsl):
                    r = None
                    for k in range(8):
                        r = te.transpose(ps_tr[:, gsl, k, :],
                                         outg[:, gsl, k * 128:(k + 1) * 128], idT[:])
                    return r
                trop = A("tensor", tr_fn,
                         deps=[gop, ido, prev.get(('cpd', gk - 2))],
                         inc=("s_mm", 1), tag=f"dtr{pc}_{gl}_{ch}")
                prev[('tr', gk)] = trop

                osl = pc * 32 + 8 * gl
                trsel = ps_tr[:, gsl, :, :].rearrange("p k (a b) -> p k a b", b=16)
                cpw = A("vector", lambda v, ch=ch, osl=osl, trsel=trsel: v.tensor_copy(
                    out=wlT_sb[:, 8 * ch:8 * ch + 8, osl:osl + 8],
                    in_=trsel[:, :, :, 0:1].rearrange("p k a o -> p k (a o)")),
                    deps=[trop], inc=("s_v", 1), tag=f"cpw{pc}_{gl}_{ch}")
                cpd = A("scalar", lambda sc, ch=ch, osl=osl, trsel=trsel: sc.copy(
                    out=ddT_sb[:, 8 * ch:8 * ch + 8, osl:osl + 8],
                    in_=trsel[:, :, :, 8:9].rearrange("p k a o -> p k (a o)")),
                    deps=[trop, cpw], inc=("s_sc", 1), tag=f"cpd{pc}_{gl}_{ch}")
                wl_copies.append(cpw)
                dd_copies.append(cpd)
                prev[('cpd', gk)] = cpd
        prev[('gather_last', pc)] = g_last

    stw = A("sync", lambda s: s.dma_start(
        out=wl_o.rearrange("(n p) c -> p n c", p=128), in_=wlT_sb[:]),
        deps=wl_copies, inc=("s_st0", 16), tag="dqstw")
    std = A("sync", lambda s: s.dma_start(
        out=dd_o.rearrange("(n p) c -> p n c", p=128), in_=ddT_sb[:]),
        deps=dd_copies, inc=("s_st1", 16), tag="dqstd")
    tstores = [stw, std]

    if not fused:
        def fin_fn(s):
            s.wait_ge(P.sem_handles["s_st0"], 16 * 128)
            s.wait_ge(P.sem_handles["s_st1"], 16 * 128)
            return s.nop()
        A("sync", fin_fn, deps=[], tag="fin")
        return P

    ccw = A("gpsimd", lambda g: g.collective_compute(
        "AllGather", ALU.bypass, replica_groups=[list(range(8))],
        ins=[wl_t.ap().opt()], outs=[wl_all_t.ap().opt()]),
        deps=tstores, inc=("s_ccw", 1), tag="ccw")
    ccd = A("gpsimd", lambda g: g.collective_compute(
        "AllGather", ALU.bypass, replica_groups=[list(range(8))],
        ins=[dd_t.ap().opt()], outs=[dd_all_t.ap().opt()]),
        deps=tstores, inc=("s_ccd", 1), tag="ccd")
    return [ccw, ccd]



def host_prep_l1f(inp):
    return {"lf_b": np.ascontiguousarray(np.asarray(inp['L1_ffn_bias']).reshape(1, DM))}


def build_l1f(nc, x_in, x_out, es, P, uniq="", wl_all=None, dd_all=None):
    wl8 = wl_all.rearrange("(r n p) c -> p r n c", p=128, r=8)
    dd8 = dd_all.rearrange("(r n p) c -> p r n c", p=128, r=8)
    bia = nc.dram_tensor(uniq + "lf_b", [1, DM], F32, kind="ExternalInput").ap()

    T = lambda n, s, d: es.enter_context(nc.sbuf_tensor(uniq + "lfs_" + n, s, d))
    b_bf = T("b_bf", [1, DM], BF16)
    ones1 = T("ones1", [1, 128], BF16)
    wstr = T("wstr", [128, 2, 4, 1024], BF16)
    dstr = T("dstr", [128, 2, 4, 1024], BF16)
    xbuf = T("xbuf", [128, 2, DM], F32)
    u_bf = T("u_bf", [128, DM], BF16)
    u_f = T("u_f", [128, DM], F32)
    sq_scr = T("sq_scr", [128, DM], F32)
    red256 = T("red256", [128, 256], F32)
    uT = T("uT", [128, NI, 128], BF16)
    tmin = T("tmin", [128, 1024], F32)
    ycb = T("ycb", [128, DM], F32)
    e2 = T("e2", [128, 1], F32)
    mask = T("mask", [128, 1], F32)
    red32 = T("red32", [128, 32], F32)
    ysum4 = T("ysum4", [128, 4], F32)
    yssq4 = T("yssq4", [128, 4], F32)
    ln_x = alloc_ln(T, "x" + uniq + "f1", NT)
    ln_y = alloc_ln(T, "y" + uniq + "f1", NT)

    ps_y = es.enter_context(nc.psum_tensor(uniq + "lf_ps_y", [128, 2, 1024], F32))
    ps_d = es.enter_context(nc.psum_tensor(uniq + "lf_ps_d", [128, 2, 1024], F32))

    ctx = {"u_bf": u_f, "uT": uT, "red32": red32, "tmin": tmin, "sq_scr": sq_scr,
           "red256": red256, "ysum4": ysum4, "yssq4": yssq4}

    A = P.add
    wload = [A("gpsimd", lambda g: g.dma_start(out=b_bf[:], in_=bia[:]), inc=("s_w", 16), tag="lfb")]
    pones = A("vector", lambda v: v.memset(ones1[:], 1.0), inc=("s_v", 1), tag="lfones")

    prev = {}
    for t in range(NT):
        sl = t % 2
        xap = xbuf[:, sl, :]
        deps_ldx = [] if t < 2 else [prev[(t - 2, 'sto')]]
        ldx = A("sync", lambda s, t=t, xap=xap: s.dma_start(
            out=xap, in_=x_in[t*128:(t+1)*128, :]), deps=deps_ldx,
            inc=(f"s_x{sl}", 16), tag=f"lfldx{t}")

        uop, xsq, _ = emit_ln_input(P, A, ctx, t, xap, ln_x, [ldx],
                                    prev.get((t - 1, 'e2o')), "lf", three_stage=True)
        ucast = A("vector", lambda v: v.tensor_copy(out=u_bf[:], in_=u_f[:]),
                  deps=[uop, prev.get((t - 1, 'utr'))], inc=("s_v", 1), tag=f"lfuc{t}")
        utr = emit_utr(P, A, {"u_bf": u_bf, "uT": uT}, ucast, prev.get((t - 1, 'lastmm')), f"lf{t}")
        prev[(t, 'utr')] = utr

        esq = A("scalar", lambda sc: sc.activation(out=sq_scr[:], in_=u_f[:], func=AF.Square),
                deps=[uop], inc=("s_sc", 1), tag=f"lfesq{t}")
        prev[(t, 'esq')] = esq

        def e2_fn(v):
            P.mid(v.tensor_reduce(out=red256[:], in_=sq_scr[:].rearrange("p (a b) -> p a b", a=256),
                                  axis=AX.X, op=ALU.add))
            P.mid(v.tensor_reduce(out=red32[:, 0:16],
                                  in_=red256[:].rearrange("p (a b) -> p a b", a=16),
                                  axis=AX.X, op=ALU.add))
            return v.tensor_reduce(out=e2[:], in_=red32[:, 0:16], axis=AX.X, op=ALU.add)
        e2o = A("vector", e2_fn, deps=[esq], inc=("s_v", 1), tag=f"lfe2{t}")
        prev[(t, 'e2o')] = e2o
        msk = A("gpsimd", lambda g: g.tensor_scalar(
            out=mask[:], in0=e2[:], scalar1=4096.0, scalar2=None, op0=ALU.is_gt),
            deps=[e2o], inc=("s_g", 1), tag=f"lfmsk{t}")

        def make_quarter(q, extra_deps, t=t, utr=utr, msk=msk):
            def bias_fn(te, q=q):
                r = None
                for c in range(2):
                    r = te.matmul(ps_y[:, q % 2, c*512:(c+1)*512], ones1[:],
                                  b_bf[:, q*1024 + c*512:q*1024 + (c+1)*512],
                                  start=True, stop=False)
                return r
            bmm = A("tensor", bias_fn, deps=[pones] + wload + list(extra_deps),
                    inc=("s_mm", 1), tag=f"lfb{q}_{t}")
            last = bmm
            for j in range(8):
                gi = q * 8 + j
                wsl = gi % 2
                wdeps = [prev.get(('g', t * 32 + gi - 2))]
                def wld_fn(s, q=q, j=j, wsl=wsl):
                    s.dma_start(out=wstr[:, wsl, :, 0:512],
                                in_=wl8[:, 2*q, 4*j:4*j+4, :]
                                ).then_inc(P.sem_handles[f"s_wl{wsl}"], 16)
                    s.dma_start(out=wstr[:, wsl, :, 512:1024],
                                in_=wl8[:, 2*q+1, 4*j:4*j+4, :]
                                ).then_inc(P.sem_handles[f"s_wl{wsl}"], 16)
                    return "self"
                wld = A("sync", wld_fn, deps=wdeps, inc=(f"s_wl{wsl}", 32),
                        tag=f"lfwl{q}_{j}_{t}", self_inc=True)

                def dld_fn(s, q=q, j=j, wsl=wsl):
                    s.dma_start(out=dstr[:, wsl, :, 0:512],
                                in_=dd8[:, 2*q, 4*j:4*j+4, :]
                                ).then_inc(P.sem_handles[f"s_dl{wsl}"], 16)
                    s.dma_start(out=dstr[:, wsl, :, 512:1024],
                                in_=dd8[:, 2*q+1, 4*j:4*j+4, :]
                                ).then_inc(P.sem_handles[f"s_dl{wsl}"], 16)
                    return "self"
                dldw = A("sync", dld_fn, deps=wdeps, inc=(f"s_dl{wsl}", 32),
                        tag=f"lfdl{q}_{j}_{t}", self_inc=True)

                def g_fn(te, q=q, j=j, wsl=wsl):
                    r = None
                    for ii in range(4):
                        for c in range(2):
                            te.matmul(ps_y[:, q % 2, c*512:(c+1)*512],
                                      uT[:, 4*j + ii, :],
                                      wstr[:, wsl, ii, c*512:(c+1)*512],
                                      start=False, stop=(j == 7 and ii == 3))
                            r = te.matmul(ps_d[:, q % 2, c*512:(c+1)*512],
                                          uT[:, 4*j + ii, :],
                                          dstr[:, wsl, ii, c*512:(c+1)*512],
                                          start=(j == 0 and ii == 0),
                                          stop=(j == 7 and ii == 3))
                    return r
                gop = A("tensor", g_fn, deps=[wld, dldw, last, utr],
                        inc=("s_mm", 1), tag=f"lfg{q}_{j}_{t}")
                prev[('g', t * 32 + gi)] = gop
                last = gop
            if q == 3:
                prev[(t, 'lastmm')] = last
            def yev_fn(sc, q=q):
                r = None
                for c2 in range(2):
                    r = sc.copy(out=ycb[:, q*1024 + c2*512:q*1024 + (c2+1)*512],
                                in_=ps_y[:, q % 2, c2*512:(c2+1)*512])
                return r
            yev = A("scalar", yev_fn, deps=[last], inc=("s_sc", 1), tag=f"lfyev{q}_{t}")
            cmb = A("vector", lambda v, q=q: v.scalar_tensor_tensor(
                out=ycb[:, q*1024:(q+1)*1024], in0=ps_d[:, q % 2, :],
                scalar=mask[:], in1=ycb[:, q*1024:(q+1)*1024],
                op0=ALU.mult, op1=ALU.add),
                deps=[yev, msk], inc=("s_v", 1), tag=f"lfcmb{q}_{t}")
            return cmb

        sto, q_readers, _ = emit_quarter_stats_epilogue(
            P, A, ctx, t, xap, ln_y, make_quarter, prev.get((t - 1, 'qr')), True,
            x_out[t*128:(t+1)*128, :], f"lf{t}_", ysrc_sbuf=ycb)
        prev[(t, 'qr')] = q_readers
        prev[(t, 'sto')] = sto

    return [prev[(t, 'sto')] for t in range(NT)]


def kernel(**inputs):
    inp = inputs
    x = np.ascontiguousarray(np.asarray(inp['L0_attn_hidden_states'],
                                        dtype=np.float32).reshape(16384, DM))

    nc = bass.Bass("TRN2", target_bir_lowering=False)
    x0 = nc.dram_tensor("x_in", [TOK, DM], F32, kind="ExternalInput").ap()
    x1 = nc.dram_tensor("x1", [TOK, DM], F32, kind="Internal").ap()
    x2 = nc.dram_tensor("x2", [TOK, DM], F32, kind="Internal").ap()
    x3 = nc.dram_tensor("x3", [TOK, DM], F32, kind="Internal").ap()
    xo = nc.dram_tensor("x_out", [TOK, DM], F32, kind="ExternalOutput").ap()
    wl_all_t = nc.dram_tensor("dq_wl_all", [8 * DM, 512], BF16, kind="Internal")
    dd_all_t = nc.dram_tensor("dq_dd_all", [8 * DM, 512], BF16, kind="Internal")

    es = contextlib.ExitStack()
    with es:
        P = Prog(nc)
        with contextlib.ExitStack() as es_p:
            term = build_dequant(nc, es_p, P, wl_all_t=wl_all_t, dd_all_t=dd_all_t)
        P.set_barrier(term, "barD")
        with contextlib.ExitStack() as es_p:
            term = build_l0a(nc, x0, x1, es_p, P)
        P.set_barrier(term, "barA")
        with contextlib.ExitStack() as es_p:
            term = build_l0f(nc, x1, x2, es_p, P)
        P.set_barrier(term, "barF")
        with contextlib.ExitStack() as es_p:
            term = build_l1a(nc, x2, x3, es_p, P)
        P.set_barrier(term, "barB")
        with contextlib.ExitStack() as es_p:
            term = build_l1f(nc, x3, xo, es_p, P,
                             wl_all=wl_all_t.ap(), dd_all=dd_all_t.ap())
        P.add("sync", lambda s: s.nop(), deps=term, inc=("s_fin", 1), tag="fin")
        P.emit(es)

    w = {}
    w.update(host_prep_l0a(inp))
    w.update(host_prep_l0f(inp))
    w.update(host_prep_l1a(inp))
    w.update(host_prep_l1f(inp))
    in_maps = []
    for c in range(8):
        m = dict(w)
        m["x_in"] = np.ascontiguousarray(x[c*TOK:(c+1)*TOK])
        m.update(host_prep_dequant(inp, c))
        in_maps.append(m)
    res = run_bass_kernel_spmd(nc, in_maps, core_ids=list(range(8)))
    xcat = np.concatenate([np.asarray(r["x_out"]) for r in res.results], axis=0)
    return np.ascontiguousarray(xcat.reshape(4, 4096, DM)).astype(np.float32)


# version 05
# speedup vs baseline: 1.1343x; 1.0024x over previous
import numpy as np
import contextlib
import math
import sys
sys.path.insert(0, '/opt/trn_rl_repo')
import concourse.bass as bass
import concourse.mybir as mybir
import bass_rust
from concourse.bass_utils import run_bass_kernel_spmd


F32 = mybir.dt.float32; BF16 = mybir.dt.bfloat16
I32 = mybir.dt.int32; I16 = mybir.dt.int16; U16 = mybir.dt.uint16
AX = mybir.AxisListType; ALU = mybir.AluOpType; AF = mybir.ActivationFunctionType
ENGINES = ("sync", "scalar", "vector", "gpsimd", "tensor")
NT = 16
TOK = NT * 128
DM = 4096
NI = DM // 128
RS32 = 1.0 / math.sqrt(32.0)
NPIECE = 16

"""framework.

Ops are declared in global program order with explicit dependency edges
(producer op objects). Each op runs on one engine; semaphore wait targets are
computed automatically. Same-engine deps rely on program order (no wait
emitted) except when force=True.
"""

import sys






class Op:
    __slots__ = ("engine", "fn", "deps", "incs", "post", "idx", "tag", "self_inc")

    def __init__(self, engine, fn, deps, incs, idx, tag, self_inc=False):
        self.engine = engine
        self.fn = fn
        self.deps = deps
        self.incs = incs
        self.post = {}
        self.idx = idx
        self.tag = tag
        self.self_inc = self_inc


class Prog:
    def __init__(self, nc):
        self.nc = nc
        self.ops = []
        self.sem_counts = {}
        self.sem_handles = {}
        self.chain_counts = {e: 0 for e in ENGINES}
        self._last = {}
        self.cur_engine = None
        self.cur_eng_obj = None
        self.barrier = None

    def set_barrier(self, deps, tag):
        bar = self.add("sync", lambda s: s.nop(), deps=deps,
                       inc=("s_bar", 1), tag=tag)
        self.barrier = bar
        return bar

    def mid(self, instr):
        e = self.cur_engine
        self.chain_counts[e] += 1
        h = self.sem_handles["ch_" + e]
        instr.then_inc(h, 1)
        self.cur_eng_obj.wait_ge(h, self.chain_counts[e])

    def add(self, engine, fn, deps=(), inc=None, tag="", self_inc=False):
        if inc is None:
            incs = []
        elif isinstance(inc, tuple):
            incs = [inc]
        else:
            incs = list(inc)
        if isinstance(deps, Op):
            deps = [deps]
        deps = [d for d in deps if d is not None]
        if not deps and self.barrier is not None:
            deps = [self.barrier]
        if engine in ("vector", "scalar", "gpsimd"):
            prev = self._last.get(engine)
            if prev is not None and prev.incs:
                deps = list(deps) + [prev]
        op = Op(engine, fn, deps, incs, len(self.ops), tag, self_inc)
        self.ops.append(op)
        self._last[engine] = op
        return op

    def _resolve(self):
        for op in self.ops:
            for (s, k) in op.incs:
                self.sem_counts[s] = self.sem_counts.get(s, 0) + k
                op.post[s] = self.sem_counts[s]

    def emit(self, es):
        self._resolve()
        for s in self.sem_counts:
            self.sem_handles[s] = es.enter_context(self.nc.semaphore(s))
        for e in ENGINES:
            self.sem_handles["ch_" + e] = es.enter_context(self.nc.semaphore("ch_" + e))
        nc = self.nc
        with nc.Block() as block:
            for engname in ENGINES:
                eng_ops = [o for o in self.ops if o.engine == engname]
                if not eng_ops:
                    continue

                def make(eng_ops=eng_ops, engname=engname):
                    def body(eng):
                        self.cur_engine = engname
                        self.cur_eng_obj = eng
                        for op in eng_ops:
                            waits = {}
                            for d in op.deps:
                                if not d.incs:
                                    raise RuntimeError(
                                        f"dep {d.tag} of {op.tag} has no inc")
                                s = d.incs[0][0]
                                v = d.post[s]
                                if waits.get(s, -1) < v:
                                    waits[s] = v
                            for s, v in waits.items():
                                eng.wait_ge(self.sem_handles[s], v)
                            ret = op.fn(eng)
                            if op.incs and not op.self_inc:
                                if ret is None:
                                    raise RuntimeError(
                                        f"op {op.tag} has incs but returned no instr")
                                last = ret[-1] if isinstance(ret, (list, tuple)) else ret
                                for (s, k) in op.incs:
                                    last.then_inc(self.sem_handles[s], k)
                        return None
                    return body

                getattr(block, engname)(make())
        return block



import sys







def alloc_ln(T, tag, slots):
    ln = {}
    for fld in ("sum", "ssq", "mneg", "varep", "r0", "rstd", "t1", "nt", "madj"):
        ln[fld] = T(f"ln{tag}_{fld}", [128, slots], F32)
    return ln


def emit_ln_smalls(P, A, ln, t, dep_stats, tag):
    def smalls_fn(g):
        P.mid(g.tensor_scalar(out=ln["mneg"][:, t:t+1], in0=ln["sum"][:, t:t+1],
                              scalar1=-1.0/DM, scalar2=None, op0=ALU.mult))
        P.mid(g.tensor_tensor(out=ln["t1"][:, t:t+1], in0=ln["mneg"][:, t:t+1],
                              in1=ln["mneg"][:, t:t+1], op=ALU.mult))
        P.mid(g.tensor_scalar(out=ln["varep"][:, t:t+1], in0=ln["ssq"][:, t:t+1],
                              scalar1=1.0/DM, scalar2=1e-5, op0=ALU.mult, op1=ALU.add))
        return g.tensor_tensor(out=ln["varep"][:, t:t+1], in0=ln["varep"][:, t:t+1],
                               in1=ln["t1"][:, t:t+1], op=ALU.subtract)
    smalls = A("gpsimd", smalls_fn, deps=dep_stats, inc=("s_g", 1), tag=f"sm{tag}{t}")

    def rstd0_fn(sc):
        P.mid(sc.activation(out=ln["t1"][:, t:t+1], in_=ln["varep"][:, t:t+1],
                            func=AF.Ln, bias=0.0, scale=1.0))
        return sc.activation(out=ln["r0"][:, t:t+1], in_=ln["t1"][:, t:t+1],
                             func=AF.Exp, bias=0.0, scale=-0.5)
    rstd0 = A("scalar", rstd0_fn, deps=[smalls], inc=("s_sc", 1), tag=f"r0{tag}{t}")

    def newton_fn(g):
        P.mid(g.tensor_tensor(out=ln["nt"][:, t:t+1], in0=ln["r0"][:, t:t+1],
                              in1=ln["r0"][:, t:t+1], op=ALU.mult))
        P.mid(g.tensor_tensor(out=ln["nt"][:, t:t+1], in0=ln["nt"][:, t:t+1],
                              in1=ln["varep"][:, t:t+1], op=ALU.mult))
        P.mid(g.tensor_scalar(out=ln["nt"][:, t:t+1], in0=ln["nt"][:, t:t+1],
                              scalar1=-0.5, scalar2=1.5, op0=ALU.mult, op1=ALU.add))
        P.mid(g.tensor_tensor(out=ln["rstd"][:, t:t+1], in0=ln["r0"][:, t:t+1],
                              in1=ln["nt"][:, t:t+1], op=ALU.mult))
        return g.tensor_tensor(out=ln["madj"][:, t:t+1], in0=ln["mneg"][:, t:t+1],
                               in1=ln["rstd"][:, t:t+1], op=ALU.mult)
    newton = A("gpsimd", newton_fn, deps=[rstd0], inc=("s_g", 1), tag=f"nw{tag}{t}")
    return newton


def emit_ln_input(P, A, ctx, t, xap, ln, deps_x, dep_u_free, tag, three_stage=False):
    u_bf = ctx["u_bf"]; uT = ctx["uT"]; red32 = ctx["red32"]
    xsq_out = ctx.get("sq_scr")
    xsq_out = u_bf[:] if xsq_out is None else xsq_out[:]
    xsq = A("scalar", lambda sc: sc.activation(
        out=xsq_out, in_=xap, func=AF.Square,
        accum_out=(None if three_stage else ln["ssq"][:, t:t+1])),
        deps=deps_x + [dep_u_free], inc=("s_sc", 1), tag=f"xsq{tag}{t}")

    if three_stage:
        def ssq_fn(v):
            P.mid(v.tensor_reduce(out=ctx["red256"][:],
                                  in_=xsq_out.rearrange("p (a b) -> p a b", a=256),
                                  axis=AX.X, op=ALU.add))
            P.mid(v.tensor_reduce(out=red32[:, 0:16],
                                  in_=ctx["red256"][:].rearrange("p (a b) -> p a b", a=16),
                                  axis=AX.X, op=ALU.add))
            return v.tensor_reduce(out=ln["ssq"][:, t:t+1], in_=red32[:, 0:16],
                                   axis=AX.X, op=ALU.add)
        xssq = A("vector", ssq_fn, deps=[xsq], inc=("s_v", 1), tag=f"xssq{tag}{t}")
        stats_dep = [xssq]
    else:
        stats_dep = [xsq]

    def xsum_fn(v):
        P.mid(v.tensor_reduce(out=red32[:], in_=xap.rearrange("p (a b) -> p a b", a=32),
                              axis=AX.X, op=ALU.add))
        return v.tensor_reduce(out=ln["sum"][:, t:t+1], in_=red32[:],
                               axis=AX.X, op=ALU.add)
    xsum = A("vector", xsum_fn, deps=deps_x, inc=("s_v", 1), tag=f"xsum{tag}{t}")

    newton = emit_ln_smalls(P, A, ln, t, stats_dep + [xsum], tag)

    uop = A("vector", lambda v: v.tensor_scalar(
        out=u_bf[:], in0=xap, scalar1=ln["mneg"][:, t:t+1],
        scalar2=ln["rstd"][:, t:t+1], op0=ALU.add, op1=ALU.mult),
        deps=[newton, xsq], inc=("s_v", 1), tag=f"u{tag}{t}")
    return uop, xsq, newton


def emit_utr(P, A, ctx, dep_u, dep_free, tag):
    u_bf = ctx["u_bf"]; uT = ctx["uT"]
    def utr_fn(s):
        for i in range(NI):
            s.dma_start(out=uT[:, i, :], in_=u_bf[:, i*128:(i+1)*128],
                        transpose=True).then_inc(P.sem_handles["s_t"], 16)
        return "self"
    return A("sync", utr_fn, deps=[dep_u, dep_free], inc=("s_t", 16*NI),
             tag=f"utr{tag}", self_inc=True)


def emit_quarter_stats_epilogue(P, A, ctx, t, xap, ln_y, make_quarter, prev_readers,
                                x_store, x_out_ap, tag, ysrc_sbuf=None):
    red32 = ctx["red32"]; tmin = ctx["tmin"]
    ysum4 = ctx["ysum4"]; yssq4 = ctx["yssq4"]
    ps_y = ctx.get("ps_y"); ysb = ctx.get("ysb")
    ystats = []; ysqs = []; evacs = [None, None]
    quarter_ops = []
    q_readers = {}
    prev_ysqr = None
    prev_act_ps = None
    for q in range(4):
        if ysrc_sbuf is not None:
            src = ysrc_sbuf[:, q*1024:(q+1)*1024]
        else:
            src = ps_y[:, q % 2, :]
        extra = []
        if q >= 2:
            extra += q_readers.get(q - 2, [])
        elif prev_readers is not None:
            extra += prev_readers.get(q + 2, [])
        at = make_quarter(q, extra)
        quarter_ops.append(at)
        q_readers[q] = [at]

        def yst_fn(v, src=src, q=q):
            P.mid(v.tensor_reduce(out=red32[:, 0:8],
                                  in_=src.rearrange("p (a b) -> p a b", a=8),
                                  axis=AX.X, op=ALU.add))
            return v.tensor_reduce(out=ysum4[:, q:q+1], in_=red32[:, 0:8],
                                   axis=AX.X, op=ALU.add)
        yst = A("vector", yst_fn, deps=[at, prev_act_ps], inc=("s_v", 1), tag=f"yst{tag}{q}_{t}")
        ystats.append(yst)

        def ysq0_fn(sc, src=src):
            r = None
            for c2 in range(2):
                r = sc.activation(out=tmin[:, c2*512:(c2+1)*512],
                                  in_=src[:, c2*512:(c2+1)*512], func=AF.Square)
            return r
        ysq0 = A("scalar", ysq0_fn, deps=[at, yst, prev_ysqr], inc=("s_sc", 1), tag=f"ysq0{tag}{q}_{t}")

        def ysqr_fn(v, q=q):
            P.mid(v.tensor_reduce(out=red32[:, 8:16],
                                  in_=tmin[:].rearrange("p (a b) -> p a b", a=8),
                                  axis=AX.X, op=ALU.add))
            return v.tensor_reduce(out=yssq4[:, q:q+1], in_=red32[:, 8:16],
                                   axis=AX.X, op=ALU.add)
        ysq = A("vector", ysqr_fn, deps=[ysq0], inc=("s_v", 1), tag=f"ysq{tag}{q}_{t}")
        ysqs.append(ysq); prev_ysqr = ysq
        if ysrc_sbuf is None:
            prev_act_ps = ysq0
        if ysrc_sbuf is None:
            q_readers[q] += [yst, ysq0]

        if ysrc_sbuf is None and q < 2:
            def yev_fn(sc, q=q, src=src):
                r = None
                for c2 in range(2):
                    r = sc.copy(out=ysb[:, q*1024 + c2*512:q*1024 + (c2+1)*512],
                                in_=src[:, c2*512:(c2+1)*512])
                return r
            ev = A("scalar", yev_fn, deps=[at, yst, ysq0], inc=("s_sc", 1),
                   tag=f"yev{tag}{q}_{t}")
            evacs[q] = ev
            prev_act_ps = ev
            q_readers[q].append(ev)

    def ysum_fn(g):
        P.mid(g.tensor_tensor(out=ln_y["sum"][:, t:t+1], in0=ysum4[:, 0:1], in1=ysum4[:, 1:2], op=ALU.add))
        P.mid(g.tensor_tensor(out=ln_y["t1"][:, t:t+1], in0=ysum4[:, 2:3], in1=ysum4[:, 3:4], op=ALU.add))
        P.mid(g.tensor_tensor(out=ln_y["sum"][:, t:t+1], in0=ln_y["sum"][:, t:t+1], in1=ln_y["t1"][:, t:t+1], op=ALU.add))
        P.mid(g.tensor_tensor(out=ln_y["ssq"][:, t:t+1], in0=yssq4[:, 0:1], in1=yssq4[:, 1:2], op=ALU.add))
        P.mid(g.tensor_tensor(out=ln_y["t1"][:, t:t+1], in0=yssq4[:, 2:3], in1=yssq4[:, 3:4], op=ALU.add))
        return g.tensor_tensor(out=ln_y["ssq"][:, t:t+1], in0=ln_y["ssq"][:, t:t+1], in1=ln_y["t1"][:, t:t+1], op=ALU.add)
    ysum = A("gpsimd", ysum_fn, deps=ystats + ysqs, inc=("s_g", 1), tag=f"ysum{tag}{t}")

    newton = emit_ln_smalls(P, A, ln_y, t, [ysum], "y" + tag)

    yaps = []
    for q in range(4):
        if ysrc_sbuf is not None:
            src = ysrc_sbuf[:, q*1024:(q+1)*1024]
            dps = [newton, quarter_ops[q], ysqs[q]]
        elif q < 2:
            src = ysb[:, q*1024:(q+1)*1024]
            dps = [newton, evacs[q]]
        else:
            src = ps_y[:, q % 2, :]
            dps = [newton, ystats[q], ysqs[q], quarter_ops[q]]
        yap = A("vector", lambda v, q=q, src=src: v.scalar_tensor_tensor(
            out=xap[:, q*1024:(q+1)*1024], in0=src,
            scalar=ln_y["rstd"][:, t:t+1], in1=xap[:, q*1024:(q+1)*1024],
            op0=ALU.mult, op1=ALU.add),
            deps=dps, inc=("s_v", 1), tag=f"yap{tag}{q}_{t}")
        yaps.append(yap)
        q_readers.setdefault(q, []).append(yap)

    xmadj = A("vector", lambda v: v.tensor_scalar(
        out=xap, in0=xap, scalar1=ln_y["madj"][:, t:t+1], scalar2=None, op0=ALU.add),
        deps=yaps + [newton], inc=("s_v", 1), tag=f"xmadj{tag}{t}")

    if x_store:
        sto = A("sync", lambda s: s.dma_start(
            out=x_out_ap, in_=xap), deps=[xmadj], inc=(f"s_st{t % 2}", 16), tag=f"sto{tag}{t}")
    else:
        sto = xmadj
    return sto, q_readers, ysqs[3]












TOK = NT * 128; DM = 4096; NI = DM // 128


def host_prep_l0a(inp):
    H, HKV, D = 32, 8, 128
    Wq = np.asarray(inp['L0_attn_Wq']); bq = np.asarray(inp['L0_attn_bq'])
    Wq_sel = np.ascontiguousarray(Wq.reshape(H, D, DM)[::4].reshape(HKV * D, DM).T)
    bq_sel = np.ascontiguousarray(bq.reshape(H, D)[::4].reshape(1, HKV * D))
    WoT = np.ascontiguousarray(np.asarray(inp['L0_attn_Wo']).T)
    Mz = np.ascontiguousarray(np.concatenate(
        [np.asarray(inp['L0_attn_M']), np.asarray(inp['L0_attn_z'])[:, :, None]], axis=2))
    return {"wqselT": Wq_sel, "bqsel": bq_sel, "woT": WoT, "mz": Mz}


def build_l0a(nc, x_in, x_out, es, P, uniq=""):
    nc_ = nc
    wqselT = nc.dram_tensor(uniq + "wqselT", [DM, 1024], F32, kind="ExternalInput").ap()
    bqsel = nc.dram_tensor(uniq + "bqsel", [1, 1024], F32, kind="ExternalInput").ap()
    woT = nc.dram_tensor(uniq + "woT", [DM, DM], F32, kind="ExternalInput").ap()
    mz = nc.dram_tensor(uniq + "mz", [8, 128, 129], F32, kind="ExternalInput").ap()

    T = lambda n, s, d: es.enter_context(nc.sbuf_tensor(uniq + n, s, d))
    wq_bf = T("wq_bf", [128, NI, 1024], BF16)
    wo_bf = T("wo_bf", [128, 8, DM], BF16)
    mz_bf = T("mz_bf", [128, 8, 129], BF16)
    bq_bf = T("bq_bf", [1, 1024], BF16)
    ones1 = T("ones1", [1, 128], BF16)
    xbuf = T("xbuf", [128, 2, DM], F32)
    u_bf = T("u_bf", [128, DM], BF16)
    uT = T("uT", [128, NI, 128], BF16)
    tmin = T("tmin", [128, 1024], F32)
    sig_bf = T("sig_bf", [128, 1024], BF16)
    sigT = T("sigT", [128, 8, 128], BF16)
    retr_bf = T("retr_bf", [128, 1024], BF16)
    retrT = T("retrT", [128, 8, 128], BF16)
    den8 = T("den8", [128, 8], F32)
    rden8 = T("rden8", [128, 8], F32)
    red32 = T("red32", [128, 32], F32)
    ysb = T("ysb", [128, 2048], F32)
    ysum4 = T("ysum4", [128, 4], F32)
    yssq4 = T("yssq4", [128, 4], F32)
    ln = {}
    for pre in ("x", "y"):
        for fld in ("sum", "ssq", "mneg", "varep", "r0", "rstd", "t1", "nt", "madj"):
            ln[pre + fld] = T(f"ln{pre}_{fld}", [128, NT], F32)

    ps_att = es.enter_context(nc.psum_tensor(uniq + "ps_att", [128, 2, 1024], F32))
    ps_q = es.enter_context(nc.psum_tensor(uniq + "ps_q", [128, 1024], F32))
    ps_num = es.enter_context(nc.psum_tensor(uniq + "ps_num", [128, 8, 128], F32))

    A = P.add
    wload = []
    for j in range(4):
        src = wqselT.rearrange("(n p) c -> p n c", p=128)[:, j * 8:(j + 1) * 8, :]
        wload.append(A("gpsimd", lambda g, j=j, src=src: g.dma_start(
            out=wq_bf[:, j * 8:(j + 1) * 8, :], in_=src), inc=("s_w", 16), tag=f"wq{j}"))
    wload.append(A("gpsimd", lambda g: g.dma_start(
        out=mz_bf[:], in_=mz.rearrange("h p c -> p h c")), inc=("s_w", 16), tag="mz"))
    wload.append(A("gpsimd", lambda g: g.dma_start(
        out=bq_bf[:], in_=bqsel[:]), inc=("s_w", 16), tag="bq"))
    pones = A("vector", lambda v: v.memset(ones1[:], 1.0), inc=("s_v", 1), tag="ones")

    fold_done = []
    prev_add = None
    for h in range(8):
        l0 = A("sync", lambda s, h=h: s.dma_start(
            out=xbuf[:, 0, :], in_=woT[(4 * h) * 128:(4 * h + 1) * 128, :]),
            deps=[prev_add], inc=("s_fold", 16), tag=f"fold_l0_{h}")
        prev = l0
        for g in range(1, 4):
            lg = A("sync", lambda s, h=h, g=g: s.dma_start(
                out=xbuf[:, 1, :], in_=woT[(4 * h + g) * 128:(4 * h + g + 1) * 128, :]),
                deps=[prev], inc=("s_fold", 16), tag=f"fold_l{g}_{h}")
            ad = A("vector", lambda v: v.tensor_tensor(
                out=xbuf[:, 0, :], in0=xbuf[:, 0, :], in1=xbuf[:, 1, :], op=ALU.add),
                deps=[lg], inc=("s_v", 1), tag=f"fold_a{g}_{h}")
            prev = ad
        cst = A("vector", lambda v, h=h: v.tensor_copy(out=wo_bf[:, h, :], in_=xbuf[:, 0, :]),
                deps=[prev], inc=("s_v", 1), tag=f"fold_c{h}")
        prev_add = cst
        fold_done.append(cst)

    prev_tile = {}
    for t in range(NT):
        sl = t % 2
        xap = xbuf[:, sl, :]
        deps_ldx = [fold_done[-1]] if t < 2 else [prev_tile[(t - 2, 'sto')]]
        if t >= 1:
            deps_ldx.append(prev_tile[(t - 1, 'xsum')])
        ldx = A("sync", lambda s, t=t, xap=xap: s.dma_start(
            out=xap, in_=x_in[t * 128:(t + 1) * 128, :]), deps=deps_ldx,
            inc=(f"s_x{t % 2}", 16), tag=f"ldx{t}")

        dep_u_free = prev_tile.get((t - 1, 'ysq3'))
        xsq = A("scalar", lambda sc, t=t, xap=xap: sc.activation(
            out=u_bf[:], in_=xap, func=AF.Square, accum_out=ln["xssq"][:, t:t + 1]),
            deps=[ldx, dep_u_free], inc=("s_sc", 1), tag=f"xsq{t}")

        def xsum_fn(v, t=t, xap=xap):
            P.mid(v.tensor_reduce(out=red32[:], in_=xap.rearrange("p (a b) -> p a b", a=32),
                            axis=AX.X, op=ALU.add))
            return v.tensor_reduce(out=ln["xsum"][:, t:t + 1], in_=red32[:],
                                   axis=AX.X, op=ALU.add)
        xsum = A("vector", xsum_fn, deps=[ldx], inc=("s_v", 1), tag=f"xsum{t}")

        def smalls_fn(g, t=t, pre="x"):
            P.mid(g.tensor_scalar(out=ln[pre + "mneg"][:, t:t + 1], in0=ln[pre + "sum"][:, t:t + 1],
                            scalar1=-1.0 / DM, scalar2=None, op0=ALU.mult))
            P.mid(g.tensor_tensor(out=ln[pre + "t1"][:, t:t + 1], in0=ln[pre + "mneg"][:, t:t + 1],
                            in1=ln[pre + "mneg"][:, t:t + 1], op=ALU.mult))
            P.mid(g.tensor_scalar(out=ln[pre + "varep"][:, t:t + 1], in0=ln[pre + "ssq"][:, t:t + 1],
                            scalar1=1.0 / DM, scalar2=1e-5, op0=ALU.mult, op1=ALU.add))
            return g.tensor_tensor(out=ln[pre + "varep"][:, t:t + 1],
                                   in0=ln[pre + "varep"][:, t:t + 1],
                                   in1=ln[pre + "t1"][:, t:t + 1], op=ALU.subtract)
        smalls = A("gpsimd", smalls_fn, deps=[xsq, xsum], inc=("s_g", 1), tag=f"smx{t}")

        def rstd0_fn(sc, t=t, pre="x"):
            P.mid(sc.activation(out=ln[pre + "t1"][:, t:t + 1], in_=ln[pre + "varep"][:, t:t + 1],
                          func=AF.Ln, bias=0.0, scale=1.0))
            return sc.activation(out=ln[pre + "r0"][:, t:t + 1], in_=ln[pre + "t1"][:, t:t + 1],
                                 func=AF.Exp, bias=0.0, scale=-0.5)
        rstd0 = A("scalar", rstd0_fn, deps=[smalls], inc=("s_sc", 1), tag=f"r0x{t}")

        def newton_fn(g, t=t, pre="x"):
            P.mid(g.tensor_tensor(out=ln[pre + "nt"][:, t:t + 1], in0=ln[pre + "r0"][:, t:t + 1],
                            in1=ln[pre + "r0"][:, t:t + 1], op=ALU.mult))
            P.mid(g.tensor_tensor(out=ln[pre + "nt"][:, t:t + 1], in0=ln[pre + "nt"][:, t:t + 1],
                            in1=ln[pre + "varep"][:, t:t + 1], op=ALU.mult))
            P.mid(g.tensor_scalar(out=ln[pre + "nt"][:, t:t + 1], in0=ln[pre + "nt"][:, t:t + 1],
                            scalar1=-0.5, scalar2=1.5, op0=ALU.mult, op1=ALU.add))
            return g.tensor_tensor(out=ln[pre + "rstd"][:, t:t + 1], in0=ln[pre + "r0"][:, t:t + 1],
                                   in1=ln[pre + "nt"][:, t:t + 1], op=ALU.mult)
        newton = A("gpsimd", newton_fn, deps=[rstd0], inc=("s_g", 1), tag=f"nwx{t}")

        uop = A("vector", lambda v, t=t, xap=xap: v.tensor_scalar(
            out=u_bf[:], in0=xap, scalar1=ln["xmneg"][:, t:t + 1],
            scalar2=ln["xrstd"][:, t:t + 1], op0=ALU.add, op1=ALU.mult),
            deps=[newton, xsq], inc=("s_v", 1), tag=f"u{t}")

        def utr_fn(s, t=t):
            for i in range(NI):
                s.dma_start(out=uT[:, i, :], in_=u_bf[:, i * 128:(i + 1) * 128],
                            transpose=True).then_inc(P.sem_handles["s_t"], 16)
            return "self"
        utr = A("sync", utr_fn, deps=[uop, prev_tile.get((t - 1, 'qselmm'))],
                inc=("s_t", 16 * NI), tag=f"utr{t}", self_inc=True)

        def qsel_fn(te, t=t):
            r = None
            for c in range(2):
                te.matmul(ps_q[:, c * 512:(c + 1) * 512], ones1[:],
                          bq_bf[:, c * 512:(c + 1) * 512], start=True, stop=False)
                for i in range(NI):
                    r = te.matmul(ps_q[:, c * 512:(c + 1) * 512], uT[:, i, :],
                                  wq_bf[:, i, c * 512:(c + 1) * 512],
                                  start=False, stop=(i == NI - 1))
            return r
        qselmm = A("tensor", qsel_fn,
                   deps=[utr, pones] + wload + ([prev_tile[(t - 1, 'sigstt')], prev_tile[(t - 1, 'dencp')]] if t else []),
                   inc=("s_mm", 1), tag=f"qselmm{t}")

        sigmin = A("vector", lambda v: v.tensor_scalar(
            out=tmin[:], in0=ps_q[:], scalar1=0.0, scalar2=None, op0=ALU.min),
            deps=[qselmm, prev_tile.get((t - 1, 'ysq3s'))], inc=("s_v", 1), tag=f"sigmin{t}")
        sigexp = A("scalar", lambda sc: sc.activation(
            out=tmin[:], in_=tmin[:], func=AF.Exp, bias=0.0, scale=1.0),
            deps=[sigmin], inc=("s_sc", 1), tag=f"sigexp{t}")
        sigstt = A("vector", lambda v: v.scalar_tensor_tensor(
            out=sig_bf[:], in0=ps_q[:], scalar=0.0, in1=tmin[:], op0=ALU.max, op1=ALU.add),
            deps=[sigexp], inc=("s_v", 1), tag=f"sigstt{t}")

        def sigtr_fn(s):
            for h in range(8):
                s.dma_start(out=sigT[:, h, :], in_=sig_bf[:, h * 128:(h + 1) * 128],
                            transpose=True).then_inc(P.sem_handles["s_t"], 16)
            return "self"
        sigtr = A("sync", sigtr_fn, deps=[sigstt, prev_tile.get((t - 1, 'num7'))],
                  inc=("s_t", 16 * 8), tag=f"sigtr{t}", self_inc=True)

        nums = []; retrs = []
        num_war = [prev_tile[(t - 1, 'retr7')]] if t >= 1 else []
        den_war = [sigstt]
        for h in range(8):
            nm = A("tensor", lambda te, h=h: te.matmul(
                ps_num[:, h, :], sigT[:, h, :], mz_bf[:, h, 0:128], start=True, stop=True),
                deps=[sigtr] + num_war, inc=("s_mm", 1), tag=f"num{h}_{t}")
            nums.append(nm)
            A("tensor", lambda te, h=h: te.matmul(
                ps_q[:, h:h + 1], sigT[:, h, :], mz_bf[:, h, 128:129], start=True, stop=True),
                deps=[sigtr] + den_war, inc=("s_mm", 1), tag=f"den{h}_{t}")
        dencp = A("scalar", lambda sc: sc.copy(out=den8[:], in_=ps_q[:, 0:8]),
                  deps=[P.ops[-1]], inc=("s_sc", 1), tag=f"dencp{t}")
        deneps = A("gpsimd", lambda g: g.tensor_scalar(
            out=den8[:], in0=den8[:], scalar1=1e-6, scalar2=None, op0=ALU.add),
            deps=[dencp], inc=("s_g", 1), tag=f"deneps{t}")

        def dln_fn(sc):
            P.mid(sc.activation(out=rden8[:], in_=den8[:], func=AF.Ln, bias=0.0, scale=1.0))
            return sc.activation(out=rden8[:], in_=rden8[:], func=AF.Exp,
                                 bias=0.0, scale=-1.0)
        dexp = A("scalar", dln_fn, deps=[deneps], inc=("s_sc", 1), tag=f"dexp{t}")
        for h in range(8):
            rt = A("vector", lambda v, h=h: v.tensor_scalar(
                out=retr_bf[:, h * 128:(h + 1) * 128], in0=ps_num[:, h, :],
                scalar1=rden8[:, h:h + 1], scalar2=None, op0=ALU.mult),
                deps=[dexp, nums[h]], inc=("s_v", 1), tag=f"retr{h}_{t}")
            retrs.append(rt)
        prev_tile[(t, 'retr7')] = retrs[7]

        def retrtr_fn(s):
            for h in range(8):
                s.dma_start(out=retrT[:, h, :], in_=retr_bf[:, h * 128:(h + 1) * 128],
                            transpose=True).then_inc(P.sem_handles["s_t"], 16)
            return "self"
        retrtr = A("sync", retrtr_fn,
                   deps=retrs + [prev_tile.get((t - 1, 'att3'))],
                   inc=("s_t", 16 * 8), tag=f"retrtr{t}", self_inc=True)

        atts = [None] * 4; ystats = []; ysqs = []; evacs = [None, None]
        yaps = [None] * 4
        for q in range(4):
            dps = [retrtr] + ([fold_done[-1]] if t == 0 else [])
            if q >= 2:
                dps += [evacs[q - 2], ystats[q - 2], ysqs[q - 2]]
            elif t >= 1:
                dps += [prev_tile[(t - 1, f'yap{q+2}')]]

            def att_fn(te, q=q):
                r = None
                for c2 in range(2):
                    for h in range(8):
                        r = te.matmul(ps_att[:, q % 2, c2 * 512:(c2 + 1) * 512],
                                      retrT[:, h, :],
                                      wo_bf[:, h, q * 1024 + c2 * 512: q * 1024 + (c2 + 1) * 512],
                                      start=(h == 0), stop=(h == 7))
                return r
            at = A("tensor", att_fn, deps=dps, inc=("s_mm", 1), tag=f"att{q}_{t}")
            atts[q] = at
            prev_tile[(t, f'att{q}')] = at

            def yst_fn(v, q=q, t=t):
                P.mid(v.tensor_reduce(out=red32[:, 0:8],
                                in_=ps_att[:, q % 2, :].rearrange("p (a b) -> p a b", a=8),
                                axis=AX.X, op=ALU.add))
                return v.tensor_reduce(out=ysum4[:, q:q + 1], in_=red32[:, 0:8],
                                       axis=AX.X, op=ALU.add)
            yst = A("vector", yst_fn, deps=[at], inc=("s_v", 1), tag=f"yst{q}_{t}")
            ystats.append(yst)
            def ysq0_fn(sc, q=q):
                r = None
                for c2 in range(2):
                    r = sc.activation(out=tmin[:, c2 * 512:(c2 + 1) * 512],
                                      in_=ps_att[:, q % 2, c2 * 512:(c2 + 1) * 512],
                                      func=AF.Square)
                return r
            ysq0 = A("scalar", ysq0_fn, deps=[at, yst], inc=("s_sc", 1), tag=f"ysq0_{q}_{t}")

            def ysqr_fn(v, q=q):
                P.mid(v.tensor_reduce(out=red32[:, 8:16],
                                      in_=tmin[:].rearrange("p (a b) -> p a b", a=8),
                                      axis=AX.X, op=ALU.add))
                return v.tensor_reduce(out=yssq4[:, q:q + 1], in_=red32[:, 8:16],
                                       axis=AX.X, op=ALU.add)
            ysq = A("vector", ysqr_fn, deps=[ysq0], inc=("s_v", 1), tag=f"ysq{q}_{t}")
            ysqs.append(ysq)
            if q == 3:
                prev_tile[(t, 'ysq3')] = ysq
                prev_tile[(t, 'ysq3s')] = ysq0
            if q < 2:
                evdeps = [at, yst, ysq]
                if t >= 1:
                    evdeps.append(prev_tile[(t - 1, f'yap{q}')])
                def yev_fn(sc, q=q):
                    r = None
                    for c2 in range(2):
                        r = sc.copy(out=ysb[:, q * 1024 + c2 * 512:q * 1024 + (c2 + 1) * 512],
                                    in_=ps_att[:, q % 2, c2 * 512:(c2 + 1) * 512])
                    return r
                ev = A("scalar", yev_fn, deps=evdeps, inc=("s_sc", 1), tag=f"yev{q}_{t}")
                evacs[q] = ev

        def ysum_fn(g, t=t):
            P.mid(g.tensor_tensor(out=ln["ysum"][:, t:t + 1], in0=ysum4[:, 0:1], in1=ysum4[:, 1:2], op=ALU.add))
            P.mid(g.tensor_tensor(out=ln["yt1"][:, t:t + 1], in0=ysum4[:, 2:3], in1=ysum4[:, 3:4], op=ALU.add))
            P.mid(g.tensor_tensor(out=ln["ysum"][:, t:t + 1], in0=ln["ysum"][:, t:t + 1], in1=ln["yt1"][:, t:t + 1], op=ALU.add))
            P.mid(g.tensor_tensor(out=ln["yssq"][:, t:t + 1], in0=yssq4[:, 0:1], in1=yssq4[:, 1:2], op=ALU.add))
            P.mid(g.tensor_tensor(out=ln["yt1"][:, t:t + 1], in0=yssq4[:, 2:3], in1=yssq4[:, 3:4], op=ALU.add))
            return g.tensor_tensor(out=ln["yssq"][:, t:t + 1], in0=ln["yssq"][:, t:t + 1], in1=ln["yt1"][:, t:t + 1], op=ALU.add)
        ysum = A("gpsimd", ysum_fn, deps=ystats + ysqs, inc=("s_g", 1), tag=f"ysum{t}")

        ysmalls = A("gpsimd", lambda g, t=t: smalls_fn(g, t=t, pre="y"), deps=[ysum],
                    inc=("s_g", 1), tag=f"smy{t}")
        yrstd0 = A("scalar", lambda sc, t=t: rstd0_fn(sc, t=t, pre="y"), deps=[ysmalls],
                   inc=("s_sc", 1), tag=f"r0y{t}")

        def ynewton_fn(g, t=t):
            P.mid(newton_fn(g, t=t, pre="y"))
            return g.tensor_tensor(out=ln["ymadj"][:, t:t + 1], in0=ln["ymneg"][:, t:t + 1],
                                   in1=ln["yrstd"][:, t:t + 1], op=ALU.mult)
        ynewton = A("gpsimd", ynewton_fn, deps=[yrstd0], inc=("s_g", 1), tag=f"nwy{t}")

        for q in range(4):
            if q < 2:
                src_ap = ysb[:, q * 1024:(q + 1) * 1024]
                dps = [ynewton, evacs[q]]
            else:
                src_ap = ps_att[:, q % 2, :]
                dps = [ynewton, ystats[q], ysqs[q], atts[q]]
            yap = A("vector", lambda v, q=q, t=t, xap=xap, src_ap=src_ap: v.scalar_tensor_tensor(
                out=xap[:, q * 1024:(q + 1) * 1024], in0=src_ap,
                scalar=ln["yrstd"][:, t:t + 1], in1=xap[:, q * 1024:(q + 1) * 1024],
                op0=ALU.mult, op1=ALU.add),
                deps=dps, inc=("s_v", 1), tag=f"yap{q}_{t}")
            yaps[q] = yap; prev_tile[(t, f'yap{q}')] = yap

        xmadj = A("vector", lambda v, t=t, xap=xap: v.tensor_scalar(
            out=xap, in0=xap, scalar1=ln["ymadj"][:, t:t + 1], scalar2=None, op0=ALU.add),
            deps=yaps + [ynewton], inc=("s_v", 1), tag=f"xmadj{t}")

        sto = A("sync", lambda s, t=t, xap=xap: s.dma_start(
            out=x_out[t * 128:(t + 1) * 128, :], in_=xap), deps=[xmadj],
            inc=(f"s_st{t % 2}", 16), tag=f"sto{t}")
        prev_tile[(t, 'sto')] = sto
        prev_tile[(t, 'xsum')] = xsum
        prev_tile[(t, 'qselmm')] = qselmm
        prev_tile[(t, 'sigstt')] = sigstt
        prev_tile[(t, 'num7')] = nums[7]
        prev_tile[(t, 'dencp')] = dencp
        prev_tile[(t, 'att3')] = atts[3]

    return [prev_tile[(t, 'sto')] for t in range(NT)]



def host_prep_l0f(inp):
    WT = np.ascontiguousarray(np.asarray(inp['L0_ffn_W']).T)
    WqT = np.ascontiguousarray(np.asarray(inp['L0_ffn_Wq']).T)
    WkT = np.ascontiguousarray(np.asarray(inp['L0_ffn_Wk']).T)
    AT = np.ascontiguousarray(np.asarray(inp['L0_ffn_A']).reshape(128, DM).T)
    Bm = np.asarray(inp['L0_ffn_Bm'])
    BmT = np.ascontiguousarray(np.transpose(Bm, (0, 2, 1)).reshape(128, DM))
    b = np.ascontiguousarray(np.asarray(inp['L0_ffn_b']).reshape(1, DM))
    return {"f_wT": WT, "f_wqT": WqT, "f_wkT": WkT, "f_aT": AT, "f_bmT": BmT, "f_b": b}


def build_l0f(nc, x_in, x_out, es, P, uniq=""):
    wT = nc.dram_tensor(uniq + "f_wT", [DM, DM], F32, kind="ExternalInput").ap()
    wqT = nc.dram_tensor(uniq + "f_wqT", [DM, 256], F32, kind="ExternalInput").ap()
    wkT = nc.dram_tensor(uniq + "f_wkT", [DM, 256], F32, kind="ExternalInput").ap()
    aT = nc.dram_tensor(uniq + "f_aT", [DM, 128], F32, kind="ExternalInput").ap()
    bmT = nc.dram_tensor(uniq + "f_bmT", [128, DM], F32, kind="ExternalInput").ap()
    bia = nc.dram_tensor(uniq + "f_b", [1, DM], F32, kind="ExternalInput").ap()
    wmaster = nc.dram_tensor(uniq + "f_wm", [DM, DM], BF16, kind="Internal").ap()

    T = lambda n, s, d: es.enter_context(nc.sbuf_tensor(uniq + "f_" + n, s, d))
    wq_bf = T("wq_bf", [128, NI, 256], BF16)
    wk_bf = T("wk_bf", [128, NI, 256], BF16)
    a_bf = T("a_bf", [128, NI, 128], BF16)
    bm_bf = T("bm_bf", [128, DM], BF16)
    b_bf = T("b_bf", [1, DM], BF16)
    ones1 = T("ones1", [1, 128], BF16)
    wstage = T("wstage", [128, 2, DM], BF16)
    wstr = T("wstr", [128, 2, 4, 1024], BF16)
    xbuf = T("xbuf", [128, 2, DM], F32)
    u_bf = T("u_bf", [128, DM], BF16)
    uT = T("uT", [128, NI, 128], BF16)
    tmin = T("tmin", [128, 1024], F32)
    ysb = T("ysb", [128, 2048], F32)
    qk_sb = T("qk_sb", [128, 256], F32)
    prod = T("prod", [128, 256], F32)
    sc8 = T("sc8", [128, 8], F32)
    m8 = T("m8", [128, 1], F32)
    mneg8 = T("mneg8", [128, 1], F32)
    e8 = T("e8", [128, 8], F32)
    s8 = T("s8", [128, 1], F32)
    rs8 = T("rs8", [128, 1], F32)
    w8 = T("w8", [128, 8], F32)
    wexp = T("wexp", [128, 128], F32)
    wh_bf = T("wh_bf", [128, 128], BF16)
    whT = T("whT", [128, 128], BF16)
    red32 = T("red32", [128, 32], F32)
    ysum4 = T("ysum4", [128, 4], F32)
    yssq4 = T("yssq4", [128, 4], F32)
    ln_x = alloc_ln(T, "x" + uniq, NT)
    ln_y = alloc_ln(T, "y" + uniq, NT)

    ps_y = es.enter_context(nc.psum_tensor(uniq + "f_ps_y", [128, 2, 1024], F32))
    ps_qk = es.enter_context(nc.psum_tensor(uniq + "f_ps_qk", [128, 512], F32))
    ps_h = es.enter_context(nc.psum_tensor(uniq + "f_ps_h", [128, 128], F32))

    ctx = {"u_bf": u_bf, "uT": uT, "red32": red32, "tmin": tmin,
           "ysum4": ysum4, "yssq4": yssq4, "ps_y": ps_y, "ysb": ysb}

    A = P.add
    wload = []
    wload.append(A("gpsimd", lambda g: g.dma_start(
        out=wq_bf[:], in_=wqT.rearrange("(n p) c -> p n c", p=128)), inc=("s_w", 16), tag="wq"))
    wload.append(A("gpsimd", lambda g: g.dma_start(
        out=wk_bf[:], in_=wkT.rearrange("(n p) c -> p n c", p=128)), inc=("s_w", 16), tag="wk"))
    wload.append(A("gpsimd", lambda g: g.dma_start(
        out=a_bf[:], in_=aT.rearrange("(n p) c -> p n c", p=128)), inc=("s_w", 16), tag="at"))
    wload.append(A("gpsimd", lambda g: g.dma_start(
        out=bm_bf[:], in_=bmT[:]), inc=("s_w", 16), tag="bm"))
    wload.append(A("gpsimd", lambda g: g.dma_start(
        out=b_bf[:], in_=bia[:]), inc=("s_w", 16), tag="b"))
    pones = A("vector", lambda v: v.memset(ones1[:], 1.0), inc=("s_v", 1), tag="ones")

    mst_done = []
    prev_st = [None, None]
    for i in range(NI):
        ld = A("gpsimd", lambda g, i=i: g.dma_start(
            out=wstage[:, i % 2, :], in_=wT[i*128:(i+1)*128, :]),
            deps=[prev_st[i % 2]], inc=(f"s_ml{i % 2}", 16), tag=f"mstl{i}")
        st = A("sync", lambda s, i=i: s.dma_start(
            out=wmaster[i*128:(i+1)*128, :], in_=wstage[:, i % 2, :]),
            deps=[ld], inc=(f"s_ms{i % 2}", 16), tag=f"msts{i}")
        prev_st[i % 2] = st
        mst_done.append(st)

    prev = {}
    for t in range(NT):
        sl = t % 2
        xap = xbuf[:, sl, :]
        deps_ldx = [mst_done[-1], mst_done[-2]] if t < 2 else [prev[(t - 2, 'sto')]]
        ldx = A("sync", lambda s, t=t, xap=xap: s.dma_start(
            out=xap, in_=x_in[t*128:(t+1)*128, :]), deps=deps_ldx,
            inc=(f"s_x{t % 2}", 16), tag=f"ldx{t}")

        uop, xsq, _ = emit_ln_input(P, A, ctx, t, xap, ln_x, [ldx],
                                    prev.get((t - 1, 'epi')), "f")
        utr = emit_utr(P, A, ctx, uop, prev.get((t - 1, 'hmm')), f"f{t}")

        def qk_fn(te, t=t):
            r = None
            for i in range(NI):
                te.matmul(ps_qk[:, 0:256], uT[:, i, :], wq_bf[:, i, :],
                          start=(i == 0), stop=(i == NI - 1))
            for i in range(NI):
                r = te.matmul(ps_qk[:, 256:512], uT[:, i, :], wk_bf[:, i, :],
                              start=(i == 0), stop=(i == NI - 1))
            return r
        qkmm = A("tensor", qk_fn, deps=[utr] + wload + [prev.get((t - 1, 'prod'))],
                 inc=("s_mm", 1), tag=f"qkmm{t}")

        def h_fn(te, t=t):
            r = None
            for i in range(NI):
                r = te.matmul(ps_h[:], uT[:, i, :], a_bf[:, i, :],
                              start=(i == 0), stop=(i == NI - 1))
            return r
        hmm_ = A("tensor", h_fn, deps=[utr] + wload + [prev.get((t - 1, 'whmul'))],
                 inc=("s_mm", 1), tag=f"hmm{t}")
        prev[(t, 'hmm')] = hmm_

        qev = A("scalar", lambda sc: sc.copy(out=qk_sb[:], in_=ps_qk[:, 0:256]),
                deps=[qkmm], inc=("s_sc", 1), tag=f"qev{t}")
        prodo = A("vector", lambda v: v.tensor_tensor(
            out=prod[:], in0=ps_qk[:, 256:512], in1=qk_sb[:], op=ALU.mult),
            deps=[qev], inc=("s_v", 1), tag=f"prod{t}")
        prev[(t, 'prod')] = prodo

        def sc8_fn(v):
            P.mid(v.tensor_reduce(out=sc8[:], in_=prod[:].rearrange("p (a b) -> p a b", a=8),
                                  axis=AX.X, op=ALU.add))
            return v.tensor_reduce(out=m8[:], in_=sc8[:], axis=AX.X, op=ALU.max)
        sc8o = A("vector", sc8_fn, deps=[prodo], inc=("s_v", 1), tag=f"sc8{t}")

        mn8 = A("gpsimd", lambda g: g.tensor_scalar(
            out=mneg8[:], in0=m8[:], scalar1=-RS32, scalar2=None, op0=ALU.mult),
            deps=[sc8o], inc=("s_g", 1), tag=f"mn8{t}")
        e8o = A("scalar", lambda sc: sc.activation(
            out=e8[:], in_=sc8[:], func=AF.Exp, bias=mneg8[:], scale=RS32),
            deps=[mn8], inc=("s_sc", 1), tag=f"e8{t}")
        s8o = A("vector", lambda v: v.tensor_reduce(
            out=s8[:], in_=e8[:], axis=AX.X, op=ALU.add),
            deps=[e8o], inc=("s_v", 1), tag=f"s8{t}")

        def rs_fn(sc):
            P.mid(sc.activation(out=rs8[:], in_=s8[:], func=AF.Ln, bias=0.0, scale=1.0))
            return sc.activation(out=rs8[:], in_=rs8[:], func=AF.Exp, bias=0.0, scale=-1.0)
        rso = A("scalar", rs_fn, deps=[s8o], inc=("s_sc", 1), tag=f"rs{t}")

        w8o = A("vector", lambda v: v.tensor_scalar(
            out=w8[:], in0=e8[:], scalar1=rs8[:], scalar2=None, op0=ALU.mult),
            deps=[rso], inc=("s_v", 1), tag=f"w8{t}")
        wexpo = A("vector", lambda v: v.tensor_copy(
            out=wexp[:].rearrange("p (a b) -> p a b", a=8),
            in_=w8[:].rearrange("p (a b) -> p a b", b=1).broadcast_to((128, 8, 16))),
            deps=[w8o], inc=("s_v", 1), tag=f"wexp{t}")
        whmul = A("vector", lambda v: v.tensor_tensor(
            out=wh_bf[:], in0=ps_h[:], in1=wexp[:], op=ALU.mult),
            deps=[wexpo, hmm_], inc=("s_v", 1), tag=f"whmul{t}")
        prev[(t, 'whmul')] = whmul
        whtr = A("sync", lambda s: s.dma_start(out=whT[:], in_=wh_bf[:], transpose=True),
                 deps=[whmul, prev.get((t - 1, 'q3mm'))], inc=("s_t", 16), tag=f"whtr{t}")

        def make_quarter(q, extra_deps):
            bdeps = [pones] + wload + list(extra_deps)

            def bias_fn(te, q=q):
                r = None
                for c in range(2):
                    r = te.matmul(ps_y[:, q % 2, c*512:(c+1)*512], ones1[:],
                                  b_bf[:, q*1024 + c*512:q*1024 + (c+1)*512],
                                  start=True, stop=False)
                return r
            bmmop = A("tensor", bias_fn, deps=bdeps, inc=("s_mm", 1), tag=f"bias{q}_{t}")

            last_g = bmmop
            for j in range(8):
                gi = q * 8 + j
                wdeps = [prev.get(('g', t * 32 + gi - 2))]
                if t == 0:
                    wdeps += [mst_done[-1], mst_done[-2]]
                wld = A("sync", lambda s, q=q, j=j, sl2=(q*8+j) % 2: s.dma_start(
                    out=wstr[:, sl2, :, :],
                    in_=wmaster.rearrange("(n p) c -> p n c", p=128)[:, 4*j:4*j+4, q*1024:(q+1)*1024]),
                    deps=wdeps, inc=(f"s_wl{(q*8+j) % 2}", 16), tag=f"wld{q}_{j}_{t}")

                def g_fn(te, q=q, j=j, sl2=(q*8+j) % 2):
                    r = None
                    for ii in range(4):
                        for c in range(2):
                            r = te.matmul(ps_y[:, q % 2, c*512:(c+1)*512],
                                          uT[:, 4*j + ii, :],
                                          wstr[:, sl2, ii, c*512:(c+1)*512],
                                          start=False, stop=False)
                    return r
                gop = A("tensor", g_fn, deps=[wld, last_g, utr],
                        inc=("s_mm", 1), tag=f"qmmg{q}_{j}_{t}")
                prev[('g', t * 32 + gi)] = gop
                last_g = gop

            def comb_fn(te, q=q):
                r = None
                for c in range(2):
                    r = te.matmul(ps_y[:, q % 2, c*512:(c+1)*512], whT[:],
                                  bm_bf[:, q*1024 + c*512:q*1024 + (c+1)*512],
                                  start=False, stop=True)
                return r
            qmm = A("tensor", comb_fn, deps=[last_g, whtr], inc=("s_mm", 1), tag=f"comb{q}_{t}")
            if q == 3:
                prev[(t, 'q3mm')] = qmm
            return qmm

        sto, q_readers, _ = emit_quarter_stats_epilogue(
            P, A, ctx, t, xap, ln_y, make_quarter, prev.get((t - 1, 'qr')), True,
            x_out[t*128:(t+1)*128, :], f"f{t}_")
        prev[(t, 'qr')] = q_readers
        prev[(t, 'sto')] = sto
        prev[(t, 'epi')] = sto

    return [prev[(t, 'sto')] for t in range(NT)]



def host_prep_l1a(inp):
    L = 4096
    k = np.arange(L)
    Om = np.exp(-2j*np.pi*k/L)
    T0 = np.float32(np.imag(2*(1-Om)/(1+Om))).reshape(128, 32)
    cc = 2.0/(1+Om)
    cr = np.float32(cc.real).reshape(128, 32)
    ci = np.float32(cc.imag).reshape(128, 32)
    n0 = np.arange(64)
    F64r = np.float32(np.cos(2*np.pi*np.outer(n0, n0)/64))
    F64i = np.float32(np.sin(2*np.pi*np.outer(n0, n0)/64))
    Twr = np.float32(np.cos(2*np.pi*np.outer(n0, n0)/L))
    Twi = np.float32(np.sin(2*np.pi*np.outer(n0, n0)/L))
    I64 = np.eye(64, dtype=np.float32)
    I128 = np.eye(128, dtype=np.float32)
    return {
        "a_T0": np.ascontiguousarray(T0), "a_cr": np.ascontiguousarray(cr),
        "a_ci": np.ascontiguousarray(ci),
        "a_f64r": F64r, "a_f64in": np.ascontiguousarray(-F64i), "a_f64i": F64i,
        "a_twr": Twr, "a_twi": Twi, "a_i64": I64, "a_i128": I128,
        "a_lam": np.ascontiguousarray(np.asarray(inp['L1_attn_Lambda_ri'])),
        "a_p": np.ascontiguousarray(np.asarray(inp['L1_attn_p_ri'])),
        "a_q": np.ascontiguousarray(np.asarray(inp['L1_attn_q_ri'])),
        "a_vc": np.ascontiguousarray(np.asarray(inp['L1_attn_Vc_ri']).reshape(64, 128)),
        "a_ct": np.ascontiguousarray(np.asarray(inp['L1_attn_Ct_ri']).reshape(64, 2)),
        "a_bp": np.ascontiguousarray(np.asarray(inp['L1_attn_Bp'])),
        "a_d": np.ascontiguousarray(np.asarray(inp['L1_attn_D']).reshape(1, 1)),
        "a_ls": np.ascontiguousarray(np.asarray(inp['L1_attn_log_step']).reshape(1, 1)),
    }


def build_l1a(nc, x_in, x_out, es, P, uniq=""):
    D = lambda n, s: nc.dram_tensor(uniq + n, s, F32, kind="ExternalInput").ap()
    t0_d = D("a_T0", [128, 32]); cr_d = D("a_cr", [128, 32]); ci_d = D("a_ci", [128, 32])
    f64r_d = D("a_f64r", [64, 64]); f64in_d = D("a_f64in", [64, 64]); f64i_d = D("a_f64i", [64, 64])
    twr_d = D("a_twr", [64, 64]); twi_d = D("a_twi", [64, 64])
    i64_d = D("a_i64", [64, 64]); i128_d = D("a_i128", [128, 128])
    lam_d = D("a_lam", [64, 2]); p_d = D("a_p", [64, 2]); q_d = D("a_q", [64, 2])
    vc_d = D("a_vc", [64, 128]); ct_d = D("a_ct", [64, 2]); bp_d = D("a_bp", [64, 1])
    dd_d = D("a_d", [1, 1]); ls_d = D("a_ls", [1, 1])
    scr_d = nc.dram_tensor(uniq + "a_scr", [16384], F32, kind="Internal").ap()

    T = lambda n, s, d: es.enter_context(nc.sbuf_tensor(uniq + "as_" + n, s, d))
    sm = T("sm", [64, 224], F32)
    vpk = T("vpk", [64, 16], F32)
    bcz = T("bcz", [128, 16, 64], F32)
    Gt = T("Gt", [128, 32], F32)
    crs = T("crs", [128, 32], F32); cis = T("cis", [128, 32], F32)
    di3 = T("di3", [128, 2048], F32)
    d23 = T("d23", [128, 2048], F32)
    recr = T("recr", [128, 2048], F32)
    reci = T("reci", [128, 2048], F32)
    kk = T("kk", [128, 8, 32], F32)
    cw = T("cw", [128, 8, 32], F32)
    f64r = T("f64r", [64, 64], F32); f64in = T("f64in", [64, 64], F32); f64i = T("f64i", [64, 64], F32)
    twr = T("twr", [64, 64], F32); twi = T("twi", [64, 64], F32)
    i64 = T("i64", [64, 64], F32); i128 = T("i128", [128, 128], F32)
    amr = T("amr", [64, 64], F32); ami = T("ami", [64, 64], F32)
    a2r = T("a2r", [64, 64], F32); a2i = T("a2i", [64, 64], F32)
    a2rt = T("a2rt", [64, 64], F32); a2it = T("a2it", [64, 64], F32)
    tw1 = T("tw1", [64, 64], F32); tw2 = T("tw2", [64, 64], F32)
    xsb = T("xsb", [64, 64], F32)
    ksb = T("ksb", [1, 4096], F32)
    kbf = T("kbf", [1, 4352], BF16)
    ball = T("ball", [128, 35 * 128], BF16)
    dscal = T("dscal", [128, 1], F32)
    dg = T("dg", [128, 128], F32)
    rstep = T("rstep", [1, 1], F32)
    rstep_b = T("rstep_b", [128, 1], F32)
    onesf = T("onesf", [1, 128], F32)
    xbuf = T("xbuf", [128, 2, DM], F32)
    u_bf = T("u_bf", [128, DM], BF16)
    uT = T("uT", [128, NI, 128], BF16)
    tmin = T("tmin", [128, 1024], F32)
    ysb = T("ysb", [128, 2048], F32)
    red32 = T("red32", [128, 32], F32)
    ysum4 = T("ysum4", [128, 4], F32)
    yssq4 = T("yssq4", [128, 4], F32)
    ln_x = alloc_ln(T, "x" + uniq, NT)
    ln_y = alloc_ln(T, "y" + uniq, NT)

    ps_y = es.enter_context(nc.psum_tensor(uniq + "a_ps_y", [128, 2, 1024], F32))
    ps_k = es.enter_context(nc.psum_tensor(uniq + "a_ps_k", [64, 256], F32))
    ps_b = es.enter_context(nc.psum_tensor(uniq + "a_ps_b", [128, 8], F32))

    ctx = {"u_bf": u_bf, "uT": uT, "red32": red32, "tmin": tmin,
           "ysum4": ysum4, "yssq4": yssq4, "ps_y": ps_y, "ysb": ysb}

    A = P.add
    lds = []
    for ap_d, ap_s, tg in [(t0_d, Gt, "t0"), (cr_d, crs, "cr"), (ci_d, cis, "ci"),
                           (f64r_d, f64r, "f64r"), (f64in_d, f64in, "f64in"), (f64i_d, f64i, "f64i"),
                           (twr_d, twr, "twr"), (twi_d, twi, "twi"), (i64_d, i64, "i64"),
                           (i128_d, i128, "i128"), (dd_d, rstep, "dd0")]:
        if tg == "dd0":
            continue
        lds.append(A("sync", lambda s, a=ap_d, b=ap_s: s.dma_start(out=b[:], in_=a[:]),
                     inc=("s_pre", 16), tag="ld" + tg))
    small_map = [(lam_d, 0, 2), (p_d, 2, 2), (q_d, 4, 2), (ct_d, 6, 2), (vc_d, 8, 128), (bp_d, 136, 1)]
    for ap_d, off, w in small_map:
        lds.append(A("sync", lambda s, a=ap_d, o=off, w=w: s.dma_start(
            out=sm[:, o:o + w], in_=a[:]), inc=("s_pre", 16), tag=f"lds{off}"))
    ldls = A("sync", lambda s: s.dma_start(out=rstep[:], in_=ls_d[:]), inc=("s_pre", 16), tag="ldls")
    lddd = A("sync", lambda s: s.dma_start(out=dscal[0:1, :], in_=dd_d[:]), inc=("s_pre", 16), tag="lddd")
    lds += [ldls, lddd]

    rse = A("scalar", lambda sc: sc.activation(out=rstep[:], in_=rstep[:], func=AF.Exp,
                                               bias=0.0, scale=-1.0),
            deps=lds, inc=("s_sc", 1), tag="rse")
    pones = A("vector", lambda v: v.memset(onesf[:], 1.0), inc=("s_v", 1), tag="onesf")

    def bc_fn(te):
        te.matmul(ps_b[:, 0:1], onesf[:], rstep[:], start=True, stop=True)
        return te.matmul(ps_b[:, 1:2], onesf[:], dscal[0:1, :], start=True, stop=True)
    bcmm = A("tensor", bc_fn, deps=[rse, pones] + lds, inc=("s_mm", 1), tag="bcmm")
    rsb = A("vector", lambda v: v.tensor_copy(out=rstep_b[:], in_=ps_b[:, 0:1]),
            deps=[bcmm], inc=("s_v", 1), tag="rsbc")
    dbc = A("vector", lambda v: v.tensor_copy(out=dscal[:], in_=ps_b[:, 1:2]),
            deps=[bcmm], inc=("s_v", 1), tag="dbcc")

    wbp = A("sync", lambda s: s.dma_start(out=scr_d[0:64], in_=sm[:, 136:137]),
            deps=lds, inc=("s_pre", 16), tag="wbp")
    rbp = A("sync", lambda s: s.dma_start(
        out=sm[:, 140:204].rearrange("p m -> p m") if False else sm[:, 140:204],
        in_=scr_d[0:64].rearrange("(o m) -> o m", o=1).broadcast_to((64, 64))),
        deps=[wbp], inc=("s_pre", 16), tag="rbp")

    def bv_fn(v):
        vcr = sm[:, 8:136].rearrange("p (m two) -> p m two", two=2)[:, :, 0:1].rearrange("p m one -> p (m one)")
        vci = sm[:, 8:136].rearrange("p (m two) -> p m two", two=2)[:, :, 1:2].rearrange("p m one -> p (m one)")
        bpb = sm[:, 140:204]
        P.mid(v.tensor_tensor(out=vpk[:, 8:9].broadcast_to((64, 1)) if False else cw[0:64, 0, 0:64] if False else sm[:, 76+128:140+64] if False else di3[0:64, 0:64], in0=vcr, in1=bpb, op=ALU.mult))
        P.mid(v.tensor_reduce(out=vpk[:, 8:9], in_=di3[0:64, 0:64], axis=AX.X, op=ALU.add))
        P.mid(v.tensor_tensor(out=di3[0:64, 0:64], in0=vci, in1=bpb, op=ALU.mult))
        return v.tensor_reduce(out=vpk[:, 9:10], in_=di3[0:64, 0:64], axis=AX.X, op=ALU.add)
    bv = A("vector", bv_fn, deps=lds + [rbp], inc=("s_v", 1), tag="bv")

    def vprod_fn(v):
        ctr = sm[:, 6:7]; cti = sm[:, 7:8]
        pr = sm[:, 2:3]; pi = sm[:, 3:4]
        qr = sm[:, 4:5]; qi = sm[:, 5:6]
        br = vpk[:, 8:9]; bi = vpk[:, 9:10]
        P.mid(v.tensor_tensor(out=vpk[:, 0:1], in0=ctr, in1=br, op=ALU.mult))
        P.mid(v.tensor_tensor(out=vpk[:, 10:11], in0=cti, in1=bi, op=ALU.mult))
        P.mid(v.tensor_tensor(out=vpk[:, 0:1], in0=vpk[:, 0:1], in1=vpk[:, 10:11], op=ALU.add))
        P.mid(v.tensor_tensor(out=vpk[:, 1:2], in0=ctr, in1=bi, op=ALU.mult))
        P.mid(v.tensor_tensor(out=vpk[:, 10:11], in0=cti, in1=br, op=ALU.mult))
        P.mid(v.tensor_tensor(out=vpk[:, 1:2], in0=vpk[:, 1:2], in1=vpk[:, 10:11], op=ALU.subtract))
        P.mid(v.tensor_tensor(out=vpk[:, 2:3], in0=ctr, in1=pr, op=ALU.mult))
        P.mid(v.tensor_tensor(out=vpk[:, 10:11], in0=cti, in1=pi, op=ALU.mult))
        P.mid(v.tensor_tensor(out=vpk[:, 2:3], in0=vpk[:, 2:3], in1=vpk[:, 10:11], op=ALU.add))
        P.mid(v.tensor_tensor(out=vpk[:, 3:4], in0=ctr, in1=pi, op=ALU.mult))
        P.mid(v.tensor_tensor(out=vpk[:, 10:11], in0=cti, in1=pr, op=ALU.mult))
        P.mid(v.tensor_tensor(out=vpk[:, 3:4], in0=vpk[:, 3:4], in1=vpk[:, 10:11], op=ALU.subtract))
        P.mid(v.tensor_tensor(out=vpk[:, 4:5], in0=qr, in1=br, op=ALU.mult))
        P.mid(v.tensor_tensor(out=vpk[:, 10:11], in0=qi, in1=bi, op=ALU.mult))
        P.mid(v.tensor_tensor(out=vpk[:, 4:5], in0=vpk[:, 4:5], in1=vpk[:, 10:11], op=ALU.add))
        P.mid(v.tensor_tensor(out=vpk[:, 5:6], in0=qr, in1=bi, op=ALU.mult))
        P.mid(v.tensor_tensor(out=vpk[:, 10:11], in0=qi, in1=br, op=ALU.mult))
        P.mid(v.tensor_tensor(out=vpk[:, 5:6], in0=vpk[:, 5:6], in1=vpk[:, 10:11], op=ALU.subtract))
        P.mid(v.tensor_tensor(out=vpk[:, 6:7], in0=qr, in1=pr, op=ALU.mult))
        P.mid(v.tensor_tensor(out=vpk[:, 10:11], in0=qi, in1=pi, op=ALU.mult))
        P.mid(v.tensor_tensor(out=vpk[:, 6:7], in0=vpk[:, 6:7], in1=vpk[:, 10:11], op=ALU.add))
        P.mid(v.tensor_tensor(out=vpk[:, 7:8], in0=qr, in1=pi, op=ALU.mult))
        P.mid(v.tensor_tensor(out=vpk[:, 10:11], in0=qi, in1=pr, op=ALU.mult))
        P.mid(v.tensor_tensor(out=vpk[:, 7:8], in0=vpk[:, 7:8], in1=vpk[:, 10:11], op=ALU.subtract))
        P.mid(v.tensor_copy(out=vpk[:, 8:9], in_=sm[:, 0:1]))
        P.mid(v.tensor_copy(out=vpk[:, 9:10], in_=sm[:, 1:2]))
        return v.tensor_tensor(out=vpk[:, 10:11], in0=sm[:, 0:1], in1=sm[:, 0:1], op=ALU.mult)
    vprod = A("vector", vprod_fn, deps=[bv], inc=("s_v", 1), tag="vprod")

    wpk = A("sync", lambda s: s.dma_start(
        out=scr_d[256:256 + 11 * 64].rearrange("(x n) -> x n", x=11).transpose([1, 0]),
        in_=vpk[:, 0:11], allow_slow_non_contiguous=True),
        deps=[vprod], inc=("s_pre", 16), tag="wpk")
    rpk = A("sync", lambda s: s.dma_start(
        out=bcz[:, 0:11, :],
        in_=scr_d[256:256 + 11 * 64].rearrange("(x n) -> x n", x=11).unsqueeze(0).broadcast_to((128, 11, 64))),
        deps=[wpk], inc=("s_pre", 16), tag="rpk")

    def gt_fn(v):
        return v.tensor_scalar(out=Gt[:], in0=Gt[:], scalar1=rstep_b[:], scalar2=None, op0=ALU.mult)
    gt = A("vector", gt_fn, deps=[rsb] + lds, inc=("s_v", 1), tag="gt")

    def cauchy_fn(v):
        lam_i = bcz[:, 9:10, :].broadcast_to((128, 32, 64))
        lam_r = bcz[:, 8:9, :].broadcast_to((128, 32, 64))
        lr2 = bcz[:, 10:11, :].broadcast_to((128, 32, 64))
        d3 = di3[:].rearrange("p (j n) -> p j n", j=32)
        d2 = d23[:].rearrange("p (j n) -> p j n", j=32)
        rr = recr[:].rearrange("p (j n) -> p j n", j=32)
        ri = reci[:].rearrange("p (j n) -> p j n", j=32)
        g3 = Gt[:].rearrange("p (j o) -> p j o", o=1).broadcast_to((128, 32, 64))
        P.mid(v.tensor_tensor(out=d3, in0=g3, in1=lam_i, op=ALU.subtract))
        P.mid(v.tensor_tensor(out=d2, in0=d3, in1=d3, op=ALU.mult))
        P.mid(v.tensor_tensor(out=d2, in0=d2, in1=lr2, op=ALU.add))
        P.mid(v.reciprocal(out=d23[:], in_=d23[:]))
        P.mid(v.tensor_tensor(out=rr, in0=lam_r, in1=d2, op=ALU.mult))
        P.mid(v.tensor_scalar(out=recr[:], in0=recr[:], scalar1=-1.0, scalar2=None, op0=ALU.mult))
        P.mid(v.tensor_tensor(out=ri, in0=d3, in1=d2, op=ALU.mult))
        return v.tensor_scalar(out=reci[:], in0=reci[:], scalar1=-1.0, scalar2=None, op0=ALU.mult)
    cau = A("vector", cauchy_fn, deps=[gt, rpk] + lds, inc=("s_v", 1), tag="cauchy")

    def ksums_fn(v):
        rr = recr[:].rearrange("p (j n) -> p j n", j=32)
        ri = reci[:].rearrange("p (j n) -> p j n", j=32)
        for vi_ in range(4):
            vr = bcz[:, 2*vi_:2*vi_+1, :].broadcast_to((128, 32, 64))
            vv = bcz[:, 2*vi_+1:2*vi_+2, :].broadcast_to((128, 32, 64))
            d3 = di3[:].rearrange("p (j n) -> p j n", j=32)
            P.mid(v.tensor_tensor(out=d3, in0=vr, in1=rr, op=ALU.mult))
            P.mid(v.tensor_reduce(out=kk[:, 2*vi_, :], in_=d3, axis=AX.X, op=ALU.add))
            P.mid(v.tensor_tensor(out=d3, in0=vv, in1=ri, op=ALU.mult))
            P.mid(v.tensor_reduce(out=cw[:, 0, :], in_=d3, axis=AX.X, op=ALU.add))
            P.mid(v.tensor_tensor(out=kk[:, 2*vi_, :], in0=kk[:, 2*vi_, :], in1=cw[:, 0, :], op=ALU.subtract))
            P.mid(v.tensor_tensor(out=d3, in0=vr, in1=ri, op=ALU.mult))
            P.mid(v.tensor_reduce(out=kk[:, 2*vi_+1, :], in_=d3, axis=AX.X, op=ALU.add))
            P.mid(v.tensor_tensor(out=d3, in0=vv, in1=rr, op=ALU.mult))
            P.mid(v.tensor_reduce(out=cw[:, 0, :], in_=d3, axis=AX.X, op=ALU.add))
            P.mid(v.tensor_tensor(out=kk[:, 2*vi_+1, :], in0=kk[:, 2*vi_+1, :], in1=cw[:, 0, :], op=ALU.add))
        return v.nop()
    ksums = A("vector", ksums_fn, deps=[cau], inc=("s_v", 1), tag="ksums")

    def aroots_fn(v):
        k00r, k00i = kk[:, 0, :], kk[:, 1, :]
        k01r, k01i = kk[:, 2, :], kk[:, 3, :]
        k10r, k10i = kk[:, 4, :], kk[:, 5, :]
        k11r, k11i = kk[:, 6, :], kk[:, 7, :]
        wr, wi = cw[:, 0, :], cw[:, 1, :]
        w2, iwr = cw[:, 2, :], cw[:, 3, :]
        iwi, ur = cw[:, 4, :], cw[:, 5, :]
        ui, t = cw[:, 6, :], cw[:, 7, :]
        P.mid(v.tensor_scalar(out=wr, in0=k11r, scalar1=1.0, scalar2=None, op0=ALU.add))
        P.mid(v.tensor_copy(out=wi, in_=k11i))
        P.mid(v.tensor_tensor(out=w2, in0=wr, in1=wr, op=ALU.mult))
        P.mid(v.tensor_tensor(out=t, in0=wi, in1=wi, op=ALU.mult))
        P.mid(v.tensor_tensor(out=w2, in0=w2, in1=t, op=ALU.add))
        P.mid(v.reciprocal(out=w2, in_=w2))
        P.mid(v.tensor_tensor(out=iwr, in0=wr, in1=w2, op=ALU.mult))
        P.mid(v.tensor_tensor(out=iwi, in0=wi, in1=w2, op=ALU.mult))
        P.mid(v.tensor_scalar(out=iwi, in0=iwi, scalar1=-1.0, scalar2=None, op0=ALU.mult))
        P.mid(v.tensor_tensor(out=ur, in0=k01r, in1=k10r, op=ALU.mult))
        P.mid(v.tensor_tensor(out=t, in0=k01i, in1=k10i, op=ALU.mult))
        P.mid(v.tensor_tensor(out=ur, in0=ur, in1=t, op=ALU.subtract))
        P.mid(v.tensor_tensor(out=ui, in0=k01r, in1=k10i, op=ALU.mult))
        P.mid(v.tensor_tensor(out=t, in0=k01i, in1=k10r, op=ALU.mult))
        P.mid(v.tensor_tensor(out=ui, in0=ui, in1=t, op=ALU.add))
        P.mid(v.tensor_tensor(out=t, in0=ur, in1=iwr, op=ALU.mult))
        P.mid(v.tensor_tensor(out=wr, in0=ui, in1=iwi, op=ALU.mult))
        P.mid(v.tensor_tensor(out=wr, in0=t, in1=wr, op=ALU.subtract))
        P.mid(v.tensor_tensor(out=t, in0=ur, in1=iwi, op=ALU.mult))
        P.mid(v.tensor_tensor(out=wi, in0=ui, in1=iwr, op=ALU.mult))
        P.mid(v.tensor_tensor(out=wi, in0=t, in1=wi, op=ALU.add))
        P.mid(v.tensor_tensor(out=ur, in0=k00r, in1=wr, op=ALU.subtract))
        P.mid(v.tensor_tensor(out=ui, in0=k00i, in1=wi, op=ALU.subtract))
        P.mid(v.tensor_tensor(out=kk[:, 0, :], in0=ur, in1=crs[:], op=ALU.mult))
        P.mid(v.tensor_tensor(out=t, in0=ui, in1=cis[:], op=ALU.mult))
        P.mid(v.tensor_tensor(out=kk[:, 0, :], in0=kk[:, 0, :], in1=t, op=ALU.subtract))
        P.mid(v.tensor_tensor(out=kk[:, 1, :], in0=ur, in1=cis[:], op=ALU.mult))
        P.mid(v.tensor_tensor(out=t, in0=ui, in1=crs[:], op=ALU.mult))
        return v.tensor_tensor(out=kk[:, 1, :], in0=kk[:, 1, :], in1=t, op=ALU.add)
    aro = A("vector", aroots_fn, deps=[ksums], inc=("s_v", 1), tag="aroots")

    war = A("sync", lambda s: s.dma_start(out=scr_d[1024:1024+4096].rearrange("(p j) -> p j", p=128),
                                          in_=kk[:, 0, :]), deps=[aro], inc=("s_pre", 16), tag="war")
    wai = A("sync", lambda s: s.dma_start(out=scr_d[5120:5120+4096].rearrange("(p j) -> p j", p=128),
                                          in_=kk[:, 1, :]), deps=[aro], inc=("s_pre", 16), tag="wai")
    rar = A("sync", lambda s: s.dma_start(out=amr[:], in_=scr_d[1024:1024+4096].rearrange("(a b) -> a b", a=64)),
            deps=[war, wai], inc=("s_pre", 16), tag="rar")
    rai = A("sync", lambda s: s.dma_start(out=ami[:], in_=scr_d[5120:5120+4096].rearrange("(a b) -> a b", a=64)),
            deps=[war, wai], inc=("s_pre", 16), tag="rai")

    def st1_fn(te):
        te.matmul(ps_k[:, 0:64], f64r[:], amr[:], start=True, stop=False)
        te.matmul(ps_k[:, 0:64], f64in[:], ami[:], start=False, stop=True)
        te.matmul(ps_k[:, 64:128], f64r[:], ami[:], start=True, stop=False)
        return te.matmul(ps_k[:, 64:128], f64i[:], amr[:], start=False, stop=True)
    st1 = A("tensor", st1_fn, deps=[rar, rai] + lds, inc=("s_mm", 1), tag="st1")

    def tw_fn(v):
        P.mid(v.tensor_tensor(out=tw1[:], in0=ps_k[:, 0:64], in1=twr[:], op=ALU.mult))
        P.mid(v.tensor_tensor(out=tw2[:], in0=ps_k[:, 64:128], in1=twi[:], op=ALU.mult))
        P.mid(v.tensor_tensor(out=a2r[:], in0=tw1[:], in1=tw2[:], op=ALU.subtract))
        P.mid(v.tensor_tensor(out=tw1[:], in0=ps_k[:, 0:64], in1=twi[:], op=ALU.mult))
        P.mid(v.tensor_tensor(out=tw2[:], in0=ps_k[:, 64:128], in1=twr[:], op=ALU.mult))
        return v.tensor_tensor(out=a2i[:], in0=tw1[:], in1=tw2[:], op=ALU.add)
    tw = A("vector", tw_fn, deps=[st1], inc=("s_v", 1), tag="tw")

    def tr_fn(te):
        te.transpose(ps_k[:, 128:192], a2r[:], i64[:])
        return te.transpose(ps_k[:, 192:256], a2i[:], i64[:])
    tr = A("tensor", tr_fn, deps=[tw], inc=("s_mm", 1), tag="tr")
    trev = A("vector", lambda v: [P.mid(v.tensor_copy(out=a2rt[:], in_=ps_k[:, 128:192])),
                                  v.tensor_copy(out=a2it[:], in_=ps_k[:, 192:256])][-1],
             deps=[tr], inc=("s_v", 1), tag="trev")

    def st3_fn(te):
        te.matmul(ps_k[:, 0:64], a2rt[:], f64r[:], start=True, stop=False)
        return te.matmul(ps_k[:, 0:64], a2it[:], f64in[:], start=False, stop=True)
    st3 = A("tensor", st3_fn, deps=[trev], inc=("s_mm", 1), tag="st3")
    xev = A("vector", lambda v: v.tensor_scalar(out=xsb[:], in0=ps_k[:, 0:64],
                                                scalar1=1.0/4096, scalar2=None, op0=ALU.mult),
            deps=[st3], inc=("s_v", 1), tag="xev")

    wk = A("sync", lambda s: s.dma_start(
        out=scr_d[1024:1024+4096].rearrange("(b a) -> a b", a=64), in_=xsb[:],
        allow_slow_non_contiguous=True), deps=[xev], inc=("s_pre", 16), tag="wk")
    rk = A("sync", lambda s: s.dma_start(out=ksb[:], in_=scr_d[1024:1024+4096].rearrange("(o n) -> o n", o=1)),
           deps=[wk], inc=("s_pre", 16), tag="rk")

    kz = A("vector", lambda v: v.memset(kbf[:], 0.0), inc=("s_v", 1), tag="kz")
    kcast = A("vector", lambda v: v.tensor_copy(out=kbf[0:1, 127:127+4096], in_=ksb[:]),
              deps=[rk, kz], inc=("s_v", 1), tag="kcast")

    bz = A("vector", lambda v: v.memset(ball[:, 0:384], 0.0), inc=("s_v", 1), tag="bz")
    def ballfn(s):
        for r in range(128):
            s.dma_start(out=ball[r:r+1, 384:4480],
                        in_=kbf[0:1, 127 - r: 4223 - r]).then_inc(P.sem_handles["s_pre"], 16)
        return "self"
    ballop = A("sync", ballfn, deps=[kcast], inc=("s_pre", 16 * 128), tag="ball", self_inc=True)

    dgo = A("vector", lambda v: v.tensor_scalar(out=dg[:], in0=i128[:], scalar1=dscal[:],
                                                scalar2=None, op0=ALU.mult),
            deps=[dbc] + lds, inc=("s_v", 1), tag="dg")
    dadd = A("vector", lambda v: v.tensor_tensor(out=ball[:, 384:512], in0=ball[:, 384:512],
                                                 in1=dg[:], op=ALU.add),
             deps=[dgo, ballop, bz], inc=("s_v", 1), tag="dadd")

    prevt = {}
    for t in range(NT):
        sl = t % 2
        xap = xbuf[:, sl, :]
        deps_ldx = [dadd] if t < 2 else [prevt[(t - 2, 'sto')]]
        ldx = A("sync", lambda s, t=t, xap=xap: s.dma_start(
            out=xap, in_=x_in[t*128:(t+1)*128, :]), deps=deps_ldx,
            inc=(f"s_x{t % 2}", 16), tag=f"ldx{t}")

        uop, xsq, _ = emit_ln_input(P, A, ctx, t, xap, ln_x, [ldx],
                                    prevt.get((t - 1, 'epi')), "a")
        utr = emit_utr(P, A, ctx, uop, prevt.get((t - 1, 'lastmm')), f"a{t}")

        def make_quarter(q, extra_deps, t=t, utr=utr):
            def q_fn(te, q=q):
                r = None
                for c2 in range(2):
                    Js = 8 * q + 4 * c2
                    imax = min(32, Js + 4)
                    for I in range(imax):
                        off = (Js - I + 3) * 128
                        r = te.matmul(ps_y[:, q % 2, c2*512:(c2+1)*512],
                                      uT[:, I, :], ball[:, off:off + 512],
                                      start=(I == 0), stop=(I == imax - 1))
                return r
            deps = [utr, dadd] + list(extra_deps)
            op = A("tensor", q_fn, deps=deps, inc=("s_mm", 1), tag=f"tq{q}_{t}")
            if q == 3:
                prevt[(t, 'lastmm')] = op
            return op

        sto, q_readers, _ = emit_quarter_stats_epilogue(
            P, A, ctx, t, xap, ln_y, make_quarter, prevt.get((t - 1, 'qr')), True,
            x_out[t*128:(t+1)*128, :], f"a{t}_")
        prevt[(t, 'qr')] = q_readers
        prevt[(t, 'sto')] = sto
        prevt[(t, 'epi')] = sto

    return [prevt[(t, 'sto')] for t in range(NT)]



def host_prep_dequant(inp, core):
    qw = np.asarray(inp['L1_ffn_qweight'])
    lut8 = np.asarray(inp['L1_ffn_lut8'])
    lut4 = np.asarray(inp['L1_ffn_lut4'])
    base = core * 512
    o_idx = np.zeros((128, 64), np.int64)
    for c in range(8):
        for r in range(16):
            o_idx[16 * c + r, :] = base + np.arange(64) * 8 + c
    qq = qw[:, o_idx, :]
    qq = np.transpose(qq, (2, 1, 0, 3))
    qq = qq.reshape(16, 4, 128, 8, 128)
    qq = np.ascontiguousarray(np.transpose(qq, (0, 2, 1, 3, 4)))
    return {
        "dq_qw": qq.reshape(16, 128, 4 * 8 * 128),
        "dq_l8": np.ascontiguousarray(lut8[base:base + 512]),
        "dq_l4": np.ascontiguousarray(lut4[base:base + 512]),
    }


def build_dequant(nc, es, P, uniq="", wl_all_t=None, dd_all_t=None):
    qw_d = nc.dram_tensor(uniq + "dq_qw", [16, 128, 4096], I32, kind="ExternalInput").ap()
    l8_d = nc.dram_tensor(uniq + "dq_l8", [512, 256], F32, kind="ExternalInput").ap()
    l4_d = nc.dram_tensor(uniq + "dq_l4", [512, 16], F32, kind="ExternalInput").ap()
    fused = wl_all_t is not None
    okind = "Internal" if fused else "ExternalOutput"
    wl_t = nc.dram_tensor(uniq + "dq_wl", [DM, 512], BF16, kind=okind)
    dd_t = nc.dram_tensor(uniq + "dq_d", [DM, 512], BF16, kind=okind)
    wl_o = wl_t.ap()
    dd_o = dd_t.ap()
    st_d = nc.dram_tensor(uniq + "dq_stage", [1024, 256], BF16, kind="Internal").ap()

    T = lambda n, s, d: es.enter_context(nc.sbuf_tensor(uniq + "dqs_" + n, s, d))
    l8b = T("l8b", [128, 4, 256], BF16)
    l4b = T("l4b", [128, 4, 16], BF16)
    l4t = T("l4t", [128, 4, 256], BF16)
    db = T("db", [128, 4, 256], BF16)
    wqw = T("wqw", [128, 2, 4096], I32)
    shp = T("shp", [128, 1024], I32)
    pvec = T("pvec", [128, 1], I32)
    bits = T("bits", [128, 1024], I32)
    idx32 = T("idx32", [128, 1024], I32)
    idx16 = T("idx16", [128, 2, 1024], U16)
    outg = T("outg", [128, 2, 4, 1024], BF16)
    dall = T("dall", [128, 64, 256], BF16)
    wlT_sb = T("wlT_sb", [128, 32, 512], BF16)
    ddT_sb = T("ddT_sb", [128, 32, 512], BF16)
    rowv = T("rowv", [128, 1], I32)
    colv = T("colv", [128, 128], I32)
    rowf = T("rowf", [128, 1], F32)
    colf = T("colf", [128, 128], F32)
    idT = T("idT", [128, 128], BF16)
    ps_tr = es.enter_context(nc.psum_tensor(uniq + "dq_ps_tr", [128, 2, 32, 128], BF16))

    A = P.add
    ld8 = A("gpsimd", lambda g: g.dma_start(
        out=l8b[:], in_=l8_d.rearrange("(n p) c -> p n c", p=128)), inc=("s_w", 16), tag="ld8")
    ld4 = A("gpsimd", lambda g: g.dma_start(
        out=l4b[:], in_=l4_d.rearrange("(n p) c -> p n c", p=128)), inc=("s_w", 16), tag="ld4")

    l4to = A("vector", lambda v: v.tensor_copy(
        out=l4t[:].rearrange("p n (hi lo) -> p n hi lo", hi=16),
        in_=l4b[:].rearrange("p n (o lo) -> p n o lo", o=1).broadcast_to((128, 4, 16, 16))),
        deps=[ld4, ld8], inc=("s_v", 1), tag="l4t")
    dbo = A("vector", lambda v: v.tensor_tensor(out=db[:], in0=l8b[:], in1=l4t[:], op=ALU.subtract),
            deps=[l4to], inc=("s_v", 1), tag="db")
    stg1 = A("sync", lambda s: s.dma_start(
        out=st_d[0:512, :].rearrange("(n p) c -> p n c", p=128), in_=l4t[:]),
        deps=[l4to, dbo], inc=("s_ms0", 16), tag="stg1")
    stg2 = A("sync", lambda s: s.dma_start(
        out=st_d[512:1024, :].rearrange("(n p) c -> p n c", p=128), in_=db[:]),
        deps=[dbo], inc=("s_ms0", 16), tag="stg2")

    def dall_fn(g):
        for c in range(8):
            for m in range(2):
                g.dma_start(
                    out=dall[16 * c + 8 * m:16 * c + 8 * m + 8, :, :],
                    in_=bass_rust.AP(tensor=st_d.tensor, offset=(512 * m + c) * 256,
                                     ap=[[0, 8], [8 * 256, 64], [1, 256]])
                ).then_inc(P.sem_handles["s_w"], 16)
        return "self"
    dallop = A("gpsimd", dall_fn, deps=[stg1, stg2], inc=("s_w", 16 * 16),
               tag="dall", self_inc=True)

    rio = A("gpsimd", lambda g: g.iota(rowv[:], [[0, 1]], base=0, channel_multiplier=1),
            inc=("s_g", 1), tag="rio")
    cio = A("gpsimd", lambda g: g.iota(colv[:], [[1, 128]], base=0, channel_multiplier=0),
            deps=[rio], inc=("s_g", 1), tag="cio")
    def ido_fn(g):
        P.mid(g.tensor_copy(out=rowf[:], in_=rowv[:]))
        P.mid(g.tensor_copy(out=colf[:], in_=colv[:]))
        return g.tensor_scalar(
            out=idT[:], in0=colf[:], scalar1=rowf[:], scalar2=None, op0=ALU.is_equal)
    ido = A("gpsimd", ido_fn, deps=[cio], inc=("s_g", 1), tag="ido")

    pio = A("gpsimd", lambda g: g.iota(pvec[:], [[0, 1]], base=0, channel_multiplier=1),
            inc=("s_g", 1), tag="pio")
    sio = A("gpsimd", lambda g: g.iota(shp[:], [[0, 4], [0, 128], [16, 2]], base=0,
                                       channel_multiplier=0),
            deps=[pio], inc=("s_g", 1), tag="sio")
    def shp_fn(v):
        P.mid(v.tensor_scalar(out=pvec[:], in0=pvec[:], scalar1=15, scalar2=None, op0=ALU.bitwise_and))
        P.mid(v.tensor_copy(out=bits[:], in_=pvec[:].broadcast_to((128, 1024))))
        return v.tensor_tensor(out=shp[:], in0=shp[:], in1=bits[:], op=ALU.add)
    shpo = A("vector", shp_fn, deps=[sio], inc=("s_v", 1), tag="shp")

    prev = {}
    wl_copies = []
    dd_copies = []
    for pc in range(NPIECE):
        sl = pc % 2
        wdeps = [prev.get(('idx', pc - 2))]
        ldq = A("sync", lambda s, pc=pc, sl=sl: s.dma_start(
            out=wqw[:, sl, :], in_=qw_d[pc]), deps=wdeps,
            inc=(f"s_x{sl}", 16), tag=f"ldq{pc}")

        def unpack_fn(v, pc=pc, sl=sl):
            wv = wqw[:, sl, :].rearrange("p (g pl w) -> p g pl w", g=4, pl=8)
            for pl in range(8):
                wap = wv[:, :, pl, :].unsqueeze(3).broadcast_to((128, 4, 128, 2))
                P.mid(v.tensor_tensor(out=bits[:].rearrange("p (g q1 q0) -> p g q1 q0", g=4, q1=128),
                                      in0=wap,
                                      in1=shp[:].rearrange("p (g q1 q0) -> p g q1 q0", g=4, q1=128),
                                      op=ALU.logical_shift_right))
                if pl == 0:
                    P.mid(v.tensor_scalar(out=idx32[:], in0=bits[:], scalar1=1, scalar2=None,
                                          op0=ALU.bitwise_and))
                else:
                    P.mid(v.tensor_scalar(out=bits[:], in0=bits[:], scalar1=1, scalar2=pl,
                                          op0=ALU.bitwise_and, op1=ALU.logical_shift_left))
                    P.mid(v.tensor_tensor(out=idx32[:], in0=idx32[:], in1=bits[:], op=ALU.bitwise_or))
            return v.tensor_copy(out=idx16[:, sl, :], in_=idx32[:])
        idxo = A("vector", unpack_fn, deps=[ldq, shpo, prev.get(('gather_last', pc - 2))],
                 inc=("s_v", 1), tag=f"idx{pc}")
        prev[('idx', pc)] = idxo

        g_last = None
        for gl in range(4):
            b = pc * 4 + gl
            bsl = b % 2
            gdeps = [idxo, dallop, prev.get(('tr', b - 2))]

            def g_fn(g, pc=pc, sl=sl, gl=gl, bsl=bsl):
                r = None
                for ch in range(4):
                    r = g.indirect_copy(
                        out=outg[:, bsl, ch, :], data=dall[:, 4 * pc + gl, :],
                        idxs=idx16[:, sl, (gl * 256 + ch * 64):(gl * 256 + (ch + 1) * 64)],
                        i_know_ap_gather_is_preferred=True)
                return r
            gop = A("gpsimd", g_fn, deps=gdeps, inc=("s_g", 1), tag=f"g{pc}_{gl}")
            g_last = gop

            def tr_fn(te, bsl=bsl):
                r = None
                for ch in range(4):
                    for k in range(8):
                        r = te.transpose(ps_tr[:, bsl, ch * 8 + k, :],
                                         outg[:, bsl, ch, k * 128:(k + 1) * 128], idT[:])
                return r
            trop = A("tensor", tr_fn,
                     deps=[gop, ido, prev.get(('cpd', b - 2))],
                     inc=("s_mm", 1), tag=f"dtr{pc}_{gl}")
            prev[('tr', b)] = trop

            osl = pc * 32 + 8 * gl
            trsel = ps_tr[:, bsl, :, :].rearrange("p k (a b) -> p k a b", b=16)
            cpw = A("vector", lambda v, osl=osl, trsel=trsel: v.tensor_copy(
                out=wlT_sb[:, :, osl:osl + 8],
                in_=trsel[:, :, :, 0:1].rearrange("p k a o -> p k (a o)")),
                deps=[trop], inc=("s_v", 1), tag=f"cpw{pc}_{gl}")
            cpd = A("scalar", lambda sc, osl=osl, trsel=trsel: sc.copy(
                out=ddT_sb[:, :, osl:osl + 8],
                in_=trsel[:, :, :, 8:9].rearrange("p k a o -> p k (a o)")),
                deps=[trop, cpw], inc=("s_sc", 1), tag=f"cpd{pc}_{gl}")
            wl_copies.append(cpw)
            dd_copies.append(cpd)
            prev[('cpd', b)] = cpd
        prev[('gather_last', pc)] = g_last

    stw = A("sync", lambda s: s.dma_start(
        out=wl_o.rearrange("(n p) c -> p n c", p=128), in_=wlT_sb[:]),
        deps=wl_copies, inc=("s_st0", 16), tag="dqstw")
    std = A("sync", lambda s: s.dma_start(
        out=dd_o.rearrange("(n p) c -> p n c", p=128), in_=ddT_sb[:]),
        deps=dd_copies, inc=("s_st1", 16), tag="dqstd")
    tstores = [stw, std]

    if not fused:
        def fin_fn(s):
            s.wait_ge(P.sem_handles["s_st0"], 16 * 128)
            s.wait_ge(P.sem_handles["s_st1"], 16 * 128)
            return s.nop()
        A("sync", fin_fn, deps=[], tag="fin")
        return P

    ccw = A("gpsimd", lambda g: g.collective_compute(
        "AllGather", ALU.bypass, replica_groups=[list(range(8))],
        ins=[wl_t.ap().opt()], outs=[wl_all_t.ap().opt()]),
        deps=tstores, inc=("s_ccw", 1), tag="ccw")
    ccd = A("gpsimd", lambda g: g.collective_compute(
        "AllGather", ALU.bypass, replica_groups=[list(range(8))],
        ins=[dd_t.ap().opt()], outs=[dd_all_t.ap().opt()]),
        deps=tstores, inc=("s_ccd", 1), tag="ccd")
    return [ccw, ccd]



def host_prep_l1f(inp):
    return {"lf_b": np.ascontiguousarray(np.asarray(inp['L1_ffn_bias']).reshape(1, DM))}


def build_l1f(nc, x_in, x_out, es, P, uniq="", wl_all=None, dd_all=None):
    wl8 = wl_all.rearrange("(r n p) c -> p r n c", p=128, r=8)
    dd8 = dd_all.rearrange("(r n p) c -> p r n c", p=128, r=8)
    bia = nc.dram_tensor(uniq + "lf_b", [1, DM], F32, kind="ExternalInput").ap()

    T = lambda n, s, d: es.enter_context(nc.sbuf_tensor(uniq + "lfs_" + n, s, d))
    b_bf = T("b_bf", [1, DM], BF16)
    ones1 = T("ones1", [1, 128], BF16)
    wstr = T("wstr", [128, 2, 4, 1024], BF16)
    dstr = T("dstr", [128, 2, 4, 1024], BF16)
    xbuf = T("xbuf", [128, 2, DM], F32)
    u_bf = T("u_bf", [128, DM], BF16)
    u_f = T("u_f", [128, DM], F32)
    sq_scr = T("sq_scr", [128, DM], F32)
    red256 = T("red256", [128, 256], F32)
    uT = T("uT", [128, NI, 128], BF16)
    tmin = T("tmin", [128, 1024], F32)
    ycb = T("ycb", [128, DM], F32)
    e2 = T("e2", [128, 1], F32)
    mask = T("mask", [128, 1], F32)
    red32 = T("red32", [128, 32], F32)
    ysum4 = T("ysum4", [128, 4], F32)
    yssq4 = T("yssq4", [128, 4], F32)
    ln_x = alloc_ln(T, "x" + uniq + "f1", NT)
    ln_y = alloc_ln(T, "y" + uniq + "f1", NT)

    ps_y = es.enter_context(nc.psum_tensor(uniq + "lf_ps_y", [128, 2, 1024], F32))
    ps_d = es.enter_context(nc.psum_tensor(uniq + "lf_ps_d", [128, 2, 1024], F32))

    ctx = {"u_bf": u_f, "uT": uT, "red32": red32, "tmin": tmin, "sq_scr": sq_scr,
           "red256": red256, "ysum4": ysum4, "yssq4": yssq4}

    A = P.add
    wload = [A("gpsimd", lambda g: g.dma_start(out=b_bf[:], in_=bia[:]), inc=("s_w", 16), tag="lfb")]
    pones = A("vector", lambda v: v.memset(ones1[:], 1.0), inc=("s_v", 1), tag="lfones")

    prev = {}
    for t in range(NT):
        sl = t % 2
        xap = xbuf[:, sl, :]
        deps_ldx = [] if t < 2 else [prev[(t - 2, 'sto')]]
        ldx = A("sync", lambda s, t=t, xap=xap: s.dma_start(
            out=xap, in_=x_in[t*128:(t+1)*128, :]), deps=deps_ldx,
            inc=(f"s_x{sl}", 16), tag=f"lfldx{t}")

        uop, xsq, _ = emit_ln_input(P, A, ctx, t, xap, ln_x, [ldx],
                                    prev.get((t - 1, 'e2o')), "lf", three_stage=True)
        ucast = A("vector", lambda v: v.tensor_copy(out=u_bf[:], in_=u_f[:]),
                  deps=[uop, prev.get((t - 1, 'utr'))], inc=("s_v", 1), tag=f"lfuc{t}")
        utr = emit_utr(P, A, {"u_bf": u_bf, "uT": uT}, ucast, prev.get((t - 1, 'lastmm')), f"lf{t}")
        prev[(t, 'utr')] = utr

        esq = A("scalar", lambda sc: sc.activation(out=sq_scr[:], in_=u_f[:], func=AF.Square),
                deps=[uop], inc=("s_sc", 1), tag=f"lfesq{t}")
        prev[(t, 'esq')] = esq

        def e2_fn(v):
            P.mid(v.tensor_reduce(out=red256[:], in_=sq_scr[:].rearrange("p (a b) -> p a b", a=256),
                                  axis=AX.X, op=ALU.add))
            P.mid(v.tensor_reduce(out=red32[:, 0:16],
                                  in_=red256[:].rearrange("p (a b) -> p a b", a=16),
                                  axis=AX.X, op=ALU.add))
            return v.tensor_reduce(out=e2[:], in_=red32[:, 0:16], axis=AX.X, op=ALU.add)
        e2o = A("vector", e2_fn, deps=[esq], inc=("s_v", 1), tag=f"lfe2{t}")
        prev[(t, 'e2o')] = e2o
        msk = A("gpsimd", lambda g: g.tensor_scalar(
            out=mask[:], in0=e2[:], scalar1=4096.0, scalar2=None, op0=ALU.is_gt),
            deps=[e2o], inc=("s_g", 1), tag=f"lfmsk{t}")

        def make_quarter(q, extra_deps, t=t, utr=utr, msk=msk):
            def bias_fn(te, q=q):
                r = None
                for c in range(2):
                    r = te.matmul(ps_y[:, q % 2, c*512:(c+1)*512], ones1[:],
                                  b_bf[:, q*1024 + c*512:q*1024 + (c+1)*512],
                                  start=True, stop=False)
                return r
            bmm = A("tensor", bias_fn, deps=[pones] + wload + list(extra_deps),
                    inc=("s_mm", 1), tag=f"lfb{q}_{t}")
            last = bmm
            for j in range(8):
                gi = q * 8 + j
                wsl = gi % 2
                wdeps = [prev.get(('g', t * 32 + gi - 2))]
                def wld_fn(s, q=q, j=j, wsl=wsl):
                    s.dma_start(out=wstr[:, wsl, :, 0:512],
                                in_=wl8[:, 2*q, 4*j:4*j+4, :]
                                ).then_inc(P.sem_handles[f"s_wl{wsl}"], 16)
                    s.dma_start(out=wstr[:, wsl, :, 512:1024],
                                in_=wl8[:, 2*q+1, 4*j:4*j+4, :]
                                ).then_inc(P.sem_handles[f"s_wl{wsl}"], 16)
                    return "self"
                wld = A("sync", wld_fn, deps=wdeps, inc=(f"s_wl{wsl}", 32),
                        tag=f"lfwl{q}_{j}_{t}", self_inc=True)

                def dld_fn(s, q=q, j=j, wsl=wsl):
                    s.dma_start(out=dstr[:, wsl, :, 0:512],
                                in_=dd8[:, 2*q, 4*j:4*j+4, :]
                                ).then_inc(P.sem_handles[f"s_dl{wsl}"], 16)
                    s.dma_start(out=dstr[:, wsl, :, 512:1024],
                                in_=dd8[:, 2*q+1, 4*j:4*j+4, :]
                                ).then_inc(P.sem_handles[f"s_dl{wsl}"], 16)
                    return "self"
                dldw = A("sync", dld_fn, deps=wdeps, inc=(f"s_dl{wsl}", 32),
                        tag=f"lfdl{q}_{j}_{t}", self_inc=True)

                def g_fn(te, q=q, j=j, wsl=wsl):
                    r = None
                    for ii in range(4):
                        for c in range(2):
                            te.matmul(ps_y[:, q % 2, c*512:(c+1)*512],
                                      uT[:, 4*j + ii, :],
                                      wstr[:, wsl, ii, c*512:(c+1)*512],
                                      start=False, stop=(j == 7 and ii == 3))
                            r = te.matmul(ps_d[:, q % 2, c*512:(c+1)*512],
                                          uT[:, 4*j + ii, :],
                                          dstr[:, wsl, ii, c*512:(c+1)*512],
                                          start=(j == 0 and ii == 0),
                                          stop=(j == 7 and ii == 3))
                    return r
                gop = A("tensor", g_fn, deps=[wld, dldw, last, utr],
                        inc=("s_mm", 1), tag=f"lfg{q}_{j}_{t}")
                prev[('g', t * 32 + gi)] = gop
                last = gop
            if q == 3:
                prev[(t, 'lastmm')] = last
            def yev_fn(sc, q=q):
                r = None
                for c2 in range(2):
                    r = sc.copy(out=ycb[:, q*1024 + c2*512:q*1024 + (c2+1)*512],
                                in_=ps_y[:, q % 2, c2*512:(c2+1)*512])
                return r
            yev = A("scalar", yev_fn, deps=[last], inc=("s_sc", 1), tag=f"lfyev{q}_{t}")
            cmb = A("vector", lambda v, q=q: v.scalar_tensor_tensor(
                out=ycb[:, q*1024:(q+1)*1024], in0=ps_d[:, q % 2, :],
                scalar=mask[:], in1=ycb[:, q*1024:(q+1)*1024],
                op0=ALU.mult, op1=ALU.add),
                deps=[yev, msk], inc=("s_v", 1), tag=f"lfcmb{q}_{t}")
            return cmb

        sto, q_readers, _ = emit_quarter_stats_epilogue(
            P, A, ctx, t, xap, ln_y, make_quarter, prev.get((t - 1, 'qr')), True,
            x_out[t*128:(t+1)*128, :], f"lf{t}_", ysrc_sbuf=ycb)
        prev[(t, 'qr')] = q_readers
        prev[(t, 'sto')] = sto

    return [prev[(t, 'sto')] for t in range(NT)]


def kernel(**inputs):
    inp = inputs
    x = np.ascontiguousarray(np.asarray(inp['L0_attn_hidden_states'],
                                        dtype=np.float32).reshape(16384, DM))

    nc = bass.Bass("TRN2", target_bir_lowering=False)
    x0 = nc.dram_tensor("x_in", [TOK, DM], F32, kind="ExternalInput").ap()
    x1 = nc.dram_tensor("x1", [TOK, DM], F32, kind="Internal").ap()
    x2 = nc.dram_tensor("x2", [TOK, DM], F32, kind="Internal").ap()
    x3 = nc.dram_tensor("x3", [TOK, DM], F32, kind="Internal").ap()
    xo = nc.dram_tensor("x_out", [TOK, DM], F32, kind="ExternalOutput").ap()
    wl_all_t = nc.dram_tensor("dq_wl_all", [8 * DM, 512], BF16, kind="Internal")
    dd_all_t = nc.dram_tensor("dq_dd_all", [8 * DM, 512], BF16, kind="Internal")

    es = contextlib.ExitStack()
    with es:
        P = Prog(nc)
        with contextlib.ExitStack() as es_p:
            term = build_dequant(nc, es_p, P, wl_all_t=wl_all_t, dd_all_t=dd_all_t)
        P.set_barrier(term, "barD")
        with contextlib.ExitStack() as es_p:
            term = build_l0a(nc, x0, x1, es_p, P)
        P.set_barrier(term, "barA")
        with contextlib.ExitStack() as es_p:
            term = build_l0f(nc, x1, x2, es_p, P)
        P.set_barrier(term, "barF")
        with contextlib.ExitStack() as es_p:
            term = build_l1a(nc, x2, x3, es_p, P)
        P.set_barrier(term, "barB")
        with contextlib.ExitStack() as es_p:
            term = build_l1f(nc, x3, xo, es_p, P,
                             wl_all=wl_all_t.ap(), dd_all=dd_all_t.ap())
        P.add("sync", lambda s: s.nop(), deps=term, inc=("s_fin", 1), tag="fin")
        P.emit(es)

    w = {}
    w.update(host_prep_l0a(inp))
    w.update(host_prep_l0f(inp))
    w.update(host_prep_l1a(inp))
    w.update(host_prep_l1f(inp))
    in_maps = []
    for c in range(8):
        m = dict(w)
        m["x_in"] = np.ascontiguousarray(x[c*TOK:(c+1)*TOK])
        m.update(host_prep_dequant(inp, c))
        in_maps.append(m)
    res = run_bass_kernel_spmd(nc, in_maps, core_ids=list(range(8)))
    xcat = np.concatenate([np.asarray(r["x_out"]) for r in res.results], axis=0)
    return np.ascontiguousarray(xcat.reshape(4, 4096, DM)).astype(np.float32)


# version 16
# speedup vs baseline: 1.2820x; 1.1303x over previous
import numpy as np
import contextlib
import math
import sys
sys.path.insert(0, '/opt/trn_rl_repo')
import concourse.bass as bass
import concourse.mybir as mybir
import bass_rust
from concourse.bass_utils import run_bass_kernel_spmd


F32 = mybir.dt.float32; BF16 = mybir.dt.bfloat16
I32 = mybir.dt.int32; I16 = mybir.dt.int16; U16 = mybir.dt.uint16
AX = mybir.AxisListType; ALU = mybir.AluOpType; AF = mybir.ActivationFunctionType
ENGINES = ("sync", "scalar", "vector", "gpsimd", "tensor")
NT = 16
TOK = NT * 128
DM = 4096
NI = DM // 128
RS32 = 1.0 / math.sqrt(32.0)
NPIECE = 16

"""framework.

Ops are declared in global program order with explicit dependency edges
(producer op objects). Each op runs on one engine; semaphore wait targets are
computed automatically. Same-engine deps rely on program order (no wait
emitted) except when force=True.
"""

import sys






class Op:
    __slots__ = ("engine", "fn", "deps", "incs", "post", "idx", "tag", "self_inc")

    def __init__(self, engine, fn, deps, incs, idx, tag, self_inc=False):
        self.engine = engine
        self.fn = fn
        self.deps = deps
        self.incs = incs
        self.post = {}
        self.idx = idx
        self.tag = tag
        self.self_inc = self_inc


class Prog:
    def __init__(self, nc):
        self.nc = nc
        self.ops = []
        self.sem_counts = {}
        self.sem_handles = {}
        self.chain_counts = {e: 0 for e in ENGINES}
        self._last = {}
        self.cur_engine = None
        self.cur_eng_obj = None
        self.barrier = None

    def set_barrier(self, deps, tag):
        bar = self.add("sync", lambda s: s.nop(), deps=deps,
                       inc=("s_bar", 1), tag=tag)
        self.barrier = bar
        return bar

    def mid(self, instr):
        e = self.cur_engine
        self.chain_counts[e] += 1
        h = self.sem_handles["ch_" + e]
        instr.then_inc(h, 1)
        self.cur_eng_obj.wait_ge(h, self.chain_counts[e])

    def add(self, engine, fn, deps=(), inc=None, tag="", self_inc=False):
        if inc is None:
            incs = []
        elif isinstance(inc, tuple):
            incs = [inc]
        else:
            incs = list(inc)
        if isinstance(deps, Op):
            deps = [deps]
        deps = [d for d in deps if d is not None]
        if not deps and self.barrier is not None:
            deps = [self.barrier]
        if engine in ("vector", "scalar", "gpsimd"):
            prev = self._last.get(engine)
            if prev is not None and prev.incs:
                deps = list(deps) + [prev]
        op = Op(engine, fn, deps, incs, len(self.ops), tag, self_inc)
        self.ops.append(op)
        self._last[engine] = op
        return op

    def _resolve(self):
        for op in self.ops:
            for (s, k) in op.incs:
                self.sem_counts[s] = self.sem_counts.get(s, 0) + k
                op.post[s] = self.sem_counts[s]

    def emit(self, es):
        self._resolve()
        for s in self.sem_counts:
            self.sem_handles[s] = es.enter_context(self.nc.semaphore(s))
        for e in ENGINES:
            self.sem_handles["ch_" + e] = es.enter_context(self.nc.semaphore("ch_" + e))
        nc = self.nc
        with nc.Block() as block:
            for engname in ENGINES:
                eng_ops = [o for o in self.ops if o.engine == engname]
                if not eng_ops:
                    continue

                def make(eng_ops=eng_ops, engname=engname):
                    def body(eng):
                        self.cur_engine = engname
                        self.cur_eng_obj = eng
                        for op in eng_ops:
                            waits = {}
                            for d in op.deps:
                                if not d.incs:
                                    raise RuntimeError(
                                        f"dep {d.tag} of {op.tag} has no inc")
                                s = d.incs[0][0]
                                v = d.post[s]
                                if waits.get(s, -1) < v:
                                    waits[s] = v
                            for s, v in waits.items():
                                eng.wait_ge(self.sem_handles[s], v)
                            ret = op.fn(eng)
                            if op.incs and not op.self_inc:
                                if ret is None:
                                    raise RuntimeError(
                                        f"op {op.tag} has incs but returned no instr")
                                last = ret[-1] if isinstance(ret, (list, tuple)) else ret
                                for (s, k) in op.incs:
                                    last.then_inc(self.sem_handles[s], k)
                        return None
                    return body

                getattr(block, engname)(make())
        return block



import sys







def alloc_ln(T, tag, slots):
    ln = {}
    for fld in ("sum", "ssq", "mneg", "varep", "r0", "rstd", "t1", "nt", "madj"):
        ln[fld] = T(f"ln{tag}_{fld}", [128, slots], F32)
    return ln


def emit_ln_smalls(P, A, ln, t, dep_stats, tag):
    def smalls_fn(g):
        P.mid(g.tensor_scalar(out=ln["mneg"][:, t:t+1], in0=ln["sum"][:, t:t+1],
                              scalar1=-1.0/DM, scalar2=None, op0=ALU.mult))
        P.mid(g.tensor_tensor(out=ln["t1"][:, t:t+1], in0=ln["mneg"][:, t:t+1],
                              in1=ln["mneg"][:, t:t+1], op=ALU.mult))
        P.mid(g.tensor_scalar(out=ln["varep"][:, t:t+1], in0=ln["ssq"][:, t:t+1],
                              scalar1=1.0/DM, scalar2=1e-5, op0=ALU.mult, op1=ALU.add))
        return g.tensor_tensor(out=ln["varep"][:, t:t+1], in0=ln["varep"][:, t:t+1],
                               in1=ln["t1"][:, t:t+1], op=ALU.subtract)
    smalls = A("gpsimd", smalls_fn, deps=dep_stats, inc=("s_g", 1), tag=f"sm{tag}{t}")

    def rstd0_fn(sc):
        P.mid(sc.activation(out=ln["t1"][:, t:t+1], in_=ln["varep"][:, t:t+1],
                            func=AF.Ln, bias=0.0, scale=1.0))
        return sc.activation(out=ln["r0"][:, t:t+1], in_=ln["t1"][:, t:t+1],
                             func=AF.Exp, bias=0.0, scale=-0.5)
    rstd0 = A("scalar", rstd0_fn, deps=[smalls], inc=("s_sc", 1), tag=f"r0{tag}{t}")

    def newton_fn(g):
        P.mid(g.tensor_tensor(out=ln["nt"][:, t:t+1], in0=ln["r0"][:, t:t+1],
                              in1=ln["r0"][:, t:t+1], op=ALU.mult))
        P.mid(g.tensor_tensor(out=ln["nt"][:, t:t+1], in0=ln["nt"][:, t:t+1],
                              in1=ln["varep"][:, t:t+1], op=ALU.mult))
        P.mid(g.tensor_scalar(out=ln["nt"][:, t:t+1], in0=ln["nt"][:, t:t+1],
                              scalar1=-0.5, scalar2=1.5, op0=ALU.mult, op1=ALU.add))
        P.mid(g.tensor_tensor(out=ln["rstd"][:, t:t+1], in0=ln["r0"][:, t:t+1],
                              in1=ln["nt"][:, t:t+1], op=ALU.mult))
        return g.tensor_tensor(out=ln["madj"][:, t:t+1], in0=ln["mneg"][:, t:t+1],
                               in1=ln["rstd"][:, t:t+1], op=ALU.mult)
    newton = A("gpsimd", newton_fn, deps=[rstd0], inc=("s_g", 1), tag=f"nw{tag}{t}")
    return newton


def emit_ln_input(P, A, ctx, t, xap, ln, deps_x, dep_u_free, tag, three_stage=False):
    u_bf = ctx["u_bf"]; uT = ctx["uT"]; red32 = ctx["red32"]
    xsq_out = ctx.get("sq_scr")
    xsq_out = u_bf[:] if xsq_out is None else xsq_out[:]
    xsq = A("scalar", lambda sc: sc.activation(
        out=xsq_out, in_=xap, func=AF.Square,
        accum_out=(None if three_stage else ln["ssq"][:, t:t+1])),
        deps=deps_x + [dep_u_free], inc=("s_sc", 1), tag=f"xsq{tag}{t}")

    if three_stage:
        def ssq_fn(v):
            P.mid(v.tensor_reduce(out=ctx["red256"][:],
                                  in_=xsq_out.rearrange("p (a b) -> p a b", a=256),
                                  axis=AX.X, op=ALU.add))
            P.mid(v.tensor_reduce(out=red32[:, 0:16],
                                  in_=ctx["red256"][:].rearrange("p (a b) -> p a b", a=16),
                                  axis=AX.X, op=ALU.add))
            return v.tensor_reduce(out=ln["ssq"][:, t:t+1], in_=red32[:, 0:16],
                                   axis=AX.X, op=ALU.add)
        xssq = A("vector", ssq_fn, deps=[xsq], inc=("s_v", 1), tag=f"xssq{tag}{t}")
        stats_dep = [xssq]
    else:
        stats_dep = [xsq]

    def xsum_fn(v):
        P.mid(v.tensor_reduce(out=red32[:], in_=xap.rearrange("p (a b) -> p a b", a=32),
                              axis=AX.X, op=ALU.add))
        return v.tensor_reduce(out=ln["sum"][:, t:t+1], in_=red32[:],
                               axis=AX.X, op=ALU.add)
    xsum = A("vector", xsum_fn, deps=deps_x, inc=("s_v", 1), tag=f"xsum{tag}{t}")

    newton = emit_ln_smalls(P, A, ln, t, stats_dep + [xsum], tag)

    uop = A("vector", lambda v: v.tensor_scalar(
        out=u_bf[:], in0=xap, scalar1=ln["mneg"][:, t:t+1],
        scalar2=ln["rstd"][:, t:t+1], op0=ALU.add, op1=ALU.mult),
        deps=[newton, xsq], inc=("s_v", 1), tag=f"u{tag}{t}")
    return uop, xsq, newton


def emit_utr(P, A, ctx, dep_u, dep_free, tag):
    u_bf = ctx["u_bf"]; uT = ctx["uT"]
    def utr_fn(s):
        for i in range(NI):
            s.dma_start(out=uT[:, i, :], in_=u_bf[:, i*128:(i+1)*128],
                        transpose=True).then_inc(P.sem_handles["s_t"], 16)
        return "self"
    return A("sync", utr_fn, deps=[dep_u, dep_free], inc=("s_t", 16*NI),
             tag=f"utr{tag}", self_inc=True)


def emit_quarter_stats_epilogue(P, A, ctx, t, xap, ln_y, make_quarter, prev_readers,
                                x_store, x_out_ap, tag, ysrc_sbuf=None,
                                yst0_extra=None):
    red32 = ctx["red32"]; tmin = ctx["tmin"]
    ysum4 = ctx["ysum4"]; yssq4 = ctx["yssq4"]
    ps_y = ctx.get("ps_y"); ysb = ctx.get("ysb")
    ystats = []; ysqs = []; evacs = [None, None]
    quarter_ops = []
    q_readers = {}
    prev_ysqr = None
    prev_act_ps = None
    for q in range(4):
        if ysrc_sbuf is not None:
            src = ysrc_sbuf[:, q*1024:(q+1)*1024]
        else:
            src = ps_y[:, q % 2, :]
        extra = []
        if q >= 2:
            extra += q_readers.get(q - 2, [])
        elif prev_readers is not None:
            extra += prev_readers.get(q + 2, [])
        at = make_quarter(q, extra)
        quarter_ops.append(at)
        q_readers[q] = [at]

        def yst_fn(v, src=src, q=q):
            P.mid(v.tensor_reduce(out=red32[:, 0:8],
                                  in_=src.rearrange("p (a b) -> p a b", a=8),
                                  axis=AX.X, op=ALU.add))
            return v.tensor_reduce(out=ysum4[:, q:q+1], in_=red32[:, 0:8],
                                   axis=AX.X, op=ALU.add)
        yst = A("vector", yst_fn,
                deps=[at, prev_act_ps] + ([yst0_extra] if q == 0 else []),
                inc=("s_v", 1), tag=f"yst{tag}{q}_{t}")
        ystats.append(yst)

        def ysq0_fn(sc, src=src):
            r = None
            for c2 in range(2):
                r = sc.activation(out=tmin[:, c2*512:(c2+1)*512],
                                  in_=src[:, c2*512:(c2+1)*512], func=AF.Square)
            return r
        ysq0 = A("scalar", ysq0_fn, deps=[at, yst, prev_ysqr], inc=("s_sc", 1), tag=f"ysq0{tag}{q}_{t}")

        def ysqr_fn(v, q=q):
            P.mid(v.tensor_reduce(out=red32[:, 8:16],
                                  in_=tmin[:].rearrange("p (a b) -> p a b", a=8),
                                  axis=AX.X, op=ALU.add))
            return v.tensor_reduce(out=yssq4[:, q:q+1], in_=red32[:, 8:16],
                                   axis=AX.X, op=ALU.add)
        ysq = A("vector", ysqr_fn, deps=[ysq0], inc=("s_v", 1), tag=f"ysq{tag}{q}_{t}")
        ysqs.append(ysq); prev_ysqr = ysq
        if ysrc_sbuf is None:
            prev_act_ps = ysq0
        if ysrc_sbuf is None:
            q_readers[q] += [yst, ysq0]

        if ysrc_sbuf is None and q < 2:
            def yev_fn(sc, q=q, src=src):
                r = None
                for c2 in range(2):
                    r = sc.copy(out=ysb[:, q*1024 + c2*512:q*1024 + (c2+1)*512],
                                in_=src[:, c2*512:(c2+1)*512])
                return r
            ev = A("scalar", yev_fn, deps=[at, yst, ysq0], inc=("s_sc", 1),
                   tag=f"yev{tag}{q}_{t}")
            evacs[q] = ev
            prev_act_ps = ev
            q_readers[q].append(ev)

    def ysum_fn(g):
        P.mid(g.tensor_tensor(out=ln_y["sum"][:, t:t+1], in0=ysum4[:, 0:1], in1=ysum4[:, 1:2], op=ALU.add))
        P.mid(g.tensor_tensor(out=ln_y["t1"][:, t:t+1], in0=ysum4[:, 2:3], in1=ysum4[:, 3:4], op=ALU.add))
        P.mid(g.tensor_tensor(out=ln_y["sum"][:, t:t+1], in0=ln_y["sum"][:, t:t+1], in1=ln_y["t1"][:, t:t+1], op=ALU.add))
        P.mid(g.tensor_tensor(out=ln_y["ssq"][:, t:t+1], in0=yssq4[:, 0:1], in1=yssq4[:, 1:2], op=ALU.add))
        P.mid(g.tensor_tensor(out=ln_y["t1"][:, t:t+1], in0=yssq4[:, 2:3], in1=yssq4[:, 3:4], op=ALU.add))
        return g.tensor_tensor(out=ln_y["ssq"][:, t:t+1], in0=ln_y["ssq"][:, t:t+1], in1=ln_y["t1"][:, t:t+1], op=ALU.add)
    ysum = A("gpsimd", ysum_fn, deps=ystats + ysqs, inc=("s_g", 1), tag=f"ysum{tag}{t}")
    q_readers['ysum'] = [ysum]

    newton = emit_ln_smalls(P, A, ln_y, t, [ysum], "y" + tag)

    yaps = []
    for q in range(4):
        if ysrc_sbuf is not None:
            src = ysrc_sbuf[:, q*1024:(q+1)*1024]
            dps = [newton, quarter_ops[q], ysqs[q]]
        elif q < 2:
            src = ysb[:, q*1024:(q+1)*1024]
            dps = [newton, evacs[q]]
        else:
            src = ps_y[:, q % 2, :]
            dps = [newton, ystats[q], ysqs[q], quarter_ops[q]]
        yap = A("vector", lambda v, q=q, src=src: v.scalar_tensor_tensor(
            out=xap[:, q*1024:(q+1)*1024], in0=src,
            scalar=ln_y["rstd"][:, t:t+1], in1=xap[:, q*1024:(q+1)*1024],
            op0=ALU.mult, op1=ALU.add),
            deps=dps, inc=("s_v", 1), tag=f"yap{tag}{q}_{t}")
        yaps.append(yap)
        q_readers.setdefault(q, []).append(yap)

    xmadj = A("vector", lambda v: v.tensor_scalar(
        out=xap, in0=xap, scalar1=ln_y["madj"][:, t:t+1], scalar2=None, op0=ALU.add),
        deps=yaps + [newton], inc=("s_v", 1), tag=f"xmadj{tag}{t}")

    if x_store:
        sto = A("sync", lambda s: s.dma_start(
            out=x_out_ap, in_=xap), deps=[xmadj], inc=(f"s_st{t % 2}", 16), tag=f"sto{tag}{t}")
    else:
        sto = xmadj
    return sto, q_readers, ysqs[3]












TOK = NT * 128; DM = 4096; NI = DM // 128


def host_prep_l0a(inp):
    H, HKV, D = 32, 8, 128
    Wq = np.asarray(inp['L0_attn_Wq']); bq = np.asarray(inp['L0_attn_bq'])
    Wq_sel = np.ascontiguousarray(Wq.reshape(H, D, DM)[::4].reshape(HKV * D, DM).T)
    bq_sel = np.ascontiguousarray(bq.reshape(H, D)[::4].reshape(1, HKV * D))
    WoT = np.ascontiguousarray(np.asarray(inp['L0_attn_Wo']).T)
    Mz = np.ascontiguousarray(np.concatenate(
        [np.asarray(inp['L0_attn_M']), np.asarray(inp['L0_attn_z'])[:, :, None]], axis=2))
    return {"wqselT": Wq_sel, "bqsel": bq_sel, "woT": WoT, "mz": Mz}


def build_l0a(nc, x_in, x_out, es, P, uniq=""):
    nc_ = nc
    wqselT = nc.dram_tensor(uniq + "wqselT", [DM, 1024], F32, kind="ExternalInput").ap()
    bqsel = nc.dram_tensor(uniq + "bqsel", [1, 1024], F32, kind="ExternalInput").ap()
    woT = nc.dram_tensor(uniq + "woT", [DM, DM], F32, kind="ExternalInput").ap()
    mz = nc.dram_tensor(uniq + "mz", [8, 128, 129], F32, kind="ExternalInput").ap()

    T = lambda n, s, d: es.enter_context(nc.sbuf_tensor(uniq + n, s, d))
    wq_bf = T("wq_bf", [128, NI, 1024], BF16)
    wo_bf = T("wo_bf", [128, 8, DM], BF16)
    mz_bf = T("mz_bf", [128, 8, 129], BF16)
    bq_bf = T("bq_bf", [1, 1024], BF16)
    ones1 = T("ones1", [1, 128], BF16)
    xbuf = T("xbuf", [128, 2, DM], F32)
    u_bf = T("u_bf", [128, DM], BF16)
    uT = T("uT", [128, NI, 128], BF16)
    tmin = T("tmin", [128, 1024], F32)
    sig_bf = T("sig_bf", [128, 1024], BF16)
    sigT = T("sigT", [128, 8, 128], BF16)
    retr_bf = T("retr_bf", [128, 1024], BF16)
    retrT = T("retrT", [128, 8, 128], BF16)
    den8 = T("den8", [128, 8], F32)
    rden8 = T("rden8", [128, 8], F32)
    red32 = T("red32", [128, 32], F32)
    ysb = T("ysb", [128, 2048], F32)
    ysum4 = T("ysum4", [128, 4], F32)
    yssq4 = T("yssq4", [128, 4], F32)
    ln = {}
    for pre in ("x", "y"):
        for fld in ("sum", "ssq", "mneg", "varep", "r0", "rstd", "t1", "nt", "madj"):
            ln[pre + fld] = T(f"ln{pre}_{fld}", [128, NT], F32)

    ps_att = es.enter_context(nc.psum_tensor(uniq + "ps_att", [128, 2, 1024], F32))
    ps_q = es.enter_context(nc.psum_tensor(uniq + "ps_q", [128, 1024], F32))
    ps_num = es.enter_context(nc.psum_tensor(uniq + "ps_num", [128, 8, 128], F32))

    A = P.add
    wload = []
    for j in range(4):
        src = wqselT.rearrange("(n p) c -> p n c", p=128)[:, j * 8:(j + 1) * 8, :]
        wload.append(A("gpsimd", lambda g, j=j, src=src: g.dma_start(
            out=wq_bf[:, j * 8:(j + 1) * 8, :], in_=src), inc=("s_w", 16), tag=f"wq{j}"))
    wload.append(A("gpsimd", lambda g: g.dma_start(
        out=mz_bf[:], in_=mz.rearrange("h p c -> p h c")), inc=("s_w", 16), tag="mz"))
    wload.append(A("gpsimd", lambda g: g.dma_start(
        out=bq_bf[:], in_=bqsel[:]), inc=("s_w", 16), tag="bq"))
    pones = A("vector", lambda v: v.memset(ones1[:], 1.0), inc=("s_v", 1), tag="ones")

    fold_done = []
    prev_add = None
    for h in range(8):
        l0 = A("sync", lambda s, h=h: s.dma_start(
            out=xbuf[:, 0, :], in_=woT[(4 * h) * 128:(4 * h + 1) * 128, :]),
            deps=[prev_add], inc=("s_fold", 16), tag=f"fold_l0_{h}")
        prev = l0
        for g in range(1, 4):
            lg = A("sync", lambda s, h=h, g=g: s.dma_start(
                out=xbuf[:, 1, :], in_=woT[(4 * h + g) * 128:(4 * h + g + 1) * 128, :]),
                deps=[prev], inc=("s_fold", 16), tag=f"fold_l{g}_{h}")
            ad = A("vector", lambda v: v.tensor_tensor(
                out=xbuf[:, 0, :], in0=xbuf[:, 0, :], in1=xbuf[:, 1, :], op=ALU.add),
                deps=[lg], inc=("s_v", 1), tag=f"fold_a{g}_{h}")
            prev = ad
        cst = A("vector", lambda v, h=h: v.tensor_copy(out=wo_bf[:, h, :], in_=xbuf[:, 0, :]),
                deps=[prev], inc=("s_v", 1), tag=f"fold_c{h}")
        prev_add = cst
        fold_done.append(cst)

    prev_tile = {}
    for t in range(NT):
        sl = t % 2
        xap = xbuf[:, sl, :]
        deps_ldx = [fold_done[-1]] if t < 2 else [prev_tile[(t - 2, 'sto')]]
        if t >= 1:
            deps_ldx.append(prev_tile[(t - 1, 'xsum')])
        ldx = A("sync", lambda s, t=t, xap=xap: s.dma_start(
            out=xap, in_=x_in[t * 128:(t + 1) * 128, :]), deps=deps_ldx,
            inc=(f"s_x{t % 2}", 16), tag=f"ldx{t}")

        dep_u_free = prev_tile.get((t - 1, 'ysq3'))
        xsq = A("scalar", lambda sc, t=t, xap=xap: sc.activation(
            out=u_bf[:], in_=xap, func=AF.Square, accum_out=ln["xssq"][:, t:t + 1]),
            deps=[ldx, dep_u_free], inc=("s_sc", 1), tag=f"xsq{t}")

        def xsum_fn(v, t=t, xap=xap):
            P.mid(v.tensor_reduce(out=red32[:], in_=xap.rearrange("p (a b) -> p a b", a=32),
                            axis=AX.X, op=ALU.add))
            return v.tensor_reduce(out=ln["xsum"][:, t:t + 1], in_=red32[:],
                                   axis=AX.X, op=ALU.add)
        xsum = A("vector", xsum_fn, deps=[ldx], inc=("s_v", 1), tag=f"xsum{t}")

        def smalls_fn(g, t=t, pre="x"):
            P.mid(g.tensor_scalar(out=ln[pre + "mneg"][:, t:t + 1], in0=ln[pre + "sum"][:, t:t + 1],
                            scalar1=-1.0 / DM, scalar2=None, op0=ALU.mult))
            P.mid(g.tensor_tensor(out=ln[pre + "t1"][:, t:t + 1], in0=ln[pre + "mneg"][:, t:t + 1],
                            in1=ln[pre + "mneg"][:, t:t + 1], op=ALU.mult))
            P.mid(g.tensor_scalar(out=ln[pre + "varep"][:, t:t + 1], in0=ln[pre + "ssq"][:, t:t + 1],
                            scalar1=1.0 / DM, scalar2=1e-5, op0=ALU.mult, op1=ALU.add))
            return g.tensor_tensor(out=ln[pre + "varep"][:, t:t + 1],
                                   in0=ln[pre + "varep"][:, t:t + 1],
                                   in1=ln[pre + "t1"][:, t:t + 1], op=ALU.subtract)
        smalls = A("gpsimd", smalls_fn, deps=[xsq, xsum], inc=("s_g", 1), tag=f"smx{t}")

        def rstd0_fn(sc, t=t, pre="x"):
            P.mid(sc.activation(out=ln[pre + "t1"][:, t:t + 1], in_=ln[pre + "varep"][:, t:t + 1],
                          func=AF.Ln, bias=0.0, scale=1.0))
            return sc.activation(out=ln[pre + "r0"][:, t:t + 1], in_=ln[pre + "t1"][:, t:t + 1],
                                 func=AF.Exp, bias=0.0, scale=-0.5)
        rstd0 = A("scalar", rstd0_fn, deps=[smalls], inc=("s_sc", 1), tag=f"r0x{t}")

        def newton_fn(g, t=t, pre="x"):
            P.mid(g.tensor_tensor(out=ln[pre + "nt"][:, t:t + 1], in0=ln[pre + "r0"][:, t:t + 1],
                            in1=ln[pre + "r0"][:, t:t + 1], op=ALU.mult))
            P.mid(g.tensor_tensor(out=ln[pre + "nt"][:, t:t + 1], in0=ln[pre + "nt"][:, t:t + 1],
                            in1=ln[pre + "varep"][:, t:t + 1], op=ALU.mult))
            P.mid(g.tensor_scalar(out=ln[pre + "nt"][:, t:t + 1], in0=ln[pre + "nt"][:, t:t + 1],
                            scalar1=-0.5, scalar2=1.5, op0=ALU.mult, op1=ALU.add))
            return g.tensor_tensor(out=ln[pre + "rstd"][:, t:t + 1], in0=ln[pre + "r0"][:, t:t + 1],
                                   in1=ln[pre + "nt"][:, t:t + 1], op=ALU.mult)
        newton = A("gpsimd", newton_fn, deps=[rstd0], inc=("s_g", 1), tag=f"nwx{t}")

        uop = A("vector", lambda v, t=t, xap=xap: v.tensor_scalar(
            out=u_bf[:], in0=xap, scalar1=ln["xmneg"][:, t:t + 1],
            scalar2=ln["xrstd"][:, t:t + 1], op0=ALU.add, op1=ALU.mult),
            deps=[newton, xsq], inc=("s_v", 1), tag=f"u{t}")

        def utr_fn(s, t=t):
            for i in range(NI):
                s.dma_start(out=uT[:, i, :], in_=u_bf[:, i * 128:(i + 1) * 128],
                            transpose=True).then_inc(P.sem_handles["s_t"], 16)
            return "self"
        utr = A("sync", utr_fn, deps=[uop, prev_tile.get((t - 1, 'qselmm'))],
                inc=("s_t", 16 * NI), tag=f"utr{t}", self_inc=True)

        def qsel_fn(te, t=t):
            r = None
            for c in range(2):
                te.matmul(ps_q[:, c * 512:(c + 1) * 512], ones1[:],
                          bq_bf[:, c * 512:(c + 1) * 512], start=True, stop=False)
                for i in range(NI):
                    r = te.matmul(ps_q[:, c * 512:(c + 1) * 512], uT[:, i, :],
                                  wq_bf[:, i, c * 512:(c + 1) * 512],
                                  start=False, stop=(i == NI - 1))
            return r
        qselmm = A("tensor", qsel_fn,
                   deps=[utr, pones] + wload + ([prev_tile[(t - 1, 'sigstt')], prev_tile[(t - 1, 'dencp')]] if t else []),
                   inc=("s_mm", 1), tag=f"qselmm{t}")

        sigmin = A("vector", lambda v: v.tensor_scalar(
            out=tmin[:], in0=ps_q[:], scalar1=0.0, scalar2=None, op0=ALU.min),
            deps=[qselmm, prev_tile.get((t - 1, 'ysq3s'))], inc=("s_v", 1), tag=f"sigmin{t}")
        sigexp = A("scalar", lambda sc: sc.activation(
            out=tmin[:], in_=tmin[:], func=AF.Exp, bias=0.0, scale=1.0),
            deps=[sigmin], inc=("s_sc", 1), tag=f"sigexp{t}")
        sigstt = A("vector", lambda v: v.scalar_tensor_tensor(
            out=sig_bf[:], in0=ps_q[:], scalar=0.0, in1=tmin[:], op0=ALU.max, op1=ALU.add),
            deps=[sigexp], inc=("s_v", 1), tag=f"sigstt{t}")

        def sigtr_fn(s):
            for h in range(8):
                s.dma_start(out=sigT[:, h, :], in_=sig_bf[:, h * 128:(h + 1) * 128],
                            transpose=True).then_inc(P.sem_handles["s_t"], 16)
            return "self"
        sigtr = A("sync", sigtr_fn, deps=[sigstt, prev_tile.get((t - 1, 'num7'))],
                  inc=("s_t", 16 * 8), tag=f"sigtr{t}", self_inc=True)

        nums = []; retrs = []
        num_war = [prev_tile[(t - 1, 'retr7')]] if t >= 1 else []
        den_war = [sigstt]
        for h in range(8):
            nm = A("tensor", lambda te, h=h: te.matmul(
                ps_num[:, h, :], sigT[:, h, :], mz_bf[:, h, 0:128], start=True, stop=True),
                deps=[sigtr] + num_war, inc=("s_mm", 1), tag=f"num{h}_{t}")
            nums.append(nm)
            A("tensor", lambda te, h=h: te.matmul(
                ps_q[:, h:h + 1], sigT[:, h, :], mz_bf[:, h, 128:129], start=True, stop=True),
                deps=[sigtr] + den_war, inc=("s_mm", 1), tag=f"den{h}_{t}")
        dencp = A("scalar", lambda sc: sc.copy(out=den8[:], in_=ps_q[:, 0:8]),
                  deps=[P.ops[-1]], inc=("s_sc", 1), tag=f"dencp{t}")
        deneps = A("gpsimd", lambda g: g.tensor_scalar(
            out=den8[:], in0=den8[:], scalar1=1e-6, scalar2=None, op0=ALU.add),
            deps=[dencp], inc=("s_g", 1), tag=f"deneps{t}")

        def dln_fn(sc):
            P.mid(sc.activation(out=rden8[:], in_=den8[:], func=AF.Ln, bias=0.0, scale=1.0))
            return sc.activation(out=rden8[:], in_=rden8[:], func=AF.Exp,
                                 bias=0.0, scale=-1.0)
        dexp = A("scalar", dln_fn, deps=[deneps], inc=("s_sc", 1), tag=f"dexp{t}")
        for h in range(8):
            rt = A("vector", lambda v, h=h: v.tensor_scalar(
                out=retr_bf[:, h * 128:(h + 1) * 128], in0=ps_num[:, h, :],
                scalar1=rden8[:, h:h + 1], scalar2=None, op0=ALU.mult),
                deps=[dexp, nums[h]], inc=("s_v", 1), tag=f"retr{h}_{t}")
            retrs.append(rt)
        prev_tile[(t, 'retr7')] = retrs[7]

        def retrtr_fn(s):
            for h in range(8):
                s.dma_start(out=retrT[:, h, :], in_=retr_bf[:, h * 128:(h + 1) * 128],
                            transpose=True).then_inc(P.sem_handles["s_t"], 16)
            return "self"
        retrtr = A("sync", retrtr_fn,
                   deps=retrs + [prev_tile.get((t - 1, 'att3'))],
                   inc=("s_t", 16 * 8), tag=f"retrtr{t}", self_inc=True)

        atts = [None] * 4; ystats = []; ysqs = []; evacs = [None, None]
        yaps = [None] * 4
        for q in range(4):
            dps = [retrtr] + ([fold_done[-1]] if t == 0 else [])
            if q >= 2:
                dps += [evacs[q - 2], ystats[q - 2], ysqs[q - 2]]
            elif t >= 1:
                dps += [prev_tile[(t - 1, f'yap{q+2}')]]

            def att_fn(te, q=q):
                r = None
                for c2 in range(2):
                    for h in range(8):
                        r = te.matmul(ps_att[:, q % 2, c2 * 512:(c2 + 1) * 512],
                                      retrT[:, h, :],
                                      wo_bf[:, h, q * 1024 + c2 * 512: q * 1024 + (c2 + 1) * 512],
                                      start=(h == 0), stop=(h == 7))
                return r
            at = A("tensor", att_fn, deps=dps, inc=("s_mm", 1), tag=f"att{q}_{t}")
            atts[q] = at
            prev_tile[(t, f'att{q}')] = at

            def yst_fn(v, q=q, t=t):
                P.mid(v.tensor_reduce(out=red32[:, 0:8],
                                in_=ps_att[:, q % 2, :].rearrange("p (a b) -> p a b", a=8),
                                axis=AX.X, op=ALU.add))
                return v.tensor_reduce(out=ysum4[:, q:q + 1], in_=red32[:, 0:8],
                                       axis=AX.X, op=ALU.add)
            yst = A("vector", yst_fn, deps=[at], inc=("s_v", 1), tag=f"yst{q}_{t}")
            ystats.append(yst)
            def ysq0_fn(sc, q=q):
                r = None
                for c2 in range(2):
                    r = sc.activation(out=tmin[:, c2 * 512:(c2 + 1) * 512],
                                      in_=ps_att[:, q % 2, c2 * 512:(c2 + 1) * 512],
                                      func=AF.Square)
                return r
            ysq0 = A("scalar", ysq0_fn, deps=[at, yst], inc=("s_sc", 1), tag=f"ysq0_{q}_{t}")

            def ysqr_fn(v, q=q):
                P.mid(v.tensor_reduce(out=red32[:, 8:16],
                                      in_=tmin[:].rearrange("p (a b) -> p a b", a=8),
                                      axis=AX.X, op=ALU.add))
                return v.tensor_reduce(out=yssq4[:, q:q + 1], in_=red32[:, 8:16],
                                       axis=AX.X, op=ALU.add)
            ysq = A("vector", ysqr_fn, deps=[ysq0], inc=("s_v", 1), tag=f"ysq{q}_{t}")
            ysqs.append(ysq)
            if q == 3:
                prev_tile[(t, 'ysq3')] = ysq
                prev_tile[(t, 'ysq3s')] = ysq0
            if q < 2:
                evdeps = [at, yst, ysq]
                if t >= 1:
                    evdeps.append(prev_tile[(t - 1, f'yap{q}')])
                def yev_fn(sc, q=q):
                    r = None
                    for c2 in range(2):
                        r = sc.copy(out=ysb[:, q * 1024 + c2 * 512:q * 1024 + (c2 + 1) * 512],
                                    in_=ps_att[:, q % 2, c2 * 512:(c2 + 1) * 512])
                    return r
                ev = A("scalar", yev_fn, deps=evdeps, inc=("s_sc", 1), tag=f"yev{q}_{t}")
                evacs[q] = ev

        def ysum_fn(g, t=t):
            P.mid(g.tensor_tensor(out=ln["ysum"][:, t:t + 1], in0=ysum4[:, 0:1], in1=ysum4[:, 1:2], op=ALU.add))
            P.mid(g.tensor_tensor(out=ln["yt1"][:, t:t + 1], in0=ysum4[:, 2:3], in1=ysum4[:, 3:4], op=ALU.add))
            P.mid(g.tensor_tensor(out=ln["ysum"][:, t:t + 1], in0=ln["ysum"][:, t:t + 1], in1=ln["yt1"][:, t:t + 1], op=ALU.add))
            P.mid(g.tensor_tensor(out=ln["yssq"][:, t:t + 1], in0=yssq4[:, 0:1], in1=yssq4[:, 1:2], op=ALU.add))
            P.mid(g.tensor_tensor(out=ln["yt1"][:, t:t + 1], in0=yssq4[:, 2:3], in1=yssq4[:, 3:4], op=ALU.add))
            return g.tensor_tensor(out=ln["yssq"][:, t:t + 1], in0=ln["yssq"][:, t:t + 1], in1=ln["yt1"][:, t:t + 1], op=ALU.add)
        ysum = A("gpsimd", ysum_fn, deps=ystats + ysqs, inc=("s_g", 1), tag=f"ysum{t}")

        ysmalls = A("gpsimd", lambda g, t=t: smalls_fn(g, t=t, pre="y"), deps=[ysum],
                    inc=("s_g", 1), tag=f"smy{t}")
        yrstd0 = A("scalar", lambda sc, t=t: rstd0_fn(sc, t=t, pre="y"), deps=[ysmalls],
                   inc=("s_sc", 1), tag=f"r0y{t}")

        def ynewton_fn(g, t=t):
            P.mid(newton_fn(g, t=t, pre="y"))
            return g.tensor_tensor(out=ln["ymadj"][:, t:t + 1], in0=ln["ymneg"][:, t:t + 1],
                                   in1=ln["yrstd"][:, t:t + 1], op=ALU.mult)
        ynewton = A("gpsimd", ynewton_fn, deps=[yrstd0], inc=("s_g", 1), tag=f"nwy{t}")

        for q in range(4):
            if q < 2:
                src_ap = ysb[:, q * 1024:(q + 1) * 1024]
                dps = [ynewton, evacs[q]]
            else:
                src_ap = ps_att[:, q % 2, :]
                dps = [ynewton, ystats[q], ysqs[q], atts[q]]
            yap = A("vector", lambda v, q=q, t=t, xap=xap, src_ap=src_ap: v.scalar_tensor_tensor(
                out=xap[:, q * 1024:(q + 1) * 1024], in0=src_ap,
                scalar=ln["yrstd"][:, t:t + 1], in1=xap[:, q * 1024:(q + 1) * 1024],
                op0=ALU.mult, op1=ALU.add),
                deps=dps, inc=("s_v", 1), tag=f"yap{q}_{t}")
            yaps[q] = yap; prev_tile[(t, f'yap{q}')] = yap

        xmadj = A("vector", lambda v, t=t, xap=xap: v.tensor_scalar(
            out=xap, in0=xap, scalar1=ln["ymadj"][:, t:t + 1], scalar2=None, op0=ALU.add),
            deps=yaps + [ynewton], inc=("s_v", 1), tag=f"xmadj{t}")

        sto = A("sync", lambda s, t=t, xap=xap: s.dma_start(
            out=x_out[t * 128:(t + 1) * 128, :], in_=xap), deps=[xmadj],
            inc=(f"s_st{t % 2}", 16), tag=f"sto{t}")
        prev_tile[(t, 'sto')] = sto
        prev_tile[(t, 'xsum')] = xsum
        prev_tile[(t, 'qselmm')] = qselmm
        prev_tile[(t, 'sigstt')] = sigstt
        prev_tile[(t, 'num7')] = nums[7]
        prev_tile[(t, 'dencp')] = dencp
        prev_tile[(t, 'att3')] = atts[3]

    return [prev_tile[(t, 'sto')] for t in range(NT)]



def host_prep_l0f(inp):
    WT = np.ascontiguousarray(np.asarray(inp['L0_ffn_W']).T)
    WqT = np.ascontiguousarray(np.asarray(inp['L0_ffn_Wq']).T)
    WkT = np.ascontiguousarray(np.asarray(inp['L0_ffn_Wk']).T)
    AT = np.ascontiguousarray(np.asarray(inp['L0_ffn_A']).reshape(128, DM).T)
    Bm = np.asarray(inp['L0_ffn_Bm'])
    BmT = np.ascontiguousarray(np.transpose(Bm, (0, 2, 1)).reshape(128, DM))
    b = np.ascontiguousarray(np.asarray(inp['L0_ffn_b']).reshape(1, DM))
    return {"f_wT": WT, "f_wqT": WqT, "f_wkT": WkT, "f_aT": AT, "f_bmT": BmT, "f_b": b}


def build_l0f(nc, x_in, x_out, es, P, uniq=""):
    wT = nc.dram_tensor(uniq + "f_wT", [DM, DM], F32, kind="ExternalInput").ap()
    wqT = nc.dram_tensor(uniq + "f_wqT", [DM, 256], F32, kind="ExternalInput").ap()
    wkT = nc.dram_tensor(uniq + "f_wkT", [DM, 256], F32, kind="ExternalInput").ap()
    aT = nc.dram_tensor(uniq + "f_aT", [DM, 128], F32, kind="ExternalInput").ap()
    bmT = nc.dram_tensor(uniq + "f_bmT", [128, DM], F32, kind="ExternalInput").ap()
    bia = nc.dram_tensor(uniq + "f_b", [1, DM], F32, kind="ExternalInput").ap()
    wmaster = nc.dram_tensor(uniq + "f_wm", [DM, DM], BF16, kind="Internal").ap()

    T = lambda n, s, d: es.enter_context(nc.sbuf_tensor(uniq + "f_" + n, s, d))
    wq_bf = T("wq_bf", [128, NI, 256], BF16)
    wk_bf = T("wk_bf", [128, NI, 256], BF16)
    a_bf = T("a_bf", [128, NI, 128], BF16)
    bm_bf = T("bm_bf", [128, DM], BF16)
    b_bf = T("b_bf", [1, DM], BF16)
    ones1 = T("ones1", [1, 128], BF16)
    wstr = T("wstr", [128, 3, 4, 1024], BF16)
    xbuf = T("xbuf", [128, 3, DM], F32)
    u_bf = T("u_bf", [128, DM], BF16)
    uT = T("uT", [128, 2, NI, 128], BF16)
    tmin = T("tmin", [128, 1024], F32)
    ycb = T("ycb", [128, 2, DM], F32)
    qk_sb = T("qk_sb", [128, 256], F32)
    prod = T("prod", [128, 256], F32)
    sc8 = T("sc8", [128, 8], F32)
    m8 = T("m8", [128, 1], F32)
    mneg8 = T("mneg8", [128, 1], F32)
    e8 = T("e8", [128, 8], F32)
    s8 = T("s8", [128, 1], F32)
    rs8 = T("rs8", [128, 1], F32)
    w8 = T("w8", [128, 8], F32)
    wexp = T("wexp", [128, 128], F32)
    wh_bf = T("wh_bf", [128, 2, 128], BF16)
    whT = T("whT", [128, 2, 128], BF16)
    red32 = T("red32", [128, 32], F32)
    ysum4 = T("ysum4", [128, 2, 4], F32)
    yssq4 = T("yssq4", [128, 2, 4], F32)
    ln_x = alloc_ln(T, "x" + uniq, NT)
    ln_y = alloc_ln(T, "y" + uniq, NT)
    ycb_bc = ycb[:].bitcast(BF16)

    ps_y = es.enter_context(nc.psum_tensor(uniq + "f_ps_y", [128, 2, 1024], F32))
    ps_qk = es.enter_context(nc.psum_tensor(uniq + "f_ps_qk", [128, 512], F32))
    ps_h = es.enter_context(nc.psum_tensor(uniq + "f_ps_h", [128, 128], F32))

    A = P.add
    wload = []
    wload.append(A("gpsimd", lambda g: g.dma_start(
        out=wq_bf[:], in_=wqT.rearrange("(n p) c -> p n c", p=128)), inc=("s_w", 16), tag="wq"))
    wload.append(A("gpsimd", lambda g: g.dma_start(
        out=wk_bf[:], in_=wkT.rearrange("(n p) c -> p n c", p=128)), inc=("s_w", 16), tag="wk"))
    wload.append(A("gpsimd", lambda g: g.dma_start(
        out=a_bf[:], in_=aT.rearrange("(n p) c -> p n c", p=128)), inc=("s_w", 16), tag="at"))
    wload.append(A("gpsimd", lambda g: g.dma_start(
        out=bm_bf[:], in_=bmT[:]), inc=("s_w", 16), tag="bm"))
    wload.append(A("gpsimd", lambda g: g.dma_start(
        out=b_bf[:], in_=bia[:]), inc=("s_w", 16), tag="b"))
    pones = A("vector", lambda v: v.memset(ones1[:], 1.0), inc=("s_v", 1), tag="ones")

    mst_done = []
    prev_st = [None, None]
    for i in range(NI):
        ld = A("gpsimd", lambda g, i=i: g.dma_start(
            out=ycb_bc[:, i % 2, 0:DM], in_=wT[i*128:(i+1)*128, :]),
            deps=[prev_st[i % 2]], inc=(f"s_ml{i % 2}", 16), tag=f"mstl{i}")
        st = A("sync", lambda s, i=i: s.dma_start(
            out=wmaster[i*128:(i+1)*128, :], in_=ycb_bc[:, i % 2, 0:DM]),
            deps=[ld], inc=(f"s_ms{i % 2}", 16), tag=f"msts{i}")
        prev_st[i % 2] = st
        mst_done.append(st)

    wm8 = wmaster.rearrange("(n p) c -> p n c", p=128)

    prev = {}
    for u in range(NT // 2):
        tiles = (2 * u, 2 * u + 1)
        utrs = [None, None]
        whtrs = [None, None]
        for s, t in enumerate(tiles):
            xap = xbuf[:, t % 3, :]
            deps_ldx = [] if t < 3 else [prev[(t - 3, 'sto')]]
            ldx = A("sync", lambda sy, t=t, xap=xap: sy.dma_start(
                out=xap, in_=x_in[t*128:(t+1)*128, :]), deps=deps_ldx,
                inc=(f"s_x{t % 3}", 16), tag=f"ldx{t}")

            ctx_ln = {"u_bf": u_bf, "uT": None, "red32": red32}
            uop, xsq, _ = emit_ln_input(P, A, ctx_ln, t, xap, ln_x, [ldx],
                                        prev.get((t - 1, 'utr')), "f")
            utr = emit_utr(P, A, {"u_bf": u_bf, "uT": uT[:, s, :, :]}, uop,
                           prev.get((t - 2, 'lastmm')), f"f{t}")
            prev[(t, 'utr')] = utr
            utrs[s] = utr

        def emit_mix(s):
            t = tiles[s]
            utr = utrs[s]

            def qk_fn(te, s=s):
                r = None
                for i in range(NI):
                    te.matmul(ps_qk[:, 0:256], uT[:, s, i, :], wq_bf[:, i, :],
                              start=(i == 0), stop=(i == NI - 1))
                for i in range(NI):
                    r = te.matmul(ps_qk[:, 256:512], uT[:, s, i, :], wk_bf[:, i, :],
                                  start=(i == 0), stop=(i == NI - 1))
                return r
            qkmm = A("tensor", qk_fn, deps=[utr] + wload + [prev.get((t - 1, 'prod'))],
                     inc=("s_mm", 1), tag=f"qkmm{t}")

            def h_fn(te, s=s):
                r = None
                for i in range(NI):
                    r = te.matmul(ps_h[:], uT[:, s, i, :], a_bf[:, i, :],
                                  start=(i == 0), stop=(i == NI - 1))
                return r
            hmm_ = A("tensor", h_fn, deps=[utr] + wload + [prev.get((t - 1, 'whmul'))],
                     inc=("s_mm", 1), tag=f"hmm{t}")

            qev = A("scalar", lambda sc: sc.copy(out=qk_sb[:], in_=ps_qk[:, 0:256]),
                    deps=[qkmm], inc=("s_sc", 1), tag=f"qev{t}")
            prodo = A("vector", lambda v: v.tensor_tensor(
                out=prod[:], in0=ps_qk[:, 256:512], in1=qk_sb[:], op=ALU.mult),
                deps=[qev], inc=("s_v", 1), tag=f"prod{t}")
            prev[(t, 'prod')] = prodo

            def sc8_fn(v):
                P.mid(v.tensor_reduce(out=sc8[:], in_=prod[:].rearrange("p (a b) -> p a b", a=8),
                                      axis=AX.X, op=ALU.add))
                return v.tensor_reduce(out=m8[:], in_=sc8[:], axis=AX.X, op=ALU.max)
            sc8o = A("vector", sc8_fn, deps=[prodo], inc=("s_v", 1), tag=f"sc8{t}")

            mn8 = A("gpsimd", lambda g: g.tensor_scalar(
                out=mneg8[:], in0=m8[:], scalar1=-RS32, scalar2=None, op0=ALU.mult),
                deps=[sc8o], inc=("s_g", 1), tag=f"mn8{t}")
            e8o = A("scalar", lambda sc: sc.activation(
                out=e8[:], in_=sc8[:], func=AF.Exp, bias=mneg8[:], scale=RS32),
                deps=[mn8], inc=("s_sc", 1), tag=f"e8{t}")
            s8o = A("vector", lambda v: v.tensor_reduce(
                out=s8[:], in_=e8[:], axis=AX.X, op=ALU.add),
                deps=[e8o], inc=("s_v", 1), tag=f"s8{t}")

            def rs_fn(sc):
                P.mid(sc.activation(out=rs8[:], in_=s8[:], func=AF.Ln, bias=0.0, scale=1.0))
                return sc.activation(out=rs8[:], in_=rs8[:], func=AF.Exp, bias=0.0, scale=-1.0)
            rso = A("scalar", rs_fn, deps=[s8o], inc=("s_sc", 1), tag=f"rs{t}")

            w8o = A("vector", lambda v: v.tensor_scalar(
                out=w8[:], in0=e8[:], scalar1=rs8[:], scalar2=None, op0=ALU.mult),
                deps=[rso], inc=("s_v", 1), tag=f"w8{t}")
            wexpo = A("vector", lambda v: v.tensor_copy(
                out=wexp[:].rearrange("p (a b) -> p a b", a=8),
                in_=w8[:].rearrange("p (a b) -> p a b", b=1).broadcast_to((128, 8, 16))),
                deps=[w8o], inc=("s_v", 1), tag=f"wexp{t}")
            whmul = A("vector", lambda v, s=s: v.tensor_tensor(
                out=wh_bf[:, s, :], in0=ps_h[:], in1=wexp[:], op=ALU.mult),
                deps=[wexpo, hmm_], inc=("s_v", 1), tag=f"whmul{t}")
            prev[(t, 'whmul')] = whmul
            whtr = A("sync", lambda sy, s=s: sy.dma_start(
                out=whT[:, s, :], in_=wh_bf[:, s, :], transpose=True),
                deps=[whmul, prev.get((t - 2, 'q3mm'))], inc=("s_t", 16), tag=f"whtr{t}")
            whtrs[s] = whtr

        emit_mix(0)

        combs = {0: [None]*4, 1: [None]*4}
        for q in range(4):
            biases = [None, None]
            lastj = [None, None]
            for s, t in enumerate(tiles):
                bdeps = [pones] + wload
                if q > 0:
                    bdeps += [prev[(t, 'yev', q-1)]]
                elif t >= 2:
                    bdeps += [prev[(t-2, 'yev', 3)]]

                def bias_fn(te, q=q, s=s):
                    r = None
                    for c in range(2):
                        r = te.matmul(ps_y[:, s, c*512:(c+1)*512], ones1[:],
                                      b_bf[:, q*1024 + c*512:q*1024 + (c+1)*512],
                                      start=True, stop=False)
                    return r
                if s == 0:
                    biases[0] = A("tensor", bias_fn, deps=bdeps, inc=("s_mm", 1),
                                  tag=f"bias{q}_{t}")
                else:
                    biases[1] = (bias_fn, bdeps)

            for j in range(8):
                gi = q * 8 + j
                G = u * 32 + gi
                wsl = G % 3
                wdeps = [prev.get(('mmb', G - 3))]
                if G < 3:
                    wdeps += [mst_done[-1], mst_done[-2]]
                weng = "sync" if j % 2 == 0 else "scalar"
                wld = A(weng, lambda e, q=q, j=j, wsl=wsl: e.dma_start(
                    out=wstr[:, wsl, :, :],
                    in_=wm8[:, 4*j:4*j+4, q*1024:(q+1)*1024]),
                    deps=wdeps, inc=(f"s_wl{wsl}", 16), tag=f"wld{q}_{j}_{u}")

                for s, t in enumerate(tiles):
                    def g_fn(te, q=q, j=j, wsl=wsl, s=s):
                        r = None
                        for ii in range(4):
                            for c in range(2):
                                r = te.matmul(ps_y[:, s, c*512:(c+1)*512],
                                              uT[:, s, 4*j + ii, :],
                                              wstr[:, wsl, ii, c*512:(c+1)*512],
                                              start=False, stop=False)
                        return r
                    gdeps = [wld, utrs[s]]
                    if j == 0 and s == 0:
                        gdeps.append(biases[0])
                    gop = A("tensor", g_fn, deps=gdeps, inc=("s_mm", 1),
                            tag=f"qmmg{q}_{j}_{s}_{u}")
                    lastj[s] = gop
                    if s == 1:
                        prev[('mmb', G)] = gop
                    if j == 0 and s == 0 and isinstance(biases[1], tuple):
                        bias_fn_b, bdeps_b = biases[1]
                        biases[1] = A("tensor", bias_fn_b, deps=bdeps_b,
                                      inc=("s_mm", 1), tag=f"biasB{q}_{u}")
                if q == 0 and j == 1:
                    emit_mix(1)

            for s, t in enumerate(tiles):
                def comb_fn(te, q=q, s=s):
                    r = None
                    for c in range(2):
                        r = te.matmul(ps_y[:, s, c*512:(c+1)*512], whT[:, s, :],
                                      bm_bf[:, q*1024 + c*512:q*1024 + (c+1)*512],
                                      start=False, stop=True)
                    return r
                qmm = A("tensor", comb_fn, deps=[lastj[s], whtrs[s]],
                        inc=("s_mm", 1), tag=f"comb{q}_{s}_{u}")
                if q == 3:
                    prev[(t, 'q3mm')] = qmm
                    prev[(t, 'lastmm')] = lastj[s]

                ydeps = [qmm]
                pq = prev.get((t - 2, 'qr'))
                if pq is not None:
                    ydeps += pq.get(q, [])
                if u == 0 and q == 0:
                    ydeps += [mst_done[-1], mst_done[-2]]

                def yev_fn(sc, q=q, s=s):
                    r = None
                    for c2 in range(2):
                        r = sc.copy(out=ycb[:, s, q*1024 + c2*512:q*1024 + (c2+1)*512],
                                    in_=ps_y[:, s, c2*512:(c2+1)*512])
                    return r
                yev = A("scalar", yev_fn, deps=ydeps, inc=("s_sc", 1), tag=f"yev{q}_{t}")
                prev[(t, 'yev', q)] = yev
                combs[s][q] = yev

        for s, t in enumerate(tiles):
            ctx_ep = {"red32": red32, "tmin": tmin,
                      "ysum4": ysum4[:, s, :], "yssq4": yssq4[:, s, :]}
            pq = prev.get((t - 2, 'qr'))
            yst0_extra = pq['ysum'][0] if pq is not None else None
            sto, q_readers, _ = emit_quarter_stats_epilogue(
                P, A, ctx_ep, t, xbuf[:, t % 3, :], ln_y,
                lambda q, extra, s=s: combs[s][q], None, True,
                x_out[t*128:(t+1)*128, :], f"f{t}_", ysrc_sbuf=ycb[:, s, :],
                yst0_extra=yst0_extra)
            prev[(t, 'qr')] = q_readers
            prev[(t, 'sto')] = sto

    return [prev[(t, 'sto')] for t in range(NT)]



def host_prep_l1a(inp):
    L = 4096
    k = np.arange(L)
    Om = np.exp(-2j*np.pi*k/L)
    T0 = np.float32(np.imag(2*(1-Om)/(1+Om))).reshape(128, 32)
    cc = 2.0/(1+Om)
    cr = np.float32(cc.real).reshape(128, 32)
    ci = np.float32(cc.imag).reshape(128, 32)
    n0 = np.arange(64)
    F64r = np.float32(np.cos(2*np.pi*np.outer(n0, n0)/64))
    F64i = np.float32(np.sin(2*np.pi*np.outer(n0, n0)/64))
    Twr = np.float32(np.cos(2*np.pi*np.outer(n0, n0)/L))
    Twi = np.float32(np.sin(2*np.pi*np.outer(n0, n0)/L))
    I64 = np.eye(64, dtype=np.float32)
    I128 = np.eye(128, dtype=np.float32)
    return {
        "a_T0": np.ascontiguousarray(T0), "a_cr": np.ascontiguousarray(cr),
        "a_ci": np.ascontiguousarray(ci),
        "a_f64r": F64r, "a_f64in": np.ascontiguousarray(-F64i), "a_f64i": F64i,
        "a_twr": Twr, "a_twi": Twi, "a_i64": I64, "a_i128": I128,
        "a_lam": np.ascontiguousarray(np.asarray(inp['L1_attn_Lambda_ri'])),
        "a_p": np.ascontiguousarray(np.asarray(inp['L1_attn_p_ri'])),
        "a_q": np.ascontiguousarray(np.asarray(inp['L1_attn_q_ri'])),
        "a_vc": np.ascontiguousarray(np.asarray(inp['L1_attn_Vc_ri']).reshape(64, 128)),
        "a_ct": np.ascontiguousarray(np.asarray(inp['L1_attn_Ct_ri']).reshape(64, 2)),
        "a_bp": np.ascontiguousarray(np.asarray(inp['L1_attn_Bp'])),
        "a_d": np.ascontiguousarray(np.asarray(inp['L1_attn_D']).reshape(1, 1)),
        "a_ls": np.ascontiguousarray(np.asarray(inp['L1_attn_log_step']).reshape(1, 1)),
    }


def build_l1a(nc, x_in, x_out, es, P, uniq=""):
    D = lambda n, s: nc.dram_tensor(uniq + n, s, F32, kind="ExternalInput").ap()
    t0_d = D("a_T0", [128, 32]); cr_d = D("a_cr", [128, 32]); ci_d = D("a_ci", [128, 32])
    f64r_d = D("a_f64r", [64, 64]); f64in_d = D("a_f64in", [64, 64]); f64i_d = D("a_f64i", [64, 64])
    twr_d = D("a_twr", [64, 64]); twi_d = D("a_twi", [64, 64])
    i64_d = D("a_i64", [64, 64]); i128_d = D("a_i128", [128, 128])
    lam_d = D("a_lam", [64, 2]); p_d = D("a_p", [64, 2]); q_d = D("a_q", [64, 2])
    vc_d = D("a_vc", [64, 128]); ct_d = D("a_ct", [64, 2]); bp_d = D("a_bp", [64, 1])
    dd_d = D("a_d", [1, 1]); ls_d = D("a_ls", [1, 1])
    scr_d = nc.dram_tensor(uniq + "a_scr", [16384], F32, kind="Internal").ap()

    T = lambda n, s, d: es.enter_context(nc.sbuf_tensor(uniq + "as_" + n, s, d))
    sm = T("sm", [64, 224], F32)
    vpk = T("vpk", [64, 16], F32)
    bcz = T("bcz", [128, 16, 64], F32)
    Gt = T("Gt", [128, 32], F32)
    crs = T("crs", [128, 32], F32); cis = T("cis", [128, 32], F32)
    di3 = T("di3", [128, 2048], F32)
    d23 = T("d23", [128, 2048], F32)
    recr = T("recr", [128, 2048], F32)
    reci = T("reci", [128, 2048], F32)
    kk = T("kk", [128, 8, 32], F32)
    cw = T("cw", [128, 8, 32], F32)
    f64r = T("f64r", [64, 64], F32); f64in = T("f64in", [64, 64], F32); f64i = T("f64i", [64, 64], F32)
    twr = T("twr", [64, 64], F32); twi = T("twi", [64, 64], F32)
    i64 = T("i64", [64, 64], F32); i128 = T("i128", [128, 128], F32)
    amr = T("amr", [64, 64], F32); ami = T("ami", [64, 64], F32)
    a2r = T("a2r", [64, 64], F32); a2i = T("a2i", [64, 64], F32)
    a2rt = T("a2rt", [64, 64], F32); a2it = T("a2it", [64, 64], F32)
    tw1 = T("tw1", [64, 64], F32); tw2 = T("tw2", [64, 64], F32)
    xsb = T("xsb", [64, 64], F32)
    ksb = T("ksb", [1, 4096], F32)
    kbf = T("kbf", [1, 4352], BF16)
    ball = T("ball", [128, 35 * 128], BF16)
    dscal = T("dscal", [128, 1], F32)
    dg = T("dg", [128, 128], F32)
    rstep = T("rstep", [1, 1], F32)
    rstep_b = T("rstep_b", [128, 1], F32)
    onesf = T("onesf", [1, 128], F32)
    xbuf = T("xbuf", [128, 2, DM], F32)
    u_bf = T("u_bf", [128, DM], BF16)
    uT = T("uT", [128, NI, 128], BF16)
    tmin = T("tmin", [128, 1024], F32)
    ysb = T("ysb", [128, 2048], F32)
    red32 = T("red32", [128, 32], F32)
    ysum4 = T("ysum4", [128, 4], F32)
    yssq4 = T("yssq4", [128, 4], F32)
    ln_x = alloc_ln(T, "x" + uniq, NT)
    ln_y = alloc_ln(T, "y" + uniq, NT)

    ps_y = es.enter_context(nc.psum_tensor(uniq + "a_ps_y", [128, 2, 1024], F32))
    ps_k = es.enter_context(nc.psum_tensor(uniq + "a_ps_k", [64, 256], F32))
    ps_b = es.enter_context(nc.psum_tensor(uniq + "a_ps_b", [128, 8], F32))

    ctx = {"u_bf": u_bf, "uT": uT, "red32": red32, "tmin": tmin,
           "ysum4": ysum4, "yssq4": yssq4, "ps_y": ps_y, "ysb": ysb}

    A = P.add
    lds = []
    for ap_d, ap_s, tg in [(t0_d, Gt, "t0"), (cr_d, crs, "cr"), (ci_d, cis, "ci"),
                           (f64r_d, f64r, "f64r"), (f64in_d, f64in, "f64in"), (f64i_d, f64i, "f64i"),
                           (twr_d, twr, "twr"), (twi_d, twi, "twi"), (i64_d, i64, "i64"),
                           (i128_d, i128, "i128"), (dd_d, rstep, "dd0")]:
        if tg == "dd0":
            continue
        lds.append(A("sync", lambda s, a=ap_d, b=ap_s: s.dma_start(out=b[:], in_=a[:]),
                     inc=("s_pre", 16), tag="ld" + tg))
    small_map = [(lam_d, 0, 2), (p_d, 2, 2), (q_d, 4, 2), (ct_d, 6, 2), (vc_d, 8, 128), (bp_d, 136, 1)]
    for ap_d, off, w in small_map:
        lds.append(A("sync", lambda s, a=ap_d, o=off, w=w: s.dma_start(
            out=sm[:, o:o + w], in_=a[:]), inc=("s_pre", 16), tag=f"lds{off}"))
    ldls = A("sync", lambda s: s.dma_start(out=rstep[:], in_=ls_d[:]), inc=("s_pre", 16), tag="ldls")
    lddd = A("sync", lambda s: s.dma_start(out=dscal[0:1, :], in_=dd_d[:]), inc=("s_pre", 16), tag="lddd")
    lds += [ldls, lddd]

    rse = A("scalar", lambda sc: sc.activation(out=rstep[:], in_=rstep[:], func=AF.Exp,
                                               bias=0.0, scale=-1.0),
            deps=lds, inc=("s_sc", 1), tag="rse")
    pones = A("vector", lambda v: v.memset(onesf[:], 1.0), inc=("s_v", 1), tag="onesf")

    def bc_fn(te):
        te.matmul(ps_b[:, 0:1], onesf[:], rstep[:], start=True, stop=True)
        return te.matmul(ps_b[:, 1:2], onesf[:], dscal[0:1, :], start=True, stop=True)
    bcmm = A("tensor", bc_fn, deps=[rse, pones] + lds, inc=("s_mm", 1), tag="bcmm")
    rsb = A("vector", lambda v: v.tensor_copy(out=rstep_b[:], in_=ps_b[:, 0:1]),
            deps=[bcmm], inc=("s_v", 1), tag="rsbc")
    dbc = A("vector", lambda v: v.tensor_copy(out=dscal[:], in_=ps_b[:, 1:2]),
            deps=[bcmm], inc=("s_v", 1), tag="dbcc")

    wbp = A("sync", lambda s: s.dma_start(out=scr_d[0:64], in_=sm[:, 136:137]),
            deps=lds, inc=("s_pre", 16), tag="wbp")
    rbp = A("sync", lambda s: s.dma_start(
        out=sm[:, 140:204].rearrange("p m -> p m") if False else sm[:, 140:204],
        in_=scr_d[0:64].rearrange("(o m) -> o m", o=1).broadcast_to((64, 64))),
        deps=[wbp], inc=("s_pre", 16), tag="rbp")

    def bv_fn(v):
        vcr = sm[:, 8:136].rearrange("p (m two) -> p m two", two=2)[:, :, 0:1].rearrange("p m one -> p (m one)")
        vci = sm[:, 8:136].rearrange("p (m two) -> p m two", two=2)[:, :, 1:2].rearrange("p m one -> p (m one)")
        bpb = sm[:, 140:204]
        P.mid(v.tensor_tensor(out=vpk[:, 8:9].broadcast_to((64, 1)) if False else cw[0:64, 0, 0:64] if False else sm[:, 76+128:140+64] if False else di3[0:64, 0:64], in0=vcr, in1=bpb, op=ALU.mult))
        P.mid(v.tensor_reduce(out=vpk[:, 8:9], in_=di3[0:64, 0:64], axis=AX.X, op=ALU.add))
        P.mid(v.tensor_tensor(out=di3[0:64, 0:64], in0=vci, in1=bpb, op=ALU.mult))
        return v.tensor_reduce(out=vpk[:, 9:10], in_=di3[0:64, 0:64], axis=AX.X, op=ALU.add)
    bv = A("vector", bv_fn, deps=lds + [rbp], inc=("s_v", 1), tag="bv")

    def vprod_fn(v):
        ctr = sm[:, 6:7]; cti = sm[:, 7:8]
        pr = sm[:, 2:3]; pi = sm[:, 3:4]
        qr = sm[:, 4:5]; qi = sm[:, 5:6]
        br = vpk[:, 8:9]; bi = vpk[:, 9:10]
        P.mid(v.tensor_tensor(out=vpk[:, 0:1], in0=ctr, in1=br, op=ALU.mult))
        P.mid(v.tensor_tensor(out=vpk[:, 10:11], in0=cti, in1=bi, op=ALU.mult))
        P.mid(v.tensor_tensor(out=vpk[:, 0:1], in0=vpk[:, 0:1], in1=vpk[:, 10:11], op=ALU.add))
        P.mid(v.tensor_tensor(out=vpk[:, 1:2], in0=ctr, in1=bi, op=ALU.mult))
        P.mid(v.tensor_tensor(out=vpk[:, 10:11], in0=cti, in1=br, op=ALU.mult))
        P.mid(v.tensor_tensor(out=vpk[:, 1:2], in0=vpk[:, 1:2], in1=vpk[:, 10:11], op=ALU.subtract))
        P.mid(v.tensor_tensor(out=vpk[:, 2:3], in0=ctr, in1=pr, op=ALU.mult))
        P.mid(v.tensor_tensor(out=vpk[:, 10:11], in0=cti, in1=pi, op=ALU.mult))
        P.mid(v.tensor_tensor(out=vpk[:, 2:3], in0=vpk[:, 2:3], in1=vpk[:, 10:11], op=ALU.add))
        P.mid(v.tensor_tensor(out=vpk[:, 3:4], in0=ctr, in1=pi, op=ALU.mult))
        P.mid(v.tensor_tensor(out=vpk[:, 10:11], in0=cti, in1=pr, op=ALU.mult))
        P.mid(v.tensor_tensor(out=vpk[:, 3:4], in0=vpk[:, 3:4], in1=vpk[:, 10:11], op=ALU.subtract))
        P.mid(v.tensor_tensor(out=vpk[:, 4:5], in0=qr, in1=br, op=ALU.mult))
        P.mid(v.tensor_tensor(out=vpk[:, 10:11], in0=qi, in1=bi, op=ALU.mult))
        P.mid(v.tensor_tensor(out=vpk[:, 4:5], in0=vpk[:, 4:5], in1=vpk[:, 10:11], op=ALU.add))
        P.mid(v.tensor_tensor(out=vpk[:, 5:6], in0=qr, in1=bi, op=ALU.mult))
        P.mid(v.tensor_tensor(out=vpk[:, 10:11], in0=qi, in1=br, op=ALU.mult))
        P.mid(v.tensor_tensor(out=vpk[:, 5:6], in0=vpk[:, 5:6], in1=vpk[:, 10:11], op=ALU.subtract))
        P.mid(v.tensor_tensor(out=vpk[:, 6:7], in0=qr, in1=pr, op=ALU.mult))
        P.mid(v.tensor_tensor(out=vpk[:, 10:11], in0=qi, in1=pi, op=ALU.mult))
        P.mid(v.tensor_tensor(out=vpk[:, 6:7], in0=vpk[:, 6:7], in1=vpk[:, 10:11], op=ALU.add))
        P.mid(v.tensor_tensor(out=vpk[:, 7:8], in0=qr, in1=pi, op=ALU.mult))
        P.mid(v.tensor_tensor(out=vpk[:, 10:11], in0=qi, in1=pr, op=ALU.mult))
        P.mid(v.tensor_tensor(out=vpk[:, 7:8], in0=vpk[:, 7:8], in1=vpk[:, 10:11], op=ALU.subtract))
        P.mid(v.tensor_copy(out=vpk[:, 8:9], in_=sm[:, 0:1]))
        P.mid(v.tensor_copy(out=vpk[:, 9:10], in_=sm[:, 1:2]))
        return v.tensor_tensor(out=vpk[:, 10:11], in0=sm[:, 0:1], in1=sm[:, 0:1], op=ALU.mult)
    vprod = A("vector", vprod_fn, deps=[bv], inc=("s_v", 1), tag="vprod")

    wpk = A("sync", lambda s: s.dma_start(
        out=scr_d[256:256 + 11 * 64].rearrange("(x n) -> x n", x=11).transpose([1, 0]),
        in_=vpk[:, 0:11], allow_slow_non_contiguous=True),
        deps=[vprod], inc=("s_pre", 16), tag="wpk")
    rpk = A("sync", lambda s: s.dma_start(
        out=bcz[:, 0:11, :],
        in_=scr_d[256:256 + 11 * 64].rearrange("(x n) -> x n", x=11).unsqueeze(0).broadcast_to((128, 11, 64))),
        deps=[wpk], inc=("s_pre", 16), tag="rpk")

    def gt_fn(v):
        return v.tensor_scalar(out=Gt[:], in0=Gt[:], scalar1=rstep_b[:], scalar2=None, op0=ALU.mult)
    gt = A("vector", gt_fn, deps=[rsb] + lds, inc=("s_v", 1), tag="gt")

    def cauchy_fn(v):
        lam_i = bcz[:, 9:10, :].broadcast_to((128, 32, 64))
        lam_r = bcz[:, 8:9, :].broadcast_to((128, 32, 64))
        lr2 = bcz[:, 10:11, :].broadcast_to((128, 32, 64))
        d3 = di3[:].rearrange("p (j n) -> p j n", j=32)
        d2 = d23[:].rearrange("p (j n) -> p j n", j=32)
        rr = recr[:].rearrange("p (j n) -> p j n", j=32)
        ri = reci[:].rearrange("p (j n) -> p j n", j=32)
        g3 = Gt[:].rearrange("p (j o) -> p j o", o=1).broadcast_to((128, 32, 64))
        P.mid(v.tensor_tensor(out=d3, in0=g3, in1=lam_i, op=ALU.subtract))
        P.mid(v.tensor_tensor(out=d2, in0=d3, in1=d3, op=ALU.mult))
        P.mid(v.tensor_tensor(out=d2, in0=d2, in1=lr2, op=ALU.add))
        P.mid(v.reciprocal(out=d23[:], in_=d23[:]))
        P.mid(v.tensor_tensor(out=rr, in0=lam_r, in1=d2, op=ALU.mult))
        P.mid(v.tensor_scalar(out=recr[:], in0=recr[:], scalar1=-1.0, scalar2=None, op0=ALU.mult))
        P.mid(v.tensor_tensor(out=ri, in0=d3, in1=d2, op=ALU.mult))
        return v.tensor_scalar(out=reci[:], in0=reci[:], scalar1=-1.0, scalar2=None, op0=ALU.mult)
    cau = A("vector", cauchy_fn, deps=[gt, rpk] + lds, inc=("s_v", 1), tag="cauchy")

    def ksums_fn(v):
        rr = recr[:].rearrange("p (j n) -> p j n", j=32)
        ri = reci[:].rearrange("p (j n) -> p j n", j=32)
        for vi_ in range(4):
            vr = bcz[:, 2*vi_:2*vi_+1, :].broadcast_to((128, 32, 64))
            vv = bcz[:, 2*vi_+1:2*vi_+2, :].broadcast_to((128, 32, 64))
            d3 = di3[:].rearrange("p (j n) -> p j n", j=32)
            P.mid(v.tensor_tensor(out=d3, in0=vr, in1=rr, op=ALU.mult))
            P.mid(v.tensor_reduce(out=kk[:, 2*vi_, :], in_=d3, axis=AX.X, op=ALU.add))
            P.mid(v.tensor_tensor(out=d3, in0=vv, in1=ri, op=ALU.mult))
            P.mid(v.tensor_reduce(out=cw[:, 0, :], in_=d3, axis=AX.X, op=ALU.add))
            P.mid(v.tensor_tensor(out=kk[:, 2*vi_, :], in0=kk[:, 2*vi_, :], in1=cw[:, 0, :], op=ALU.subtract))
            P.mid(v.tensor_tensor(out=d3, in0=vr, in1=ri, op=ALU.mult))
            P.mid(v.tensor_reduce(out=kk[:, 2*vi_+1, :], in_=d3, axis=AX.X, op=ALU.add))
            P.mid(v.tensor_tensor(out=d3, in0=vv, in1=rr, op=ALU.mult))
            P.mid(v.tensor_reduce(out=cw[:, 0, :], in_=d3, axis=AX.X, op=ALU.add))
            P.mid(v.tensor_tensor(out=kk[:, 2*vi_+1, :], in0=kk[:, 2*vi_+1, :], in1=cw[:, 0, :], op=ALU.add))
        return v.nop()
    ksums = A("vector", ksums_fn, deps=[cau], inc=("s_v", 1), tag="ksums")

    def aroots_fn(v):
        k00r, k00i = kk[:, 0, :], kk[:, 1, :]
        k01r, k01i = kk[:, 2, :], kk[:, 3, :]
        k10r, k10i = kk[:, 4, :], kk[:, 5, :]
        k11r, k11i = kk[:, 6, :], kk[:, 7, :]
        wr, wi = cw[:, 0, :], cw[:, 1, :]
        w2, iwr = cw[:, 2, :], cw[:, 3, :]
        iwi, ur = cw[:, 4, :], cw[:, 5, :]
        ui, t = cw[:, 6, :], cw[:, 7, :]
        P.mid(v.tensor_scalar(out=wr, in0=k11r, scalar1=1.0, scalar2=None, op0=ALU.add))
        P.mid(v.tensor_copy(out=wi, in_=k11i))
        P.mid(v.tensor_tensor(out=w2, in0=wr, in1=wr, op=ALU.mult))
        P.mid(v.tensor_tensor(out=t, in0=wi, in1=wi, op=ALU.mult))
        P.mid(v.tensor_tensor(out=w2, in0=w2, in1=t, op=ALU.add))
        P.mid(v.reciprocal(out=w2, in_=w2))
        P.mid(v.tensor_tensor(out=iwr, in0=wr, in1=w2, op=ALU.mult))
        P.mid(v.tensor_tensor(out=iwi, in0=wi, in1=w2, op=ALU.mult))
        P.mid(v.tensor_scalar(out=iwi, in0=iwi, scalar1=-1.0, scalar2=None, op0=ALU.mult))
        P.mid(v.tensor_tensor(out=ur, in0=k01r, in1=k10r, op=ALU.mult))
        P.mid(v.tensor_tensor(out=t, in0=k01i, in1=k10i, op=ALU.mult))
        P.mid(v.tensor_tensor(out=ur, in0=ur, in1=t, op=ALU.subtract))
        P.mid(v.tensor_tensor(out=ui, in0=k01r, in1=k10i, op=ALU.mult))
        P.mid(v.tensor_tensor(out=t, in0=k01i, in1=k10r, op=ALU.mult))
        P.mid(v.tensor_tensor(out=ui, in0=ui, in1=t, op=ALU.add))
        P.mid(v.tensor_tensor(out=t, in0=ur, in1=iwr, op=ALU.mult))
        P.mid(v.tensor_tensor(out=wr, in0=ui, in1=iwi, op=ALU.mult))
        P.mid(v.tensor_tensor(out=wr, in0=t, in1=wr, op=ALU.subtract))
        P.mid(v.tensor_tensor(out=t, in0=ur, in1=iwi, op=ALU.mult))
        P.mid(v.tensor_tensor(out=wi, in0=ui, in1=iwr, op=ALU.mult))
        P.mid(v.tensor_tensor(out=wi, in0=t, in1=wi, op=ALU.add))
        P.mid(v.tensor_tensor(out=ur, in0=k00r, in1=wr, op=ALU.subtract))
        P.mid(v.tensor_tensor(out=ui, in0=k00i, in1=wi, op=ALU.subtract))
        P.mid(v.tensor_tensor(out=kk[:, 0, :], in0=ur, in1=crs[:], op=ALU.mult))
        P.mid(v.tensor_tensor(out=t, in0=ui, in1=cis[:], op=ALU.mult))
        P.mid(v.tensor_tensor(out=kk[:, 0, :], in0=kk[:, 0, :], in1=t, op=ALU.subtract))
        P.mid(v.tensor_tensor(out=kk[:, 1, :], in0=ur, in1=cis[:], op=ALU.mult))
        P.mid(v.tensor_tensor(out=t, in0=ui, in1=crs[:], op=ALU.mult))
        return v.tensor_tensor(out=kk[:, 1, :], in0=kk[:, 1, :], in1=t, op=ALU.add)
    aro = A("vector", aroots_fn, deps=[ksums], inc=("s_v", 1), tag="aroots")

    war = A("sync", lambda s: s.dma_start(out=scr_d[1024:1024+4096].rearrange("(p j) -> p j", p=128),
                                          in_=kk[:, 0, :]), deps=[aro], inc=("s_pre", 16), tag="war")
    wai = A("sync", lambda s: s.dma_start(out=scr_d[5120:5120+4096].rearrange("(p j) -> p j", p=128),
                                          in_=kk[:, 1, :]), deps=[aro], inc=("s_pre", 16), tag="wai")
    rar = A("sync", lambda s: s.dma_start(out=amr[:], in_=scr_d[1024:1024+4096].rearrange("(a b) -> a b", a=64)),
            deps=[war, wai], inc=("s_pre", 16), tag="rar")
    rai = A("sync", lambda s: s.dma_start(out=ami[:], in_=scr_d[5120:5120+4096].rearrange("(a b) -> a b", a=64)),
            deps=[war, wai], inc=("s_pre", 16), tag="rai")

    def st1_fn(te):
        te.matmul(ps_k[:, 0:64], f64r[:], amr[:], start=True, stop=False)
        te.matmul(ps_k[:, 0:64], f64in[:], ami[:], start=False, stop=True)
        te.matmul(ps_k[:, 64:128], f64r[:], ami[:], start=True, stop=False)
        return te.matmul(ps_k[:, 64:128], f64i[:], amr[:], start=False, stop=True)
    st1 = A("tensor", st1_fn, deps=[rar, rai] + lds, inc=("s_mm", 1), tag="st1")

    def tw_fn(v):
        P.mid(v.tensor_tensor(out=tw1[:], in0=ps_k[:, 0:64], in1=twr[:], op=ALU.mult))
        P.mid(v.tensor_tensor(out=tw2[:], in0=ps_k[:, 64:128], in1=twi[:], op=ALU.mult))
        P.mid(v.tensor_tensor(out=a2r[:], in0=tw1[:], in1=tw2[:], op=ALU.subtract))
        P.mid(v.tensor_tensor(out=tw1[:], in0=ps_k[:, 0:64], in1=twi[:], op=ALU.mult))
        P.mid(v.tensor_tensor(out=tw2[:], in0=ps_k[:, 64:128], in1=twr[:], op=ALU.mult))
        return v.tensor_tensor(out=a2i[:], in0=tw1[:], in1=tw2[:], op=ALU.add)
    tw = A("vector", tw_fn, deps=[st1], inc=("s_v", 1), tag="tw")

    def tr_fn(te):
        te.transpose(ps_k[:, 128:192], a2r[:], i64[:])
        return te.transpose(ps_k[:, 192:256], a2i[:], i64[:])
    tr = A("tensor", tr_fn, deps=[tw], inc=("s_mm", 1), tag="tr")
    trev = A("vector", lambda v: [P.mid(v.tensor_copy(out=a2rt[:], in_=ps_k[:, 128:192])),
                                  v.tensor_copy(out=a2it[:], in_=ps_k[:, 192:256])][-1],
             deps=[tr], inc=("s_v", 1), tag="trev")

    def st3_fn(te):
        te.matmul(ps_k[:, 0:64], a2rt[:], f64r[:], start=True, stop=False)
        return te.matmul(ps_k[:, 0:64], a2it[:], f64in[:], start=False, stop=True)
    st3 = A("tensor", st3_fn, deps=[trev], inc=("s_mm", 1), tag="st3")
    xev = A("vector", lambda v: v.tensor_scalar(out=xsb[:], in0=ps_k[:, 0:64],
                                                scalar1=1.0/4096, scalar2=None, op0=ALU.mult),
            deps=[st3], inc=("s_v", 1), tag="xev")

    wk = A("sync", lambda s: s.dma_start(
        out=scr_d[1024:1024+4096].rearrange("(b a) -> a b", a=64), in_=xsb[:],
        allow_slow_non_contiguous=True), deps=[xev], inc=("s_pre", 16), tag="wk")
    rk = A("sync", lambda s: s.dma_start(out=ksb[:], in_=scr_d[1024:1024+4096].rearrange("(o n) -> o n", o=1)),
           deps=[wk], inc=("s_pre", 16), tag="rk")

    kz = A("vector", lambda v: v.memset(kbf[:], 0.0), inc=("s_v", 1), tag="kz")
    kcast = A("vector", lambda v: v.tensor_copy(out=kbf[0:1, 127:127+4096], in_=ksb[:]),
              deps=[rk, kz], inc=("s_v", 1), tag="kcast")

    bz = A("vector", lambda v: v.memset(ball[:, 0:384], 0.0), inc=("s_v", 1), tag="bz")
    def ballfn(s):
        for r in range(128):
            s.dma_start(out=ball[r:r+1, 384:4480],
                        in_=kbf[0:1, 127 - r: 4223 - r]).then_inc(P.sem_handles["s_pre"], 16)
        return "self"
    ballop = A("sync", ballfn, deps=[kcast], inc=("s_pre", 16 * 128), tag="ball", self_inc=True)

    dgo = A("vector", lambda v: v.tensor_scalar(out=dg[:], in0=i128[:], scalar1=dscal[:],
                                                scalar2=None, op0=ALU.mult),
            deps=[dbc] + lds, inc=("s_v", 1), tag="dg")
    dadd = A("vector", lambda v: v.tensor_tensor(out=ball[:, 384:512], in0=ball[:, 384:512],
                                                 in1=dg[:], op=ALU.add),
             deps=[dgo, ballop, bz], inc=("s_v", 1), tag="dadd")

    prevt = {}
    for t in range(NT):
        sl = t % 2
        xap = xbuf[:, sl, :]
        deps_ldx = [dadd] if t < 2 else [prevt[(t - 2, 'sto')]]
        ldx = A("sync", lambda s, t=t, xap=xap: s.dma_start(
            out=xap, in_=x_in[t*128:(t+1)*128, :]), deps=deps_ldx,
            inc=(f"s_x{t % 2}", 16), tag=f"ldx{t}")

        uop, xsq, _ = emit_ln_input(P, A, ctx, t, xap, ln_x, [ldx],
                                    prevt.get((t - 1, 'epi')), "a")
        utr = emit_utr(P, A, ctx, uop, prevt.get((t - 1, 'lastmm')), f"a{t}")

        def make_quarter(q, extra_deps, t=t, utr=utr):
            def q_fn(te, q=q):
                r = None
                for c2 in range(2):
                    Js = 8 * q + 4 * c2
                    imax = min(32, Js + 4)
                    for I in range(imax):
                        off = (Js - I + 3) * 128
                        r = te.matmul(ps_y[:, q % 2, c2*512:(c2+1)*512],
                                      uT[:, I, :], ball[:, off:off + 512],
                                      start=(I == 0), stop=(I == imax - 1))
                return r
            deps = [utr, dadd] + list(extra_deps)
            op = A("tensor", q_fn, deps=deps, inc=("s_mm", 1), tag=f"tq{q}_{t}")
            if q == 3:
                prevt[(t, 'lastmm')] = op
            return op

        sto, q_readers, _ = emit_quarter_stats_epilogue(
            P, A, ctx, t, xap, ln_y, make_quarter, prevt.get((t - 1, 'qr')), True,
            x_out[t*128:(t+1)*128, :], f"a{t}_")
        prevt[(t, 'qr')] = q_readers
        prevt[(t, 'sto')] = sto
        prevt[(t, 'epi')] = sto

    return [prevt[(t, 'sto')] for t in range(NT)]



def host_prep_dequant(inp, core):
    qw = np.asarray(inp['L1_ffn_qweight'])
    lut8 = np.asarray(inp['L1_ffn_lut8'])
    lut4 = np.asarray(inp['L1_ffn_lut4'])
    base = core * 512
    o_idx = np.zeros((128, 64), np.int64)
    for c in range(8):
        for r in range(16):
            o_idx[16 * c + r, :] = base + np.arange(64) * 8 + c
    qq = qw[:, o_idx, :]
    qq = np.transpose(qq, (2, 1, 0, 3))
    qq = qq.reshape(16, 4, 128, 8, 128)
    qq = np.ascontiguousarray(np.transpose(qq, (0, 2, 1, 3, 4)))
    return {
        "dq_qw": qq.reshape(16, 128, 4 * 8 * 128),
        "dq_l8": np.ascontiguousarray(lut8[base:base + 512]),
        "dq_l4": np.ascontiguousarray(lut4[base:base + 512]),
    }


def build_dequant(nc, es, P, uniq="", wl_all_t=None, dd_all_t=None):
    qw_d = nc.dram_tensor(uniq + "dq_qw", [16, 128, 4096], I32, kind="ExternalInput").ap()
    l8_d = nc.dram_tensor(uniq + "dq_l8", [512, 256], F32, kind="ExternalInput").ap()
    l4_d = nc.dram_tensor(uniq + "dq_l4", [512, 16], F32, kind="ExternalInput").ap()
    fused = wl_all_t is not None
    okind = "Internal" if fused else "ExternalOutput"
    wl_t = nc.dram_tensor(uniq + "dq_wl", [DM, 512], BF16, kind=okind)
    dd_t = nc.dram_tensor(uniq + "dq_d", [DM, 512], BF16, kind=okind)
    wl_o = wl_t.ap()
    dd_o = dd_t.ap()
    st_d = nc.dram_tensor(uniq + "dq_stage", [1024, 256], BF16, kind="Internal").ap()

    T = lambda n, s, d: es.enter_context(nc.sbuf_tensor(uniq + "dqs_" + n, s, d))
    l8b = T("l8b", [128, 4, 256], BF16)
    l4b = T("l4b", [128, 4, 16], BF16)
    l4t = T("l4t", [128, 4, 256], BF16)
    db = T("db", [128, 4, 256], BF16)
    wqw = T("wqw", [128, 2, 4096], I32)
    shp = T("shp", [128, 1024], I32)
    pvec = T("pvec", [128, 1], I32)
    bits = T("bits", [128, 1024], I32)
    idx32 = T("idx32", [128, 1024], I32)
    idx16 = T("idx16", [128, 2, 1024], U16)
    outg = T("outg", [128, 2, 4, 1024], BF16)
    dall = T("dall", [128, 64, 256], BF16)
    wlT_sb = T("wlT_sb", [128, 32, 512], BF16)
    ddT_sb = T("ddT_sb", [128, 32, 512], BF16)
    rowv = T("rowv", [128, 1], I32)
    colv = T("colv", [128, 128], I32)
    rowf = T("rowf", [128, 1], F32)
    colf = T("colf", [128, 128], F32)
    idT = T("idT", [128, 128], BF16)
    ps_tr = es.enter_context(nc.psum_tensor(uniq + "dq_ps_tr", [128, 2, 32, 128], BF16))

    A = P.add
    ld8 = A("gpsimd", lambda g: g.dma_start(
        out=l8b[:], in_=l8_d.rearrange("(n p) c -> p n c", p=128)), inc=("s_w", 16), tag="ld8")
    ld4 = A("gpsimd", lambda g: g.dma_start(
        out=l4b[:], in_=l4_d.rearrange("(n p) c -> p n c", p=128)), inc=("s_w", 16), tag="ld4")

    l4to = A("vector", lambda v: v.tensor_copy(
        out=l4t[:].rearrange("p n (hi lo) -> p n hi lo", hi=16),
        in_=l4b[:].rearrange("p n (o lo) -> p n o lo", o=1).broadcast_to((128, 4, 16, 16))),
        deps=[ld4, ld8], inc=("s_v", 1), tag="l4t")
    dbo = A("vector", lambda v: v.tensor_tensor(out=db[:], in0=l8b[:], in1=l4t[:], op=ALU.subtract),
            deps=[l4to], inc=("s_v", 1), tag="db")
    stg1 = A("sync", lambda s: s.dma_start(
        out=st_d[0:512, :].rearrange("(n p) c -> p n c", p=128), in_=l4t[:]),
        deps=[l4to, dbo], inc=("s_ms0", 16), tag="stg1")
    stg2 = A("sync", lambda s: s.dma_start(
        out=st_d[512:1024, :].rearrange("(n p) c -> p n c", p=128), in_=db[:]),
        deps=[dbo], inc=("s_ms0", 16), tag="stg2")

    def dall_fn(g):
        for c in range(8):
            for m in range(2):
                g.dma_start(
                    out=dall[16 * c + 8 * m:16 * c + 8 * m + 8, :, :],
                    in_=bass_rust.AP(tensor=st_d.tensor, offset=(512 * m + c) * 256,
                                     ap=[[0, 8], [8 * 256, 64], [1, 256]])
                ).then_inc(P.sem_handles["s_w"], 16)
        return "self"
    dallop = A("gpsimd", dall_fn, deps=[stg1, stg2], inc=("s_w", 16 * 16),
               tag="dall", self_inc=True)

    rio = A("gpsimd", lambda g: g.iota(rowv[:], [[0, 1]], base=0, channel_multiplier=1),
            inc=("s_g", 1), tag="rio")
    cio = A("gpsimd", lambda g: g.iota(colv[:], [[1, 128]], base=0, channel_multiplier=0),
            deps=[rio], inc=("s_g", 1), tag="cio")
    def ido_fn(g):
        P.mid(g.tensor_copy(out=rowf[:], in_=rowv[:]))
        P.mid(g.tensor_copy(out=colf[:], in_=colv[:]))
        return g.tensor_scalar(
            out=idT[:], in0=colf[:], scalar1=rowf[:], scalar2=None, op0=ALU.is_equal)
    ido = A("gpsimd", ido_fn, deps=[cio], inc=("s_g", 1), tag="ido")

    pio = A("gpsimd", lambda g: g.iota(pvec[:], [[0, 1]], base=0, channel_multiplier=1),
            inc=("s_g", 1), tag="pio")
    sio = A("gpsimd", lambda g: g.iota(shp[:], [[0, 4], [0, 128], [16, 2]], base=0,
                                       channel_multiplier=0),
            deps=[pio], inc=("s_g", 1), tag="sio")
    def shp_fn(v):
        P.mid(v.tensor_scalar(out=pvec[:], in0=pvec[:], scalar1=15, scalar2=None, op0=ALU.bitwise_and))
        P.mid(v.tensor_copy(out=bits[:], in_=pvec[:].broadcast_to((128, 1024))))
        return v.tensor_tensor(out=shp[:], in0=shp[:], in1=bits[:], op=ALU.add)
    shpo = A("vector", shp_fn, deps=[sio], inc=("s_v", 1), tag="shp")

    prev = {}
    wl_copies = []
    dd_copies = []
    for pc in range(NPIECE):
        sl = pc % 2
        wdeps = [prev.get(('idx', pc - 2))]
        ldq = A("sync", lambda s, pc=pc, sl=sl: s.dma_start(
            out=wqw[:, sl, :], in_=qw_d[pc]), deps=wdeps,
            inc=(f"s_x{sl}", 16), tag=f"ldq{pc}")

        def unpack_fn(v, pc=pc, sl=sl):
            wv = wqw[:, sl, :].rearrange("p (g pl w) -> p g pl w", g=4, pl=8)
            for pl in range(8):
                wap = wv[:, :, pl, :].unsqueeze(3).broadcast_to((128, 4, 128, 2))
                P.mid(v.tensor_tensor(out=bits[:].rearrange("p (g q1 q0) -> p g q1 q0", g=4, q1=128),
                                      in0=wap,
                                      in1=shp[:].rearrange("p (g q1 q0) -> p g q1 q0", g=4, q1=128),
                                      op=ALU.logical_shift_right))
                if pl == 0:
                    P.mid(v.tensor_scalar(out=idx32[:], in0=bits[:], scalar1=1, scalar2=None,
                                          op0=ALU.bitwise_and))
                else:
                    P.mid(v.tensor_scalar(out=bits[:], in0=bits[:], scalar1=1, scalar2=pl,
                                          op0=ALU.bitwise_and, op1=ALU.logical_shift_left))
                    P.mid(v.tensor_tensor(out=idx32[:], in0=idx32[:], in1=bits[:], op=ALU.bitwise_or))
            return v.tensor_copy(out=idx16[:, sl, :], in_=idx32[:])
        idxo = A("vector", unpack_fn, deps=[ldq, shpo, prev.get(('gather_last', pc - 2))],
                 inc=("s_v", 1), tag=f"idx{pc}")
        prev[('idx', pc)] = idxo

        g_last = None
        for gl in range(4):
            b = pc * 4 + gl
            bsl = b % 2
            gdeps = [idxo, dallop, prev.get(('tr', b - 2))]

            def g_fn(g, pc=pc, sl=sl, gl=gl, bsl=bsl):
                r = None
                for ch in range(4):
                    r = g.indirect_copy(
                        out=outg[:, bsl, ch, :], data=dall[:, 4 * pc + gl, :],
                        idxs=idx16[:, sl, (gl * 256 + ch * 64):(gl * 256 + (ch + 1) * 64)],
                        i_know_ap_gather_is_preferred=True)
                return r
            gop = A("gpsimd", g_fn, deps=gdeps, inc=("s_g", 1), tag=f"g{pc}_{gl}")
            g_last = gop

            def tr_fn(te, bsl=bsl):
                r = None
                for ch in range(4):
                    for k in range(8):
                        r = te.transpose(ps_tr[:, bsl, ch * 8 + k, :],
                                         outg[:, bsl, ch, k * 128:(k + 1) * 128], idT[:])
                return r
            trop = A("tensor", tr_fn,
                     deps=[gop, ido, prev.get(('cpd', b - 2))],
                     inc=("s_mm", 1), tag=f"dtr{pc}_{gl}")
            prev[('tr', b)] = trop

            osl = pc * 32 + 8 * gl
            trsel = ps_tr[:, bsl, :, :].rearrange("p k (a b) -> p k a b", b=16)
            cpw = A("vector", lambda v, osl=osl, trsel=trsel: v.tensor_copy(
                out=wlT_sb[:, :, osl:osl + 8],
                in_=trsel[:, :, :, 0:1].rearrange("p k a o -> p k (a o)")),
                deps=[trop], inc=("s_v", 1), tag=f"cpw{pc}_{gl}")
            cpd = A("scalar", lambda sc, osl=osl, trsel=trsel: sc.copy(
                out=ddT_sb[:, :, osl:osl + 8],
                in_=trsel[:, :, :, 8:9].rearrange("p k a o -> p k (a o)")),
                deps=[trop, cpw], inc=("s_sc", 1), tag=f"cpd{pc}_{gl}")
            wl_copies.append(cpw)
            dd_copies.append(cpd)
            prev[('cpd', b)] = cpd
        prev[('gather_last', pc)] = g_last

    stw = A("sync", lambda s: s.dma_start(
        out=wl_o.rearrange("(n p) c -> p n c", p=128), in_=wlT_sb[:]),
        deps=wl_copies, inc=("s_st0", 16), tag="dqstw")
    std = A("sync", lambda s: s.dma_start(
        out=dd_o.rearrange("(n p) c -> p n c", p=128), in_=ddT_sb[:]),
        deps=dd_copies, inc=("s_st1", 16), tag="dqstd")
    tstores = [stw, std]

    if not fused:
        def fin_fn(s):
            s.wait_ge(P.sem_handles["s_st0"], 16 * 128)
            s.wait_ge(P.sem_handles["s_st1"], 16 * 128)
            return s.nop()
        A("sync", fin_fn, deps=[], tag="fin")
        return P

    ccw = A("gpsimd", lambda g: g.collective_compute(
        "AllGather", ALU.bypass, replica_groups=[list(range(8))],
        ins=[wl_t.ap().opt()], outs=[wl_all_t.ap().opt()]),
        deps=tstores, inc=("s_ccw", 1), tag="ccw")
    ccd = A("gpsimd", lambda g: g.collective_compute(
        "AllGather", ALU.bypass, replica_groups=[list(range(8))],
        ins=[dd_t.ap().opt()], outs=[dd_all_t.ap().opt()]),
        deps=tstores, inc=("s_ccd", 1), tag="ccd")
    return [ccw, ccd]



def host_prep_l1f(inp):
    return {"lf_b": np.ascontiguousarray(np.asarray(inp['L1_ffn_bias']).reshape(1, DM))}


def build_l1f(nc, x_in, x_out, es, P, uniq="", wl_all=None, dd_all=None):
    wl8 = wl_all.rearrange("(r n p) c -> p r n c", p=128, r=8)
    dd8 = dd_all.rearrange("(r n p) c -> p r n c", p=128, r=8)
    bia = nc.dram_tensor(uniq + "lf_b", [1, DM], F32, kind="ExternalInput").ap()

    T = lambda n, s, d: es.enter_context(nc.sbuf_tensor(uniq + "lfs_" + n, s, d))
    b_bf = T("b_bf", [1, DM], BF16)
    ones1 = T("ones1", [1, 128], BF16)
    wstr = T("wstr", [128, 3, 4, 1024], BF16)
    dstr = T("dstr", [128, 3, 4, 1024], BF16)
    xbuf = T("xbuf", [128, 3, DM], F32)
    u_bf = T("u_bf", [128, DM], BF16)
    u_f = T("u_f", [128, DM], F32)
    uT = T("uT", [128, 2, NI, 128], BF16)
    tmin = T("tmin", [128, 1024], F32)
    ycb = T("ycb", [128, 2, DM], F32)
    e2 = T("e2", [128, 2], F32)
    mask = T("mask", [128, 2], F32)
    red32 = T("red32", [128, 32], F32)
    ysum4 = T("ysum4", [128, 2, 4], F32)
    yssq4 = T("yssq4", [128, 2, 4], F32)
    ln_x = alloc_ln(T, "x" + uniq + "f1", NT)
    ln_y = alloc_ln(T, "y" + uniq + "f1", NT)

    ps_y = es.enter_context(nc.psum_tensor(uniq + "lf_ps_y", [128, 2, 1024], F32))
    ps_d = es.enter_context(nc.psum_tensor(uniq + "lf_ps_d", [128, 2, 1024], F32))

    A = P.add
    wload = [A("gpsimd", lambda g: g.dma_start(out=b_bf[:], in_=bia[:]), inc=("s_w", 16), tag="lfb")]
    pones = A("vector", lambda v: v.memset(ones1[:], 1.0), inc=("s_v", 1), tag="lfones")

    prev = {}
    for u in range(NT // 2):
        tiles = (2 * u, 2 * u + 1)
        utrs = [None, None]
        msks = [None, None]
        for s, t in enumerate(tiles):
            xap = xbuf[:, t % 3, :]
            deps_ldx = [] if t < 3 else [prev[(t - 3, 'sto')]]
            ldx = A("sync", lambda sy, t=t, xap=xap: sy.dma_start(
                out=xap, in_=x_in[t*128:(t+1)*128, :]), deps=deps_ldx,
                inc=(f"s_x{t % 3}", 16), tag=f"lfldx{t}")

            ctx_ln = {"u_bf": u_f, "uT": None, "sq_scr": u_bf, "red32": red32}
            uop, xsq, _ = emit_ln_input(P, A, ctx_ln, t, xap, ln_x, [ldx],
                                        prev.get((t - 1, 'utr')), "lf")
            ucast = A("vector", lambda v: v.tensor_copy(out=u_bf[:], in_=u_f[:]),
                      deps=[uop, prev.get((t - 1, 'utr'))], inc=("s_v", 1), tag=f"lfuc{t}")
            utr = emit_utr(P, A, {"u_bf": u_bf, "uT": uT[:, s, :, :]}, ucast,
                           prev.get((t - 2, 'lastmm')), f"lf{t}")
            prev[(t, 'utr')] = utr
            utrs[s] = utr

            def e2_fn(v, s=s, t=t):
                P.mid(v.tensor_scalar(out=e2[:, s:s+1], in0=ln_x["varep"][:, t:t+1],
                                      scalar1=DM, scalar2=-DM * 1e-5,
                                      op0=ALU.mult, op1=ALU.add))
                P.mid(v.tensor_tensor(out=e2[:, s:s+1], in0=e2[:, s:s+1],
                                      in1=ln_x["rstd"][:, t:t+1], op=ALU.mult))
                return v.tensor_tensor(out=e2[:, s:s+1], in0=e2[:, s:s+1],
                                       in1=ln_x["rstd"][:, t:t+1], op=ALU.mult)
            e2o = A("vector", e2_fn, deps=[uop], inc=("s_v", 1), tag=f"lfe2{t}")
            prev[(t, 'e2o')] = e2o
            msks[s] = A("gpsimd", lambda g, s=s: g.tensor_scalar(
                out=mask[:, s:s+1], in0=e2[:, s:s+1], scalar1=4096.0, scalar2=None,
                op0=ALU.is_gt), deps=[e2o], inc=("s_g", 1), tag=f"lfmsk{t}")

        cmbs = {0: [None]*4, 1: [None]*4}
        for q in range(4):
            biases = [None, None]
            lastj = [None, None]
            for s, t in enumerate(tiles):
                bdeps = [pones] + wload
                if q > 0:
                    bdeps += [prev[(t, 'yev', q-1)], prev[(t, 'cmb', q-1)]]
                elif t >= 2:
                    bdeps += [prev[(t-2, 'yev', 3)], prev[(t-2, 'cmb', 3)]]

                def bias_fn(te, q=q, s=s):
                    r = None
                    for c in range(2):
                        r = te.matmul(ps_y[:, s, c*512:(c+1)*512], ones1[:],
                                      b_bf[:, q*1024 + c*512:q*1024 + (c+1)*512],
                                      start=True, stop=False)
                    return r
                if s == 0:
                    biases[0] = A("tensor", bias_fn, deps=bdeps, inc=("s_mm", 1),
                                  tag=f"lfb{q}_{t}")
                else:
                    biases[1] = (bias_fn, bdeps)

            for j in range(8):
                gi = q * 8 + j
                G = u * 32 + gi
                wsl = G % 3
                wdeps = [prev.get(('mmb', G - 3))]
                def wld_fn(sy, q=q, j=j, wsl=wsl):
                    sy.dma_start(out=wstr[:, wsl, :, 0:512],
                                 in_=wl8[:, 2*q, 4*j:4*j+4, :]
                                 ).then_inc(P.sem_handles[f"s_wl{wsl}"], 16)
                    sy.dma_start(out=wstr[:, wsl, :, 512:1024],
                                 in_=wl8[:, 2*q+1, 4*j:4*j+4, :]
                                 ).then_inc(P.sem_handles[f"s_wl{wsl}"], 16)
                    return "self"
                wld = A("sync", wld_fn, deps=wdeps, inc=(f"s_wl{wsl}", 32),
                        tag=f"lfwl{q}_{j}_{u}", self_inc=True)

                def dld_fn(sc, q=q, j=j, wsl=wsl):
                    sc.dma_start(out=dstr[:, wsl, :, 0:512],
                                 in_=dd8[:, 2*q, 4*j:4*j+4, :]
                                 ).then_inc(P.sem_handles[f"s_dl{wsl}"], 16)
                    sc.dma_start(out=dstr[:, wsl, :, 512:1024],
                                 in_=dd8[:, 2*q+1, 4*j:4*j+4, :]
                                 ).then_inc(P.sem_handles[f"s_dl{wsl}"], 16)
                    return "self"
                dld = A("scalar", dld_fn, deps=wdeps, inc=(f"s_dl{wsl}", 32),
                        tag=f"lfdl{q}_{j}_{u}", self_inc=True)

                for s, t in enumerate(tiles):
                    def g_fn(te, q=q, j=j, wsl=wsl, s=s):
                        r = None
                        for ii in range(4):
                            for c in range(2):
                                te.matmul(ps_y[:, s, c*512:(c+1)*512],
                                          uT[:, s, 4*j + ii, :],
                                          wstr[:, wsl, ii, c*512:(c+1)*512],
                                          start=False, stop=(j == 7 and ii == 3))
                                r = te.matmul(ps_d[:, s, c*512:(c+1)*512],
                                              uT[:, s, 4*j + ii, :],
                                              dstr[:, wsl, ii, c*512:(c+1)*512],
                                              start=(j == 0 and ii == 0),
                                              stop=(j == 7 and ii == 3))
                        return r
                    gdeps = [wld, dld, utrs[s], biases[s] if j == 0 and s == 0 else None]
                    if j == 0 and s == 0:
                        pass
                    gop = A("tensor", g_fn, deps=[d for d in gdeps if d is not None],
                            inc=("s_mm", 1), tag=f"lfg{q}_{j}_{s}_{u}")
                    lastj[s] = gop
                    if s == 1:
                        prev[('mmb', G)] = gop
                    if j == 0 and s == 0 and isinstance(biases[1], tuple):
                        bias_fn_b, bdeps_b = biases[1]
                        biases[1] = A("tensor", bias_fn_b, deps=bdeps_b,
                                      inc=("s_mm", 1), tag=f"lfbB{q}_{u}")

            for s, t in enumerate(tiles):
                lastmm = lastj[s]
                if q == 3:
                    prev[(t, 'lastmm')] = lastmm
                ydeps = [lastmm]
                pq = prev.get((t - 2, 'qr'))
                if pq is not None:
                    ydeps += pq.get(q, [])

                def yev_fn(sc, q=q, s=s):
                    r = None
                    for c2 in range(2):
                        r = sc.copy(out=ycb[:, s, q*1024 + c2*512:q*1024 + (c2+1)*512],
                                    in_=ps_y[:, s, c2*512:(c2+1)*512])
                    return r
                yev = A("scalar", yev_fn, deps=ydeps, inc=("s_sc", 1), tag=f"lfyev{q}_{t}")
                cmb = A("vector", lambda v, q=q, s=s: v.scalar_tensor_tensor(
                    out=ycb[:, s, q*1024:(q+1)*1024], in0=ps_d[:, s, :],
                    scalar=mask[:, s:s+1], in1=ycb[:, s, q*1024:(q+1)*1024],
                    op0=ALU.mult, op1=ALU.add),
                    deps=[yev, msks[s]], inc=("s_v", 1), tag=f"lfcmb{q}_{t}")
                prev[(t, 'yev', q)] = yev
                prev[(t, 'cmb', q)] = cmb
                cmbs[s][q] = cmb

        for s, t in enumerate(tiles):
            ctx_ep = {"red32": red32, "tmin": tmin,
                      "ysum4": ysum4[:, s, :], "yssq4": yssq4[:, s, :]}
            pq = prev.get((t - 2, 'qr'))
            yst0_extra = pq['ysum'][0] if pq is not None else None
            sto, q_readers, _ = emit_quarter_stats_epilogue(
                P, A, ctx_ep, t, xbuf[:, t % 3, :], ln_y,
                lambda q, extra, s=s: cmbs[s][q], None, True,
                x_out[t*128:(t+1)*128, :], f"lf{t}_", ysrc_sbuf=ycb[:, s, :],
                yst0_extra=yst0_extra)
            prev[(t, 'qr')] = q_readers
            prev[(t, 'sto')] = sto

    return [prev[(t, 'sto')] for t in range(NT)]


def kernel(**inputs):
    inp = inputs
    x = np.ascontiguousarray(np.asarray(inp['L0_attn_hidden_states'],
                                        dtype=np.float32).reshape(16384, DM))

    nc = bass.Bass("TRN2", target_bir_lowering=False)
    x0 = nc.dram_tensor("x_in", [TOK, DM], F32, kind="ExternalInput").ap()
    x1 = nc.dram_tensor("x1", [TOK, DM], F32, kind="Internal").ap()
    x2 = nc.dram_tensor("x2", [TOK, DM], F32, kind="Internal").ap()
    x3 = nc.dram_tensor("x3", [TOK, DM], F32, kind="Internal").ap()
    xo = nc.dram_tensor("x_out", [TOK, DM], F32, kind="ExternalOutput").ap()
    wl_all_t = nc.dram_tensor("dq_wl_all", [8 * DM, 512], BF16, kind="Internal")
    dd_all_t = nc.dram_tensor("dq_dd_all", [8 * DM, 512], BF16, kind="Internal")

    es = contextlib.ExitStack()
    with es:
        P = Prog(nc)
        with contextlib.ExitStack() as es_p:
            term = build_dequant(nc, es_p, P, wl_all_t=wl_all_t, dd_all_t=dd_all_t)
        P.set_barrier(term, "barD")
        with contextlib.ExitStack() as es_p:
            term = build_l0a(nc, x0, x1, es_p, P)
        P.set_barrier(term, "barA")
        with contextlib.ExitStack() as es_p:
            term = build_l0f(nc, x1, x2, es_p, P)
        P.set_barrier(term, "barF")
        with contextlib.ExitStack() as es_p:
            term = build_l1a(nc, x2, x3, es_p, P)
        P.set_barrier(term, "barB")
        with contextlib.ExitStack() as es_p:
            term = build_l1f(nc, x3, xo, es_p, P,
                             wl_all=wl_all_t.ap(), dd_all=dd_all_t.ap())
        P.add("sync", lambda s: s.nop(), deps=term, inc=("s_fin", 1), tag="fin")
        P.emit(es)

    w = {}
    w.update(host_prep_l0a(inp))
    w.update(host_prep_l0f(inp))
    w.update(host_prep_l1a(inp))
    w.update(host_prep_l1f(inp))
    in_maps = []
    for c in range(8):
        m = dict(w)
        m["x_in"] = np.ascontiguousarray(x[c*TOK:(c+1)*TOK])
        m.update(host_prep_dequant(inp, c))
        in_maps.append(m)
    res = run_bass_kernel_spmd(nc, in_maps, core_ids=list(range(8)))
    xcat = np.concatenate([np.asarray(r["x_out"]) for r in res.results], axis=0)
    return np.ascontiguousarray(xcat.reshape(4, 4096, DM)).astype(np.float32)
